# version 03
# baseline (speedup 1.0000x reference)
from collections import defaultdict

ENGINES = ("sync", "vector", "scalar", "tensor", "gpsimd")
N_DMA_SEMS = 40
SLOT_RANGES = {"sync": (0, 20), "gpsimd": (20, 32), "scalar": (32, 36),
               "vector": (36, 40), "tensor": (36, 40)}


class _Op:
    __slots__ = ("stream", "fn", "reads", "writes", "idx", "tag",
                 "is_dma", "group")

    def __init__(self, stream, fn, reads, writes, tag, is_dma=False, group=None):
        self.stream = stream
        self.fn = fn
        self.reads = reads
        self.writes = writes
        self.tag = tag
        self.idx = None
        self.is_dma = is_dma
        self.group = group


class DmaGroup:
    __slots__ = ("sem_slot", "base", "count", "gid", "prev_group")

    def __init__(self, gid, sem_slot, base, prev_group):
        self.gid = gid
        self.sem_slot = sem_slot
        self.base = base
        self.count = 0
        self.prev_group = prev_group

    @property
    def total(self):
        return self.base + self.count


class DepCtx:
    def __init__(self, nc, es):
        self.nc = nc
        self.es = es
        self.ops = []
        self.per_engine = defaultdict(list)
        self.stream_counts = defaultdict(int)
        self.groups = []
        self._next_slot = {e: SLOT_RANGES[e][0] for e in SLOT_RANGES}
        self._slot_total = [0] * N_DMA_SEMS
        self._slot_last_group = [None] * N_DMA_SEMS

    def group(self, engine="sync"):
        lo, hi = SLOT_RANGES[engine]
        slot = self._next_slot[engine]
        nxt = slot + 1
        if nxt >= hi:
            nxt = lo
        self._next_slot[engine] = nxt
        g = DmaGroup(len(self.groups), slot, self._slot_total[slot],
                     self._slot_last_group[slot])
        self._slot_last_group[slot] = g
        self.groups.append(g)
        return g

    def op(self, engine, fn, reads=(), writes=(), tag=""):
        o = _Op(engine, fn, list(reads), list(writes), tag)
        self._record(o)
        return o

    def dma(self, engine, fn, reads=(), writes=(), tag="", group=None):
        assert group is not None, "dma needs a group"
        lo, hi = SLOT_RANGES[engine]
        assert lo <= group.sem_slot < hi, \
            f"dma group slot {group.sem_slot} not in {engine} range (pass engine to group())"
        o = _Op(engine, fn, list(reads), list(writes), tag, is_dma=True, group=group)
        group.count += 1
        self._slot_total[group.sem_slot] += 1
        self._record(o)
        return o

    def _record(self, o):
        self.ops.append(o)
        self.per_engine[o.stream].append(o)
        if not o.is_dma:
            self.stream_counts[o.stream] += 1
            o.idx = self.stream_counts[o.stream]
        else:
            o.idx = None

    def emit(self, block):
        def tok(o):
            if o.is_dma:
                return ("dmag%d" % o.group.sem_slot, o.group)
            return (o.stream, o.idx)

        mem = {}
        deps = {}
        seen_groups = set()
        pos = {}
        for i, o in enumerate(self.ops):
            pos[id(o)] = i
        stream_pos = defaultdict(dict)
        for o in self.ops:
            need = {}
            own_group = o.group if o.is_dma else None

            def add(t):
                if t is None:
                    return
                s, v = t
                if own_group is not None and s.startswith("dmag") and v is own_group:
                    return
                cur = need.get(s)
                if cur is None:
                    need[s] = v
                else:
                    if s.startswith("dmag"):
                        if v.total > cur.total:
                            need[s] = v
                    elif v > cur:
                        need[s] = v
            for k in o.reads:
                st = mem.get(k)
                if st is not None and st[0] is not None:
                    add(st[0])
            for k in o.writes:
                st = mem.get(k)
                if st is not None:
                    if st[0] is not None:
                        add(st[0])
                    for r in st[1]:
                        add(r)
            if o.is_dma and o.group.gid not in seen_groups:
                seen_groups.add(o.group.gid)
                prev = o.group.prev_group
                if prev is not None:
                    add(("dmag%d" % o.group.sem_slot, prev))
            if o.stream == "tensor" and not o.is_dma:
                need.pop("tensor", None)
            deps[id(o)] = need
            t = tok(o)
            if o.is_dma:
                stream_pos[t[0]][id(t[1])] = pos[id(o)]
            else:
                stream_pos[o.stream][o.idx] = pos[id(o)]
            for k in o.reads:
                st = mem.setdefault(k, [None, []])
                st[1].append(t)
            for k in o.writes:
                mem[k] = [t, []]

        for o in self.ops:
            need = deps[id(o)]
            for s in list(need):
                if s.startswith("dmag"):
                    g = need[s]
                    gp = stream_pos[s].get(id(g), -1)
                    assert gp <= pos[id(o)], \
                        f"FORWARD dma-group ref: op@{pos[id(o)]} tag={o.tag} waits group gid={g.gid} slot={g.sem_slot} last-dma@{gp}"
                    need[s] = g.total
                else:
                    vp = stream_pos[s].get(need[s], None)
                    assert vp is not None and vp <= pos[id(o)], \
                        f"FORWARD ref: op@{pos[id(o)]} tag={o.tag} stream {o.stream} waits {s}>={need[s]} recorded@{vp}"


        self.engine_sems = {}
        for e in ENGINES:
            if self.per_engine[e]:
                self.engine_sems[e] = self.es.enter_context(
                    self.nc.semaphore(f"esem_{e}"))
        used_slots = sorted({g.sem_slot for g in self.groups if g.count})
        self.dma_sems = {}
        for slot in used_slots:
            self.dma_sems[slot] = self.es.enter_context(
                self.nc.semaphore(f"dsem{slot}"))

        snap = defaultdict(dict)
        knows = {e: defaultdict(int) for e in ENGINES}
        plans = defaultdict(list)
        for o in self.ops:
            e = o.stream
            kn = knows[e]
            for s, v in sorted(deps[id(o)].items()):
                if kn[s] >= v:
                    continue
                plans[e].append(("wait", s, v))
                sn = snap[s].get(v)
                if sn:
                    for s2, v2 in sn.items():
                        if v2 > kn[s2]:
                            kn[s2] = v2
                kn[s] = max(kn[s], v)
            plans[e].append(("op", o))
            if o.is_dma:
                key = o.group.total
                s = "dmag%d" % o.group.sem_slot
                sn = snap[s].get(key)
                if sn is None:
                    sn = dict(kn)
                else:
                    for s2, v2 in kn.items():
                        pass
                    newsn = {}
                    for s2 in set(sn) & set(kn):
                        newsn[s2] = min(sn[s2], kn[s2])
                    sn = newsn
                sn[s] = max(sn.get(s, 0), key)
                snap[s][key] = sn
            else:
                sn = dict(kn)
                sn[e] = o.idx
                snap[e][o.idx] = sn

        self._plans = plans
        for e in ENGINES:
            plan = plans[e]
            if not plan:
                continue
            section = getattr(block, e)

            def make(plan=plan, e=e):
                def body(eng):
                    for item in plan:
                        if item[0] == "wait":
                            _, s, v = item
                            if s.startswith("dmag"):
                                eng.wait_ge(self.dma_sems[int(s[4:])], 16 * v)
                            else:
                                eng.wait_ge(self.engine_sems[s], v)
                        else:
                            o = item[1]
                            instr = o.fn(eng)
                            if o.is_dma:
                                instr.then_inc(self.dma_sems[o.group.sem_slot], 16)
                            else:
                                instr.then_inc(self.engine_sems[e], 1)
                return body
            section(make())


def sbuf(nc, es, name, shape, dtype):
    return es.enter_context(nc.sbuf_tensor(name, shape, dtype))


def psum(nc, es, name, shape, dtype):
    return es.enter_context(nc.psum_tensor(name, shape, dtype))


"""Block emitters for the megakernel. All layouts feature-major:
stream xf (fp32) / xb (bf16) as (128, 8*2048): chunk c holds channels
[128c, 128c+128), free = 2048 tokens.
"""
import numpy as np
import concourse.bass as bass
import concourse.mybir as mybir

F32 = mybir.dt.float32
BF16 = mybir.dt.bfloat16
AF = mybir.ActivationFunctionType
ALU = mybir.AluOpType

NT = 2048
TB = 512
NTB = NT // TB


def ap3(t, dims, offset=0):
    a = t[:, :] if not isinstance(t, bass.AP) else t
    return bass.AP(a.tensor, a.offset + offset, [list(d) for d in dims])


class KCtx:
    def __init__(self, nc, es):
        self.nc, self.es = nc, es
        self.ctx = DepCtx(nc, es)
        self.tiles = {}
        self.psums = {}
        self.sbuf_bytes = 0
        self.pb = []
        for i in range(7):
            h = es.enter_context(nc.psum_tensor(f"pb{i}", [128, 512], F32))
            self.pb.append(h)
        self.pbB = es.enter_context(nc.psum_tensor("pbB", [128, 1024], BF16))
        self._ps_rr = 0

    def t(self, name, shape=None, dtype=None):
        if name in self.tiles and shape is not None:
            assert list(self.tiles[name].shape) == list(shape), \
                f"tile {name} shape mismatch {self.tiles[name].shape} vs {shape}"
        if name not in self.tiles:
            assert shape is not None, f"tile {name} not allocated yet"
            cm = self.nc.sbuf_tensor(name, list(shape), dtype)
            h = cm.__enter__()
            self.tiles[name] = h
            self._cms = getattr(self, "_cms", {})
            self._cms[name] = cm
            per = int(np.prod(shape[1:])) * mybir.dt.size(dtype)
            self.sbuf_bytes += per * 128
        return self.tiles[name]

    def free_tile(self, name):
        cm = self._cms.pop(name)
        cm.__exit__(None, None, None)
        del self.tiles[name]
        self._retired = getattr(self, "_retired", [])
        self._retired.append(name)

    def alias_fence(self, newkeys):
        old = getattr(self, "_retired", [])
        if old:
            self.ctx.op("sync", lambda e: e.nop(), list(old), list(newkeys))
            self._retired = []

    def bank(self):
        for _ in range(7):
            i = self._ps_rr
            self._ps_rr = (self._ps_rr + 1) % 7
            if i not in getattr(self, "_reserved", set()):
                return self.pb[i], f"pb{i}"
        raise RuntimeError("no free psum bank")

    def bankB(self):
        return self.pbB, "pbB"

    def take_bank(self):
        if not hasattr(self, "_reserved"):
            self._reserved = set()
        for i in range(7):
            if i not in self._reserved:
                self._reserved.add(i)
                return self.pb[i], f"pb{i}", i
        raise RuntimeError("all banks reserved")

    def free_bank(self, i):
        self._reserved.discard(i)

    def v(self, fn, r, w, tag=""):
        return self.ctx.op("vector", fn, r, w, tag)

    def s(self, fn, r, w, tag=""):
        return self.ctx.op("scalar", fn, r, w, tag)

    def g(self, fn, r, w, tag=""):
        return self.ctx.op("gpsimd", fn, r, w, tag)

    def te(self, fn, r, w, tag=""):
        return self.ctx.op("tensor", fn, r, w, tag)

    def dma(self, eng, fn, r, w, group, tag=""):
        return self.ctx.dma(eng, fn, r, w, tag, group=group)

    def group(self, engine="sync"):
        return self.ctx.group(engine)

    def constb(self, val):
        name = "cb_%x" % (np.float32(val).view(np.uint32))
        if name not in self.tiles:
            t = self.t(name, (128, 1), BF16)
            self.v(lambda e, t=t, val=val: e.memset(t[:, :], float(val)), [], [name])
        return self.tiles[name], name

    def ones_row(self, dtype="f32"):
        name = "c_ones_row" if dtype == "f32" else "c_ones_rowb"
        if name not in self.tiles:
            t = self.t(name, (1, 128), F32 if dtype == "f32" else BF16)
            self.v(lambda e, t=t: e.memset(t[:, :], 1.0), [], [name])
        return self.tiles[name], name

    def constf(self, val):
        name = "cf_%x" % (np.float32(val).view(np.uint32))
        if name not in self.tiles:
            t = self.t(name, (128, 1), F32)
            self.v(lambda e, t=t, val=val: e.memset(t[:, :], float(val)), [], [name])
        return self.tiles[name], name


def load_const(k, name, hbm_ap, shape, dtype, g=None):
    t = k.t("c_" + name, shape, dtype)
    g = g or k.group()
    k.dma("sync", lambda e: e.dma_start(out=t[:, :], in_=hbm_ap), [], ["c_" + name], g)
    return t


def gemm_fm(k, name, wap, cin, cout, xtile, evac, wq="sync", cast_eng="scalar",
            tokb=TB, ntok=NT, cin_off=0):
    nci = (cin + 127) // 128
    nco = (cout + 127) // 128
    ntb = ntok // tokb
    SLAB = 4
    nslab = (nci + SLAB - 1) // SLAB
    fresh = "gw_stg0" not in k.tiles
    wstg0 = k.t("gw_stg0", (128, SLAB * 128), F32)
    if fresh:
        k.v(lambda e: e.memset(wstg0[:, :], 0.0), [], ["gw_stg0"])
    wstg = [wstg0, wstg0]
    wbf0 = k.t("gw_bf0", (128, SLAB * 128), BF16)
    wbf = [wbf0, wbf0]
    par = 0
    for o in range(nco):
        ocols = min(128, cout - o * 128)
        for sl in range(nslab):
            c0 = sl * SLAB
            ncs = min(SLAB, nci - c0)
            stg, sbf = wstg[par], wbf[par]
            kst, kbf = "gw_stg0", "gw_bf0"
            prows = 128
            if c0 + ncs == nci and cin % 128:
                assert ncs == 1, "partial cin chunk must be alone in its slab"
                prows = cin % 128
            gld = k.group()
            src = bass.AP(wap.tensor,
                          wap.offset + (cin_off + c0 * 128) * cout + o * 128,
                          [[cout, prows], [128 * cout, ncs], [1, ocols]])
            dst = ap3(stg, [[SLAB * 128, prows], [128, ncs], [1, ocols]])
            k.dma(wq, lambda e, src=src, dst=dst: e.dma_start(out=dst, in_=src),
                  [], [kst], gld, tag=f"{name}_w{o}_{sl}")
            if ocols == 128:
                cin_ap = lambda stg=stg, n=ncs * 128, prows=prows: stg[0:prows, 0:n]
                cout_ap = lambda sbf=sbf, n=ncs * 128, prows=prows: sbf[0:prows, 0:n]
            else:
                cin_ap = lambda stg=stg, ncs=ncs, ocols=ocols, prows=prows: ap3(
                    stg, [[SLAB * 128, prows], [128, ncs], [1, ocols]])
                cout_ap = lambda sbf=sbf, ncs=ncs, ocols=ocols, prows=prows: ap3(
                    sbf, [[SLAB * 128, prows], [128, ncs], [1, ocols]])
            if cast_eng == "scalar":
                k.s(lambda e, cin_ap=cin_ap, cout_ap=cout_ap: e.activation(
                    out=cout_ap(), in_=cin_ap(), func=AF.Copy), [kst], [kbf])
            else:
                k.v(lambda e, cin_ap=cin_ap, cout_ap=cout_ap: e.tensor_copy(
                    out=cout_ap(), in_=cin_ap()), [kst], [kbf])
            if nslab == 1:
                for tb in range(ntb):
                    ps, pskey = k.bank()
                    for c in range(ncs):
                        xh, xk, xoff = xtile(c0 + c)
                        k.te(lambda e, ps=ps, sbf=sbf, c=c, xh=xh, xoff=xoff, tb=tb,
                             ocols=ocols, prows=prows,
                             first=(c == 0), last=(c == ncs - 1):
                             e.matmul(out=ps[0:ocols, 0:tokb],
                                      lhsT=sbf[0:prows, c * 128:c * 128 + ocols],
                                      rhs=xh[0:prows, xoff + tb * tokb: xoff + (tb + 1) * tokb],
                                      start=first, stop=last),
                             [kbf, xk], [pskey], tag=f"{name}_mm")
                    evac(o, tb, ps, pskey)
            else:
                if sl == 0:
                    held = [k.take_bank() for _ in range(ntb)]
                    k._gemm_held = held
                held = k._gemm_held
                for tb in range(ntb):
                    ps, pskey, _ = held[tb]
                    for c in range(ncs):
                        xh, xk, xoff = xtile(c0 + c)
                        k.te(lambda e, ps=ps, sbf=sbf, c=c, xh=xh, xoff=xoff, tb=tb,
                             ocols=ocols, first=(sl == 0 and c == 0),
                             last=(sl == nslab - 1 and c == ncs - 1):
                             e.matmul(out=ps[0:ocols, 0:tokb],
                                      lhsT=sbf[:, c * 128:c * 128 + ocols],
                                      rhs=xh[:, xoff + tb * tokb: xoff + (tb + 1) * tokb],
                                      start=first, stop=last),
                             [kbf, xk], [pskey], tag=f"{name}_mm")
                if sl == nslab - 1:
                    for tb in range(ntb):
                        ps, pskey, bi = held[tb]
                        evac(o, tb, ps, pskey)
                        k.free_bank(bi)
            par ^= 1


def bcast_row(k, name, row_t, row_key, row_off, dst, dst_key, dst_off, ones1,
              ntok=NT, scalemul=None):
    for tb in range(ntok // TB):
        ps, pskey = k.bank()
        k.te(lambda e, ps=ps, tb=tb: e.matmul(
            out=ps[:, 0:TB], lhsT=ones1[0:1, 0:128],
            rhs=row_t[0:1, row_off + tb * TB: row_off + (tb + 1) * TB],
            start=True, stop=True), ["c_ones_row", row_key], [pskey])
        k.v(lambda e, ps=ps, tb=tb: e.tensor_copy(
            out=dst[:, dst_off + tb * TB: dst_off + (tb + 1) * TB], in_=ps[:, 0:TB]),
            [pskey], [dst_key])


def ln_fm(k, tag, gamma=None, beta=None, nchunks=8, eps=1e-5, dinv=None, rows_t=None):
    xf = k.tiles["xf"]
    xb = k.tiles["xb"]
    sq = k.t("ln_sqb", (128, TB), BF16)
    rows = rows_t if rows_t is not None else k.t("ln_rows", (1, 2 * TB), BF16)
    onesN, onesNk = k.constf(1.0 / (nchunks * 128.0))
    onesNb, onesNbk = k.constb(1.0 / (nchunks * 128.0))
    ones1, ones1k = k.ones_row()
    for tb in range(NTB):
        psm, kpm = k.bank()
        psv, kpv = k.bank()
        for c in range(nchunks):
            k.te(lambda e, psm=psm, c=c, tb=tb: e.matmul(
                out=psm[0:1, 0:TB], lhsT=onesN[:, 0:1],
                rhs=xf[:, c * NT + tb * TB: c * NT + tb * TB + TB],
                start=(c == 0), stop=(c == nchunks - 1)),
                [onesNk, "xf"], [kpm])
        for c in range(nchunks):
            k.s(lambda e, c=c, tb=tb: e.activation(
                out=sq[:, 0:TB],
                in_=xf[:, c * NT + tb * TB: c * NT + tb * TB + TB],
                func=AF.Square), ["xf"], ["ln_sqb"], tag=tag + "_sq")
            k.te(lambda e, psv=psv, c=c: e.matmul(
                out=psv[0:1, 0:TB], lhsT=onesNb[:, 0:1],
                rhs=sq[:, 0:TB],
                start=(c == 0), stop=(c == nchunks - 1)),
                [onesNbk, "ln_sqb"], [kpv])
        k.v(lambda e, psm=psm: e.tensor_copy(out=rows[0:1, 0:TB], in_=psm[0:1, 0:TB]),
            [kpm], ["ln_rows"])
        k.s(lambda e: e.activation(out=rows[0:1, TB:2 * TB], in_=rows[0:1, 0:TB],
                                   func=AF.Square), ["ln_rows"], ["ln_rows"])
        k.v(lambda e, psv=psv: e.tensor_sub(out=rows[0:1, TB:2 * TB],
                                            in0=psv[0:1, 0:TB],
                                            in1=rows[0:1, TB:2 * TB]),
            [kpv, "ln_rows"], ["ln_rows"])
        epst, epsk = k.constf(eps)
        k.s(lambda e, epst=epst: e.activation(out=rows[0:1, TB:2 * TB],
                                   in_=rows[0:1, TB:2 * TB],
                                   func=AF.Ln, bias=epst[0:1, 0:1]),
            ["ln_rows", epsk], ["ln_rows"])
        k.s(lambda e: e.activation(out=rows[0:1, TB:2 * TB],
                                   in_=rows[0:1, TB:2 * TB],
                                   func=AF.Exp, scale=-0.5), ["ln_rows"], ["ln_rows"])
        pmb, kmb = k.bank()
        prb, krb = k.bank()
        ones1b, ones1bk = k.ones_row("bf16")
        k.te(lambda e, pmb=pmb, ones1b=ones1b: e.matmul(
            out=pmb[:, 0:TB], lhsT=ones1b[0:1, 0:128],
            rhs=rows[0:1, 0:TB], start=True, stop=True),
             [ones1bk, "ln_rows"], [kmb])
        k.te(lambda e, prb=prb, ones1b=ones1b: e.matmul(
            out=prb[:, 0:TB], lhsT=ones1b[0:1, 0:128],
            rhs=rows[0:1, TB:2 * TB], start=True, stop=True),
             [ones1bk, "ln_rows"], [krb])
        for c in range(nchunks):
            sl = slice(c * NT + tb * TB, c * NT + tb * TB + TB)
            k.v(lambda e, sl=sl, pmb=pmb: e.tensor_sub(
                out=xf[:, sl], in0=xf[:, sl], in1=pmb[:, 0:TB]),
                ["xf", kmb], ["xf"])
            k.v(lambda e, sl=sl, prb=prb: e.tensor_mul(
                out=xf[:, sl], in0=xf[:, sl], in1=prb[:, 0:TB]),
                ["xf", krb], ["xf"])
            if gamma is not None:
                gt, gk, goff = gamma
                bt, bk, boff = beta
                k.v(lambda e, sl=sl, c=c: e.tensor_scalar(
                    out=xf[:, sl], in0=xf[:, sl],
                    scalar1=gt[:, goff + c:goff + c + 1],
                    scalar2=bt[:, boff + c:boff + c + 1],
                    op0=ALU.mult, op1=ALU.add), ["xf", gk, bk], ["xf"])
            k.s(lambda e, sl=sl: e.activation(out=xb[:, sl], in_=xf[:, sl],
                                              func=AF.Copy),
                ["xf"], ["xb"], tag=tag + "_cast")


def xb_tile(k):
    xb = k.tiles["xb"]
    return lambda c: (xb, "xb", c * NT)


class TView:
    def __init__(self, h, off, size):
        self.h, self.off, self.size = h, off, size
        self.dtype = h.dtype

    def __getitem__(self, idx):
        p, f = idx
        st = self.off if f.start is None else self.off + f.start
        sp = self.off + self.size if f.stop is None else self.off + f.stop
        return self.h[p, st:sp]


BBREG = 8 * NT


def bb(k, idx, init=False):
    nbb = getattr(k, "nbb", 3)
    t = k.t("BBIG", (128, nbb * BBREG), BF16)
    name = f"R{idx}"
    v = TView(t, idx * BBREG, BBREG)
    if init:
        kk = ("bbinit", idx)
        if kk not in getattr(k, "_done", set()):
            k._done = getattr(k, "_done", set())
            k._done.add(kk)
            k.v(lambda e, v=v: e.memset(v[:, slice(None, None)], 0.0), [], [name])
    return v, name


def bb2(k, idx0=0, idx1=1):
    nbb = getattr(k, "nbb", 3)
    t = k.t("BBIG", (128, nbb * BBREG), BF16)
    assert idx1 == idx0 + 1
    return TView(t, idx0 * BBREG, 2 * BBREG), (f"R{idx0}", f"R{idx1}")


def emit_l1_ffn(k, W):
    xf, xb = k.tiles["xf"], k.tiles["xb"]
    load_const(k, "l1f_mur", W["l1f_mur"], (128, 8), F32)
    load_const(k, "l1f_muk", W["l1f_muk"], (128, 8), F32)
    b0, kb0 = bb(k, 0)
    b1, kb1 = bb(k, 1)
    xsh = k.t("sc_bf_a", (128, NT), BF16)
    dsc = k.t("sc_bf_b", (128, NT), BF16)
    relu_s = k.t("sc_f_a", (128, TB), F32)
    prod = k.t("sc_f_b", (128, TB), F32)
    mur = k.tiles["c_l1f_mur"]
    muk = k.tiles["c_l1f_muk"]
    for c in range(8):
        sl = slice(c * NT, (c + 1) * NT)
        k.v(lambda e: e.memset(xsh[:, 0:1], 0.0), [], ["sc_bf_a"])
        k.v(lambda e, c=c: e.tensor_copy(out=xsh[:, 1:NT],
                                         in_=xf[:, c * NT: c * NT + NT - 1]),
            ["xf"], ["sc_bf_a"])
        k.v(lambda e, sl=sl: e.tensor_sub(out=dsc[:, :], in0=xf[:, sl], in1=xsh[:, :]),
            ["xf", "sc_bf_a"], ["sc_bf_b"])
        k.v(lambda e, sl=sl, c=c: e.scalar_tensor_tensor(
            out=b0[:, sl], in0=dsc[:, :], scalar=muk[:, c:c + 1], in1=xsh[:, :],
            op0=ALU.mult, op1=ALU.add), ["sc_bf_b", "c_l1f_muk", "sc_bf_a"], [kb0])
        k.v(lambda e, sl=sl, c=c: e.scalar_tensor_tensor(
            out=xb[:, sl], in0=dsc[:, :], scalar=mur[:, c:c + 1], in1=xsh[:, :],
            op0=ALU.mult, op1=ALU.add), ["sc_bf_b", "c_l1f_mur", "sc_bf_a"], ["xb"])

    def evac_r(o, tb, ps, pskey):
        k.s(lambda e, o=o, tb=tb, ps=ps: e.activation(
            out=b1[:, o * NT + tb * TB: o * NT + (tb + 1) * TB],
            in_=ps[:, 0:TB], func=AF.Sigmoid), [pskey], [kb1])

    gemm_fm(k, "l1f_r", W["l1f_wr"], 1024, 1024,
            lambda c: (xb, "xb", c * NT), evac_r)

    def evac_k(o, tb, ps, pskey):
        k.v(lambda e, ps=ps: e.tensor_relu(out=relu_s[:, 0:TB], in_=ps[:, 0:TB]),
            [pskey], ["sc_f_a"])
        k.s(lambda e, o=o, tb=tb: e.activation(
            out=xb[:, o * NT + tb * TB: o * NT + (tb + 1) * TB],
            in_=relu_s[:, 0:TB], func=AF.Square), ["sc_f_a"], ["xb"])

    gemm_fm(k, "l1f_k", W["l1f_wk"], 1024, 1024,
            lambda c: (b0, kb0, c * NT), evac_k)

    def evac_v(o, tb, ps, pskey):
        sl = slice(o * NT + tb * TB, o * NT + (tb + 1) * TB)
        k.v(lambda e, ps=ps, sl=sl: e.tensor_mul(out=prod[:, 0:TB], in0=ps[:, 0:TB],
                                                 in1=b1[:, sl]),
            [pskey, kb1], ["sc_f_b"])
        k.v(lambda e, sl=sl: e.tensor_add(out=xf[:, sl], in0=xf[:, sl],
                                          in1=prod[:, 0:TB]),
            ["xf", "sc_f_b"], ["xf"])

    gemm_fm(k, "l1f_v", W["l1f_wv"], 1024, 1024,
            lambda c: (xb, "xb", c * NT), evac_v)

    ln_fm(k, "l1f_ln")


def get_ident(k, W, dtype="f32"):
    if dtype == "f32":
        name, hname = "c_ident_f", "cst_ident_f"
        dt = F32
    else:
        name, hname = "c_ident_b", "cst_ident_b"
        dt = BF16
    if name not in k.tiles:
        t = k.t(name, (128, 128), dt)
        g = k.group()
        k.dma("sync", lambda e, t=t: e.dma_start(out=t[:, :], in_=W[hname]),
              [], [name], g)
    return k.tiles[name], name


def emit_l3_attn(k, W):
    xf, xb = k.tiles["xf"], k.tiles["xb"]
    identf, identfk = get_ident(k, W, "f32")
    qb_, kqb = bb(k, 0)
    ob_, kob = bb(k, 1, init=True)
    ktil = k.t("l3a_k", (128, NT), BF16)
    vtil = k.t("l3a_v", (64, NT), F32)
    vtt = k.t("l3a_vtt", (128, 16 * 65), BF16)
    lrow = k.t("sc_row", (1, TB), BF16)
    rbc = k.t("sc_rbc", (128, TB), BF16)
    ptb = k.t("l3a_pt", (128, TB), BF16)
    sst = k.t("l3a_ss", (128, 1), F32)
    g = k.group()
    src = bass.AP(W["l3a_scale"].tensor, W["l3a_scale"].offset, [[0, 128], [1, 1]])
    k.dma("sync", lambda e: e.dma_start(out=sst[:, :], in_=src), [], ["l3a_ss"], g)

    def evac_q(o, tb, ps, pskey):
        k.s(lambda e, o=o, tb=tb, ps=ps: e.activation(
            out=qb_[:, o * NT + tb * TB: o * NT + (tb + 1) * TB],
            in_=ps[:, 0:TB], func=AF.Copy, scale=sst[:, 0:1]),
            [pskey, "l3a_ss"], [kqb])

    gemm_fm(k, "l3a_q", W["l3a_wq"], 1024, 1024, xb_tile(k), evac_q)

    def evac_k(o, tb, ps, pskey):
        k.s(lambda e, tb=tb, ps=ps: e.activation(
            out=ktil[0:64, tb * TB:(tb + 1) * TB], in_=ps[0:64, 0:TB],
            func=AF.Copy), [pskey], ["l3a_k"])

    gemm_fm(k, "l3a_kk", W["l3a_wk"], 1024, 64, xb_tile(k), evac_k)
    k.v(lambda e: e.tensor_copy(out=ktil[64:128, :], in_=ktil[0:64, :]),
        ["l3a_k"], ["l3a_k"])

    def evac_v(o, tb, ps, pskey):
        k.v(lambda e, tb=tb, ps=ps: e.tensor_copy(
            out=vtil[0:64, tb * TB:(tb + 1) * TB], in_=ps[0:64, 0:TB]),
            [pskey], ["l3a_v"])

    gemm_fm(k, "l3a_vv", W["l3a_wv"], 1024, 64, xb_tile(k), evac_v)

    k.v(lambda e: e.memset(vtt[:, :], 1.0), [], ["l3a_vtt"])
    for kt in range(16):
        ps, pskey = k.bank()
        k.te(lambda e, kt=kt, ps=ps: e.transpose(
            out=ps[:, 0:64], in_=vtil[0:64, kt * 128:(kt + 1) * 128],
            identity=identf[0:64, 0:64]),
            ["l3a_v", identfk], [pskey])
        k.v(lambda e, kt=kt, ps=ps: e.tensor_copy(
            out=vtt[:, kt * 65: kt * 65 + 64], in_=ps[:, 0:64]),
            [pskey], ["l3a_vtt"])

    ones1, ones1k = k.ones_row()
    for h in range(16):
        qsl = (h // 2) * NT
        qp = (h % 2) * 64
        for qb in range(NTB):
            pso, psok, io = k.take_bank()
            pst, pstk, ist = k.take_bank()
            for kt in range(16):
                k.te(lambda e, kt=kt, qb=qb, pst=pst, qp=qp, qsl=qsl: e.matmul(
                    out=pst[:, 0:TB],
                    lhsT=ktil[qp:qp + 64, kt * 128:(kt + 1) * 128],
                    rhs=qb_[qp:qp + 64, qsl + qb * TB: qsl + (qb + 1) * TB],
                    start=True, stop=True), ["l3a_k", kqb], [pstk])
                k.s(lambda e, pst=pst: e.activation(
                    out=ptb[:, 0:TB], in_=pst[:, 0:TB], func=AF.Exp),
                    [pstk], ["l3a_pt"])
                k.te(lambda e, kt=kt, pso=pso, first=(kt == 0), last=(kt == 15):
                     e.matmul(out=pso[0:65, 0:TB],
                              lhsT=vtt[:, kt * 65:(kt + 1) * 65],
                              rhs=ptb[:, 0:TB], start=first, stop=last),
                     ["l3a_vtt", "l3a_pt"], [psok])
            def _recip(e, pso=pso):
                with k.nc.allow_low_precision(reason="bf16 attn normalizer"):
                    return e.reciprocal(out=lrow[0:1, 0:TB], in_=pso[64:65, 0:TB])
            k.v(_recip, [psok], ["sc_row"])
            psr, psrk = k.bank()
            ones1b, ones1bk = k.ones_row("bf16")
            k.te(lambda e, psr=psr, ones1b=ones1b: e.matmul(
                out=psr[:, 0:TB], lhsT=ones1b[0:1, 0:128],
                rhs=lrow[0:1, 0:TB], start=True, stop=True),
                 [ones1bk, "sc_row"], [psrk])
            k.v(lambda e, psr=psr: e.tensor_copy(out=rbc[:, 0:TB], in_=psr[:, 0:TB]),
                [psrk], ["sc_rbc"])
            k.v(lambda e, pso=pso, qb=qb, qp=qp, qsl=qsl: e.tensor_mul(
                out=ob_[qp:qp + 64, qsl + qb * TB: qsl + (qb + 1) * TB],
                in0=pso[0:64, 0:TB], in1=rbc[0:64, 0:TB]),
                [psok, "sc_rbc"], [kob])
            k.free_bank(io)
            k.free_bank(ist)

    def evac_o(o, tb, ps, pskey):
        sl = slice(o * NT + tb * TB, o * NT + (tb + 1) * TB)
        k.v(lambda e, ps=ps, sl=sl: e.tensor_add(
            out=xf[:, sl], in0=xf[:, sl], in1=ps[:, 0:TB]), [pskey, "xf"], ["xf"])

    gemm_fm(k, "l3a_o", W["l3a_wo"], 1024, 1024, lambda c: (ob_, kob, c * NT), evac_o)
    ln_fm(k, "l3a_ln")


def apoff(ap, elems):
    return bass.AP(ap.tensor, ap.offset + elems, [list(d) for d in ap.ap])


def emit_l3_ffn(k, W):
    xf, xb = k.tiles["xf"], k.tiles["xb"]
    identf, identfk = get_ident(k, W, "f32")
    gbuf, kgbs = bb2(k, 0, 1)
    kgb = kgbs
    rt = k.t("l3f_rta", (8, NT), F32)
    rtm = k.t("l3f_rtm", (128, 16 * 8), F32)
    et = k.t("l3f_et", (128, 8), F32)
    pt = k.t("l3f_p", (128, 8), F32)
    sc1 = k.t("l3f_s1", (128, 1), F32)
    sc2 = k.t("l3f_s2", (128, 1), F32)
    sc3 = k.t("l3f_s3", (128, 1), F32)
    m12 = k.t("l3f_m12", (128, 2), F32)
    wtm = k.t("l3f_wtm", (128, 16 * 8), F32)
    wrows = rt
    wbc = k.t("l3f_wbc", (128, NT), BF16)
    tmp = k.t("sc_f_a", (128, TB), F32)

    def evac_r(o, tb, ps, pskey):
        k.v(lambda e, tb=tb, ps=ps: e.tensor_copy(
            out=rt[0:8, tb * TB:(tb + 1) * TB], in_=ps[0:8, 0:TB]), [pskey], ["l3f_rta"])

    gemm_fm(k, "l3f_r", W["l3f_wr"], 1024, 8, xb_tile(k), evac_r)

    for t16 in range(16):
        ps, pskey = k.bank()
        k.te(lambda e, t16=t16, ps=ps: e.transpose(
            out=ps[:, 0:8], in_=rt[0:8, t16 * 128:(t16 + 1) * 128],
            identity=identf[0:8, 0:8]), ["l3f_rta", identfk], [pskey])
        k.v(lambda e, t16=t16, ps=ps: e.tensor_copy(
            out=rtm[:, t16 * 8:(t16 + 1) * 8], in_=ps[:, 0:8]), [pskey], ["l3f_rtm"])
    for t16 in range(16):
        sl = slice(t16 * 8, (t16 + 1) * 8)
        k.v(lambda e, sl=sl: e.reduce_max(out=sc1[:, 0:1], in_=rtm[:, sl],
                                          axis=mybir.AxisListType.X, negate=True),
            ["l3f_rtm"], ["l3f_s1"])
        k.s(lambda e, sl=sl: e.activation(out=et[:, :], in_=rtm[:, sl], func=AF.Exp,
                                          bias=sc1[:, 0:1]), ["l3f_rtm", "l3f_s1"], ["l3f_et"])
        k.v(lambda e: e.reduce_sum(out=sc2[:, 0:1], in_=et[:, :],
                                   axis=mybir.AxisListType.X), ["l3f_et"], ["l3f_s2"])
        k.v(lambda e: e.reciprocal(out=sc2[:, 0:1], in_=sc2[:, 0:1]), ["l3f_s2"], ["l3f_s2"])
        k.v(lambda e: e.tensor_scalar_mul(out=pt[:, :], in0=et[:, :], scalar1=sc2[:, 0:1]),
            ["l3f_et", "l3f_s2"], ["l3f_p"])
        k.v(lambda e: e.reduce_max(out=m12[:, 0:1], in_=pt[:, :],
                                   axis=mybir.AxisListType.X), ["l3f_p"], ["l3f_m12"])
        k.v(lambda e: e.tensor_single_scalar(out=et[:, :], in_=pt[:, :],
                                             scalar=m12[:, 0:1], op=ALU.is_equal),
            ["l3f_p", "l3f_m12"], ["l3f_et"])
        k.v(lambda e: e.scalar_tensor_tensor(out=sc3[:, 0:1] if False else tmp[:, 0:8],
                                             in0=et[:, :], scalar=-1e30, in1=pt[:, :],
                                             op0=ALU.mult, op1=ALU.add),
            ["l3f_et", "l3f_p"], ["sc_f_a"])
        k.v(lambda e: e.reduce_max(out=m12[:, 1:2], in_=tmp[:, 0:8],
                                   axis=mybir.AxisListType.X), ["sc_f_a"], ["l3f_m12"])
        k.v(lambda e: e.tensor_single_scalar(out=tmp[:, 0:8], in_=pt[:, :],
                                             scalar=m12[:, 1:2], op=ALU.is_equal),
            ["l3f_p", "l3f_m12"], ["sc_f_a"])
        k.v(lambda e: e.tensor_add(out=et[:, :], in0=et[:, :], in1=tmp[:, 0:8]),
            ["l3f_et", "sc_f_a"], ["l3f_et"])
        k.v(lambda e: e.reduce_sum(out=sc3[:, 0:1], in_=m12[:, 0:2],
                                   axis=mybir.AxisListType.X), ["l3f_m12"], ["l3f_s3"])
        k.v(lambda e: e.tensor_scalar_add(out=sc3[:, 0:1], in0=sc3[:, 0:1],
                                          scalar1=1e-20), ["l3f_s3"], ["l3f_s3"])
        k.v(lambda e: e.reciprocal(out=sc3[:, 0:1], in_=sc3[:, 0:1]), ["l3f_s3"], ["l3f_s3"])
        k.v(lambda e: e.tensor_mul(out=et[:, :], in0=et[:, :], in1=pt[:, :]),
            ["l3f_et", "l3f_p"], ["l3f_et"])
        k.v(lambda e, sl=sl: e.tensor_scalar_mul(out=wtm[:, sl], in0=et[:, :],
                                                 scalar1=sc3[:, 0:1]),
            ["l3f_et", "l3f_s3"], ["l3f_wtm"])
    for t16 in range(16):
        ps, pskey = k.bank()
        k.te(lambda e, t16=t16, ps=ps: e.transpose(
            out=ps[0:8, 0:128], in_=wtm[:, t16 * 8:(t16 + 1) * 8],
            identity=identf[:, :]), ["l3f_wtm", identfk], [pskey])
        k.v(lambda e, t16=t16, ps=ps: e.tensor_copy(
            out=wrows[0:8, t16 * 128:(t16 + 1) * 128], in_=ps[0:8, 0:128]),
            [pskey], ["l3f_rta"])

    def evac_sg(o, tb, ps, pskey):
        k.s(lambda e, o=o, tb=tb, ps=ps: e.activation(
            out=gbuf[:, o * NT + tb * TB: o * NT + (tb + 1) * TB],
            in_=ps[:, 0:TB], func=AF.Silu), [pskey], list(kgb))

    gemm_fm(k, "l3f_sg", W["l3f_sg"], 1024, 2048, xb_tile(k), evac_sg)

    def evac_su(o, tb, ps, pskey):
        sl = slice(o * NT + tb * TB, o * NT + (tb + 1) * TB)
        k.v(lambda e, ps=ps, sl=sl: e.tensor_mul(
            out=gbuf[:, sl], in0=ps[:, 0:TB], in1=gbuf[:, sl]), [pskey] + list(kgb), list(kgb))

    gemm_fm(k, "l3f_su", W["l3f_su"], 1024, 2048, xb_tile(k), evac_su)

    def evac_sd(o, tb, ps, pskey):
        sl = slice(o * NT + tb * TB, o * NT + (tb + 1) * TB)
        k.v(lambda e, ps=ps, sl=sl: e.tensor_add(
            out=xf[:, sl], in0=xf[:, sl], in1=ps[:, 0:TB]), [pskey, "xf"], ["xf"])

    gemm_fm(k, "l3f_sd", W["l3f_sd"], 2048, 1024,
            lambda c: (gbuf, kgbs[0] if c < 8 else kgbs[1], c * NT), evac_sd)

    sel = k.t("c_sel8", (8, 1024), F32)
    gsel = k.group()
    k.dma("sync", lambda e: e.dma_start(out=sel[:, :], in_=W["cst_sel8"]),
          [], ["c_sel8"], gsel)
    for ex in range(8):
        for tb in range(NTB):
            ps, pskey = k.bank()
            k.te(lambda e, ex=ex, tb=tb, ps=ps: e.matmul(
                out=ps[:, 0:TB], lhsT=sel[0:8, ex * 128:(ex + 1) * 128],
                rhs=wrows[0:8, tb * TB:(tb + 1) * TB], start=True, stop=True),
                ["c_sel8", "l3f_rta"], [pskey])
            k.v(lambda e, tb=tb, ps=ps: e.tensor_copy(
                out=wbc[:, tb * TB:(tb + 1) * TB], in_=ps[:, 0:TB]), [pskey], ["l3f_wbc"])
        gemm_fm(k, f"l3f_g{ex}", apoff(W["l3f_wg"], ex * 1024 * 2048), 1024, 2048,
                xb_tile(k), evac_sg)
        gemm_fm(k, f"l3f_u{ex}", apoff(W["l3f_wu"], ex * 1024 * 2048), 1024, 2048,
                xb_tile(k), evac_su)

        def evac_d(o, tb, ps, pskey):
            sl = slice(o * NT + tb * TB, o * NT + (tb + 1) * TB)
            k.v(lambda e, ps=ps, tb=tb: e.tensor_mul(
                out=tmp[:, 0:TB], in0=ps[:, 0:TB],
                in1=wbc[:, tb * TB:(tb + 1) * TB]), [pskey, "l3f_wbc"], ["sc_f_a"])
            k.v(lambda e, sl=sl: e.tensor_add(
                out=xf[:, sl], in0=xf[:, sl], in1=tmp[:, 0:TB]), ["xf", "sc_f_a"], ["xf"])

        gemm_fm(k, f"l3f_d{ex}", apoff(W["l3f_wd"], ex * 2048 * 1024), 2048, 1024,
                lambda c: (gbuf, kgbs[0] if c < 8 else kgbs[1], c * NT), evac_d)
    ln_fm(k, "l3f_ln")


def emit_l0_ffn(k, W):
    xf, xb = k.tiles["xf"], k.tiles["xb"]
    hb, khbs = bb2(k, 0, 1)
    ybuf = k.t("l0f_y", (128, 4 * NT), BF16)
    rm1 = k.t("sb_l0f_rm1", (128, 4096), BF16)
    rm2 = [k.t(f"l0f_rm2_{i}", (128, 2048), BF16) for i in range(2)]
    stg = k.t("gw_stg0", (128, 512), F32)
    for i in range(8):
        g = k.group()
        k.dma("sync", lambda e, i=i: e.dma_start(
            out=stg[:, :], in_=W["l0f_rm1"][0:128, i * 512:(i + 1) * 512]),
            [], ["gw_stg0"], g)
        k.s(lambda e, i=i: e.activation(out=rm1[:, i * 512:(i + 1) * 512],
                                        in_=stg[:, :], func=AF.Copy),
            ["gw_stg0"], ["sb_l0f_rm1"])
    for s in range(8):
        h, rhalf = s // 2, s % 2
        for rg4 in range(4):
            rg = rhalf * 4 + rg4
            t1i = h * 8 + rg
            for tb in range(NTB):
                ps, pskey = k.bank()
                k.te(lambda e, ps=ps, t1i=t1i, rg=rg, tb=tb: e.matmul(
                    out=ps[:, 0:TB], lhsT=rm1[:, t1i * 128:(t1i + 1) * 128],
                    rhs=xb[:, rg * NT + tb * TB: rg * NT + (tb + 1) * TB],
                    start=True, stop=True), ["sb_l0f_rm1", "xb"], [pskey])
                k.s(lambda e, ps=ps, rg4=rg4, tb=tb: e.activation(
                    out=ybuf[:, rg4 * NT + tb * TB: rg4 * NT + (tb + 1) * TB],
                    in_=ps[:, 0:TB], func=AF.Copy), [pskey], ["l0f_y"])
        par = s % 2
        r2, kr2 = rm2[par], f"l0f_rm2_{par}"
        for i in range(4):
            g = k.group()
            rm2ap = W["l0f_rm2"]
            src_ap = bass.AP(rm2ap.tensor,
                             rm2ap.offset + (s * 16 + i * 4) * 128 * 128,
                             [[128, 128], [128 * 128, 4], [1, 128]])
            dst_ap = ap3(stg, [[512, 128], [128, 4], [1, 128]])
            k.dma("sync", lambda e, src_ap=src_ap, dst_ap=dst_ap: e.dma_start(
                out=dst_ap, in_=src_ap), [], ["gw_stg0"], g)
            k.s(lambda e, i=i, r2=r2: e.activation(
                out=r2[:, i * 512:(i + 1) * 512], in_=stg[:, :], func=AF.Copy),
                ["gw_stg0"], [kr2])
        for rg4 in range(4):
            for sub in range(4):
                sc = rg4 * 4 + sub
                for tb in range(NTB):
                    ps, pskey = k.bank()
                    k.te(lambda e, ps=ps, sc=sc, rg4=rg4, tb=tb, r2=r2: e.matmul(
                        out=ps[:, 0:TB], lhsT=r2[:, sc * 128:(sc + 1) * 128],
                        rhs=ybuf[:, rg4 * NT + tb * TB: rg4 * NT + (tb + 1) * TB],
                        start=True, stop=True), [kr2, "l0f_y"], [pskey])
                    k.s(lambda e, ps=ps, sc=sc, tb=tb: e.activation(
                        out=hb[:, sc * NT + tb * TB: sc * NT + (tb + 1) * TB],
                        in_=ps[:, 0:TB], func=AF.Gelu), [pskey], [khbs[sc // 8]])

        def evac_c(o, tb, ps, pskey):
            sl = slice(o * NT + tb * TB, o * NT + (tb + 1) * TB)
            k.v(lambda e, ps=ps, sl=sl: e.tensor_add(
                out=xf[:, sl], in0=xf[:, sl], in1=ps[:, 0:TB]), [pskey, "xf"], ["xf"])

        gemm_fm(k, f"l0f_c{s}", W["l0f_cproj"], 2048, 1024,
                lambda c: (hb, khbs[c // 8], c * NT), evac_c, cin_off=s * 2048)
    ln_fm(k, "l0f_ln")


def emit_l0_attn(k, W):
    xf, xb = k.tiles["xf"], k.tiles["xb"]
    identf, identfk = get_ident(k, W, "bf16")
    qasm, kqa = bb(k, 0)
    kbuf, kkb = bb(k, 1)
    vbuf, kvb = bb(k, 2)
    SCALE = float((2 * 1024) ** (-0.5))
    vtt = k.t("l0a_vtt", (128, 16 * 65), BF16)
    kstage = k.t("l0a_ks", (128, 8 * 128), BF16)
    ptb = k.t("l0a_pt", (128, TB), BF16)
    lrow = ptb
    rbc = k.t("sc_rbc", (128, TB), BF16)
    mask = k.t("sb_l0a_mask", (128, 256), BF16)
    gm = k.group()
    stg = k.t("gw_stg0", (128, 512), F32)
    k.dma("sync", lambda e: e.dma_start(
        out=stg[:, 0:256], in_=W["l0a_mask"][0:128, 0:256]),
        [], ["gw_stg0"], k.group())
    k.s(lambda e: e.activation(out=mask[:, 0:256], in_=stg[:, 0:256], func=AF.Copy),
        ["gw_stg0"], ["sb_l0a_mask"])
    for nm in ("bq", "bk", "bv", "bo"):
        load_const(k, "l0a_" + nm, W["l0a_" + nm], (128, 8), F32)
    bq = k.tiles["c_l0a_bq"]
    k.v(lambda e: e.tensor_scalar_mul(out=bq[:, :], in0=bq[:, :], scalar1=SCALE),
        ["c_l0a_bq"], ["c_l0a_bq"])

    def evac_v(o, tb, ps, pskey):
        k.v(lambda e, o=o, tb=tb, ps=ps: e.tensor_scalar(
            out=vbuf[:, o * NT + tb * TB: o * NT + (tb + 1) * TB],
            in0=ps[:, 0:TB], scalar1=k.tiles["c_l0a_bv"][:, o:o + 1],
            scalar2=None, op0=ALU.add),
            [pskey, "c_l0a_bv"], [kvb])

    gemm_fm(k, "l0a_v", W["l0a_wv"], 1024, 1024, xb_tile(k), evac_v)

    def evac_k(o, tb, ps, pskey):
        k.v(lambda e, o=o, tb=tb, ps=ps: e.tensor_scalar(
            out=kbuf[:, o * NT + tb * TB: o * NT + (tb + 1) * TB],
            in0=ps[:, 0:TB], scalar1=k.tiles["c_l0a_bk"][:, o:o + 1],
            scalar2=None, op0=ALU.add),
            [pskey, "c_l0a_bk"], [kkb])

    gemm_fm(k, "l0a_k", W["l0a_wk"], 1024, 1024, xb_tile(k), evac_k)

    def evac_q(o, tb, ps, pskey):
        for h4 in range(4):
            h = tb * 4 + h4
            for jh in range(2):
                j = 2 * o + jh
                k.v(lambda e, ps=ps, h=h, jh=jh, j=j, h4=h4, o=o: e.tensor_scalar(
                    out=qasm[(h % 2) * 64:(h % 2) * 64 + 64,
                             (h // 2) * NT + j * 128:(h // 2) * NT + (j + 1) * 128],
                    in0=ps[jh * 64:jh * 64 + 64, h4 * 128:(h4 + 1) * 128],
                    scalar1=SCALE,
                    scalar2=k.tiles["c_l0a_bq"][jh * 64:jh * 64 + 64, o:o + 1],
                    op0=ALU.mult, op1=ALU.add),
                    [pskey, "c_l0a_bq"], [kqa])

    gemm_fm(k, "l0a_q", W["l0a_wq"], 1024, 1024, xb_tile(k), evac_q)

    ones1, ones1k = k.ones_row()
    for h in range(16):
        qp = (h % 2) * 64
        hsl = (h // 2) * NT
        k.v(lambda e: e.memset(vtt[:, :], 1.0), [], ["l0a_vtt"])
        for j in range(16):
            jp = (j % 2) * 64
            ps, pskey = k.bankB()
            k.te(lambda e, ps=ps, j=j, jp=jp, h=h: e.transpose(
                out=ps[:, 0:64],
                in_=vbuf[jp:jp + 64, (j // 2) * NT + h * 128:(j // 2) * NT + (h + 1) * 128],
                identity=identf[jp:jp + 64, jp:jp + 64]),
                [kvb, identfk], [pskey])
            k.v(lambda e, ps=ps, j=j: e.tensor_copy(
                out=vtt[:, j * 65: j * 65 + 64], in_=ps[:, 0:64]), [pskey], ["l0a_vtt"])
        nst = 0
        koff = {}
        for j in range(16):
            if (j % 2) != (h % 2):
                k.v(lambda e, j=j, qp=qp, h=h, nst=nst: e.tensor_copy(
                    out=kstage[qp:qp + 64, nst * 128:(nst + 1) * 128],
                    in_=kbuf[(j % 2) * 64:(j % 2) * 64 + 64,
                             (j // 2) * NT + h * 128:(j // 2) * NT + (h + 1) * 128]),
                    [kkb], ["l0a_ks"])
                koff[j] = nst
                nst += 1
        for qb in range(NTB):
            pso, psok, io = k.take_bank()
            pst, pstk, ist = k.take_bank()
            for j in range(16):
                if (j % 2) == (h % 2):
                    lhsT_fn = (lambda j=j, h=h: kbuf[(j % 2) * 64:(j % 2) * 64 + 64,
                               (j // 2) * NT + h * 128:(j // 2) * NT + (h + 1) * 128])
                    lkey = kkb
                else:
                    lhsT_fn = (lambda j=j, qp=qp, ko=koff[j]: kstage[qp:qp + 64,
                               ko * 128:(ko + 1) * 128])
                    lkey = "l0a_ks"
                k.te(lambda e, pst=pst, lhsT_fn=lhsT_fn, qp=qp, hsl=hsl, qb=qb: e.matmul(
                    out=pst[:, 0:TB], lhsT=lhsT_fn(),
                    rhs=qasm[qp:qp + 64, hsl + qb * TB: hsl + (qb + 1) * TB],
                    start=True, stop=True), [lkey, kqa], [pstk])
                k.s(lambda e, pst=pst: e.activation(
                    out=ptb[:, 0:TB], in_=pst[:, 0:TB], func=AF.Exp),
                    [pstk], ["l0a_pt"])
                for j4 in range(4):
                    jq = qb * 4 + j4
                    moff = 0 if jq < j else 128
                    k.v(lambda e, j4=j4, moff=moff: e.tensor_mul(
                        out=ptb[:, j4 * 128:(j4 + 1) * 128],
                        in0=ptb[:, j4 * 128:(j4 + 1) * 128],
                        in1=mask[:, moff:moff + 128]),
                        ["l0a_pt", "sb_l0a_mask"], ["l0a_pt"])
                k.te(lambda e, j=j, pso=pso, first=(j == 0), last=(j == 15): e.matmul(
                    out=pso[0:65, 0:TB], lhsT=vtt[:, j * 65:(j + 1) * 65],
                    rhs=ptb[:, 0:TB], start=first, stop=last),
                    ["l0a_vtt", "l0a_pt"], [psok])
            def _recip(e, pso=pso):
                with k.nc.allow_low_precision(reason="bf16 attn normalizer"):
                    return e.reciprocal(out=lrow[0:1, 0:TB], in_=pso[64:65, 0:TB])
            k.v(_recip, [psok], ["l0a_pt"])
            psr, psrk = k.bank()
            ones1b, ones1bk = k.ones_row("bf16")
            k.te(lambda e, psr=psr, ones1b=ones1b: e.matmul(
                out=psr[:, 0:TB], lhsT=ones1b[0:1, 0:128],
                rhs=lrow[0:1, 0:TB], start=True, stop=True),
                 [ones1bk, "l0a_pt"], [psrk])
            k.v(lambda e, psr=psr: e.tensor_copy(out=rbc[:, 0:TB], in_=psr[:, 0:TB]),
                [psrk], ["sc_rbc"])
            for j4 in range(4):
                j = qb * 4 + j4
                k.v(lambda e, pso=pso, j=j, j4=j4, h=h: e.tensor_mul(
                    out=xb[(j % 2) * 64:(j % 2) * 64 + 64,
                           (j // 2) * NT + h * 128:(j // 2) * NT + (h + 1) * 128],
                    in0=pso[0:64, j4 * 128:(j4 + 1) * 128],
                    in1=rbc[0:64, j4 * 128:(j4 + 1) * 128]),
                    [psok, "sc_rbc"], ["xb"])
            k.free_bank(io)
            k.free_bank(ist)

    def evac_o(o, tb, ps, pskey):
        sl = slice(o * NT + tb * TB, o * NT + (tb + 1) * TB)
        k.v(lambda e, ps=ps, sl=sl, o=o: e.scalar_tensor_tensor(
            out=xf[:, sl], in0=ps[:, 0:TB], scalar=k.tiles["c_l0a_bo"][:, o:o + 1],
            in1=xf[:, sl], op0=ALU.add, op1=ALU.add),
            [pskey, "c_l0a_bo", "xf"], ["xf"])

    gemm_fm(k, "l0a_o", W["l0a_wo"], 1024, 1024, lambda c: (xb, "xb", c * NT), evac_o)
    ln_fm(k, "l0a_ln")


def shift_xb(k, tag):
    xb = k.tiles["xb"]
    sc = k.t("sc_shift", (128, NT), BF16)
    for c in range(4, 8):
        k.v(lambda e, c=c: e.memset(sc[:, 0:1], 0.0), [], ["sc_shift"])
        k.v(lambda e, c=c: e.tensor_copy(out=sc[:, 1:NT],
                                         in_=xb[:, c * NT: c * NT + NT - 1]),
            ["xb"], ["sc_shift"])
        k.v(lambda e, c=c: e.tensor_copy(out=xb[:, c * NT:(c + 1) * NT], in_=sc[:, :]),
            ["sc_shift", "xb"], ["xb"])


def emit_l2_ffn(k, W):
    import os
    STOP = os.environ.get("L2F_STOP", "")
    xf, xb = k.tiles["xf"], k.tiles["xb"]
    xsave = k.nc.dram_tensor("l2f_xsave", [128, 8 * NT], BF16).ap()
    gsv = k.group("gpsimd")
    k.dma("gpsimd", lambda e: e.dma_start(out=xsave, in_=xb[:, :]),
          ["xb"], ["l2f_xsave"], gsv)
    identb, identbk = get_ident(k, W, "bf16")
    r0, kr0 = bb(k, 0)
    r1, kr1 = bb(k, 1)
    vtt = k.t("l2f_vtt", (128, 16 * 65), BF16)
    ptb = k.t("l0a_pt", (128, TB), BF16)
    lrow = ptb
    rbc = k.t("sc_rbc", (128, TB), BF16)
    band = k.t("sb_band", (128, 1408), BF16)
    tbl = [k.t(f"l2f_tbl{i}", (128, NT), BF16) for i in range(4)]
    nrow = k.t("l2f_nrow", (2, TB), F32)
    qperm = k.t("sc_shift", (128, NT), BF16)
    stg = k.t("gw_stg0", (128, 512), F32)
    for nm in ("l2f_qs", "l2f_qsp", "l2f_ks", "l2f_ksp"):
        load_const(k, nm, W[nm], (128, 2), F32)
    load_const(k, "l2f_lng", W["l2f_lng"], (128, 16), F32)
    load_const(k, "l2f_lnb", W["l2f_lnb"], (128, 16), F32)
    load_const(k, "l2f_b1", W["l2f_b1"], (128, 88), F32)
    load_const(k, "l2f_b2", W["l2f_b2"], (128, 16), F32)
    sel2h = k.t("c_sel2h", (2, 128), F32)
    g = k.group()
    k.dma("sync", lambda e: e.dma_start(out=sel2h[:, :], in_=W["cst_sel2h"]),
          [], ["c_sel2h"], g)
    for i in range(3):
        n = min(512, 1408 - i * 512)
        k.dma("sync", lambda e, i=i, n=n: e.dma_start(
            out=stg[:, 0:n], in_=W["l2f_band"][0:128, i * 512:i * 512 + n]),
            [], ["gw_stg0"], k.group())
        k.s(lambda e, i=i, n=n: e.activation(out=band[:, i * 512:i * 512 + n],
                                             in_=stg[:, 0:n], func=AF.Copy),
            ["gw_stg0"], ["sb_band"])

    ones1b, ones1bk = k.ones_row("bf16")
    LAYERS = [int(c) for c in os.environ.get("L2F_LAYERS", "01")]
    for l in LAYERS:
        shift_xb(k, f"l2f{l}")

        def evac_qkv(o, tb, ps, pskey):
            dst, dk = (r0, kr0) if o < 8 else (r1, kr1)
            oo = o if o < 8 else o - 8
            k.s(lambda e, ps=ps, dst=dst, oo=oo, tb=tb: e.activation(
                out=dst[:, oo * NT + tb * TB: oo * NT + (tb + 1) * TB],
                in_=ps[:, 0:TB], func=AF.Copy), [pskey], [dk])

        gemm_fm(k, f"l2f_qkv{l}", apoff(W["l2f_wqkv"], l * 1024 * 1536),
                1024, 1536, xb_tile(k), evac_qkv)

        specs = [("l2f_cq1", "c_l2f_qs", 8.0, 0), ("l2f_cq2", "c_l2f_qsp", 8.0, 1),
                 ("l2f_ck1", "c_l2f_ks", 1.0, 2), ("l2f_ck2", "c_l2f_ksp", 1.0, 3)]
        for src_name, sc_name, mulc, ti in specs:
            for i in range(4):
                k.dma("sync", lambda e, src_name=src_name, i=i: e.dma_start(
                    out=stg[:, :], in_=W[src_name][0:128, i * 512:(i + 1) * 512]),
                    [], ["gw_stg0"], k.group())
                k.v(lambda e, i=i, ti=ti, sc_name=sc_name, mulc=mulc, l=l: e.tensor_scalar(
                    out=tbl[ti][:, i * 512:(i + 1) * 512], in0=stg[:, :],
                    scalar1=k.tiles[sc_name][:, l:l + 1], scalar2=float(mulc),
                    op0=ALU.mult, op1=ALU.mult),
                    ["gw_stg0", sc_name], [f"l2f_tbl{ti}"])
        if "c_ones2h" not in k.tiles:
            o2 = k.t("c_ones2h", (128, 2), BF16)
            k.v(lambda e, o2=o2: e.memset(o2[:, :], 0.0), [], ["c_ones2h"])
            k.v(lambda e, o2=o2: e.memset(o2[0:64, 0:1], 1.0), [], ["c_ones2h"])
            k.v(lambda e, o2=o2: e.memset(o2[64:128, 1:2], 1.0), [], ["c_ones2h"])
        for qk in range(2):
            for c4 in range(4):
                src_c = qk * 4 + c4
                sqv = k.t("ln_sqb", (128, TB), BF16)
                for tb in range(NTB):
                    psn, kpn = k.bank()
                    k.s(lambda e, src_c=src_c, tb=tb: e.activation(
                        out=sqv[:, 0:TB],
                        in_=r0[:, src_c * NT + tb * TB: src_c * NT + tb * TB + TB],
                        func=AF.Square), [kr0], ["ln_sqb"])
                    k.te(lambda e, psn=psn: e.matmul(
                        out=psn[0:2, 0:TB],
                        lhsT=k.tiles["c_ones2h"][:, 0:2],
                        rhs=sqv[:, 0:TB], start=True, stop=True),
                        ["c_ones2h", "ln_sqb"], [kpn])
                    k.v(lambda e, psn=psn: e.tensor_copy(out=nrow[0:2, 0:TB],
                                                         in_=psn[0:2, 0:TB]),
                        [kpn], ["l2f_nrow"])
                    epst, epsk = k.constf(1e-24)
                    k.s(lambda e, epst=epst: e.activation(
                        out=nrow[0:2, 0:TB], in_=nrow[0:2, 0:TB], func=AF.Ln,
                        bias=epst[0:2, 0:1]), ["l2f_nrow", epsk], ["l2f_nrow"])
                    k.s(lambda e: e.activation(out=nrow[0:2, 0:TB], in_=nrow[0:2, 0:TB],
                                               func=AF.Exp, scale=-0.5),
                        ["l2f_nrow"], ["l2f_nrow"])
                    psb_, kpb = k.bank()
                    k.te(lambda e, psb_=psb_: e.matmul(
                        out=psb_[:, 0:TB], lhsT=sel2h[0:2, 0:128],
                        rhs=nrow[0:2, 0:TB], start=True, stop=True),
                        ["c_sel2h", "l2f_nrow"], [kpb])
                    k.v(lambda e, psb_=psb_, src_c=src_c, tb=tb: e.tensor_mul(
                        out=xb[:, src_c * NT + tb * TB: src_c * NT + tb * TB + TB],
                        in0=r0[:, src_c * NT + tb * TB: src_c * NT + tb * TB + TB],
                        in1=psb_[:, 0:TB]), [kpb, kr0], ["xb"])
                gp = k.group("gpsimd")
                for half in range(2):
                    b0 = half * 64
                    k.dma("gpsimd", lambda e, src_c=src_c, b0=b0: e.dma_start(
                        out=qperm[b0:b0 + 32, :],
                        in_=xb[b0 + 32:b0 + 64, src_c * NT:(src_c + 1) * NT]),
                        ["xb"], ["sc_shift"], gp)
                    k.dma("gpsimd", lambda e, src_c=src_c, b0=b0: e.dma_start(
                        out=qperm[b0 + 32:b0 + 64, :],
                        in_=xb[b0:b0 + 32, src_c * NT:(src_c + 1) * NT]),
                        ["xb"], ["sc_shift"], gp)
                t1i = qk * 2
                k.v(lambda e, src_c=src_c, t1i=t1i: e.tensor_mul(
                    out=xb[:, src_c * NT:(src_c + 1) * NT],
                    in0=xb[:, src_c * NT:(src_c + 1) * NT], in1=tbl[t1i][:, :]),
                    ["xb", f"l2f_tbl{t1i}"], ["xb"])
                k.v(lambda e, t1i=t1i: e.tensor_mul(
                    out=qperm[:, :], in0=qperm[:, :], in1=tbl[t1i + 1][:, :]),
                    ["sc_shift", f"l2f_tbl{t1i + 1}"], ["sc_shift"])
                k.v(lambda e, src_c=src_c: e.tensor_add(
                    out=xb[:, src_c * NT:(src_c + 1) * NT],
                    in0=xb[:, src_c * NT:(src_c + 1) * NT], in1=qperm[:, :]),
                    ["xb", "sc_shift"], ["xb"])
        if STOP == "qk":
            return
        for h in range(8):
            qp = (h % 2) * 64
            qc = h // 2
            kc = 4 + h // 2
            k.v(lambda e: e.memset(vtt[:, :], 1.0), [], ["l2f_vtt"])
            for m in range(16):
                ps, pskey = k.bankB()
                k.te(lambda e, ps=ps, m=m, qp=qp, qc=qc: e.transpose(
                    out=ps[:, 0:64],
                    in_=r1[qp:qp + 64, qc * NT + m * 128: qc * NT + (m + 1) * 128],
                    identity=identb[qp:qp + 64, qp:qp + 64]),
                    [kr1, identbk], [pskey])
                k.v(lambda e, ps=ps, m=m: e.tensor_copy(
                    out=vtt[:, m * 65: m * 65 + 64], in_=ps[:, 0:64]),
                    [pskey], ["l2f_vtt"])
            for w in range(4):
                m0 = max(0, 4 * w - 4)
                m1 = 4 * w + 4
                pso, psok, io = k.take_bank()
                pst, pstk, ist = k.take_bank()
                nmm = m1 - m0
                for mi, m in enumerate(range(m0, m1)):
                    k.te(lambda e, pst=pst, m=m, qp=qp, kc=kc, qc=qc, w=w: e.matmul(
                        out=pst[:, 0:TB],
                        lhsT=xb[qp:qp + 64, kc * NT + m * 128: kc * NT + (m + 1) * 128],
                        rhs=xb[qp:qp + 64, qc * NT + w * TB: qc * NT + (w + 1) * TB],
                        start=True, stop=True), ["xb"], [pstk])
                    k.s(lambda e, pst=pst: e.activation(
                        out=ptb[:, 0:TB], in_=pst[:, 0:TB], func=AF.Exp),
                        [pstk], ["l0a_pt"])
                    D0 = 512 * w - 128 * m
                    k.v(lambda e, D0=D0: e.tensor_mul(
                        out=ptb[:, 0:TB], in0=ptb[:, 0:TB],
                        in1=band[:, D0 + 384: D0 + 384 + TB]),
                        ["l0a_pt", "sb_band"], ["l0a_pt"])
                    k.te(lambda e, m=m, pso=pso, first=(mi == 0), last=(mi == nmm - 1):
                         e.matmul(out=pso[0:65, 0:TB],
                                  lhsT=vtt[:, m * 65:(m + 1) * 65],
                                  rhs=ptb[:, 0:TB], start=first, stop=last),
                         ["l2f_vtt", "l0a_pt"], [psok])
                def _recip(e, pso=pso):
                    with k.nc.allow_low_precision(reason="bf16 attn normalizer"):
                        return e.reciprocal(out=lrow[0:1, 0:TB], in_=pso[64:65, 0:TB])
                k.v(_recip, [psok], ["l0a_pt"])
                psr, psrk = k.bank()
                k.te(lambda e, psr=psr: e.matmul(
                    out=psr[:, 0:TB], lhsT=ones1b[0:1, 0:128],
                    rhs=lrow[0:1, 0:TB], start=True, stop=True),
                    [ones1bk, "l0a_pt"], [psrk])
                k.v(lambda e, psr=psr: e.tensor_copy(out=rbc[:, 0:TB], in_=psr[:, 0:TB]),
                    [psrk], ["sc_rbc"])
                k.v(lambda e, pso=pso, h=h, w=w, qp=qp: e.tensor_mul(
                    out=r1[qp:qp + 64, (4 + h // 2) * NT + w * TB:(4 + h // 2) * NT + (w + 1) * TB],
                    in0=pso[0:64, 0:TB], in1=rbc[0:64, 0:TB]),
                    [psok, "sc_rbc"], [kr1])
                k.free_bank(io)
                k.free_bank(ist)

        if STOP == "attn":
            return

        def evac_wo(o, tb, ps, pskey):
            sl = slice(o * NT + tb * TB, o * NT + (tb + 1) * TB)
            k.v(lambda e, ps=ps, sl=sl: e.tensor_add(
                out=xf[:, sl], in0=xf[:, sl], in1=ps[:, 0:TB]), [pskey, "xf"], ["xf"])

        gemm_fm(k, f"l2f_wo{l}", apoff(W["l2f_wo"], l * 512 * 1024), 512, 1024,
                lambda c: (r1, kr1, (4 + c) * NT), evac_wo)

        for c in range(8):
            k.s(lambda e, c=c: e.activation(out=xb[:, c * NT:(c + 1) * NT],
                                            in_=xf[:, c * NT:(c + 1) * NT],
                                            func=AF.Copy), ["xf"], ["xb"])
        shift_xb(k, f"l2f{l}b")
        ln_from_xb_to(k, f"l2f_iln{l}", r0, kr0,
                      gamma=(k.tiles["c_l2f_lng"], "c_l2f_lng", l * 8),
                      beta=(k.tiles["c_l2f_lnb"], "c_l2f_lnb", l * 8))

        astash = k.t("l2f_astash", (128, 2 * TB), BF16)
        for th in range(2):
            def hb_place(c):
                if c < 16:
                    return r1, kr1, c * 1024
                return xb, "xb", (c - 16) * 1024

            def evac_w1(o, tb, ps, pskey, l=l, th=th):
                pair = o // 2
                if o % 2 == 0:
                    k.v(lambda e, ps=ps, tb=tb, o=o, l=l: e.tensor_scalar(
                        out=astash[:, tb * TB:(tb + 1) * TB], in0=ps[:, 0:TB],
                        scalar1=k.tiles["c_l2f_b1"][:, l * 44 + o:l * 44 + o + 1],
                        scalar2=None, op0=ALU.add),
                        [pskey, "c_l2f_b1"], ["l2f_astash"])
                else:
                    gl = k.t("sc_gel", (128, TB), BF16)
                    k.s(lambda e, ps=ps, o=o, l=l: e.activation(
                        out=gl[:, 0:TB], in_=ps[:, 0:TB], func=AF.Gelu,
                        bias=k.tiles["c_l2f_b1"][:, l * 44 + o:l * 44 + o + 1]),
                        [pskey, "c_l2f_b1"], ["sc_gel"])
                    hh, hk, hoff = hb_place(pair)
                    k.v(lambda e, hh=hh, hoff=hoff, tb=tb, gl=gl: e.tensor_mul(
                        out=hh[:, hoff + tb * TB: hoff + (tb + 1) * TB],
                        in0=astash[:, tb * TB:(tb + 1) * TB], in1=gl[:, 0:TB]),
                        ["l2f_astash", "sc_gel"], [hk])

            gemm_fm(k, f"l2f_w1_{l}_{th}", apoff(W["l2f_w1"], l * 1024 * 5632),
                    1024, 5632, lambda c: (r0, kr0, c * NT + th * 1024),
                    evac_w1, ntok=1024)

            def evac_w2(o, tb, ps, pskey, l=l, th=th):
                sl = slice(o * NT + th * 1024 + tb * TB, o * NT + th * 1024 + (tb + 1) * TB)
                k.v(lambda e, ps=ps, sl=sl, o=o, l=l: e.scalar_tensor_tensor(
                    out=xf[:, sl], in0=ps[:, 0:TB],
                    scalar=k.tiles["c_l2f_b2"][:, l * 8 + o:l * 8 + o + 1],
                    in1=xf[:, sl], op0=ALU.add, op1=ALU.add),
                    [pskey, "c_l2f_b2", "xf"], ["xf"])

            gemm_fm(k, f"l2f_w2_{l}_{th}", apoff(W["l2f_w2"], l * 2816 * 1024),
                    2816, 1024, hb_place, evac_w2, ntok=1024)
        for c in range(8):
            k.s(lambda e, c=c: e.activation(out=xb[:, c * NT:(c + 1) * NT],
                                            in_=xf[:, c * NT:(c + 1) * NT],
                                            func=AF.Copy), ["xf"], ["xb"])
        if STOP == "layer0":
            return
    if STOP == "noln":
        return
    sc = k.t("sc_shift", (128, NT), BF16)
    gld2 = k.group("gpsimd")
    for c in range(8):
        k.dma("gpsimd", lambda e, c=c: e.dma_start(
            out=sc[:, :], in_=xsave[0:128, c * NT:(c + 1) * NT]),
            ["l2f_xsave"], ["sc_shift"], gld2)
        k.v(lambda e, c=c: e.tensor_add(
            out=xf[:, c * NT:(c + 1) * NT], in0=xf[:, c * NT:(c + 1) * NT],
            in1=sc[:, :]), ["sc_shift", "xf"], ["xf"])
        gld2 = k.group("gpsimd")
    ln_fm(k, "l2f_ln")


def ln_from_xb_to(k, tag, dst, dstk, gamma=None, beta=None, eps=1e-5, rows_t=None):
    xb = k.tiles["xb"]
    sq = k.t("ln_sqb", (128, TB), BF16)
    rows = rows_t if rows_t is not None else k.t("ln_rows", (1, 2 * TB), BF16)
    onesNb, onesNbk = k.constb(1.0 / 1024.0)
    ones1b, ones1bk = k.ones_row("bf16")
    for tb in range(NTB):
        psm, kpm = k.bank()
        psv, kpv = k.bank()
        for c in range(8):
            k.te(lambda e, psm=psm, c=c, tb=tb: e.matmul(
                out=psm[0:1, 0:TB], lhsT=onesNb[:, 0:1],
                rhs=xb[:, c * NT + tb * TB: c * NT + tb * TB + TB],
                start=(c == 0), stop=(c == 7)), [onesNbk, "xb"], [kpm])
        for c in range(8):
            k.s(lambda e, c=c, tb=tb: e.activation(
                out=sq[:, 0:TB], in_=xb[:, c * NT + tb * TB: c * NT + tb * TB + TB],
                func=AF.Square), ["xb"], ["ln_sqb"])
            k.te(lambda e, psv=psv, c=c: e.matmul(
                out=psv[0:1, 0:TB], lhsT=onesNb[:, 0:1], rhs=sq[:, 0:TB],
                start=(c == 0), stop=(c == 7)), [onesNbk, "ln_sqb"], [kpv])
        k.v(lambda e, psm=psm: e.tensor_copy(out=rows[0:1, 0:TB], in_=psm[0:1, 0:TB]),
            [kpm], ["ln_rows"])
        k.s(lambda e: e.activation(out=rows[0:1, TB:2 * TB], in_=rows[0:1, 0:TB],
                                   func=AF.Square), ["ln_rows"], ["ln_rows"])
        k.v(lambda e, psv=psv: e.tensor_sub(out=rows[0:1, TB:2 * TB],
                                            in0=psv[0:1, 0:TB],
                                            in1=rows[0:1, TB:2 * TB]),
            [kpv, "ln_rows"], ["ln_rows"])
        epst, epsk = k.constf(eps)
        k.s(lambda e, epst=epst: e.activation(out=rows[0:1, TB:2 * TB],
                                              in_=rows[0:1, TB:2 * TB],
                                              func=AF.Ln, bias=epst[0:1, 0:1]),
            ["ln_rows", epsk], ["ln_rows"])
        k.s(lambda e: e.activation(out=rows[0:1, TB:2 * TB], in_=rows[0:1, TB:2 * TB],
                                   func=AF.Exp, scale=-0.5), ["ln_rows"], ["ln_rows"])
        pmb, kmb = k.bank()
        prb, krb = k.bank()
        k.te(lambda e, pmb=pmb: e.matmul(out=pmb[:, 0:TB], lhsT=ones1b[0:1, 0:128],
                                         rhs=rows[0:1, 0:TB], start=True, stop=True),
             [ones1bk, "ln_rows"], [kmb])
        k.te(lambda e, prb=prb: e.matmul(out=prb[:, 0:TB], lhsT=ones1b[0:1, 0:128],
                                         rhs=rows[0:1, TB:2 * TB], start=True, stop=True),
             [ones1bk, "ln_rows"], [krb])
        sc2 = k.t("sc_ln2", (128, TB), F32)
        for c in range(8):
            sl = slice(c * NT + tb * TB, c * NT + tb * TB + TB)
            k.v(lambda e, sl=sl, pmb=pmb: e.tensor_sub(
                out=sc2[:, 0:TB], in0=xb[:, sl], in1=pmb[:, 0:TB]),
                ["xb", kmb], ["sc_ln2"])
            k.v(lambda e, prb=prb: e.tensor_mul(
                out=sc2[:, 0:TB], in0=sc2[:, 0:TB], in1=prb[:, 0:TB]),
                ["sc_ln2", krb], ["sc_ln2"])
            gt, gk, goff = gamma
            bt, bk, boff = beta
            k.v(lambda e, sl=sl, c=c, gt=gt, bt=bt, goff=goff, boff=boff: e.tensor_scalar(
                out=dst[:, sl], in0=sc2[:, 0:TB],
                scalar1=gt[:, goff + c:goff + c + 1],
                scalar2=bt[:, boff + c:boff + c + 1],
                op0=ALU.mult, op1=ALU.add), ["sc_ln2", gk, bk], [dstk])


def emit_l2_attn(k, W):
    xf, xb = k.tiles["xf"], k.tiles["xb"]
    xs, kxs2 = bb2(k, 0, 1)
    xsave = k.nc.dram_tensor("l2a_xsave", [128, 8 * NT], BF16).ap()
    zdram = k.nc.dram_tensor("l2a_z", [128, 16 * NT], BF16).ap()
    gsv = k.group("gpsimd")
    k.dma("gpsimd", lambda e: e.dma_start(out=xsave, in_=xb[:, :]),
          ["xb"], ["l2a_xsave"], gsv)
    for nm, shp in [("l2a_convw", (128, 256)), ("l2a_convb", (128, 64)),
                    ("l2a_dtb", (128, 64)), ("l2a_dsk", (128, 64)),
                    ("l2a_lnw", (128, 32)), ("l2a_lnb", (128, 32))]:
        load_const(k, nm, W[nm], shp, F32)
    stg = k.t("gw_stg0", (128, 512), F32)
    sel = k.t("sb_sel", (32, 4096), BF16)
    gs = k.group()
    k.dma("sync", lambda e: e.dma_start(out=sel[:, :], in_=W["l2a_sel"]),
          [], ["sb_sel"], gs)
    Af = k.t("l2a_Af", (128, 256), F32)
    bt = TView(xb, 0 * NT, NT)
    ht = TView(xb, 1 * NT, NT)
    dub = TView(xb, 2 * NT, NT)
    scf2 = TView(xb, 3 * NT, NT)
    dtb = TView(xb, 4 * NT, NT)
    bcBC = TView(xb, 5 * NT, 2 * NT)
    delt = k.t("l2a_delt", (128, NT), F32)
    af32 = k.t("l2a_a", (128, NT), F32)
    scf1 = af32
    prod = bt
    ych = k.t("l2a_y", (128, NT), F32)
    zt = scf2
    VKEYS = ["l2a_bt", "l2a_ht", "l2a_dub", "l2a_scf2", "l2a_dtbb",
             "l2a_bcBC"]
    convw = k.tiles["c_l2a_convw"]
    convb = k.tiles["c_l2a_convb"]

    for li in range(4):
        k.dma("sync", lambda e, li=li: e.dma_start(
            out=stg[:, 0:256], in_=W["l2a_alog"][0:128, li * 256:(li + 1) * 256]),
            [], ["gw_stg0"], k.group())
        k.s(lambda e: e.activation(out=Af[:, :], in_=stg[:, 0:256],
                                   func=AF.Exp), ["gw_stg0"], ["l2a_Af"])
        k.v(lambda e: e.tensor_scalar_mul(out=Af[:, :], in0=Af[:, :], scalar1=-1.0),
            ["l2a_Af"], ["l2a_Af"])

        def evac_in(o, tb, ps, pskey, li=li):
            if o < 16:
                k.s(lambda e, ps=ps, o=o, tb=tb: e.activation(
                    out=xs[:, o * NT + tb * TB: o * NT + (tb + 1) * TB],
                    in_=ps[:, 0:TB], func=AF.Copy), [pskey], list(kxs2))
            else:
                zt2 = k.t("l2a_zt2", (128, TB), BF16)
                k.v(lambda e, ps=ps, zt2=zt2: e.tensor_copy(out=zt2[:, 0:TB],
                                                            in_=ps[:, 0:TB]),
                    [pskey], ["l2a_zt2"])
                gz = k.group("gpsimd")
                k.dma("gpsimd", lambda e, o=o, tb=tb, zt2=zt2: e.dma_start(
                    out=zdram[0:128, (o - 16) * NT + tb * TB:(o - 16) * NT + (tb + 1) * TB],
                    in_=zt2[:, 0:TB]), ["l2a_zt2"], ["l2a_zdram"], gz)

        gemm_fm(k, f"l2a_in{li}", apoff(W["l2a_inw"], li * 1024 * 4096),
                1024, 4096, xb_tile(k), evac_in)
        k.ctx.op("sync", lambda e: e.nop(), ["xb"], list(VKEYS) + ["xb"])

        for c in range(16):
            k.v(lambda e, c=c, li=li: e.tensor_scalar(
                out=scf2[:, 0:NT], in0=xs[:, c * NT:c * NT + NT],
                scalar1=0.0,
                scalar2=convb[:, li * 16 + c:li * 16 + c + 1],
                op0=ALU.mult, op1=ALU.add),
                list(kxs2) + ["c_l2a_convb"], ["l2a_scf2"])
            for j in range(4):
                lo = 3 - j
                k.v(lambda e, c=c, j=j, li=li, lo=lo: e.scalar_tensor_tensor(
                    out=scf2[:, lo:NT], in0=xs[:, c * NT: c * NT + NT - lo],
                    scalar=convw[:, li * 64 + c * 4 + j:li * 64 + c * 4 + j + 1],
                    in1=scf2[:, lo:NT],
                    op0=ALU.mult, op1=ALU.add),
                    list(kxs2) + ["c_l2a_convw", "l2a_scf2"], ["l2a_scf2"])
            k.s(lambda e, c=c: e.activation(out=xs[:, c * NT:(c + 1) * NT],
                                            in_=scf2[:, 0:NT], func=AF.Silu),
                ["l2a_scf2"], list(kxs2))

        def evac_xp(o, tb, ps, pskey):
            k.v(lambda e, ps=ps, tb=tb: e.tensor_copy(
                out=dtb[0:64, tb * TB:(tb + 1) * TB], in_=ps[0:64, 0:TB]),
                [pskey], ["l2a_dtbb"])
            k.v(lambda e, ps=ps, tb=tb: e.tensor_copy(
                out=bcBC[0:32, tb * TB:(tb + 1) * TB], in_=ps[64:96, 0:TB]),
                [pskey], ["l2a_bcBC"])

        gemm_fm(k, f"l2a_xp{li}", apoff(W["l2a_xpw"], li * 2048 * 96),
                2048, 96, lambda c: (xs, kxs2[c // 8], c * NT), evac_xp)

        dtw = k.t("sb_dtw", (64, NT), BF16)
        for i4 in range(4):
            k.dma("sync", lambda e, i4=i4, li=li: e.dma_start(
                out=stg[0:64, 0:TB],
                in_=W["l2a_dtw"][li * 64:(li + 1) * 64, i4 * TB:(i4 + 1) * TB]),
                [], ["gw_stg0"], k.group())
            k.v(lambda e, i4=i4: e.tensor_copy(
                out=dtw[0:64, i4 * TB:(i4 + 1) * TB], in_=stg[0:64, 0:TB]),
                ["gw_stg0"], ["sb_dtw"])
        scan_tog = [0]
        for c in range(16):
            psd = [k.bank() for _ in range(4)]
            for tb in range(NTB):
                ps, pskey = psd[tb]
                k.te(lambda e, ps=ps, c=c, tb=tb, li=li: e.matmul(
                    out=ps[:, 0:TB],
                    lhsT=k.tiles["sb_dtw"][0:64, c * 128:(c + 1) * 128],
                    rhs=dtb[0:64, tb * TB:(tb + 1) * TB],
                    start=True, stop=True), ["sb_dtw", "l2a_dtbb"], [pskey])
                k.v(lambda e, ps=ps, c=c, tb=tb: e.tensor_scalar(
                    out=delt[:, tb * TB:(tb + 1) * TB], in0=ps[:, 0:TB],
                    scalar1=k.tiles["c_l2a_dtb"][:, li * 16 + c:li * 16 + c + 1],
                    scalar2=None, op0=ALU.add),
                    [pskey, "c_l2a_dtb"], ["l2a_delt"])
            k.s(lambda e: e.activation(out=scf1[:, :], in_=delt[:, :], func=AF.Abs),
                ["l2a_delt"], ["l2a_a"])
            k.s(lambda e: e.activation(out=scf1[:, :], in_=scf1[:, :], func=AF.Exp,
                                       scale=-1.0), ["l2a_a"], ["l2a_a"])
            onet, onek = k.constf(1.0)
            k.s(lambda e, onet=onet: e.activation(out=scf1[:, :], in_=scf1[:, :],
                                                  func=AF.Ln, bias=onet[:, 0:1]),
                ["l2a_a", onek], ["l2a_a"])
            k.v(lambda e: e.tensor_relu(out=scf2[:, 0:NT], in_=delt[:, :]),
                ["l2a_delt"], ["l2a_scf2"])
            k.v(lambda e: e.tensor_add(out=delt[:, :], in0=scf1[:, :], in1=scf2[:, 0:NT]),
                ["l2a_a", "l2a_scf2"], ["l2a_delt"])
            k.v(lambda e, c=c: e.tensor_mul(out=dub[:, :], in0=delt[:, :],
                                            in1=xs[:, c * NT:(c + 1) * NT]),
                ["l2a_delt"] + list(kxs2), ["l2a_dub"])
            k.v(lambda e: e.memset(ych[:, :], 0.0), [], ["l2a_y"])
            for n in range(16):
                k.s(lambda e, c=c, n=n: e.activation(
                    out=af32[:, :], in_=delt[:, :], func=AF.Exp,
                    scale=Af[:, c * 16 + n:c * 16 + n + 1]),
                    ["l2a_delt", "l2a_Af"], ["l2a_a"])
                for tb in range(NTB):
                    psb_, kpb = k.bank()
                    k.te(lambda e, psb_=psb_, n=n, tb=tb: e.matmul(
                        out=psb_[:, 0:TB], lhsT=sel[0:32, n * 128:(n + 1) * 128],
                        rhs=bcBC[0:32, tb * TB:(tb + 1) * TB],
                        start=True, stop=True), ["sb_sel", "l2a_bcBC"], [kpb])
                    k.v(lambda e, psb_=psb_, tb=tb: e.tensor_mul(
                        out=bt[:, tb * TB:(tb + 1) * TB],
                        in0=dub[:, tb * TB:(tb + 1) * TB], in1=psb_[:, 0:TB]),
                        ["l2a_dub", kpb], ["l2a_bt"])
                eng = k.v
                scan_tog[0] += 1
                eng(lambda e: e.tensor_tensor_scan(
                    out=ht[:, :], data0=af32[:, :], data1=bt[:, :],
                    initial=0.0, op0=ALU.mult, op1=ALU.add),
                    ["l2a_a", "l2a_bt"], ["l2a_ht"])
                for tb in range(NTB):
                    psc, kpc = k.bank()
                    k.te(lambda e, psc=psc, n=n, tb=tb: e.matmul(
                        out=psc[:, 0:TB], lhsT=sel[0:32, 2048 + n * 128:2048 + (n + 1) * 128],
                        rhs=bcBC[0:32, tb * TB:(tb + 1) * TB],
                        start=True, stop=True), ["sb_sel", "l2a_bcBC"], [kpc])
                    k.v(lambda e, psc=psc, tb=tb: e.tensor_mul(
                        out=prod[:, 0:TB], in0=ht[:, tb * TB:(tb + 1) * TB],
                        in1=psc[:, 0:TB]), ["l2a_ht", kpc], ["l2a_bt"])
                    k.v(lambda e, tb=tb: e.tensor_add(
                        out=ych[:, tb * TB:(tb + 1) * TB],
                        in0=ych[:, tb * TB:(tb + 1) * TB], in1=prod[:, 0:TB]),
                        ["l2a_y", "l2a_bt"], ["l2a_y"])
            k.v(lambda e, c=c, li=li: e.scalar_tensor_tensor(
                out=ych[:, :], in0=xs[:, c * NT:(c + 1) * NT],
                scalar=k.tiles["c_l2a_dsk"][:, li * 16 + c:li * 16 + c + 1],
                in1=ych[:, :], op0=ALU.mult, op1=ALU.add),
                list(kxs2) + ["c_l2a_dsk", "l2a_y"], ["l2a_y"])
            for tb in range(NTB):
                gz = k.group("gpsimd")
                k.dma("gpsimd", lambda e, c=c, tb=tb: e.dma_start(
                    out=zt[:, 0:TB],
                    in_=zdram[0:128, c * NT + tb * TB: c * NT + (tb + 1) * TB]),
                    ["l2a_zdram"], ["l2a_scf2"], gz)
                k.s(lambda e: e.activation(out=zt[:, 0:TB], in_=zt[:, 0:TB],
                                           func=AF.Silu), ["l2a_scf2"], ["l2a_scf2"])
                k.v(lambda e, c=c, tb=tb: e.tensor_mul(
                    out=xs[:, c * NT + tb * TB: c * NT + (tb + 1) * TB],
                    in0=ych[:, tb * TB:(tb + 1) * TB], in1=zt[:, 0:TB]),
                    ["l2a_y", "l2a_scf2"], list(kxs2))

        def evac_out(o, tb, ps, pskey):
            sl = slice(o * NT + tb * TB, o * NT + (tb + 1) * TB)
            k.v(lambda e, ps=ps, sl=sl: e.tensor_add(
                out=xf[:, sl], in0=xf[:, sl], in1=ps[:, 0:TB]), [pskey, "xf"], ["xf"])

        gemm_fm(k, f"l2a_out{li}", apoff(W["l2a_outw"], li * 2048 * 1024),
                2048, 1024, lambda c: (xs, kxs2[c // 8], c * NT), evac_out)
        k.ctx.op("sync", lambda e: e.nop(), list(VKEYS), ["xb"])
        ln_fm(k, f"l2a_ln{li}",
              gamma=(k.tiles["c_l2a_lnw"], "c_l2a_lnw", li * 8),
              beta=(k.tiles["c_l2a_lnb"], "c_l2a_lnb", li * 8))
    k.ctx.op("sync", lambda e: e.nop(), ["xb"], list(VKEYS) + ["xb"])
    gld2 = k.group("gpsimd")
    for c in range(8):
        k.dma("gpsimd", lambda e, c=c: e.dma_start(
            out=scf2[:, 0:NT],
            in_=xsave[0:128, c * NT:(c + 1) * NT]),
            ["l2a_xsave"], ["l2a_scf2"], gld2)
        k.v(lambda e, c=c: e.tensor_add(
            out=xf[:, c * NT:(c + 1) * NT], in0=xf[:, c * NT:(c + 1) * NT],
            in1=scf2[:, 0:NT]), ["l2a_scf2", "xf"], ["xf"])
        gld2 = k.group("gpsimd")
    k.ctx.op("sync", lambda e: e.nop(), list(VKEYS), ["xb"])
    ln_fm(k, "l2a_lnf")


def emit_l1_attn(k, W):
    xf, xb = k.tiles["xf"], k.tiles["xb"]
    identf, identfk = get_ident(k, W, "f32")
    qh_, kq = bb(k, 0)
    kh_, kk_ = bb(k, 1)
    vdram = k.nc.dram_tensor("l1a_v", [128, 8 * NT], BF16).ap()
    glin = TView(xb, 0, 8 * NT)
    odram = k.nc.dram_tensor("l1a_o", [128, 8 * NT], BF16).ap()
    for nm, shp in [("l1a_cw", (128, 96)), ("l1a_cb", (128, 24)),
                    ("l1a_alog", (128, 8)), ("l1a_dtb", (128, 8)),
                    ("l1a_onw", (128, 1))]:
        load_const(k, nm, W[nm], shp, F32)
    load_const(k, "l1a_eye", W["l1a_eye"], (128, 64), BF16)
    t1 = k.t("l1a_t1", (64, NT), BF16)
    brows = k.t("l1a_br", (16, NT), BF16)
    btall = k.t("l1a_btT", (64, 512), F32)
    gf = k.t("l1a_gf", (128, NT), F32)
    kbeta = k.t("l1a_kbeta", (128, NT), BF16)
    nrow = k.t("l2f_nrow", (2, TB), F32)
    sel2h = k.t("c_sel2h", (2, 128), F32)
    g0 = k.group()
    k.dma("sync", lambda e: e.dma_start(out=sel2h[:, :], in_=W["cst_sel2h"]),
          [], ["c_sel2h"], g0)
    Af = k.t("l1a_Af", (128, 8), F32)
    k.s(lambda e: e.activation(out=Af[:, :], in_=k.tiles["c_l1a_alog"][:, :],
                               func=AF.Exp), ["c_l1a_alog"], ["l1a_Af"])
    k.v(lambda e: e.tensor_scalar_mul(out=Af[:, :], in0=Af[:, :], scalar1=-1.0),
        ["c_l1a_Af" if False else "l1a_Af"], ["l1a_Af"])

    cwt = k.tiles["c_l1a_cw"]
    cbt = k.tiles["c_l1a_cb"]
    sc1 = k.t("l1a_sc1", (128, NT), F32)

    def mk_evac(dst, dk, qkv):
        def evac(o, tb, ps, pskey):
            k.v(lambda e, ps=ps, o=o, tb=tb: e.tensor_copy(
                out=dst[:, o * NT + tb * TB: o * NT + (tb + 1) * TB],
                in_=ps[:, 0:TB]), [pskey], [dk])
        return evac

    gemm_fm(k, "l1a_q", W["l1a_wq"], 1024, 1024, xb_tile(k), mk_evac(qh_, kq, 0))
    gemm_fm(k, "l1a_k", W["l1a_wk"], 1024, 1024, xb_tile(k), mk_evac(kh_, kk_, 1))
    vtmp = k.t("l1a_vtmp", (128, NT), BF16)

    def evac_v(o, tb, ps, pskey):
        k.v(lambda e, ps=ps, tb=tb: e.tensor_copy(
            out=vtmp[:, tb * TB:(tb + 1) * TB], in_=ps[:, 0:TB]),
            [pskey], ["l1a_vtmp"])
        if tb == NTB - 1:
            qkv = 2
            k.v(lambda e, o=o: e.tensor_scalar(
                out=sc1[:, 0:NT], in0=vtmp[:, 0:NT],
                scalar1=0.0, scalar2=cbt[:, qkv * 8 + o:qkv * 8 + o + 1],
                op0=ALU.mult, op1=ALU.add), ["l1a_vtmp", "c_l1a_cb"], ["l1a_sc1"])
            for j in range(4):
                lo = 3 - j
                k.v(lambda e, o=o, j=j, lo=lo: e.scalar_tensor_tensor(
                    out=sc1[:, lo:NT], in0=vtmp[:, 0:NT - lo],
                    scalar=cwt[:, qkv * 32 + o * 4 + j:qkv * 32 + o * 4 + j + 1],
                    in1=sc1[:, lo:NT], op0=ALU.mult, op1=ALU.add),
                    ["l1a_vtmp", "c_l1a_cw", "l1a_sc1"], ["l1a_sc1"])
            k.s(lambda e: e.activation(out=vtmp[:, 0:NT], in_=sc1[:, 0:NT],
                                       func=AF.Silu), ["l1a_sc1"], ["l1a_vtmp"])
            gv = k.group("gpsimd")
            k.dma("gpsimd", lambda e, o=o: e.dma_start(
                out=vdram[0:128, o * NT:(o + 1) * NT], in_=vtmp[:, 0:NT]),
                ["l1a_vtmp"], ["l1a_vdram"], gv)

    gemm_fm(k, "l1a_v", W["l1a_wv"], 1024, 1024, xb_tile(k), evac_v)
    for qkv, (dst, dk) in enumerate([(qh_, kq), (kh_, kk_)]):
        for c in range(8):
            k.v(lambda e, c=c, qkv=qkv, dst=dst: e.tensor_scalar(
                out=sc1[:, 0:NT], in0=dst[:, c * NT:c * NT + NT],
                scalar1=0.0, scalar2=cbt[:, qkv * 8 + c:qkv * 8 + c + 1],
                op0=ALU.mult, op1=ALU.add), [dk, "c_l1a_cb"], ["l1a_sc1"])
            for j in range(4):
                lo = 3 - j
                k.v(lambda e, c=c, j=j, qkv=qkv, lo=lo, dst=dst: e.scalar_tensor_tensor(
                    out=sc1[:, lo:NT], in0=dst[:, c * NT: c * NT + NT - lo],
                    scalar=cwt[:, qkv * 32 + c * 4 + j:qkv * 32 + c * 4 + j + 1],
                    in1=sc1[:, lo:NT], op0=ALU.mult, op1=ALU.add),
                    [dk, "c_l1a_cw", "l1a_sc1"], ["l1a_sc1"])
            k.s(lambda e, c=c, dst=dst: e.activation(
                out=dst[:, c * NT:(c + 1) * NT], in_=sc1[:, 0:NT], func=AF.Silu),
                ["l1a_sc1"], [dk])
    def evac_t1(o, tb, ps, pskey):
        k.v(lambda e, ps=ps, tb=tb: e.tensor_copy(
            out=t1[0:64, tb * TB:(tb + 1) * TB], in_=ps[0:64, 0:TB]), [pskey], ["l1a_t1"])

    gemm_fm(k, "l1a_f1", W["l1a_wf1"], 1024, 64, xb_tile(k), evac_t1)

    def evac_beta(o, tb, ps, pskey):
        k.s(lambda e, ps=ps, tb=tb: e.activation(
            out=brows[0:16, tb * TB:(tb + 1) * TB], in_=ps[0:16, 0:TB],
            func=AF.Sigmoid), [pskey], ["l1a_br"])

    gemm_fm(k, "l1a_wb", W["l1a_wbmat"], 1024, 16, xb_tile(k), evac_beta)
    k.ctx.op("sync", lambda e: e.nop(), ["xb"], ["l1a_glin", "xb"])

    def evac_glin(o, tb, ps, pskey):
        k.s(lambda e, ps=ps, o=o, tb=tb: e.activation(
            out=glin[:, o * NT + tb * TB: o * NT + (tb + 1) * TB],
            in_=ps[:, 0:TB], func=AF.Copy), [pskey], ["l1a_glin"])

    gemm_fm(k, "l1a_f2", W["l1a_wf2"], 64, 1024,
            lambda c: (t1, "l1a_t1", 0), evac_glin)
    identb, identbk = get_ident(k, W, "bf16")
    for c in range(32):
        ps, pskey = k.bankB()
        k.te(lambda e, c=c, ps=ps: e.transpose(
            out=ps[0:64, 0:16], in_=brows[0:16, c * 64:(c + 1) * 64],
            identity=identb[0:16, 0:16]), ["l1a_br", identbk], [pskey])
        k.v(lambda e, c=c, ps=ps: e.tensor_copy(
            out=btall[0:64, c * 16:(c + 1) * 16], in_=ps[0:64, 0:16]),
            [pskey], ["l1a_btT"])

    sqv = k.t("ln_sqb", (128, TB), BF16)
    o2h = k.t("c_ones2h", (128, 2), BF16)
    if True:
        k.v(lambda e: e.memset(o2h[:, :], 0.0), [], ["c_ones2h"])
        k.v(lambda e: e.memset(o2h[0:64, 0:1], 1.0), [], ["c_ones2h"])
        k.v(lambda e: e.memset(o2h[64:128, 1:2], 1.0), [], ["c_ones2h"])
    for qk, (src, skey) in enumerate([(qh_, kq), (kh_, kk_)]):
        for c in range(8):
            for tb in range(NTB):
                psn, kpn = k.bank()
                k.s(lambda e, c=c, tb=tb, src=src: e.activation(
                    out=sqv[:, 0:TB],
                    in_=src[:, c * NT + tb * TB: c * NT + tb * TB + TB],
                    func=AF.Square), [skey], ["ln_sqb"])
                k.te(lambda e, psn=psn: e.matmul(
                    out=psn[0:2, 0:TB], lhsT=o2h[:, 0:2], rhs=sqv[:, 0:TB],
                    start=True, stop=True), ["c_ones2h", "ln_sqb"], [kpn])
                k.v(lambda e, psn=psn: e.tensor_copy(out=nrow[0:2, 0:TB],
                                                     in_=psn[0:2, 0:TB]),
                    [kpn], ["l2f_nrow"])
                k.s(lambda e: e.activation(out=nrow[0:2, 0:TB], in_=nrow[0:2, 0:TB],
                                           func=AF.Sqrt), ["l2f_nrow"], ["l2f_nrow"])
                k.v(lambda e: e.tensor_scalar_add(out=nrow[0:2, 0:TB],
                                                  in0=nrow[0:2, 0:TB], scalar1=1e-6),
                    ["l2f_nrow"], ["l2f_nrow"])
                k.v(lambda e: e.reciprocal(out=nrow[0:2, 0:TB], in_=nrow[0:2, 0:TB]),
                    ["l2f_nrow"], ["l2f_nrow"])
                psb_, kpb = k.bank()
                k.te(lambda e, psb_=psb_: e.matmul(
                    out=psb_[:, 0:TB], lhsT=sel2h[0:2, 0:128],
                    rhs=nrow[0:2, 0:TB], start=True, stop=True),
                    ["c_sel2h", "l2f_nrow"], [kpb])
                mulc = 0.125 if qk == 0 else 1.0
                k.v(lambda e, psb_=psb_, c=c, tb=tb, src=src, mulc=mulc: e.scalar_tensor_tensor(
                    out=src[:, c * NT + tb * TB: c * NT + tb * TB + TB],
                    in0=src[:, c * NT + tb * TB: c * NT + tb * TB + TB],
                    scalar=mulc, in1=psb_[:, 0:TB], op0=ALU.mult, op1=ALU.mult),
                    [skey, kpb], [skey])

    BBIGh = k.tiles["BBIG"]
    BPITCH = getattr(k, "nbb", 3) * BBREG

    def bbap(region, off, dims):
        a = BBIGh[:, :]
        return bass.AP(a.tensor, a.offset + region * BBREG + off,
                       [list(d) for d in dims])

    ones64 = k.t("c_ones64", (128, 64), BF16)
    k.v(lambda e: e.memset(ones64[:, :], 1.0), [], ["c_ones64"])
    for nm, shp in [("l1a_m8neg", (128, 64)), ("l1a_m8incl", (128, 64)),
                    ("l1a_moff", (128, 64))]:
        load_const(k, nm, W[nm], shp, BF16)
    load_const(k, "l1a_bg2", W["l1a_bg2"], (128, 8), F32)
    selbt = k.t("c_l1a_selb", (16, 1024), BF16)
    gsb = k.group()
    k.dma("sync", lambda e: e.dma_start(out=selbt[:, :], in_=W["l1a_selb"]),
          [], ["c_l1a_selb"], gsb)
    eye = k.tiles["c_l1a_eye"]
    m8n = k.tiles["c_l1a_m8neg"]
    m8i = k.tiles["c_l1a_m8incl"]
    moff = k.tiles["c_l1a_moff"]
    selb = k.tiles["c_l1a_selb"]
    UCOLS = [392]
    UKEYS = []

    def ut(name, ncols):
        v = TView(sc1, UCOLS[0], ncols)
        UCOLS[0] += ncols
        UKEYS.append(name)
        return v

    def uap(v, dims, offset=0):
        d2 = [list(d) for d in dims]
        d2[0][0] = NT
        a = sc1[:, :]
        return bass.AP(a.tensor, a.offset + v.off + offset, d2)

    Gx = ut("u_Gx", 65)
    d8 = ut("u_d8", 64)
    exp8 = ut("u_exp8", 64)
    A8C = k.t("u_A8C", (128, 64), BF16)
    A8Q = k.t("u_A8Q", (128, 64), BF16)
    BdE = ut("u_BdE", 64)
    Bdv = k.t("u_Bdv", (128, 576), BF16)
    k.v(lambda e: e.memset(Bdv[:, :], 0.0), [], ["u_Bdv"])
    Bst = k.t("u_Bst", (128, 392), BF16)
    eG = exp8
    A0n = ut("u_A0n", 64)
    A0q = ut("u_A0q", 64)
    e2 = d8
    K2 = ut("u_K2", 64)
    K2T = K2
    eGL = ut("u_eGL", 1)
    negNT = ut("u_negNT", 64)
    YdgT = ut("u_YdgT", 64)
    Yblk = YdgT
    WTa = ut("u_WTa", 64)
    WTb = ut("u_WTb", 64)
    DTo = ut("u_DTo", 64)
    DTd = ut("u_DTd", 64)
    Rr = ut("u_R", 64)
    Pp = ut("u_P", 64)
    Mm = ut("u_M", 64)
    Sa = ut("u_Sa", 64)
    Sb = ut("u_Sb", 64)
    assert UCOLS[0] <= NT, UCOLS[0]
    otile = k.t("u_ot", (128, 64), BF16)
    uvst = k.t("u_vst", (128, 64), BF16)

    import os
    _cut = os.environ.get("L1A_CUT", "")
    _nhp = 8
    if _cut.startswith("units:"):
        _nhp = int(_cut.split(":")[1])
    for hp in range(_nhp):
        k.ctx.op("sync", lambda e: e.nop(), list(UKEYS), ["l1a_sc1"])
        k.v(lambda e, hp=hp: e.tensor_scalar(
            out=sc1[:, 0:NT], in0=glin[:, hp * NT: (hp + 1) * NT],
            scalar1=k.tiles["c_l1a_dtb"][:, hp:hp + 1], scalar2=None, op0=ALU.add),
            ["l1a_glin", "c_l1a_dtb"], ["l1a_sc1"])
        k.v(lambda e: e.tensor_relu(out=gf[:, :], in_=sc1[:, 0:NT]),
            ["l1a_sc1"], ["l1a_gf"])
        k.s(lambda e: e.activation(out=sc1[:, 0:NT], in_=sc1[:, 0:NT], func=AF.Abs),
            ["l1a_sc1"], ["l1a_sc1"])
        k.s(lambda e: e.activation(out=sc1[:, 0:NT], in_=sc1[:, 0:NT], func=AF.Exp,
                                   scale=-1.0), ["l1a_sc1"], ["l1a_sc1"])
        onet, onek = k.constf(1.0)
        k.s(lambda e, onet=onet: e.activation(out=sc1[:, 0:NT], in_=sc1[:, 0:NT],
                                              func=AF.Ln, bias=onet[:, 0:1]),
            ["l1a_sc1", onek], ["l1a_sc1"])
        k.v(lambda e: e.tensor_add(out=gf[:, :], in0=gf[:, :], in1=sc1[:, 0:NT]),
            ["l1a_gf", "l1a_sc1"], ["l1a_gf"])
        k.v(lambda e, hp=hp: e.tensor_scalar(
            out=gf[:, :], in0=gf[:, :], scalar1=Af[:, hp:hp + 1], scalar2=-10.0,
            op0=ALU.mult, op1=ALU.max), ["l1a_gf", "l1a_Af"], ["l1a_gf"])
        k.ctx.op("sync", lambda e: e.nop(), [], ["l1a_sc1"] + list(UKEYS))
        for tb in range(NTB):
            psb_, kpb = k.bank()
            k.te(lambda e, hp=hp, psb_=psb_, tb=tb: e.matmul(
                out=psb_[:, 0:TB], lhsT=selb[0:16, hp * 128:(hp + 1) * 128],
                rhs=brows[0:16, tb * TB:(tb + 1) * TB], start=True, stop=True),
                ["c_l1a_selb", "l1a_br"], [kpb])
            k.v(lambda e, hp=hp, psb_=psb_, tb=tb: e.tensor_mul(
                out=kbeta[:, tb * TB:(tb + 1) * TB],
                in0=kh_[:, hp * NT + tb * TB: hp * NT + tb * TB + TB],
                in1=psb_[:, 0:TB]), [kk_, kpb], ["l1a_kbeta"])
        k.v(lambda e: e.memset(Sa[:, :], 0.0), [], ["u_Sa"])
        scur, snxt = (Sa, "u_Sa"), (Sb, "u_Sb")
        for c in range(32):
            coff = hp * NT + c * 64
            k.v(lambda e: e.memset(Gx[:, 0:1], 0.0), [], ["u_Gx"])
            k.v(lambda e, c=c: e.tensor_tensor_scan(
                out=Gx[:, 1:65], data0=ones64[:, :], data1=gf[:, c * 64:(c + 1) * 64],
                initial=0.0, op0=ALU.mult, op1=ALU.add),
                ["c_ones64", "l1a_gf"], ["u_Gx"])
            k.v(lambda e: e.tensor_sub(out=d8[:, :], in0=uap(Gx, [[65, 128], [1, 64]], 1),
                                       in1=uap(Gx, [[65, 128], [8, 8], [0, 8]])),
                ["u_Gx"], ["u_d8"])
            k.s(lambda e: e.activation(out=exp8[:, :], in_=d8[:, :], func=AF.Exp),
                ["u_d8"], ["u_exp8"])
            k.v(lambda e, coff=coff: e.tensor_mul(
                out=A8C[:, :], in0=exp8[:, :], in1=kbeta[:, c * 64:(c + 1) * 64]),
                ["u_exp8", "l1a_kbeta"], ["u_A8C"])
            k.v(lambda e, coff=coff: e.tensor_mul(
                out=A8Q[:, :], in0=exp8[:, :],
                in1=qh_[:, coff:coff + 64]), ["u_exp8", kq], ["u_A8Q"])
            k.s(lambda e: e.activation(out=BdE[:, :], in_=d8[:, :], func=AF.Exp,
                                       scale=-1.0), ["u_d8"], ["u_BdE"])
            k.v(lambda e, coff=coff: e.tensor_mul(
                out=BdE[:, :], in0=BdE[:, :], in1=kh_[:, coff:coff + 64]),
                [kk_, "u_BdE"], ["u_BdE"])
            k.v(lambda e: e.tensor_copy(
                out=ap3(Bdv, [[576, 128], [72, 8], [1, 8]]),
                in_=uap(BdE, [[64, 128], [8, 8], [1, 8]])), ["u_BdE"], ["u_Bdv"])
            k.v(lambda e: e.tensor_copy(
                out=sc1[:, 0:392], in_=uap(Gx, [[65, 128], [8, 7], [0, 56]], 8)),
                ["u_Gx"], ["l1a_sc1"])
            k.v(lambda e: e.tensor_sub(
                out=sc1[:, 0:392], in0=sc1[:, 0:392],
                in1=uap(Gx, [[65, 128], [0, 7], [1, 56]], 1)), ["u_Gx", "l1a_sc1"], ["l1a_sc1"])
            k.s(lambda e: e.activation(out=sc1[:, 0:392], in_=sc1[:, 0:392], func=AF.Exp),
                ["l1a_sc1"], ["l1a_sc1"])
            k.v(lambda e, hp=hp, c=c: e.tensor_mul(
                out=Bst[:, :], in0=sc1[:, 0:392],
                in1=bbap(1, hp * NT + c * 64, [[BPITCH, 128], [0, 7], [1, 56]])),
                ["l1a_sc1", kk_], ["u_Bst"])
            k.s(lambda e: e.activation(out=eG[:, :],
                                       in_=uap(Gx, [[65, 128], [1, 64]], 1),
                                       func=AF.Exp), ["u_Gx"], ["u_exp8"])
            k.v(lambda e, coff=coff: e.scalar_tensor_tensor(
                out=A0n[:, :], in0=eG[:, :], scalar=-1.0,
                in1=kh_[:, coff:coff + 64], op0=ALU.mult, op1=ALU.mult),
                ["u_exp8", kk_], ["u_A0n"])
            k.v(lambda e, coff=coff: e.tensor_mul(
                out=A0q[:, :], in0=eG[:, :], in1=qh_[:, coff:coff + 64]),
                ["u_exp8", kq], ["u_A0q"])
            k.s(lambda e: e.activation(out=e2[:, :],
                                       in_=uap(Gx, [[65, 128], [1, 64]], 1),
                                       func=AF.Exp, scale=-1.0, bias=Gx[:, 64:65]),
                ["u_Gx"], ["u_d8"])
            k.v(lambda e, coff=coff: e.tensor_mul(
                out=K2[:, :], in0=e2[:, :], in1=kh_[:, coff:coff + 64]),
                ["u_d8", kk_], ["u_K2"])
            k.s(lambda e: e.activation(out=eGL[:, :], in_=Gx[:, 64:65], func=AF.Exp),
                ["u_Gx"], ["u_eGL"])
            for hh in range(2):
                hb = hh * 64
                h = 2 * hp + hh
                psC, kpC, iC = k.take_bank()
                psCd, kpCd, iCd = k.take_bank()
                for i in range(1, 8):
                    k.te(lambda e, psC=psC, i=i, hb=hb: e.matmul(
                        out=psC[0:8 * i, 8 * i:8 * i + 8],
                        lhsT=Bst[hb:hb + 64, (i - 1) * 56:(i - 1) * 56 + 8 * i],
                        rhs=A8C[hb:hb + 64, 8 * i:8 * i + 8],
                        start=True, stop=True), ["u_Bst", "u_A8C"], [kpC])
                for j in range(8):
                    k.te(lambda e, psCd=psCd, j=j, hb=hb: e.matmul(
                        out=psCd[0:64, 8 * j:8 * j + 8],
                        lhsT=Bdv[hb:hb + 64, j * 64:(j + 1) * 64],
                        rhs=A8C[hb:hb + 64, 8 * j:8 * j + 8],
                        start=True, stop=True), ["u_Bdv", "u_A8C"], [kpCd])
                k.v(lambda e, psC=psC, hb=hb: e.scalar_tensor_tensor(
                    out=negNT[hb:hb + 64, :], in0=psC[0:64, 0:64], scalar=-1.0,
                    in1=moff[hb:hb + 64, :], op0=ALU.mult, op1=ALU.mult),
                    [kpC, "c_l1a_moff"], ["u_negNT"])
                k.v(lambda e, psCd=psCd, hb=hb: e.tensor_mul(
                    out=YdgT[hb:hb + 64, :], in0=psCd[0:64, 0:64],
                    in1=m8n[hb:hb + 64, :]), [kpCd, "c_l1a_m8neg"], ["u_YdgT"])
                k.free_bank(iC)
                k.free_bank(iCd)
                psD, kpD, iD = k.take_bank()
                psDd, kpDd, iDd = k.take_bank()
                for i in range(1, 8):
                    k.te(lambda e, psD=psD, i=i, hb=hb: e.matmul(
                        out=psD[0:8 * i, 8 * i:8 * i + 8],
                        lhsT=Bst[hb:hb + 64, (i - 1) * 56:(i - 1) * 56 + 8 * i],
                        rhs=A8Q[hb:hb + 64, 8 * i:8 * i + 8],
                        start=True, stop=True), ["u_Bst", "u_A8Q"], [kpD])
                for j in range(8):
                    k.te(lambda e, psDd=psDd, j=j, hb=hb: e.matmul(
                        out=psDd[0:64, 8 * j:8 * j + 8],
                        lhsT=Bdv[hb:hb + 64, j * 64:(j + 1) * 64],
                        rhs=A8Q[hb:hb + 64, 8 * j:8 * j + 8],
                        start=True, stop=True), ["u_Bdv", "u_A8Q"], [kpDd])
                k.v(lambda e, psD=psD, hb=hb: e.tensor_mul(
                    out=DTo[hb:hb + 64, :], in0=psD[0:64, 0:64],
                    in1=moff[hb:hb + 64, :]), [kpD, "c_l1a_moff"], ["u_DTo"])
                k.v(lambda e, psDd=psDd, hb=hb: e.tensor_mul(
                    out=DTd[hb:hb + 64, :], in0=psDd[0:64, 0:64],
                    in1=m8i[hb:hb + 64, :]), [kpDd, "c_l1a_m8incl"], ["u_DTd"])
                k.free_bank(iD)
                k.free_bank(iDd)
                psT, kpT = k.bank()
                k.te(lambda e, psT=psT, hb=hb: e.transpose(
                    out=psT[0:64, 0:64], in_=YdgT[hb:hb + 64, :],
                    identity=identf[hb:hb + 64, hb:hb + 64]),
                    ["u_YdgT", identfk], [kpT])
                k.v(lambda e, psT=psT, hb=hb: e.tensor_copy(
                    out=Yblk[hb:hb + 64, :], in_=psT[0:64, 0:64]), [kpT], ["u_YdgT"])
                k.v(lambda e, hb=hb: e.tensor_copy(out=WTa[hb:hb + 64, :],
                                                   in_=eye[hb:hb + 64, :]),
                    ["c_l1a_eye"], ["u_WTa"])
                cur, nxt = (WTa, "u_WTa"), (WTb, "u_WTb")
                for it in range(7):
                    psW, kpW = k.bank()
                    k.te(lambda e, psW=psW, hb=hb, cur=cur: e.matmul(
                        out=psW[0:64, 0:64], lhsT=Yblk[hb:hb + 64, :],
                        rhs=cur[0][hb:hb + 64, :], start=True, stop=True),
                        ["u_YdgT", cur[1]], [kpW])
                    k.v(lambda e, psW=psW, hb=hb, nxt=nxt: e.tensor_add(
                        out=nxt[0][hb:hb + 64, :], in0=psW[0:64, 0:64],
                        in1=eye[hb:hb + 64, :]), [kpW, "c_l1a_eye"], [nxt[1]])
                    cur, nxt = nxt, cur
                WT, kWT = cur
                psR, kpR, iR = k.take_bank()
                if hh == 0:
                    gvl = k.group("gpsimd")
                    k.dma("gpsimd", lambda e, hp=hp, c=c: e.dma_start(
                        out=uvst[:, :],
                        in_=vdram[0:128, hp * NT + c * 64: hp * NT + (c + 1) * 64]),
                        ["l1a_vdram"], ["u_vst"], gvl)
                psB, kpB = k.bankB()
                k.te(lambda e, psB=psB, hb=hb: e.transpose(
                    out=psB[0:64, 0:64], in_=uvst[hb:hb + 64, :],
                    identity=identb[hb:hb + 64, hb:hb + 64]),
                    ["u_vst", identbk], [kpB])
                k.te(lambda e, psR=psR, hb=hb, scur=scur: e.matmul(
                    out=psR[0:64, 0:64], lhsT=A0n[hb:hb + 64, :],
                    rhs=scur[0][hb:hb + 64, :], start=True, stop=True),
                    ["u_A0n", scur[1]], [kpR])
                k.v(lambda e, psB=psB, hb=hb: e.tensor_copy(
                    out=Rr[hb:hb + 64, :], in_=psB[0:64, 0:64]), [kpB], ["u_R"])
                k.v(lambda e, psR=psR, hb=hb: e.tensor_add(
                    out=Rr[hb:hb + 64, :], in0=Rr[hb:hb + 64, :],
                    in1=psR[0:64, 0:64]), [kpR, "u_R"], ["u_R"])
                k.v(lambda e, hb=hb, c=c, h=h: e.tensor_scalar_mul(
                    out=Rr[hb:hb + 64, :], in0=Rr[hb:hb + 64, :],
                    scalar1=btall[0:64, c * 16 + h:c * 16 + h + 1]),
                    ["u_R", "l1a_btT"], ["u_R"])
                k.free_bank(iR)
                for i in range(8):
                    if i > 0:
                        psP, kpP = k.bank()
                        k.te(lambda e, psP=psP, hb=hb, i=i: e.matmul(
                            out=psP[0:64, 0:64], lhsT=negNT[hb:hb + 8 * i, 0:64],
                            rhs=Mm[hb:hb + 8 * i, :], start=True, stop=True),
                            ["u_negNT", "u_M"], [kpP])
                        k.v(lambda e, psP=psP, hb=hb: e.tensor_add(
                            out=Pp[hb:hb + 64, :], in0=psP[0:64, 0:64],
                            in1=Rr[hb:hb + 64, :]), [kpP, "u_R"], ["u_P"])
                        src_p = (Pp, "u_P")
                    else:
                        src_p = (Rr, "u_R")
                    psM, kpM = k.bank()
                    k.te(lambda e, psM=psM, hb=hb, WT=WT, src_p=src_p: e.matmul(
                        out=psM[0:64, 0:64], lhsT=WT[hb:hb + 64, :],
                        rhs=src_p[0][hb:hb + 64, :], start=True, stop=True),
                        [kWT, src_p[1]], [kpM])
                    k.v(lambda e, psM=psM, hb=hb: e.tensor_copy(
                        out=Mm[hb:hb + 64, :], in_=psM[0:64, 0:64]), [kpM], ["u_M"])
                psO, kpO, iO = k.take_bank()
                k.te(lambda e, psO=psO, hb=hb, scur=scur: e.matmul(
                    out=psO[0:64, 0:64], lhsT=scur[0][hb:hb + 64, :],
                    rhs=A0q[hb:hb + 64, :], start=True, stop=False),
                    [scur[1], "u_A0q"], [kpO])
                k.te(lambda e, psO=psO, hb=hb: e.matmul(
                    out=psO[0:64, 0:64], lhsT=Mm[hb:hb + 64, :],
                    rhs=DTo[hb:hb + 64, :], start=False, stop=False),
                    ["u_M", "u_DTo"], [kpO])
                k.te(lambda e, psO=psO, hb=hb: e.matmul(
                    out=psO[0:64, 0:64], lhsT=Mm[hb:hb + 64, :],
                    rhs=DTd[hb:hb + 64, :], start=False, stop=True),
                    ["u_M", "u_DTd"], [kpO])
                k.v(lambda e, psO=psO, hb=hb: e.tensor_copy(
                    out=otile[hb:hb + 64, :], in_=psO[0:64, 0:64]), [kpO], ["u_ot"])
                k.free_bank(iO)
                psK, kpK = k.bank()
                k.te(lambda e, psK=psK, hb=hb: e.transpose(
                    out=psK[0:64, 0:64], in_=K2[hb:hb + 64, :],
                    identity=identf[hb:hb + 64, hb:hb + 64]),
                    ["u_K2", identfk], [kpK])
                k.v(lambda e, psK=psK, hb=hb: e.tensor_copy(
                    out=K2T[hb:hb + 64, :], in_=psK[0:64, 0:64]), [kpK], ["u_K2"])
                psS, kpS = k.bank()
                k.te(lambda e, psS=psS, hb=hb: e.matmul(
                    out=psS[0:64, 0:64], lhsT=K2T[hb:hb + 64, :],
                    rhs=Mm[hb:hb + 64, :], start=True, stop=True),
                    ["u_K2", "u_M"], [kpS])
                k.v(lambda e, psS=psS, hb=hb, scur=scur, snxt=snxt: e.scalar_tensor_tensor(
                    out=snxt[0][hb:hb + 64, :], in0=scur[0][hb:hb + 64, :],
                    scalar=eGL[hb:hb + 64, 0:1], in1=psS[0:64, 0:64],
                    op0=ALU.mult, op1=ALU.add),
                    [scur[1], "u_eGL", kpS], [snxt[1]])
            scur, snxt = snxt, scur
            god = k.group("gpsimd")
            k.dma("gpsimd", lambda e, hp=hp, c=c: e.dma_start(
                out=odram[0:128, hp * NT + c * 64: hp * NT + (c + 1) * 64],
                in_=otile[:, :]), ["u_ot"], ["l1a_odram"], god)

    if _cut == "noend" or _cut.startswith("units:"):
        return
    for c in range(8):
        k.s(lambda e, c=c: e.activation(out=xb[:, c * NT:(c + 1) * NT],
                                        in_=xf[:, c * NT:(c + 1) * NT],
                                        func=AF.Copy), ["xf"], ["xb", "l1a_glin"])
    gemm_fm(k, "l1a_g1", W["l1a_wg1"], 1024, 64, xb_tile(k), evac_t1)
    sgate, ksg = kh_, kk_

    def evac_g2(o, tb, ps, pskey):
        k.s(lambda e, ps=ps, o=o, tb=tb: e.activation(
            out=sgate[:, o * NT + tb * TB: o * NT + (tb + 1) * TB],
            in_=ps[:, 0:TB], func=AF.Sigmoid,
            bias=k.tiles["c_l1a_bg2"][:, o:o + 1]), [pskey, "c_l1a_bg2"], [ksg])

    gemm_fm(k, "l1a_g2", W["l1a_wg2"], 64, 1024, lambda cc_: (t1, "l1a_t1", 0), evac_g2)
    o2h64 = k.t("c_o2h64", (128, 2), BF16)
    k.v(lambda e: e.memset(o2h64[:, :], 0.0), [], ["c_o2h64"])
    k.v(lambda e: e.memset(o2h64[0:64, 0:1], 1.0 / 64.0), [], ["c_o2h64"])
    k.v(lambda e: e.memset(o2h64[64:128, 1:2], 1.0 / 64.0), [], ["c_o2h64"])
    och = k.t("gw_bf0", (128, 512), BF16)
    for c in range(8):
        for tb in range(NTB):
            go = k.group("gpsimd")
            k.dma("gpsimd", lambda e, c=c, tb=tb: e.dma_start(
                out=och[:, 0:TB],
                in_=odram[0:128, c * NT + tb * TB: c * NT + (tb + 1) * TB]),
                ["l1a_odram"], ["gw_bf0"], go)
            psn, kpn = k.bank()
            k.s(lambda e: e.activation(out=sqv[:, 0:TB], in_=och[:, 0:TB],
                                       func=AF.Square), ["gw_bf0"], ["ln_sqb"])
            k.te(lambda e, psn=psn: e.matmul(
                out=psn[0:2, 0:TB], lhsT=o2h64[:, 0:2], rhs=sqv[:, 0:TB],
                start=True, stop=True), ["c_o2h64", "ln_sqb"], [kpn])
            k.v(lambda e, psn=psn: e.tensor_copy(out=nrow[0:2, 0:TB],
                                                 in_=psn[0:2, 0:TB]), [kpn], ["l2f_nrow"])
            epst, epsk = k.constf(1e-5)
            k.s(lambda e, epst=epst: e.activation(out=nrow[0:2, 0:TB],
                                                  in_=nrow[0:2, 0:TB], func=AF.Ln,
                                                  bias=epst[0:2, 0:1]),
                ["l2f_nrow", epsk], ["l2f_nrow"])
            k.s(lambda e: e.activation(out=nrow[0:2, 0:TB], in_=nrow[0:2, 0:TB],
                                       func=AF.Exp, scale=-0.5), ["l2f_nrow"], ["l2f_nrow"])
            psb_, kpb = k.bank()
            k.te(lambda e, psb_=psb_: e.matmul(
                out=psb_[:, 0:TB], lhsT=sel2h[0:2, 0:128], rhs=nrow[0:2, 0:TB],
                start=True, stop=True), ["c_sel2h", "l2f_nrow"], [kpb])
            k.v(lambda e, psb_=psb_: e.tensor_mul(
                out=och[:, 0:TB], in0=och[:, 0:TB], in1=psb_[:, 0:TB]),
                ["gw_bf0", kpb], ["gw_bf0"])
            k.v(lambda e, c=c, tb=tb: e.scalar_tensor_tensor(
                out=qh_[:, c * NT + tb * TB: c * NT + (tb + 1) * TB],
                in0=och[:, 0:TB], scalar=k.tiles["c_l1a_onw"][:, 0:1],
                in1=sgate[:, c * NT + tb * TB: c * NT + (tb + 1) * TB],
                op0=ALU.mult, op1=ALU.mult),
                ["gw_bf0", "c_l1a_onw", ksg], [kq])

    def evac_wo(o, tb, ps, pskey):
        sl = slice(o * NT + tb * TB, o * NT + (tb + 1) * TB)
        k.v(lambda e, ps=ps, sl=sl: e.tensor_add(
            out=xf[:, sl], in0=xf[:, sl], in1=ps[:, 0:TB]), [pskey, "xf"], ["xf"])

    gemm_fm(k, "l1a_wo", W["l1a_wo"], 1024, 1024, lambda cc_: (qh_, kq, cc_ * NT), evac_wo)
    k.ctx.op("sync", lambda e: e.nop(), ["l1a_br"], ["ln_rows"])
    ln_fm(k, "l1a_ln", rows_t=TView(brows, 0, 2 * TB))


"""Host-side weight preparation (pure layout: transposes, reshapes,
permutations, block-diagonal packing, constant tables)."""
import numpy as np
import ml_dtypes

NPF32 = np.float32


def cc(a):
    return np.ascontiguousarray(a, dtype=NPF32)


def prep_common():
    d = {}
    d["cst_ones_row"] = np.ones((1, 128), NPF32)
    d["cst_ones_mean"] = np.full((128, 1), 1.0 / 1024.0, NPF32)
    return d


def prep_l1f(ins):
    return {
        "l1f_wr": cc(ins["L1_ffn_W_r"].T),
        "l1f_wk": cc(ins["L1_ffn_W_k"].T),
        "l1f_wv": cc(ins["L1_ffn_W_v"].T),
        "l1f_mur": cc(ins["L1_ffn_mu_r"].reshape(8, 128).T),
        "l1f_muk": cc(ins["L1_ffn_mu_k"].reshape(8, 128).T),
    }


def prep_l3a(ins):
    return {
        "l3a_wq": cc(ins["L3_attn_w_q"].T),
        "l3a_wk": cc(ins["L3_attn_w_k"].T),
        "l3a_wv": cc(ins["L3_attn_w_v"].T),
        "l3a_wo": cc(ins["L3_attn_w_out"].T),
        "l3a_scale": cc(ins["L3_attn_softmax_scale"].reshape(1, 1)),
        "cst_ident_f": np.eye(128, dtype=NPF32),
    }


def prep_l3f(ins):
    sel = np.zeros((8, 1024), NPF32)
    for e in range(8):
        sel[e, e * 128:(e + 1) * 128] = 1.0
    return {
        "cst_sel8": sel,
        "l3f_wr": cc(ins["L3_ffn_Wr"]),
        "l3f_wg": cc(ins["L3_ffn_Wg"].reshape(8 * 1024, 2048)),
        "l3f_wu": cc(ins["L3_ffn_Wu"].reshape(8 * 1024, 2048)),
        "l3f_wd": cc(ins["L3_ffn_Wd"].reshape(8 * 2048, 1024)),
        "l3f_sg": cc(ins["L3_ffn_Sg"]),
        "l3f_su": cc(ins["L3_ffn_Su"]),
        "l3f_sd": cc(ins["L3_ffn_Sd"]),
        "cst_ident_f": np.eye(128, dtype=NPF32),
    }


def prep_l0f(ins):
    rm = ins["L0_ffn_rail_maker"]
    ru = ins["L0_ffn_rail_up_proj"]
    rm1 = np.zeros((128, 4096), NPF32)
    for h in range(4):
        for rg in range(8):
            t = h * 8 + rg
            blk = np.zeros((128, 128), NPF32)
            for ri in range(8):
                blk[ri * 16:(ri + 1) * 16, ri * 16:(ri + 1) * 16] = rm[h, 8 * rg + ri]
            rm1[:, t * 128:(t + 1) * 128] = blk
    rm2 = np.zeros((128 * 128, 128), NPF32)
    ti = 0
    for h in range(4):
        for rhalf in range(2):
            for rg4 in range(4):
                rg = rhalf * 4 + rg4
                for sub in range(4):
                    blk = np.zeros((128, 128), NPF32)
                    for ccv in range(2):
                        ri = sub * 2 + ccv
                        blk[ri * 16:(ri + 1) * 16, ccv * 64:(ccv + 1) * 64] = \
                            ru[h, 8 * rg + ri].T
                    rm2[ti * 128:(ti + 1) * 128, :] = blk
                    ti += 1
    return {
        "l0f_rm1": cc(rm1),
        "l0f_rm2": cc(rm2),
        "l0f_cproj": cc(ins["L0_ffn_w_cproj"].T),
    }


def prep_l0a(ins):
    Wqkv = ins["L0_attn_W_qkv"]
    b = ins["L0_attn_b_qkv"]
    qi = np.array([j * 192 + e for j in range(16) for e in range(64)])
    ki = qi + 64
    vi = qi + 128
    r = np.arange(128)
    Ult = (r[:, None] < r[None, :]).astype(NPF32)
    Ule = (r[:, None] <= r[None, :]).astype(NPF32)
    strip = np.concatenate([Ult, Ule], axis=1)
    return {
        "l0a_wq": cc(Wqkv[:, qi]),
        "l0a_wk": cc(Wqkv[:, ki]),
        "l0a_wv": cc(Wqkv[:, vi]),
        "l0a_bq": cc(b[qi].reshape(8, 128).T),
        "l0a_bk": cc(b[ki].reshape(8, 128).T),
        "l0a_bv": cc(b[vi].reshape(8, 128).T),
        "l0a_wo": cc(ins["L0_attn_W_out"]),
        "l0a_bo": cc(ins["L0_attn_b_out"].reshape(8, 128).T),
        "l0a_mask": cc(strip),
        "cst_ident_b": np.eye(128, dtype=NPF32).astype(ml_dtypes.bfloat16),
    }


def prep_l2f(ins):
    DIM_HEAD, WINDOW, n = 64, 512, 2048
    half_idx = np.arange(0, DIM_HEAD, 2, dtype=np.float64)
    inv_freq = 1.0 / 10000.0 ** (half_idx / DIM_HEAD)
    t = np.arange(n, dtype=np.float64)
    freqs = t[:, None] * inv_freq[None, :]
    freqs = np.concatenate([freqs, freqs], -1)
    base = (half_idx + 0.4 * DIM_HEAD) / (1.4 * DIM_HEAD)
    power = (t - n // 2) / float(WINDOW // 2)
    scale = base[None, :] ** power[:, None]
    scale = np.concatenate([scale, scale], -1)
    cos, sin = np.cos(freqs), np.sin(freqs)
    sgn = np.where(np.arange(DIM_HEAD) < 32, -1.0, 1.0)
    cq1 = (cos * scale).T
    cq2 = (sin * scale).T * sgn[:, None]
    ck1 = (cos / scale).T
    ck2 = (sin / scale).T * sgn[:, None]
    def dup(a):
        return cc(np.concatenate([a, a], axis=0))
    qs = ins["L2_ffn_q_scale"]
    ks = ins["L2_ffn_k_scale"]
    p = (np.arange(64) + 32) % 64
    def sc2(a):
        return cc(np.tile(a.T, (2, 1)))
    kj = np.arange(128)[:, None]
    v = np.arange(1408)[None, :]
    band = ((v >= 384 + kj) & (v <= 896 + kj)).astype(NPF32)
    sel2h = np.zeros((2, 128), NPF32)
    sel2h[0, 0:64] = 1.0
    sel2h[1, 64:128] = 1.0
    w1 = ins["L2_ffn_w1"]
    b1 = ins["L2_ffn_b1"]
    w1p = np.zeros((2, 1024, 5632), NPF32)
    b1p = np.zeros((2, 5632), NPF32)
    for i in range(22):
        na = min(128, 2730 - i * 128)
        w1p[:, :, (2 * i) * 128:(2 * i) * 128 + na] = w1[:, :, i * 128:i * 128 + na]
        w1p[:, :, (2 * i + 1) * 128:(2 * i + 1) * 128 + na] = \
            w1[:, :, 2730 + i * 128:2730 + i * 128 + na]
        b1p[:, (2 * i) * 128:(2 * i) * 128 + na] = b1[:, i * 128:i * 128 + na]
        b1p[:, (2 * i + 1) * 128:(2 * i + 1) * 128 + na] = \
            b1[:, 2730 + i * 128:2730 + i * 128 + na]
    w2 = ins["L2_ffn_w2"]
    w2p = np.zeros((2, 2816, 1024), NPF32)
    w2p[:, 0:2730, :] = w2
    return {
        "l2f_wqkv": cc(ins["L2_ffn_wqkv"].reshape(2 * 1024, 1536)),
        "l2f_wo": cc(ins["L2_ffn_wo"].reshape(2 * 512, 1024)),
        "l2f_qs": sc2(qs), "l2f_qsp": sc2(qs[:, p]),
        "l2f_ks": sc2(ks), "l2f_ksp": sc2(ks[:, p]),
        "l2f_cq1": dup(cq1), "l2f_cq2": dup(cq2),
        "l2f_ck1": dup(ck1), "l2f_ck2": dup(ck2),
        "l2f_lng": cc(ins["L2_ffn_ln_g"].reshape(2, 8, 128).transpose(2, 0, 1).reshape(128, 16)),
        "l2f_lnb": cc(ins["L2_ffn_ln_b"].reshape(2, 8, 128).transpose(2, 0, 1).reshape(128, 16)),
        "l2f_w1": cc(w1p.reshape(2 * 1024, 5632)),
        "l2f_b1": cc(b1p.reshape(2, 44, 128).transpose(2, 0, 1).reshape(128, 88)),
        "l2f_w2": cc(w2p.reshape(2 * 2816, 1024)),
        "l2f_b2": cc(ins["L2_ffn_b2"].reshape(2, 8, 128).transpose(2, 0, 1).reshape(128, 16)),
        "l2f_band": cc(band),
        "cst_sel2h": sel2h,
        "cst_ident_b": np.eye(128, dtype=NPF32).astype(ml_dtypes.bfloat16),
    }


def prep_l2a(ins):
    cw = ins["L2_attn_conv_w"]
    cb = ins["L2_attn_conv_b"]
    convw = np.zeros((128, 256), NPF32)
    convb = np.zeros((128, 64), NPF32)
    dtbt = np.zeros((128, 64), NPF32)
    dskt = np.zeros((128, 64), NPF32)
    alog = np.zeros((128, 1024), NPF32)
    for li in range(4):
        for c in range(16):
            for p in range(128):
                convw[p, li * 64 + c * 4: li * 64 + c * 4 + 4] = cw[li, c * 128 + p]
            convb[:, li * 16 + c] = cb[li, c * 128:(c + 1) * 128]
            dtbt[:, li * 16 + c] = ins["L2_attn_dt_proj_b"][li, c * 128:(c + 1) * 128]
            dskt[:, li * 16 + c] = ins["L2_attn_D_skip"][li, c * 128:(c + 1) * 128]
            alog[:, li * 256 + c * 16: li * 256 + (c + 1) * 16] = \
                ins["L2_attn_A_log"][li, c * 128:(c + 1) * 128, :]
    sel = np.zeros((32, 4096), NPF32)
    for n in range(16):
        sel[n, n * 128:(n + 1) * 128] = 1.0
        sel[16 + n, 2048 + n * 128:2048 + (n + 1) * 128] = 1.0
    lnw = np.zeros((128, 32), NPF32)
    lnb = np.zeros((128, 32), NPF32)
    for li in range(4):
        for c in range(8):
            lnw[:, li * 8 + c] = ins["L2_attn_ln_w"][li, c * 128:(c + 1) * 128]
            lnb[:, li * 8 + c] = ins["L2_attn_ln_b"][li, c * 128:(c + 1) * 128]
    return {
        "l2a_inw": cc(ins["L2_attn_in_proj_w"].transpose(0, 2, 1).reshape(4 * 1024, 4096)),
        "l2a_convw": cc(convw), "l2a_convb": cc(convb),
        "l2a_xpw": cc(ins["L2_attn_x_proj_w"].transpose(0, 2, 1).reshape(4 * 2048, 96)),
        "l2a_dtw": cc(ins["L2_attn_dt_proj_w"].transpose(0, 2, 1).reshape(4 * 64, 2048)),
        "l2a_dtb": cc(dtbt), "l2a_alog": cc(alog), "l2a_dsk": cc(dskt),
        "l2a_outw": cc(ins["L2_attn_out_proj_w"].transpose(0, 2, 1).reshape(4 * 2048, 1024)),
        "l2a_lnw": cc(lnw), "l2a_lnb": cc(lnb),
        "l2a_sel": sel.astype(ml_dtypes.bfloat16),
    }


def prep_l1a(ins):
    cw = np.zeros((128, 96), NPF32)
    cb = np.zeros((128, 24), NPF32)
    for qkv, (w_, b_) in enumerate([(ins["L1_attn_qcw"], ins["L1_attn_qcb"]),
                                    (ins["L1_attn_kcw"], ins["L1_attn_kcb"]),
                                    (ins["L1_attn_vcw"], ins["L1_attn_vcb"])]):
        for c in range(8):
            for p in range(128):
                cw[p, qkv * 32 + c * 4: qkv * 32 + c * 4 + 4] = w_[c * 128 + p]
            cb[:, qkv * 8 + c] = b_[c * 128:(c + 1) * 128]
    p = np.arange(128)
    alog = np.zeros((128, 8), NPF32)
    dtb = np.zeros((128, 8), NPF32)
    for hp in range(8):
        h = 2 * hp + p // 64
        alog[:, hp] = ins["L1_attn_A_log"][h]
        dtb[:, hp] = ins["L1_attn_dt_bias"][(h) * 64 + p % 64]
    eye = np.vstack([np.eye(64, dtype=NPF32)] * 2)
    s_ = np.arange(64)[:, None]
    t_ = np.arange(64)[None, :]
    w8 = (s_ // 8) == (t_ // 8)
    m8neg = np.vstack([(-1.0 * (w8 & (s_ < t_))).astype(NPF32)] * 2)
    m8incl = np.vstack([(1.0 * (w8 & (s_ <= t_))).astype(NPF32)] * 2)
    moff = np.vstack([(1.0 * ((s_ // 8) < (t_ // 8))).astype(NPF32)] * 2)
    selb = np.zeros((16, 1024), NPF32)
    for hp in range(8):
        for pp in range(128):
            selb[2 * hp + pp // 64, hp * 128 + pp] = 1.0
    return {
        "l1a_wq": cc(ins["L1_attn_Wq"]),
        "l1a_wk": cc(ins["L1_attn_Wk"]),
        "l1a_wv": cc(ins["L1_attn_Wv"]),
        "l1a_wf1": cc(ins["L1_attn_Wf1"]),
        "l1a_wf2": cc(ins["L1_attn_Wf2"]),
        "l1a_wbmat": cc(ins["L1_attn_Wb"]),
        "l1a_wg1": cc(ins["L1_attn_Wg1"]),
        "l1a_wg2": cc(ins["L1_attn_Wg2"]),
        "l1a_wo": cc(ins["L1_attn_Wo"]),
        "l1a_cw": cc(cw), "l1a_cb": cc(cb),
        "l1a_alog": cc(alog), "l1a_dtb": cc(dtb),
        "l1a_onw": cc(np.tile(ins["L1_attn_o_norm_w"], 2).reshape(128, 1)),
        "cst_ident_b": np.eye(128, dtype=NPF32).astype(ml_dtypes.bfloat16),
        "l1a_eye": cc(eye).astype(ml_dtypes.bfloat16),
        "l1a_m8neg": cc(m8neg).astype(ml_dtypes.bfloat16),
        "l1a_m8incl": cc(m8incl).astype(ml_dtypes.bfloat16),
        "l1a_moff": cc(moff).astype(ml_dtypes.bfloat16),
        "l1a_selb": cc(selb).astype(ml_dtypes.bfloat16),
        "l1a_bg2": cc(ins["L1_attn_bg2"].reshape(8, 128).T),
        "cst_sel2h": np.concatenate([np.concatenate([np.ones((1, 64), NPF32), np.zeros((1, 64), NPF32)], 1),
                                     np.concatenate([np.zeros((1, 64), NPF32), np.ones((1, 64), NPF32)], 1)], 0),
        "cst_ident_f": np.eye(128, dtype=NPF32),
    }


from concourse.bass_utils import run_bass_kernel_spmd
import ml_dtypes as _mld

_BLOCKS = ["l0a", "l0f", "l1a", "l1f", "l2a", "l2f", "l3a", "l3f"]
_EMIT = {"l1f": emit_l1_ffn, "l3a": emit_l3_attn, "l3f": emit_l3_ffn,
         "l0f": emit_l0_ffn, "l0a": emit_l0_attn, "l2f": emit_l2_ffn,
         "l2a": emit_l2_attn, "l1a": emit_l1_attn}
_NBB = {"l1f": 2, "l3a": 2, "l3f": 2, "l0f": 2, "l2f": 2, "l2a": 2, "l1a": 2}


def _build_block_program(tag, wnames):
    import contextlib
    nc = bass.Bass("TRN2", target_bir_lowering=False)
    es = contextlib.ExitStack()
    k = KCtx(nc, es)
    x_in = nc.dram_tensor("x_in", [1024, NT], F32, kind="ExternalInput").ap()
    x_out = nc.dram_tensor("x_out", [1024, NT], F32, kind="ExternalOutput").ap()
    waps = {}
    for name, arr in wnames.items():
        dt = BF16 if arr.dtype == _mld.bfloat16 else F32
        waps[name] = nc.dram_tensor(name, list(arr.shape), dt,
                                    kind="ExternalInput").ap()
    xf = k.t("xf", (128, 8 * NT), F32)
    xb = k.t("xb", (128, 8 * NT), BF16)
    gld = k.group("gpsimd")
    for c in range(8):
        k.dma("gpsimd", lambda e, c=c: e.dma_start(
            out=xf[:, c * NT:(c + 1) * NT], in_=x_in[c * 128:(c + 1) * 128, :]),
            [], ["xf"], gld)
    for c in range(8):
        k.s(lambda e, c=c: e.activation(out=xb[:, c * NT:(c + 1) * NT],
                                        in_=xf[:, c * NT:(c + 1) * NT],
                                        func=AF.Copy), ["xf"], ["xb"])
    k.nbb = _NBB.get(tag, 3)
    _EMIT[tag](k, waps)
    gst = k.group("gpsimd")
    for c in range(8):
        k.dma("gpsimd", lambda e, c=c: e.dma_start(
            out=x_out[c * 128:(c + 1) * 128, :], in_=xf[:, c * NT:(c + 1) * NT]),
            ["xf"], [f"xo{c}"], gst)
    k.ctx.op("sync", lambda e: e.nop(), [f"xo{c}" for c in range(8)], [])
    with nc.Block() as block:
        k.ctx.emit(block)
    es.close()
    return nc


_PREPPED = None
_PROGS = None


def kernel(**inputs):
    global _PREPPED, _PROGS
    x = np.asarray(inputs["L0_attn_inputs"], dtype=np.float32)
    Wc = prep_common()
    per = {}
    for tag in _BLOCKS:
        W = dict(Wc)
        W.update(globals()["prep_" + tag](inputs))
        per[tag] = W
    if _PROGS is None:
        _PROGS = {tag: _build_block_program(tag, per[tag]) for tag in _BLOCKS}
    cur = [np.ascontiguousarray(x[b].T) for b in range(8)]
    for tag in _BLOCKS:
        in_maps = []
        for b in range(8):
            m = {"x_in": cur[b]}
            m.update(per[tag])
            in_maps.append(m)
        res = run_bass_kernel_spmd(_PROGS[tag], in_maps, core_ids=list(range(8)))
        cur = [np.ascontiguousarray(r["x_out"]) for r in res.results]
    return np.stack([c.T for c in cur]).astype(np.float32)


# version 04
# speedup vs baseline: 1.0041x; 1.0041x over previous
from collections import defaultdict

ENGINES = ("sync", "vector", "scalar", "tensor", "gpsimd")
N_DMA_SEMS = 40
SLOT_RANGES = {"sync": (0, 20), "gpsimd": (20, 32), "scalar": (32, 36),
               "vector": (36, 40), "tensor": (36, 40)}


class _Op:
    __slots__ = ("stream", "fn", "reads", "writes", "idx", "tag",
                 "is_dma", "group")

    def __init__(self, stream, fn, reads, writes, tag, is_dma=False, group=None):
        self.stream = stream
        self.fn = fn
        self.reads = reads
        self.writes = writes
        self.tag = tag
        self.idx = None
        self.is_dma = is_dma
        self.group = group


class DmaGroup:
    __slots__ = ("sem_slot", "base", "count", "gid", "prev_group")

    def __init__(self, gid, sem_slot, base, prev_group):
        self.gid = gid
        self.sem_slot = sem_slot
        self.base = base
        self.count = 0
        self.prev_group = prev_group

    @property
    def total(self):
        return self.base + self.count


class DepCtx:
    def __init__(self, nc, es):
        self.nc = nc
        self.es = es
        self.ops = []
        self.per_engine = defaultdict(list)
        self.stream_counts = defaultdict(int)
        self.groups = []
        self._next_slot = {e: SLOT_RANGES[e][0] for e in SLOT_RANGES}
        self.extra_reads = []
        self._slot_total = [0] * N_DMA_SEMS
        self._slot_last_group = [None] * N_DMA_SEMS

    def group(self, engine="sync"):
        lo, hi = SLOT_RANGES[engine]
        slot = self._next_slot[engine]
        nxt = slot + 1
        if nxt >= hi:
            nxt = lo
        self._next_slot[engine] = nxt
        g = DmaGroup(len(self.groups), slot, self._slot_total[slot],
                     self._slot_last_group[slot])
        self._slot_last_group[slot] = g
        self.groups.append(g)
        return g

    def op(self, engine, fn, reads=(), writes=(), tag=""):
        o = _Op(engine, fn, list(reads) + self.extra_reads, list(writes), tag)
        self._record(o)
        return o

    def dma(self, engine, fn, reads=(), writes=(), tag="", group=None):
        assert group is not None, "dma needs a group"
        lo, hi = SLOT_RANGES[engine]
        assert lo <= group.sem_slot < hi, \
            f"dma group slot {group.sem_slot} not in {engine} range (pass engine to group())"
        o = _Op(engine, fn, list(reads) + self.extra_reads, list(writes), tag,
                is_dma=True, group=group)
        group.count += 1
        self._slot_total[group.sem_slot] += 1
        self._record(o)
        return o

    def _record(self, o):
        self.ops.append(o)
        self.per_engine[o.stream].append(o)
        if not o.is_dma:
            self.stream_counts[o.stream] += 1
            o.idx = self.stream_counts[o.stream]
        else:
            o.idx = None

    def emit(self, block):
        def tok(o):
            if o.is_dma:
                return ("dmag%d" % o.group.sem_slot, o.group)
            return (o.stream, o.idx)

        mem = {}
        deps = {}
        seen_groups = set()
        pos = {}
        for i, o in enumerate(self.ops):
            pos[id(o)] = i
        stream_pos = defaultdict(dict)
        for o in self.ops:
            need = {}
            own_group = o.group if o.is_dma else None

            def add(t):
                if t is None:
                    return
                s, v = t
                if own_group is not None and s.startswith("dmag") and v is own_group:
                    return
                cur = need.get(s)
                if cur is None:
                    need[s] = v
                else:
                    if s.startswith("dmag"):
                        if v.total > cur.total:
                            need[s] = v
                    elif v > cur:
                        need[s] = v
            for k in o.reads:
                st = mem.get(k)
                if st is not None and st[0] is not None:
                    add(st[0])
            for k in o.writes:
                st = mem.get(k)
                if st is not None:
                    if st[0] is not None:
                        add(st[0])
                    for r in st[1]:
                        add(r)
            if o.is_dma and o.group.gid not in seen_groups:
                seen_groups.add(o.group.gid)
                prev = o.group.prev_group
                if prev is not None:
                    add(("dmag%d" % o.group.sem_slot, prev))
            if o.stream == "tensor" and not o.is_dma:
                need.pop("tensor", None)
            deps[id(o)] = need
            t = tok(o)
            if o.is_dma:
                stream_pos[t[0]][id(t[1])] = pos[id(o)]
            else:
                stream_pos[o.stream][o.idx] = pos[id(o)]
            for k in o.reads:
                st = mem.setdefault(k, [None, []])
                st[1].append(t)
            for k in o.writes:
                mem[k] = [t, []]

        for o in self.ops:
            need = deps[id(o)]
            for s in list(need):
                if s.startswith("dmag"):
                    g = need[s]
                    gp = stream_pos[s].get(id(g), -1)
                    assert gp <= pos[id(o)], \
                        f"FORWARD dma-group ref: op@{pos[id(o)]} tag={o.tag} waits group gid={g.gid} slot={g.sem_slot} last-dma@{gp}"
                    need[s] = g.total
                else:
                    vp = stream_pos[s].get(need[s], None)
                    assert vp is not None and vp <= pos[id(o)], \
                        f"FORWARD ref: op@{pos[id(o)]} tag={o.tag} stream {o.stream} waits {s}>={need[s]} recorded@{vp}"


        self.engine_sems = {}
        for e in ENGINES:
            if self.per_engine[e]:
                self.engine_sems[e] = self.es.enter_context(
                    self.nc.semaphore(f"esem_{e}"))
        used_slots = sorted({g.sem_slot for g in self.groups if g.count})
        self.dma_sems = {}
        for slot in used_slots:
            self.dma_sems[slot] = self.es.enter_context(
                self.nc.semaphore(f"dsem{slot}"))

        snap = defaultdict(dict)
        knows = {e: defaultdict(int) for e in ENGINES}
        plans = defaultdict(list)
        for o in self.ops:
            e = o.stream
            kn = knows[e]
            for s, v in sorted(deps[id(o)].items()):
                if kn[s] >= v:
                    continue
                plans[e].append(("wait", s, v))
                sn = snap[s].get(v)
                if sn:
                    for s2, v2 in sn.items():
                        if v2 > kn[s2]:
                            kn[s2] = v2
                kn[s] = max(kn[s], v)
            plans[e].append(("op", o))
            if o.is_dma:
                key = o.group.total
                s = "dmag%d" % o.group.sem_slot
                sn = snap[s].get(key)
                if sn is None:
                    sn = dict(kn)
                else:
                    for s2, v2 in kn.items():
                        pass
                    newsn = {}
                    for s2 in set(sn) & set(kn):
                        newsn[s2] = min(sn[s2], kn[s2])
                    sn = newsn
                sn[s] = max(sn.get(s, 0), key)
                snap[s][key] = sn
            else:
                sn = dict(kn)
                sn[e] = o.idx
                snap[e][o.idx] = sn

        self._plans = plans
        for e in ENGINES:
            plan = plans[e]
            if not plan:
                continue
            section = getattr(block, e)

            def make(plan=plan, e=e):
                def body(eng):
                    for item in plan:
                        if item[0] == "wait":
                            _, s, v = item
                            if s.startswith("dmag"):
                                eng.wait_ge(self.dma_sems[int(s[4:])], 16 * v)
                            else:
                                eng.wait_ge(self.engine_sems[s], v)
                        else:
                            o = item[1]
                            instr = o.fn(eng)
                            if o.is_dma:
                                instr.then_inc(self.dma_sems[o.group.sem_slot], 16)
                            else:
                                instr.then_inc(self.engine_sems[e], 1)
                return body
            section(make())


def sbuf(nc, es, name, shape, dtype):
    return es.enter_context(nc.sbuf_tensor(name, shape, dtype))


def psum(nc, es, name, shape, dtype):
    return es.enter_context(nc.psum_tensor(name, shape, dtype))


"""Block emitters for the megakernel. All layouts feature-major:
stream xf (fp32) / xb (bf16) as (128, 8*2048): chunk c holds channels
[128c, 128c+128), free = 2048 tokens.
"""
import numpy as np
import concourse.bass as bass
import concourse.mybir as mybir

F32 = mybir.dt.float32
BF16 = mybir.dt.bfloat16
AF = mybir.ActivationFunctionType
ALU = mybir.AluOpType

NT = 2048
TB = 512
NTB = NT // TB


def ap3(t, dims, offset=0):
    a = t[:, :] if not isinstance(t, bass.AP) else t
    return bass.AP(a.tensor, a.offset + offset, [list(d) for d in dims])


class TileDict(dict):
    def __init__(self):
        super().__init__()
        self.freed = {}

    def __missing__(self, key):
        return self.freed[key]


class KCtx:
    def __init__(self, nc, es):
        self.nc, self.es = nc, es
        self.ctx = DepCtx(nc, es)
        self.tiles = TileDict()
        self.psums = {}
        self.sbuf_bytes = 0
        self.pb = []
        for i in range(7):
            h = es.enter_context(nc.psum_tensor(f"pb{i}", [128, 512], F32))
            self.pb.append(h)
        self.pbB = es.enter_context(nc.psum_tensor("pbB", [128, 1024], BF16))
        self._ps_rr = 0

    def t(self, name, shape=None, dtype=None):
        if name in self.tiles and shape is not None:
            assert list(self.tiles[name].shape) == list(shape), \
                f"tile {name} shape mismatch {self.tiles[name].shape} vs {shape}"
        if name not in self.tiles:
            assert shape is not None, f"tile {name} not allocated yet"
            arena = getattr(self, "arena", None)
            if arena is not None and not self.arena_persist(name):
                h = arena.sbuf_tensor(name + f"_e{self.arena_epoch}",
                                      list(shape), dtype)
                cm = None
            else:
                cm = self.nc.sbuf_tensor(name, list(shape), dtype)
                h = cm.__enter__()
            self.tiles[name] = h
            self._cms = getattr(self, "_cms", {})
            self._cms[name] = cm
            per = int(np.prod(shape[1:])) * mybir.dt.size(dtype)
            self.sbuf_bytes += per * 128
        return self.tiles[name]

    def free_tile(self, name):
        cm = self._cms.pop(name)
        if cm is not None:
            cm.__exit__(None, None, None)
        self.tiles.freed[name] = self.tiles[name]
        del self.tiles[name]
        self._retired = getattr(self, "_retired", [])
        self._retired.append(name)

    def alias_fence(self, newkeys):
        old = getattr(self, "_retired", [])
        if old:
            self.ctx.op("sync", lambda e: e.nop(), list(old), list(newkeys))
            self._retired = []

    def bank(self):
        for _ in range(7):
            i = self._ps_rr
            self._ps_rr = (self._ps_rr + 1) % 7
            if i not in getattr(self, "_reserved", set()):
                return self.pb[i], f"pb{i}"
        raise RuntimeError("no free psum bank")

    def bankB(self):
        return self.pbB, "pbB"

    def take_bank(self):
        if not hasattr(self, "_reserved"):
            self._reserved = set()
        for i in range(7):
            if i not in self._reserved:
                self._reserved.add(i)
                return self.pb[i], f"pb{i}", i
        raise RuntimeError("all banks reserved")

    def free_bank(self, i):
        self._reserved.discard(i)

    def v(self, fn, r, w, tag=""):
        return self.ctx.op("vector", fn, r, w, tag)

    def s(self, fn, r, w, tag=""):
        return self.ctx.op("scalar", fn, r, w, tag)

    def g(self, fn, r, w, tag=""):
        return self.ctx.op("gpsimd", fn, r, w, tag)

    def te(self, fn, r, w, tag=""):
        return self.ctx.op("tensor", fn, r, w, tag)

    def dma(self, eng, fn, r, w, group, tag=""):
        return self.ctx.dma(eng, fn, r, w, tag, group=group)

    def group(self, engine="sync"):
        return self.ctx.group(engine)

    def constb(self, val):
        name = "cb_%x" % (np.float32(val).view(np.uint32))
        if name not in self.tiles:
            t = self.t(name, (128, 1), BF16)
            self.v(lambda e, t=t, val=val: e.memset(t[:, :], float(val)), [], [name])
        return self.tiles[name], name

    def ones_row(self, dtype="f32"):
        name = "c_ones_row" if dtype == "f32" else "c_ones_rowb"
        if name not in self.tiles:
            t = self.t(name, (1, 128), F32 if dtype == "f32" else BF16)
            self.v(lambda e, t=t: e.memset(t[:, :], 1.0), [], [name])
        return self.tiles[name], name

    def constf(self, val):
        name = "cf_%x" % (np.float32(val).view(np.uint32))
        if name not in self.tiles:
            t = self.t(name, (128, 1), F32)
            self.v(lambda e, t=t, val=val: e.memset(t[:, :], float(val)), [], [name])
        return self.tiles[name], name


def load_const(k, name, hbm_ap, shape, dtype, g=None):
    t = k.t("c_" + name, shape, dtype)
    g = g or k.group()
    k.dma("sync", lambda e: e.dma_start(out=t[:, :], in_=hbm_ap), [], ["c_" + name], g)
    return t


def gemm_fm(k, name, wap, cin, cout, xtile, evac, wq="sync", cast_eng="scalar",
            tokb=TB, ntok=NT, cin_off=0):
    nci = (cin + 127) // 128
    nco = (cout + 127) // 128
    ntb = ntok // tokb
    SLAB = 4
    nslab = (nci + SLAB - 1) // SLAB
    fresh = "gw_stg0" not in k.tiles
    wstg0 = k.t("gw_stg0", (128, SLAB * 128), F32)
    if fresh:
        k.v(lambda e: e.memset(wstg0[:, :], 0.0), [], ["gw_stg0"])
    wstg = [wstg0, wstg0]
    wbf0 = k.t("gw_bf0", (128, SLAB * 128), BF16)
    wbf = [wbf0, wbf0]
    par = 0
    for o in range(nco):
        ocols = min(128, cout - o * 128)
        for sl in range(nslab):
            c0 = sl * SLAB
            ncs = min(SLAB, nci - c0)
            stg, sbf = wstg[par], wbf[par]
            kst, kbf = "gw_stg0", "gw_bf0"
            prows = 128
            if c0 + ncs == nci and cin % 128:
                assert ncs == 1, "partial cin chunk must be alone in its slab"
                prows = cin % 128
            gld = k.group()
            src = bass.AP(wap.tensor,
                          wap.offset + (cin_off + c0 * 128) * cout + o * 128,
                          [[cout, prows], [128 * cout, ncs], [1, ocols]])
            dst = ap3(stg, [[SLAB * 128, prows], [128, ncs], [1, ocols]])
            k.dma(wq, lambda e, src=src, dst=dst: e.dma_start(out=dst, in_=src),
                  [], [kst], gld, tag=f"{name}_w{o}_{sl}")
            if ocols == 128:
                cin_ap = lambda stg=stg, n=ncs * 128, prows=prows: stg[0:prows, 0:n]
                cout_ap = lambda sbf=sbf, n=ncs * 128, prows=prows: sbf[0:prows, 0:n]
            else:
                cin_ap = lambda stg=stg, ncs=ncs, ocols=ocols, prows=prows: ap3(
                    stg, [[SLAB * 128, prows], [128, ncs], [1, ocols]])
                cout_ap = lambda sbf=sbf, ncs=ncs, ocols=ocols, prows=prows: ap3(
                    sbf, [[SLAB * 128, prows], [128, ncs], [1, ocols]])
            if cast_eng == "scalar":
                k.s(lambda e, cin_ap=cin_ap, cout_ap=cout_ap: e.activation(
                    out=cout_ap(), in_=cin_ap(), func=AF.Copy), [kst], [kbf])
            else:
                k.v(lambda e, cin_ap=cin_ap, cout_ap=cout_ap: e.tensor_copy(
                    out=cout_ap(), in_=cin_ap()), [kst], [kbf])
            if nslab == 1:
                for tb in range(ntb):
                    ps, pskey = k.bank()
                    for c in range(ncs):
                        xh, xk, xoff = xtile(c0 + c)
                        k.te(lambda e, ps=ps, sbf=sbf, c=c, xh=xh, xoff=xoff, tb=tb,
                             ocols=ocols, prows=prows,
                             first=(c == 0), last=(c == ncs - 1):
                             e.matmul(out=ps[0:ocols, 0:tokb],
                                      lhsT=sbf[0:prows, c * 128:c * 128 + ocols],
                                      rhs=xh[0:prows, xoff + tb * tokb: xoff + (tb + 1) * tokb],
                                      start=first, stop=last),
                             [kbf, xk], [pskey], tag=f"{name}_mm")
                    evac(o, tb, ps, pskey)
            else:
                if sl == 0:
                    held = [k.take_bank() for _ in range(ntb)]
                    k._gemm_held = held
                held = k._gemm_held
                for tb in range(ntb):
                    ps, pskey, _ = held[tb]
                    for c in range(ncs):
                        xh, xk, xoff = xtile(c0 + c)
                        k.te(lambda e, ps=ps, sbf=sbf, c=c, xh=xh, xoff=xoff, tb=tb,
                             ocols=ocols, first=(sl == 0 and c == 0),
                             last=(sl == nslab - 1 and c == ncs - 1):
                             e.matmul(out=ps[0:ocols, 0:tokb],
                                      lhsT=sbf[:, c * 128:c * 128 + ocols],
                                      rhs=xh[:, xoff + tb * tokb: xoff + (tb + 1) * tokb],
                                      start=first, stop=last),
                             [kbf, xk], [pskey], tag=f"{name}_mm")
                if sl == nslab - 1:
                    for tb in range(ntb):
                        ps, pskey, bi = held[tb]
                        evac(o, tb, ps, pskey)
                        k.free_bank(bi)
            par ^= 1


def bcast_row(k, name, row_t, row_key, row_off, dst, dst_key, dst_off, ones1,
              ntok=NT, scalemul=None):
    for tb in range(ntok // TB):
        ps, pskey = k.bank()
        k.te(lambda e, ps=ps, tb=tb: e.matmul(
            out=ps[:, 0:TB], lhsT=ones1[0:1, 0:128],
            rhs=row_t[0:1, row_off + tb * TB: row_off + (tb + 1) * TB],
            start=True, stop=True), ["c_ones_row", row_key], [pskey])
        k.v(lambda e, ps=ps, tb=tb: e.tensor_copy(
            out=dst[:, dst_off + tb * TB: dst_off + (tb + 1) * TB], in_=ps[:, 0:TB]),
            [pskey], [dst_key])


def ln_fm(k, tag, gamma=None, beta=None, nchunks=8, eps=1e-5, dinv=None, rows_t=None):
    xf = k.tiles["xf"]
    xb = k.tiles["xb"]
    sq = k.t("ln_sqb", (128, TB), BF16)
    rows = rows_t if rows_t is not None else k.t("ln_rows", (1, 2 * TB), BF16)
    onesN, onesNk = k.constf(1.0 / (nchunks * 128.0))
    onesNb, onesNbk = k.constb(1.0 / (nchunks * 128.0))
    ones1, ones1k = k.ones_row()
    for tb in range(NTB):
        psm, kpm = k.bank()
        psv, kpv = k.bank()
        for c in range(nchunks):
            k.te(lambda e, psm=psm, c=c, tb=tb: e.matmul(
                out=psm[0:1, 0:TB], lhsT=onesN[:, 0:1],
                rhs=xf[:, c * NT + tb * TB: c * NT + tb * TB + TB],
                start=(c == 0), stop=(c == nchunks - 1)),
                [onesNk, "xf"], [kpm])
        for c in range(nchunks):
            k.s(lambda e, c=c, tb=tb: e.activation(
                out=sq[:, 0:TB],
                in_=xf[:, c * NT + tb * TB: c * NT + tb * TB + TB],
                func=AF.Square), ["xf"], ["ln_sqb"], tag=tag + "_sq")
            k.te(lambda e, psv=psv, c=c: e.matmul(
                out=psv[0:1, 0:TB], lhsT=onesNb[:, 0:1],
                rhs=sq[:, 0:TB],
                start=(c == 0), stop=(c == nchunks - 1)),
                [onesNbk, "ln_sqb"], [kpv])
        k.v(lambda e, psm=psm: e.tensor_copy(out=rows[0:1, 0:TB], in_=psm[0:1, 0:TB]),
            [kpm], ["ln_rows"])
        k.s(lambda e: e.activation(out=rows[0:1, TB:2 * TB], in_=rows[0:1, 0:TB],
                                   func=AF.Square), ["ln_rows"], ["ln_rows"])
        k.v(lambda e, psv=psv: e.tensor_sub(out=rows[0:1, TB:2 * TB],
                                            in0=psv[0:1, 0:TB],
                                            in1=rows[0:1, TB:2 * TB]),
            [kpv, "ln_rows"], ["ln_rows"])
        epst, epsk = k.constf(eps)
        k.s(lambda e, epst=epst: e.activation(out=rows[0:1, TB:2 * TB],
                                   in_=rows[0:1, TB:2 * TB],
                                   func=AF.Ln, bias=epst[0:1, 0:1]),
            ["ln_rows", epsk], ["ln_rows"])
        k.s(lambda e: e.activation(out=rows[0:1, TB:2 * TB],
                                   in_=rows[0:1, TB:2 * TB],
                                   func=AF.Exp, scale=-0.5), ["ln_rows"], ["ln_rows"])
        pmb, kmb = k.bank()
        prb, krb = k.bank()
        ones1b, ones1bk = k.ones_row("bf16")
        k.te(lambda e, pmb=pmb, ones1b=ones1b: e.matmul(
            out=pmb[:, 0:TB], lhsT=ones1b[0:1, 0:128],
            rhs=rows[0:1, 0:TB], start=True, stop=True),
             [ones1bk, "ln_rows"], [kmb])
        k.te(lambda e, prb=prb, ones1b=ones1b: e.matmul(
            out=prb[:, 0:TB], lhsT=ones1b[0:1, 0:128],
            rhs=rows[0:1, TB:2 * TB], start=True, stop=True),
             [ones1bk, "ln_rows"], [krb])
        for c in range(nchunks):
            sl = slice(c * NT + tb * TB, c * NT + tb * TB + TB)
            k.v(lambda e, sl=sl, pmb=pmb: e.tensor_sub(
                out=xf[:, sl], in0=xf[:, sl], in1=pmb[:, 0:TB]),
                ["xf", kmb], ["xf"])
            k.v(lambda e, sl=sl, prb=prb: e.tensor_mul(
                out=xf[:, sl], in0=xf[:, sl], in1=prb[:, 0:TB]),
                ["xf", krb], ["xf"])
            if gamma is not None:
                gt, gk, goff = gamma
                bt, bk, boff = beta
                k.v(lambda e, sl=sl, c=c: e.tensor_scalar(
                    out=xf[:, sl], in0=xf[:, sl],
                    scalar1=gt[:, goff + c:goff + c + 1],
                    scalar2=bt[:, boff + c:boff + c + 1],
                    op0=ALU.mult, op1=ALU.add), ["xf", gk, bk], ["xf"])
            k.s(lambda e, sl=sl: e.activation(out=xb[:, sl], in_=xf[:, sl],
                                              func=AF.Copy),
                ["xf"], ["xb"], tag=tag + "_cast")


def xb_tile(k):
    xb = k.tiles["xb"]
    return lambda c: (xb, "xb", c * NT)


class TView:
    def __init__(self, h, off, size):
        self.h, self.off, self.size = h, off, size
        self.dtype = h.dtype

    def __getitem__(self, idx):
        p, f = idx
        st = self.off if f.start is None else self.off + f.start
        sp = self.off + self.size if f.stop is None else self.off + f.stop
        return self.h[p, st:sp]


BBREG = 8 * NT


def bb(k, idx, init=False):
    nbb = getattr(k, "nbb", 3)
    t = k.t("BBIG", (128, nbb * BBREG), BF16)
    name = f"R{idx}"
    v = TView(t, idx * BBREG, BBREG)
    if init:
        kk = ("bbinit", idx)
        if kk not in getattr(k, "_done", set()):
            k._done = getattr(k, "_done", set())
            k._done.add(kk)
            k.v(lambda e, v=v: e.memset(v[:, slice(None, None)], 0.0), [], [name])
    return v, name


def bb2(k, idx0=0, idx1=1):
    nbb = getattr(k, "nbb", 3)
    t = k.t("BBIG", (128, nbb * BBREG), BF16)
    assert idx1 == idx0 + 1
    return TView(t, idx0 * BBREG, 2 * BBREG), (f"R{idx0}", f"R{idx1}")


def emit_l1_ffn(k, W):
    xf, xb = k.tiles["xf"], k.tiles["xb"]
    load_const(k, "l1f_mur", W["l1f_mur"], (128, 8), F32)
    load_const(k, "l1f_muk", W["l1f_muk"], (128, 8), F32)
    b0, kb0 = bb(k, 0)
    b1, kb1 = bb(k, 1)
    xsh = k.t("sc_bf_a", (128, NT), BF16)
    dsc = k.t("sc_bf_b", (128, NT), BF16)
    relu_s = k.t("sc_f_a", (128, TB), F32)
    prod = k.t("sc_f_b", (128, TB), F32)
    mur = k.tiles["c_l1f_mur"]
    muk = k.tiles["c_l1f_muk"]
    for c in range(8):
        sl = slice(c * NT, (c + 1) * NT)
        k.v(lambda e: e.memset(xsh[:, 0:1], 0.0), [], ["sc_bf_a"])
        k.v(lambda e, c=c: e.tensor_copy(out=xsh[:, 1:NT],
                                         in_=xf[:, c * NT: c * NT + NT - 1]),
            ["xf"], ["sc_bf_a"])
        k.v(lambda e, sl=sl: e.tensor_sub(out=dsc[:, :], in0=xf[:, sl], in1=xsh[:, :]),
            ["xf", "sc_bf_a"], ["sc_bf_b"])
        k.v(lambda e, sl=sl, c=c: e.scalar_tensor_tensor(
            out=b0[:, sl], in0=dsc[:, :], scalar=muk[:, c:c + 1], in1=xsh[:, :],
            op0=ALU.mult, op1=ALU.add), ["sc_bf_b", "c_l1f_muk", "sc_bf_a"], [kb0])
        k.v(lambda e, sl=sl, c=c: e.scalar_tensor_tensor(
            out=xb[:, sl], in0=dsc[:, :], scalar=mur[:, c:c + 1], in1=xsh[:, :],
            op0=ALU.mult, op1=ALU.add), ["sc_bf_b", "c_l1f_mur", "sc_bf_a"], ["xb"])

    def evac_r(o, tb, ps, pskey):
        k.s(lambda e, o=o, tb=tb, ps=ps: e.activation(
            out=b1[:, o * NT + tb * TB: o * NT + (tb + 1) * TB],
            in_=ps[:, 0:TB], func=AF.Sigmoid), [pskey], [kb1])

    gemm_fm(k, "l1f_r", W["l1f_wr"], 1024, 1024,
            lambda c: (xb, "xb", c * NT), evac_r)

    def evac_k(o, tb, ps, pskey):
        k.v(lambda e, ps=ps: e.tensor_relu(out=relu_s[:, 0:TB], in_=ps[:, 0:TB]),
            [pskey], ["sc_f_a"])
        k.s(lambda e, o=o, tb=tb: e.activation(
            out=xb[:, o * NT + tb * TB: o * NT + (tb + 1) * TB],
            in_=relu_s[:, 0:TB], func=AF.Square), ["sc_f_a"], ["xb"])

    gemm_fm(k, "l1f_k", W["l1f_wk"], 1024, 1024,
            lambda c: (b0, kb0, c * NT), evac_k)

    def evac_v(o, tb, ps, pskey):
        sl = slice(o * NT + tb * TB, o * NT + (tb + 1) * TB)
        k.v(lambda e, ps=ps, sl=sl: e.tensor_mul(out=prod[:, 0:TB], in0=ps[:, 0:TB],
                                                 in1=b1[:, sl]),
            [pskey, kb1], ["sc_f_b"])
        k.v(lambda e, sl=sl: e.tensor_add(out=xf[:, sl], in0=xf[:, sl],
                                          in1=prod[:, 0:TB]),
            ["xf", "sc_f_b"], ["xf"])

    gemm_fm(k, "l1f_v", W["l1f_wv"], 1024, 1024,
            lambda c: (xb, "xb", c * NT), evac_v)

    ln_fm(k, "l1f_ln")


def get_ident(k, W, dtype="f32"):
    if dtype == "f32":
        name, hname = "c_ident_f", "cst_ident_f"
        dt = F32
    else:
        name, hname = "c_ident_b", "cst_ident_b"
        dt = BF16
    if name not in k.tiles:
        t = k.t(name, (128, 128), dt)
        g = k.group()
        k.dma("sync", lambda e, t=t: e.dma_start(out=t[:, :], in_=W[hname]),
              [], [name], g)
    return k.tiles[name], name


def emit_l3_attn(k, W):
    xf, xb = k.tiles["xf"], k.tiles["xb"]
    identf, identfk = get_ident(k, W, "f32")
    qb_, kqb = bb(k, 0)
    ob_, kob = bb(k, 1, init=True)
    ktil = k.t("l3a_k", (128, NT), BF16)
    vtil = k.t("l3a_v", (64, NT), F32)
    vtt = k.t("l3a_vtt", (128, 16 * 65), BF16)
    lrow = k.t("sc_row", (1, TB), BF16)
    rbc = k.t("sc_rbc", (128, TB), BF16)
    ptb = k.t("l3a_pt", (128, TB), BF16)
    sst = k.t("l3a_ss", (128, 1), F32)
    g = k.group()
    src = bass.AP(W["l3a_scale"].tensor, W["l3a_scale"].offset, [[0, 128], [1, 1]])
    k.dma("sync", lambda e: e.dma_start(out=sst[:, :], in_=src), [], ["l3a_ss"], g)

    def evac_q(o, tb, ps, pskey):
        k.s(lambda e, o=o, tb=tb, ps=ps: e.activation(
            out=qb_[:, o * NT + tb * TB: o * NT + (tb + 1) * TB],
            in_=ps[:, 0:TB], func=AF.Copy, scale=sst[:, 0:1]),
            [pskey, "l3a_ss"], [kqb])

    gemm_fm(k, "l3a_q", W["l3a_wq"], 1024, 1024, xb_tile(k), evac_q)

    def evac_k(o, tb, ps, pskey):
        k.s(lambda e, tb=tb, ps=ps: e.activation(
            out=ktil[0:64, tb * TB:(tb + 1) * TB], in_=ps[0:64, 0:TB],
            func=AF.Copy), [pskey], ["l3a_k"])

    gemm_fm(k, "l3a_kk", W["l3a_wk"], 1024, 64, xb_tile(k), evac_k)
    k.v(lambda e: e.tensor_copy(out=ktil[64:128, :], in_=ktil[0:64, :]),
        ["l3a_k"], ["l3a_k"])

    def evac_v(o, tb, ps, pskey):
        k.v(lambda e, tb=tb, ps=ps: e.tensor_copy(
            out=vtil[0:64, tb * TB:(tb + 1) * TB], in_=ps[0:64, 0:TB]),
            [pskey], ["l3a_v"])

    gemm_fm(k, "l3a_vv", W["l3a_wv"], 1024, 64, xb_tile(k), evac_v)

    k.v(lambda e: e.memset(vtt[:, :], 1.0), [], ["l3a_vtt"])
    for kt in range(16):
        ps, pskey = k.bank()
        k.te(lambda e, kt=kt, ps=ps: e.transpose(
            out=ps[:, 0:64], in_=vtil[0:64, kt * 128:(kt + 1) * 128],
            identity=identf[0:64, 0:64]),
            ["l3a_v", identfk], [pskey])
        k.v(lambda e, kt=kt, ps=ps: e.tensor_copy(
            out=vtt[:, kt * 65: kt * 65 + 64], in_=ps[:, 0:64]),
            [pskey], ["l3a_vtt"])

    ones1, ones1k = k.ones_row()
    for h in range(16):
        qsl = (h // 2) * NT
        qp = (h % 2) * 64
        for qb in range(NTB):
            pso, psok, io = k.take_bank()
            pst, pstk, ist = k.take_bank()
            for kt in range(16):
                k.te(lambda e, kt=kt, qb=qb, pst=pst, qp=qp, qsl=qsl: e.matmul(
                    out=pst[:, 0:TB],
                    lhsT=ktil[qp:qp + 64, kt * 128:(kt + 1) * 128],
                    rhs=qb_[qp:qp + 64, qsl + qb * TB: qsl + (qb + 1) * TB],
                    start=True, stop=True), ["l3a_k", kqb], [pstk])
                k.s(lambda e, pst=pst: e.activation(
                    out=ptb[:, 0:TB], in_=pst[:, 0:TB], func=AF.Exp),
                    [pstk], ["l3a_pt"])
                k.te(lambda e, kt=kt, pso=pso, first=(kt == 0), last=(kt == 15):
                     e.matmul(out=pso[0:65, 0:TB],
                              lhsT=vtt[:, kt * 65:(kt + 1) * 65],
                              rhs=ptb[:, 0:TB], start=first, stop=last),
                     ["l3a_vtt", "l3a_pt"], [psok])
            def _recip(e, pso=pso):
                with k.nc.allow_low_precision(reason="bf16 attn normalizer"):
                    return e.reciprocal(out=lrow[0:1, 0:TB], in_=pso[64:65, 0:TB])
            k.v(_recip, [psok], ["sc_row"])
            psr, psrk = k.bank()
            ones1b, ones1bk = k.ones_row("bf16")
            k.te(lambda e, psr=psr, ones1b=ones1b: e.matmul(
                out=psr[:, 0:TB], lhsT=ones1b[0:1, 0:128],
                rhs=lrow[0:1, 0:TB], start=True, stop=True),
                 [ones1bk, "sc_row"], [psrk])
            k.v(lambda e, psr=psr: e.tensor_copy(out=rbc[:, 0:TB], in_=psr[:, 0:TB]),
                [psrk], ["sc_rbc"])
            k.v(lambda e, pso=pso, qb=qb, qp=qp, qsl=qsl: e.tensor_mul(
                out=ob_[qp:qp + 64, qsl + qb * TB: qsl + (qb + 1) * TB],
                in0=pso[0:64, 0:TB], in1=rbc[0:64, 0:TB]),
                [psok, "sc_rbc"], [kob])
            k.free_bank(io)
            k.free_bank(ist)

    def evac_o(o, tb, ps, pskey):
        sl = slice(o * NT + tb * TB, o * NT + (tb + 1) * TB)
        k.v(lambda e, ps=ps, sl=sl: e.tensor_add(
            out=xf[:, sl], in0=xf[:, sl], in1=ps[:, 0:TB]), [pskey, "xf"], ["xf"])

    gemm_fm(k, "l3a_o", W["l3a_wo"], 1024, 1024, lambda c: (ob_, kob, c * NT), evac_o)
    ln_fm(k, "l3a_ln")


def apoff(ap, elems):
    return bass.AP(ap.tensor, ap.offset + elems, [list(d) for d in ap.ap])


def emit_l3_ffn(k, W):
    xf, xb = k.tiles["xf"], k.tiles["xb"]
    identf, identfk = get_ident(k, W, "f32")
    gbuf, kgbs = bb2(k, 0, 1)
    kgb = kgbs
    rt = k.t("l3f_rta", (8, NT), F32)
    rtm = k.t("l3f_rtm", (128, 16 * 8), F32)
    et = k.t("l3f_et", (128, 8), F32)
    pt = k.t("l3f_p", (128, 8), F32)
    sc1 = k.t("l3f_s1", (128, 1), F32)
    sc2 = k.t("l3f_s2", (128, 1), F32)
    sc3 = k.t("l3f_s3", (128, 1), F32)
    m12 = k.t("l3f_m12", (128, 2), F32)
    wtm = k.t("l3f_wtm", (128, 16 * 8), F32)
    wrows = rt
    wbc = k.t("l3f_wbc", (128, NT), BF16)
    tmp = k.t("sc_f_a", (128, TB), F32)

    def evac_r(o, tb, ps, pskey):
        k.v(lambda e, tb=tb, ps=ps: e.tensor_copy(
            out=rt[0:8, tb * TB:(tb + 1) * TB], in_=ps[0:8, 0:TB]), [pskey], ["l3f_rta"])

    gemm_fm(k, "l3f_r", W["l3f_wr"], 1024, 8, xb_tile(k), evac_r)

    for t16 in range(16):
        ps, pskey = k.bank()
        k.te(lambda e, t16=t16, ps=ps: e.transpose(
            out=ps[:, 0:8], in_=rt[0:8, t16 * 128:(t16 + 1) * 128],
            identity=identf[0:8, 0:8]), ["l3f_rta", identfk], [pskey])
        k.v(lambda e, t16=t16, ps=ps: e.tensor_copy(
            out=rtm[:, t16 * 8:(t16 + 1) * 8], in_=ps[:, 0:8]), [pskey], ["l3f_rtm"])
    for t16 in range(16):
        sl = slice(t16 * 8, (t16 + 1) * 8)
        k.v(lambda e, sl=sl: e.reduce_max(out=sc1[:, 0:1], in_=rtm[:, sl],
                                          axis=mybir.AxisListType.X, negate=True),
            ["l3f_rtm"], ["l3f_s1"])
        k.s(lambda e, sl=sl: e.activation(out=et[:, :], in_=rtm[:, sl], func=AF.Exp,
                                          bias=sc1[:, 0:1]), ["l3f_rtm", "l3f_s1"], ["l3f_et"])
        k.v(lambda e: e.reduce_sum(out=sc2[:, 0:1], in_=et[:, :],
                                   axis=mybir.AxisListType.X), ["l3f_et"], ["l3f_s2"])
        k.v(lambda e: e.reciprocal(out=sc2[:, 0:1], in_=sc2[:, 0:1]), ["l3f_s2"], ["l3f_s2"])
        k.v(lambda e: e.tensor_scalar_mul(out=pt[:, :], in0=et[:, :], scalar1=sc2[:, 0:1]),
            ["l3f_et", "l3f_s2"], ["l3f_p"])
        k.v(lambda e: e.reduce_max(out=m12[:, 0:1], in_=pt[:, :],
                                   axis=mybir.AxisListType.X), ["l3f_p"], ["l3f_m12"])
        k.v(lambda e: e.tensor_single_scalar(out=et[:, :], in_=pt[:, :],
                                             scalar=m12[:, 0:1], op=ALU.is_equal),
            ["l3f_p", "l3f_m12"], ["l3f_et"])
        k.v(lambda e: e.scalar_tensor_tensor(out=sc3[:, 0:1] if False else tmp[:, 0:8],
                                             in0=et[:, :], scalar=-1e30, in1=pt[:, :],
                                             op0=ALU.mult, op1=ALU.add),
            ["l3f_et", "l3f_p"], ["sc_f_a"])
        k.v(lambda e: e.reduce_max(out=m12[:, 1:2], in_=tmp[:, 0:8],
                                   axis=mybir.AxisListType.X), ["sc_f_a"], ["l3f_m12"])
        k.v(lambda e: e.tensor_single_scalar(out=tmp[:, 0:8], in_=pt[:, :],
                                             scalar=m12[:, 1:2], op=ALU.is_equal),
            ["l3f_p", "l3f_m12"], ["sc_f_a"])
        k.v(lambda e: e.tensor_add(out=et[:, :], in0=et[:, :], in1=tmp[:, 0:8]),
            ["l3f_et", "sc_f_a"], ["l3f_et"])
        k.v(lambda e: e.reduce_sum(out=sc3[:, 0:1], in_=m12[:, 0:2],
                                   axis=mybir.AxisListType.X), ["l3f_m12"], ["l3f_s3"])
        k.v(lambda e: e.tensor_scalar_add(out=sc3[:, 0:1], in0=sc3[:, 0:1],
                                          scalar1=1e-20), ["l3f_s3"], ["l3f_s3"])
        k.v(lambda e: e.reciprocal(out=sc3[:, 0:1], in_=sc3[:, 0:1]), ["l3f_s3"], ["l3f_s3"])
        k.v(lambda e: e.tensor_mul(out=et[:, :], in0=et[:, :], in1=pt[:, :]),
            ["l3f_et", "l3f_p"], ["l3f_et"])
        k.v(lambda e, sl=sl: e.tensor_scalar_mul(out=wtm[:, sl], in0=et[:, :],
                                                 scalar1=sc3[:, 0:1]),
            ["l3f_et", "l3f_s3"], ["l3f_wtm"])
    for t16 in range(16):
        ps, pskey = k.bank()
        k.te(lambda e, t16=t16, ps=ps: e.transpose(
            out=ps[0:8, 0:128], in_=wtm[:, t16 * 8:(t16 + 1) * 8],
            identity=identf[:, :]), ["l3f_wtm", identfk], [pskey])
        k.v(lambda e, t16=t16, ps=ps: e.tensor_copy(
            out=wrows[0:8, t16 * 128:(t16 + 1) * 128], in_=ps[0:8, 0:128]),
            [pskey], ["l3f_rta"])

    def evac_sg(o, tb, ps, pskey):
        k.s(lambda e, o=o, tb=tb, ps=ps: e.activation(
            out=gbuf[:, o * NT + tb * TB: o * NT + (tb + 1) * TB],
            in_=ps[:, 0:TB], func=AF.Silu), [pskey], list(kgb))

    gemm_fm(k, "l3f_sg", W["l3f_sg"], 1024, 2048, xb_tile(k), evac_sg)

    def evac_su(o, tb, ps, pskey):
        sl = slice(o * NT + tb * TB, o * NT + (tb + 1) * TB)
        k.v(lambda e, ps=ps, sl=sl: e.tensor_mul(
            out=gbuf[:, sl], in0=ps[:, 0:TB], in1=gbuf[:, sl]), [pskey] + list(kgb), list(kgb))

    gemm_fm(k, "l3f_su", W["l3f_su"], 1024, 2048, xb_tile(k), evac_su)

    def evac_sd(o, tb, ps, pskey):
        sl = slice(o * NT + tb * TB, o * NT + (tb + 1) * TB)
        k.v(lambda e, ps=ps, sl=sl: e.tensor_add(
            out=xf[:, sl], in0=xf[:, sl], in1=ps[:, 0:TB]), [pskey, "xf"], ["xf"])

    gemm_fm(k, "l3f_sd", W["l3f_sd"], 2048, 1024,
            lambda c: (gbuf, kgbs[0] if c < 8 else kgbs[1], c * NT), evac_sd)

    sel = k.t("c_sel8", (8, 1024), F32)
    gsel = k.group()
    k.dma("sync", lambda e: e.dma_start(out=sel[:, :], in_=W["cst_sel8"]),
          [], ["c_sel8"], gsel)
    for ex in range(8):
        for tb in range(NTB):
            ps, pskey = k.bank()
            k.te(lambda e, ex=ex, tb=tb, ps=ps: e.matmul(
                out=ps[:, 0:TB], lhsT=sel[0:8, ex * 128:(ex + 1) * 128],
                rhs=wrows[0:8, tb * TB:(tb + 1) * TB], start=True, stop=True),
                ["c_sel8", "l3f_rta"], [pskey])
            k.v(lambda e, tb=tb, ps=ps: e.tensor_copy(
                out=wbc[:, tb * TB:(tb + 1) * TB], in_=ps[:, 0:TB]), [pskey], ["l3f_wbc"])
        gemm_fm(k, f"l3f_g{ex}", apoff(W["l3f_wg"], ex * 1024 * 2048), 1024, 2048,
                xb_tile(k), evac_sg)
        gemm_fm(k, f"l3f_u{ex}", apoff(W["l3f_wu"], ex * 1024 * 2048), 1024, 2048,
                xb_tile(k), evac_su)

        def evac_d(o, tb, ps, pskey):
            sl = slice(o * NT + tb * TB, o * NT + (tb + 1) * TB)
            k.v(lambda e, ps=ps, tb=tb: e.tensor_mul(
                out=tmp[:, 0:TB], in0=ps[:, 0:TB],
                in1=wbc[:, tb * TB:(tb + 1) * TB]), [pskey, "l3f_wbc"], ["sc_f_a"])
            k.v(lambda e, sl=sl: e.tensor_add(
                out=xf[:, sl], in0=xf[:, sl], in1=tmp[:, 0:TB]), ["xf", "sc_f_a"], ["xf"])

        gemm_fm(k, f"l3f_d{ex}", apoff(W["l3f_wd"], ex * 2048 * 1024), 2048, 1024,
                lambda c: (gbuf, kgbs[0] if c < 8 else kgbs[1], c * NT), evac_d)
    ln_fm(k, "l3f_ln")


def emit_l0_ffn(k, W):
    xf, xb = k.tiles["xf"], k.tiles["xb"]
    hb, khbs = bb2(k, 0, 1)
    ybuf = k.t("l0f_y", (128, 4 * NT), BF16)
    rm1 = k.t("sb_l0f_rm1", (128, 4096), BF16)
    rm2 = [k.t(f"l0f_rm2_{i}", (128, 2048), BF16) for i in range(2)]
    stg = k.t("gw_stg0", (128, 512), F32)
    for i in range(8):
        g = k.group()
        k.dma("sync", lambda e, i=i: e.dma_start(
            out=stg[:, :], in_=W["l0f_rm1"][0:128, i * 512:(i + 1) * 512]),
            [], ["gw_stg0"], g)
        k.s(lambda e, i=i: e.activation(out=rm1[:, i * 512:(i + 1) * 512],
                                        in_=stg[:, :], func=AF.Copy),
            ["gw_stg0"], ["sb_l0f_rm1"])
    for s in range(8):
        h, rhalf = s // 2, s % 2
        for rg4 in range(4):
            rg = rhalf * 4 + rg4
            t1i = h * 8 + rg
            for tb in range(NTB):
                ps, pskey = k.bank()
                k.te(lambda e, ps=ps, t1i=t1i, rg=rg, tb=tb: e.matmul(
                    out=ps[:, 0:TB], lhsT=rm1[:, t1i * 128:(t1i + 1) * 128],
                    rhs=xb[:, rg * NT + tb * TB: rg * NT + (tb + 1) * TB],
                    start=True, stop=True), ["sb_l0f_rm1", "xb"], [pskey])
                k.s(lambda e, ps=ps, rg4=rg4, tb=tb: e.activation(
                    out=ybuf[:, rg4 * NT + tb * TB: rg4 * NT + (tb + 1) * TB],
                    in_=ps[:, 0:TB], func=AF.Copy), [pskey], ["l0f_y"])
        par = s % 2
        r2, kr2 = rm2[par], f"l0f_rm2_{par}"
        for i in range(4):
            g = k.group()
            rm2ap = W["l0f_rm2"]
            src_ap = bass.AP(rm2ap.tensor,
                             rm2ap.offset + (s * 16 + i * 4) * 128 * 128,
                             [[128, 128], [128 * 128, 4], [1, 128]])
            dst_ap = ap3(stg, [[512, 128], [128, 4], [1, 128]])
            k.dma("sync", lambda e, src_ap=src_ap, dst_ap=dst_ap: e.dma_start(
                out=dst_ap, in_=src_ap), [], ["gw_stg0"], g)
            k.s(lambda e, i=i, r2=r2: e.activation(
                out=r2[:, i * 512:(i + 1) * 512], in_=stg[:, :], func=AF.Copy),
                ["gw_stg0"], [kr2])
        for rg4 in range(4):
            for sub in range(4):
                sc = rg4 * 4 + sub
                for tb in range(NTB):
                    ps, pskey = k.bank()
                    k.te(lambda e, ps=ps, sc=sc, rg4=rg4, tb=tb, r2=r2: e.matmul(
                        out=ps[:, 0:TB], lhsT=r2[:, sc * 128:(sc + 1) * 128],
                        rhs=ybuf[:, rg4 * NT + tb * TB: rg4 * NT + (tb + 1) * TB],
                        start=True, stop=True), [kr2, "l0f_y"], [pskey])
                    k.s(lambda e, ps=ps, sc=sc, tb=tb: e.activation(
                        out=hb[:, sc * NT + tb * TB: sc * NT + (tb + 1) * TB],
                        in_=ps[:, 0:TB], func=AF.Gelu), [pskey], [khbs[sc // 8]])

        def evac_c(o, tb, ps, pskey):
            sl = slice(o * NT + tb * TB, o * NT + (tb + 1) * TB)
            k.v(lambda e, ps=ps, sl=sl: e.tensor_add(
                out=xf[:, sl], in0=xf[:, sl], in1=ps[:, 0:TB]), [pskey, "xf"], ["xf"])

        gemm_fm(k, f"l0f_c{s}", W["l0f_cproj"], 2048, 1024,
                lambda c: (hb, khbs[c // 8], c * NT), evac_c, cin_off=s * 2048)
    ln_fm(k, "l0f_ln")


def emit_l0_attn(k, W):
    xf, xb = k.tiles["xf"], k.tiles["xb"]
    identf, identfk = get_ident(k, W, "bf16")
    qasm, kqa = bb(k, 0)
    kbuf, kkb = bb(k, 1)
    vdram0 = k.nc.dram_tensor("l0a_v", [128, 8 * NT], BF16).ap()
    vstg = k.t("l0a_vstg", (128, TB), BF16)
    vld = [k.t("l0a_vld0", (64, 128), BF16), k.t("l0a_vld1", (64, 128), BF16)]
    SCALE = float((2 * 1024) ** (-0.5))
    vtt = k.t("l0a_vtt", (128, 16 * 65), BF16)
    kstage = k.t("l0a_ks", (128, 8 * 128), BF16)
    ptb = k.t("l0a_pt", (128, TB), BF16)
    lrow = ptb
    rbc = k.t("sc_rbc", (128, TB), BF16)
    mask = k.t("sb_l0a_mask", (128, 256), BF16)
    gm = k.group()
    stg = k.t("gw_stg0", (128, 512), F32)
    k.dma("sync", lambda e: e.dma_start(
        out=stg[:, 0:256], in_=W["l0a_mask"][0:128, 0:256]),
        [], ["gw_stg0"], k.group())
    k.s(lambda e: e.activation(out=mask[:, 0:256], in_=stg[:, 0:256], func=AF.Copy),
        ["gw_stg0"], ["sb_l0a_mask"])
    for nm in ("bq", "bk", "bv", "bo"):
        load_const(k, "l0a_" + nm, W["l0a_" + nm], (128, 8), F32)
    bq = k.tiles["c_l0a_bq"]
    k.v(lambda e: e.tensor_scalar_mul(out=bq[:, :], in0=bq[:, :], scalar1=SCALE),
        ["c_l0a_bq"], ["c_l0a_bq"])

    def evac_v(o, tb, ps, pskey):
        k.v(lambda e, o=o, tb=tb, ps=ps: e.tensor_scalar(
            out=vstg[:, 0:TB],
            in0=ps[:, 0:TB], scalar1=k.tiles["c_l0a_bv"][:, o:o + 1],
            scalar2=None, op0=ALU.add),
            [pskey, "c_l0a_bv"], ["l0a_vstg"])
        gvs = k.group("gpsimd")
        k.dma("gpsimd", lambda e, o=o, tb=tb: e.dma_start(
            out=vdram0[0:128, o * NT + tb * TB: o * NT + (tb + 1) * TB],
            in_=vstg[:, 0:TB]), ["l0a_vstg"], ["l0a_vdram"], gvs)

    gemm_fm(k, "l0a_v", W["l0a_wv"], 1024, 1024, xb_tile(k), evac_v)

    def evac_k(o, tb, ps, pskey):
        k.v(lambda e, o=o, tb=tb, ps=ps: e.tensor_scalar(
            out=kbuf[:, o * NT + tb * TB: o * NT + (tb + 1) * TB],
            in0=ps[:, 0:TB], scalar1=k.tiles["c_l0a_bk"][:, o:o + 1],
            scalar2=None, op0=ALU.add),
            [pskey, "c_l0a_bk"], [kkb])

    gemm_fm(k, "l0a_k", W["l0a_wk"], 1024, 1024, xb_tile(k), evac_k)

    def evac_q(o, tb, ps, pskey):
        for h4 in range(4):
            h = tb * 4 + h4
            for jh in range(2):
                j = 2 * o + jh
                k.v(lambda e, ps=ps, h=h, jh=jh, j=j, h4=h4, o=o: e.tensor_scalar(
                    out=qasm[(h % 2) * 64:(h % 2) * 64 + 64,
                             (h // 2) * NT + j * 128:(h // 2) * NT + (j + 1) * 128],
                    in0=ps[jh * 64:jh * 64 + 64, h4 * 128:(h4 + 1) * 128],
                    scalar1=SCALE,
                    scalar2=k.tiles["c_l0a_bq"][jh * 64:jh * 64 + 64, o:o + 1],
                    op0=ALU.mult, op1=ALU.add),
                    [pskey, "c_l0a_bq"], [kqa])

    gemm_fm(k, "l0a_q", W["l0a_wq"], 1024, 1024, xb_tile(k), evac_q)

    ones1, ones1k = k.ones_row()
    for h in range(16):
        qp = (h % 2) * 64
        hsl = (h // 2) * NT
        k.v(lambda e: e.memset(vtt[:, :], 1.0), [], ["l0a_vtt"])
        for j in range(16):
            jp = (j % 2) * 64
            vl, vlk = vld[j % 2], f"l0a_vld{j % 2}"
            gvl = k.group("gpsimd")
            k.dma("gpsimd", lambda e, j=j, jp=jp, h=h, vl=vl: e.dma_start(
                out=vl[0:64, 0:128],
                in_=vdram0[jp:jp + 64,
                           (j // 2) * NT + h * 128:(j // 2) * NT + (h + 1) * 128]),
                ["l0a_vdram"], [vlk], gvl)
            ps, pskey = k.bankB()
            k.te(lambda e, ps=ps, vl=vl: e.transpose(
                out=ps[:, 0:64], in_=vl[0:64, 0:128],
                identity=identf[0:64, 0:64]),
                [vlk, identfk], [pskey])
            k.v(lambda e, ps=ps, j=j: e.tensor_copy(
                out=vtt[:, j * 65: j * 65 + 64], in_=ps[:, 0:64]), [pskey], ["l0a_vtt"])
        nst = 0
        koff = {}
        for j in range(16):
            if (j % 2) != (h % 2):
                k.v(lambda e, j=j, qp=qp, h=h, nst=nst: e.tensor_copy(
                    out=kstage[qp:qp + 64, nst * 128:(nst + 1) * 128],
                    in_=kbuf[(j % 2) * 64:(j % 2) * 64 + 64,
                             (j // 2) * NT + h * 128:(j // 2) * NT + (h + 1) * 128]),
                    [kkb], ["l0a_ks"])
                koff[j] = nst
                nst += 1
        for qb in range(NTB):
            pso, psok, io = k.take_bank()
            pst, pstk, ist = k.take_bank()
            for j in range(16):
                if (j % 2) == (h % 2):
                    lhsT_fn = (lambda j=j, h=h: kbuf[(j % 2) * 64:(j % 2) * 64 + 64,
                               (j // 2) * NT + h * 128:(j // 2) * NT + (h + 1) * 128])
                    lkey = kkb
                else:
                    lhsT_fn = (lambda j=j, qp=qp, ko=koff[j]: kstage[qp:qp + 64,
                               ko * 128:(ko + 1) * 128])
                    lkey = "l0a_ks"
                k.te(lambda e, pst=pst, lhsT_fn=lhsT_fn, qp=qp, hsl=hsl, qb=qb: e.matmul(
                    out=pst[:, 0:TB], lhsT=lhsT_fn(),
                    rhs=qasm[qp:qp + 64, hsl + qb * TB: hsl + (qb + 1) * TB],
                    start=True, stop=True), [lkey, kqa], [pstk])
                k.s(lambda e, pst=pst: e.activation(
                    out=ptb[:, 0:TB], in_=pst[:, 0:TB], func=AF.Exp),
                    [pstk], ["l0a_pt"])
                for j4 in range(4):
                    jq = qb * 4 + j4
                    moff = 0 if jq < j else 128
                    k.v(lambda e, j4=j4, moff=moff: e.tensor_mul(
                        out=ptb[:, j4 * 128:(j4 + 1) * 128],
                        in0=ptb[:, j4 * 128:(j4 + 1) * 128],
                        in1=mask[:, moff:moff + 128]),
                        ["l0a_pt", "sb_l0a_mask"], ["l0a_pt"])
                k.te(lambda e, j=j, pso=pso, first=(j == 0), last=(j == 15): e.matmul(
                    out=pso[0:65, 0:TB], lhsT=vtt[:, j * 65:(j + 1) * 65],
                    rhs=ptb[:, 0:TB], start=first, stop=last),
                    ["l0a_vtt", "l0a_pt"], [psok])
            def _recip(e, pso=pso):
                with k.nc.allow_low_precision(reason="bf16 attn normalizer"):
                    return e.reciprocal(out=lrow[0:1, 0:TB], in_=pso[64:65, 0:TB])
            k.v(_recip, [psok], ["l0a_pt"])
            psr, psrk = k.bank()
            ones1b, ones1bk = k.ones_row("bf16")
            k.te(lambda e, psr=psr, ones1b=ones1b: e.matmul(
                out=psr[:, 0:TB], lhsT=ones1b[0:1, 0:128],
                rhs=lrow[0:1, 0:TB], start=True, stop=True),
                 [ones1bk, "l0a_pt"], [psrk])
            k.v(lambda e, psr=psr: e.tensor_copy(out=rbc[:, 0:TB], in_=psr[:, 0:TB]),
                [psrk], ["sc_rbc"])
            for j4 in range(4):
                j = qb * 4 + j4
                k.v(lambda e, pso=pso, j=j, j4=j4, h=h: e.tensor_mul(
                    out=xb[(j % 2) * 64:(j % 2) * 64 + 64,
                           (j // 2) * NT + h * 128:(j // 2) * NT + (h + 1) * 128],
                    in0=pso[0:64, j4 * 128:(j4 + 1) * 128],
                    in1=rbc[0:64, j4 * 128:(j4 + 1) * 128]),
                    [psok, "sc_rbc"], ["xb"])
            k.free_bank(io)
            k.free_bank(ist)

    def evac_o(o, tb, ps, pskey):
        sl = slice(o * NT + tb * TB, o * NT + (tb + 1) * TB)
        k.v(lambda e, ps=ps, sl=sl, o=o: e.scalar_tensor_tensor(
            out=xf[:, sl], in0=ps[:, 0:TB], scalar=k.tiles["c_l0a_bo"][:, o:o + 1],
            in1=xf[:, sl], op0=ALU.add, op1=ALU.add),
            [pskey, "c_l0a_bo", "xf"], ["xf"])

    gemm_fm(k, "l0a_o", W["l0a_wo"], 1024, 1024, lambda c: (xb, "xb", c * NT), evac_o)
    ln_fm(k, "l0a_ln")


def shift_xb(k, tag):
    xb = k.tiles["xb"]
    sc = k.t("sc_shift", (128, NT), BF16)
    for c in range(4, 8):
        k.v(lambda e, c=c: e.memset(sc[:, 0:1], 0.0), [], ["sc_shift"])
        k.v(lambda e, c=c: e.tensor_copy(out=sc[:, 1:NT],
                                         in_=xb[:, c * NT: c * NT + NT - 1]),
            ["xb"], ["sc_shift"])
        k.v(lambda e, c=c: e.tensor_copy(out=xb[:, c * NT:(c + 1) * NT], in_=sc[:, :]),
            ["sc_shift", "xb"], ["xb"])


def emit_l2_ffn(k, W):
    import os
    STOP = os.environ.get("L2F_STOP", "")
    xf, xb = k.tiles["xf"], k.tiles["xb"]
    xsave = k.nc.dram_tensor("l2f_xsave", [128, 8 * NT], BF16).ap()
    gsv = k.group("gpsimd")
    k.dma("gpsimd", lambda e: e.dma_start(out=xsave, in_=xb[:, :]),
          ["xb"], ["l2f_xsave"], gsv)
    identb, identbk = get_ident(k, W, "bf16")
    r0, kr0 = bb(k, 0)
    r1, kr1 = bb(k, 1)
    vtt = k.t("l2f_vtt", (128, 16 * 65), BF16)
    ptb = k.t("l0a_pt", (128, TB), BF16)
    lrow = ptb
    rbc = k.t("sc_rbc", (128, TB), BF16)
    band = k.t("sb_band", (128, 1408), BF16)
    tbl = [k.t(f"l2f_tbl{i}", (128, NT), BF16) for i in range(4)]
    nrow = k.t("l2f_nrow", (2, TB), F32)
    qperm = k.t("sc_shift", (128, NT), BF16)
    stg = k.t("gw_stg0", (128, 512), F32)
    for nm in ("l2f_qs", "l2f_qsp", "l2f_ks", "l2f_ksp"):
        load_const(k, nm, W[nm], (128, 2), F32)
    load_const(k, "l2f_lng", W["l2f_lng"], (128, 16), F32)
    load_const(k, "l2f_lnb", W["l2f_lnb"], (128, 16), F32)
    load_const(k, "l2f_b1", W["l2f_b1"], (128, 88), F32)
    load_const(k, "l2f_b2", W["l2f_b2"], (128, 16), F32)
    sel2h = k.t("c_sel2h", (2, 128), F32)
    g = k.group()
    k.dma("sync", lambda e: e.dma_start(out=sel2h[:, :], in_=W["cst_sel2h"]),
          [], ["c_sel2h"], g)
    for i in range(3):
        n = min(512, 1408 - i * 512)
        k.dma("sync", lambda e, i=i, n=n: e.dma_start(
            out=stg[:, 0:n], in_=W["l2f_band"][0:128, i * 512:i * 512 + n]),
            [], ["gw_stg0"], k.group())
        k.s(lambda e, i=i, n=n: e.activation(out=band[:, i * 512:i * 512 + n],
                                             in_=stg[:, 0:n], func=AF.Copy),
            ["gw_stg0"], ["sb_band"])

    ones1b, ones1bk = k.ones_row("bf16")
    LAYERS = [int(c) for c in os.environ.get("L2F_LAYERS", "01")]
    for l in LAYERS:
        shift_xb(k, f"l2f{l}")

        def evac_qkv(o, tb, ps, pskey):
            dst, dk = (r0, kr0) if o < 8 else (r1, kr1)
            oo = o if o < 8 else o - 8
            k.s(lambda e, ps=ps, dst=dst, oo=oo, tb=tb: e.activation(
                out=dst[:, oo * NT + tb * TB: oo * NT + (tb + 1) * TB],
                in_=ps[:, 0:TB], func=AF.Copy), [pskey], [dk])

        gemm_fm(k, f"l2f_qkv{l}", apoff(W["l2f_wqkv"], l * 1024 * 1536),
                1024, 1536, xb_tile(k), evac_qkv)

        specs = [("l2f_cq1", "c_l2f_qs", 8.0, 0), ("l2f_cq2", "c_l2f_qsp", 8.0, 1),
                 ("l2f_ck1", "c_l2f_ks", 1.0, 2), ("l2f_ck2", "c_l2f_ksp", 1.0, 3)]
        for src_name, sc_name, mulc, ti in specs:
            for i in range(4):
                k.dma("sync", lambda e, src_name=src_name, i=i: e.dma_start(
                    out=stg[:, :], in_=W[src_name][0:128, i * 512:(i + 1) * 512]),
                    [], ["gw_stg0"], k.group())
                k.v(lambda e, i=i, ti=ti, sc_name=sc_name, mulc=mulc, l=l: e.tensor_scalar(
                    out=tbl[ti][:, i * 512:(i + 1) * 512], in0=stg[:, :],
                    scalar1=k.tiles[sc_name][:, l:l + 1], scalar2=float(mulc),
                    op0=ALU.mult, op1=ALU.mult),
                    ["gw_stg0", sc_name], [f"l2f_tbl{ti}"])
        if "c_ones2h" not in k.tiles:
            o2 = k.t("c_ones2h", (128, 2), BF16)
            k.v(lambda e, o2=o2: e.memset(o2[:, :], 0.0), [], ["c_ones2h"])
            k.v(lambda e, o2=o2: e.memset(o2[0:64, 0:1], 1.0), [], ["c_ones2h"])
            k.v(lambda e, o2=o2: e.memset(o2[64:128, 1:2], 1.0), [], ["c_ones2h"])
        for qk in range(2):
            for c4 in range(4):
                src_c = qk * 4 + c4
                sqv = k.t("ln_sqb", (128, TB), BF16)
                for tb in range(NTB):
                    psn, kpn = k.bank()
                    k.s(lambda e, src_c=src_c, tb=tb: e.activation(
                        out=sqv[:, 0:TB],
                        in_=r0[:, src_c * NT + tb * TB: src_c * NT + tb * TB + TB],
                        func=AF.Square), [kr0], ["ln_sqb"])
                    k.te(lambda e, psn=psn: e.matmul(
                        out=psn[0:2, 0:TB],
                        lhsT=k.tiles["c_ones2h"][:, 0:2],
                        rhs=sqv[:, 0:TB], start=True, stop=True),
                        ["c_ones2h", "ln_sqb"], [kpn])
                    k.v(lambda e, psn=psn: e.tensor_copy(out=nrow[0:2, 0:TB],
                                                         in_=psn[0:2, 0:TB]),
                        [kpn], ["l2f_nrow"])
                    epst, epsk = k.constf(1e-24)
                    k.s(lambda e, epst=epst: e.activation(
                        out=nrow[0:2, 0:TB], in_=nrow[0:2, 0:TB], func=AF.Ln,
                        bias=epst[0:2, 0:1]), ["l2f_nrow", epsk], ["l2f_nrow"])
                    k.s(lambda e: e.activation(out=nrow[0:2, 0:TB], in_=nrow[0:2, 0:TB],
                                               func=AF.Exp, scale=-0.5),
                        ["l2f_nrow"], ["l2f_nrow"])
                    psb_, kpb = k.bank()
                    k.te(lambda e, psb_=psb_: e.matmul(
                        out=psb_[:, 0:TB], lhsT=sel2h[0:2, 0:128],
                        rhs=nrow[0:2, 0:TB], start=True, stop=True),
                        ["c_sel2h", "l2f_nrow"], [kpb])
                    k.v(lambda e, psb_=psb_, src_c=src_c, tb=tb: e.tensor_mul(
                        out=xb[:, src_c * NT + tb * TB: src_c * NT + tb * TB + TB],
                        in0=r0[:, src_c * NT + tb * TB: src_c * NT + tb * TB + TB],
                        in1=psb_[:, 0:TB]), [kpb, kr0], ["xb"])
                gp = k.group("gpsimd")
                for half in range(2):
                    b0 = half * 64
                    k.dma("gpsimd", lambda e, src_c=src_c, b0=b0: e.dma_start(
                        out=qperm[b0:b0 + 32, :],
                        in_=xb[b0 + 32:b0 + 64, src_c * NT:(src_c + 1) * NT]),
                        ["xb"], ["sc_shift"], gp)
                    k.dma("gpsimd", lambda e, src_c=src_c, b0=b0: e.dma_start(
                        out=qperm[b0 + 32:b0 + 64, :],
                        in_=xb[b0:b0 + 32, src_c * NT:(src_c + 1) * NT]),
                        ["xb"], ["sc_shift"], gp)
                t1i = qk * 2
                k.v(lambda e, src_c=src_c, t1i=t1i: e.tensor_mul(
                    out=xb[:, src_c * NT:(src_c + 1) * NT],
                    in0=xb[:, src_c * NT:(src_c + 1) * NT], in1=tbl[t1i][:, :]),
                    ["xb", f"l2f_tbl{t1i}"], ["xb"])
                k.v(lambda e, t1i=t1i: e.tensor_mul(
                    out=qperm[:, :], in0=qperm[:, :], in1=tbl[t1i + 1][:, :]),
                    ["sc_shift", f"l2f_tbl{t1i + 1}"], ["sc_shift"])
                k.v(lambda e, src_c=src_c: e.tensor_add(
                    out=xb[:, src_c * NT:(src_c + 1) * NT],
                    in0=xb[:, src_c * NT:(src_c + 1) * NT], in1=qperm[:, :]),
                    ["xb", "sc_shift"], ["xb"])
        if STOP == "qk":
            return
        for h in range(8):
            qp = (h % 2) * 64
            qc = h // 2
            kc = 4 + h // 2
            k.v(lambda e: e.memset(vtt[:, :], 1.0), [], ["l2f_vtt"])
            for m in range(16):
                ps, pskey = k.bankB()
                k.te(lambda e, ps=ps, m=m, qp=qp, qc=qc: e.transpose(
                    out=ps[:, 0:64],
                    in_=r1[qp:qp + 64, qc * NT + m * 128: qc * NT + (m + 1) * 128],
                    identity=identb[qp:qp + 64, qp:qp + 64]),
                    [kr1, identbk], [pskey])
                k.v(lambda e, ps=ps, m=m: e.tensor_copy(
                    out=vtt[:, m * 65: m * 65 + 64], in_=ps[:, 0:64]),
                    [pskey], ["l2f_vtt"])
            for w in range(4):
                m0 = max(0, 4 * w - 4)
                m1 = 4 * w + 4
                pso, psok, io = k.take_bank()
                pst, pstk, ist = k.take_bank()
                nmm = m1 - m0
                for mi, m in enumerate(range(m0, m1)):
                    k.te(lambda e, pst=pst, m=m, qp=qp, kc=kc, qc=qc, w=w: e.matmul(
                        out=pst[:, 0:TB],
                        lhsT=xb[qp:qp + 64, kc * NT + m * 128: kc * NT + (m + 1) * 128],
                        rhs=xb[qp:qp + 64, qc * NT + w * TB: qc * NT + (w + 1) * TB],
                        start=True, stop=True), ["xb"], [pstk])
                    k.s(lambda e, pst=pst: e.activation(
                        out=ptb[:, 0:TB], in_=pst[:, 0:TB], func=AF.Exp),
                        [pstk], ["l0a_pt"])
                    D0 = 512 * w - 128 * m
                    k.v(lambda e, D0=D0: e.tensor_mul(
                        out=ptb[:, 0:TB], in0=ptb[:, 0:TB],
                        in1=band[:, D0 + 384: D0 + 384 + TB]),
                        ["l0a_pt", "sb_band"], ["l0a_pt"])
                    k.te(lambda e, m=m, pso=pso, first=(mi == 0), last=(mi == nmm - 1):
                         e.matmul(out=pso[0:65, 0:TB],
                                  lhsT=vtt[:, m * 65:(m + 1) * 65],
                                  rhs=ptb[:, 0:TB], start=first, stop=last),
                         ["l2f_vtt", "l0a_pt"], [psok])
                def _recip(e, pso=pso):
                    with k.nc.allow_low_precision(reason="bf16 attn normalizer"):
                        return e.reciprocal(out=lrow[0:1, 0:TB], in_=pso[64:65, 0:TB])
                k.v(_recip, [psok], ["l0a_pt"])
                psr, psrk = k.bank()
                k.te(lambda e, psr=psr: e.matmul(
                    out=psr[:, 0:TB], lhsT=ones1b[0:1, 0:128],
                    rhs=lrow[0:1, 0:TB], start=True, stop=True),
                    [ones1bk, "l0a_pt"], [psrk])
                k.v(lambda e, psr=psr: e.tensor_copy(out=rbc[:, 0:TB], in_=psr[:, 0:TB]),
                    [psrk], ["sc_rbc"])
                k.v(lambda e, pso=pso, h=h, w=w, qp=qp: e.tensor_mul(
                    out=r1[qp:qp + 64, (4 + h // 2) * NT + w * TB:(4 + h // 2) * NT + (w + 1) * TB],
                    in0=pso[0:64, 0:TB], in1=rbc[0:64, 0:TB]),
                    [psok, "sc_rbc"], [kr1])
                k.free_bank(io)
                k.free_bank(ist)

        if STOP == "attn":
            return

        def evac_wo(o, tb, ps, pskey):
            sl = slice(o * NT + tb * TB, o * NT + (tb + 1) * TB)
            k.v(lambda e, ps=ps, sl=sl: e.tensor_add(
                out=xf[:, sl], in0=xf[:, sl], in1=ps[:, 0:TB]), [pskey, "xf"], ["xf"])

        gemm_fm(k, f"l2f_wo{l}", apoff(W["l2f_wo"], l * 512 * 1024), 512, 1024,
                lambda c: (r1, kr1, (4 + c) * NT), evac_wo)

        for c in range(8):
            k.s(lambda e, c=c: e.activation(out=xb[:, c * NT:(c + 1) * NT],
                                            in_=xf[:, c * NT:(c + 1) * NT],
                                            func=AF.Copy), ["xf"], ["xb"])
        shift_xb(k, f"l2f{l}b")
        ln_from_xb_to(k, f"l2f_iln{l}", r0, kr0,
                      gamma=(k.tiles["c_l2f_lng"], "c_l2f_lng", l * 8),
                      beta=(k.tiles["c_l2f_lnb"], "c_l2f_lnb", l * 8))

        astash = k.t("l2f_astash", (128, 2 * TB), BF16)
        for th in range(2):
            def hb_place(c):
                if c < 16:
                    return r1, kr1, c * 1024
                return xb, "xb", (c - 16) * 1024

            def evac_w1(o, tb, ps, pskey, l=l, th=th):
                pair = o // 2
                if o % 2 == 0:
                    k.v(lambda e, ps=ps, tb=tb, o=o, l=l: e.tensor_scalar(
                        out=astash[:, tb * TB:(tb + 1) * TB], in0=ps[:, 0:TB],
                        scalar1=k.tiles["c_l2f_b1"][:, l * 44 + o:l * 44 + o + 1],
                        scalar2=None, op0=ALU.add),
                        [pskey, "c_l2f_b1"], ["l2f_astash"])
                else:
                    gl = k.t("sc_gel", (128, TB), BF16)
                    k.s(lambda e, ps=ps, o=o, l=l: e.activation(
                        out=gl[:, 0:TB], in_=ps[:, 0:TB], func=AF.Gelu,
                        bias=k.tiles["c_l2f_b1"][:, l * 44 + o:l * 44 + o + 1]),
                        [pskey, "c_l2f_b1"], ["sc_gel"])
                    hh, hk, hoff = hb_place(pair)
                    k.v(lambda e, hh=hh, hoff=hoff, tb=tb, gl=gl: e.tensor_mul(
                        out=hh[:, hoff + tb * TB: hoff + (tb + 1) * TB],
                        in0=astash[:, tb * TB:(tb + 1) * TB], in1=gl[:, 0:TB]),
                        ["l2f_astash", "sc_gel"], [hk])

            gemm_fm(k, f"l2f_w1_{l}_{th}", apoff(W["l2f_w1"], l * 1024 * 5632),
                    1024, 5632, lambda c: (r0, kr0, c * NT + th * 1024),
                    evac_w1, ntok=1024)

            def evac_w2(o, tb, ps, pskey, l=l, th=th):
                sl = slice(o * NT + th * 1024 + tb * TB, o * NT + th * 1024 + (tb + 1) * TB)
                k.v(lambda e, ps=ps, sl=sl, o=o, l=l: e.scalar_tensor_tensor(
                    out=xf[:, sl], in0=ps[:, 0:TB],
                    scalar=k.tiles["c_l2f_b2"][:, l * 8 + o:l * 8 + o + 1],
                    in1=xf[:, sl], op0=ALU.add, op1=ALU.add),
                    [pskey, "c_l2f_b2", "xf"], ["xf"])

            gemm_fm(k, f"l2f_w2_{l}_{th}", apoff(W["l2f_w2"], l * 2816 * 1024),
                    2816, 1024, hb_place, evac_w2, ntok=1024)
        for c in range(8):
            k.s(lambda e, c=c: e.activation(out=xb[:, c * NT:(c + 1) * NT],
                                            in_=xf[:, c * NT:(c + 1) * NT],
                                            func=AF.Copy), ["xf"], ["xb"])
        if STOP == "layer0":
            return
    if STOP == "noln":
        return
    sc = k.t("sc_shift", (128, NT), BF16)
    gld2 = k.group("gpsimd")
    for c in range(8):
        k.dma("gpsimd", lambda e, c=c: e.dma_start(
            out=sc[:, :], in_=xsave[0:128, c * NT:(c + 1) * NT]),
            ["l2f_xsave"], ["sc_shift"], gld2)
        k.v(lambda e, c=c: e.tensor_add(
            out=xf[:, c * NT:(c + 1) * NT], in0=xf[:, c * NT:(c + 1) * NT],
            in1=sc[:, :]), ["sc_shift", "xf"], ["xf"])
        gld2 = k.group("gpsimd")
    ln_fm(k, "l2f_ln")


def ln_from_xb_to(k, tag, dst, dstk, gamma=None, beta=None, eps=1e-5, rows_t=None):
    xb = k.tiles["xb"]
    sq = k.t("ln_sqb", (128, TB), BF16)
    rows = rows_t if rows_t is not None else k.t("ln_rows", (1, 2 * TB), BF16)
    onesNb, onesNbk = k.constb(1.0 / 1024.0)
    ones1b, ones1bk = k.ones_row("bf16")
    for tb in range(NTB):
        psm, kpm = k.bank()
        psv, kpv = k.bank()
        for c in range(8):
            k.te(lambda e, psm=psm, c=c, tb=tb: e.matmul(
                out=psm[0:1, 0:TB], lhsT=onesNb[:, 0:1],
                rhs=xb[:, c * NT + tb * TB: c * NT + tb * TB + TB],
                start=(c == 0), stop=(c == 7)), [onesNbk, "xb"], [kpm])
        for c in range(8):
            k.s(lambda e, c=c, tb=tb: e.activation(
                out=sq[:, 0:TB], in_=xb[:, c * NT + tb * TB: c * NT + tb * TB + TB],
                func=AF.Square), ["xb"], ["ln_sqb"])
            k.te(lambda e, psv=psv, c=c: e.matmul(
                out=psv[0:1, 0:TB], lhsT=onesNb[:, 0:1], rhs=sq[:, 0:TB],
                start=(c == 0), stop=(c == 7)), [onesNbk, "ln_sqb"], [kpv])
        k.v(lambda e, psm=psm: e.tensor_copy(out=rows[0:1, 0:TB], in_=psm[0:1, 0:TB]),
            [kpm], ["ln_rows"])
        k.s(lambda e: e.activation(out=rows[0:1, TB:2 * TB], in_=rows[0:1, 0:TB],
                                   func=AF.Square), ["ln_rows"], ["ln_rows"])
        k.v(lambda e, psv=psv: e.tensor_sub(out=rows[0:1, TB:2 * TB],
                                            in0=psv[0:1, 0:TB],
                                            in1=rows[0:1, TB:2 * TB]),
            [kpv, "ln_rows"], ["ln_rows"])
        epst, epsk = k.constf(eps)
        k.s(lambda e, epst=epst: e.activation(out=rows[0:1, TB:2 * TB],
                                              in_=rows[0:1, TB:2 * TB],
                                              func=AF.Ln, bias=epst[0:1, 0:1]),
            ["ln_rows", epsk], ["ln_rows"])
        k.s(lambda e: e.activation(out=rows[0:1, TB:2 * TB], in_=rows[0:1, TB:2 * TB],
                                   func=AF.Exp, scale=-0.5), ["ln_rows"], ["ln_rows"])
        pmb, kmb = k.bank()
        prb, krb = k.bank()
        k.te(lambda e, pmb=pmb: e.matmul(out=pmb[:, 0:TB], lhsT=ones1b[0:1, 0:128],
                                         rhs=rows[0:1, 0:TB], start=True, stop=True),
             [ones1bk, "ln_rows"], [kmb])
        k.te(lambda e, prb=prb: e.matmul(out=prb[:, 0:TB], lhsT=ones1b[0:1, 0:128],
                                         rhs=rows[0:1, TB:2 * TB], start=True, stop=True),
             [ones1bk, "ln_rows"], [krb])
        sc2 = k.t("sc_ln2", (128, TB), F32)
        for c in range(8):
            sl = slice(c * NT + tb * TB, c * NT + tb * TB + TB)
            k.v(lambda e, sl=sl, pmb=pmb: e.tensor_sub(
                out=sc2[:, 0:TB], in0=xb[:, sl], in1=pmb[:, 0:TB]),
                ["xb", kmb], ["sc_ln2"])
            k.v(lambda e, prb=prb: e.tensor_mul(
                out=sc2[:, 0:TB], in0=sc2[:, 0:TB], in1=prb[:, 0:TB]),
                ["sc_ln2", krb], ["sc_ln2"])
            gt, gk, goff = gamma
            bt, bk, boff = beta
            k.v(lambda e, sl=sl, c=c, gt=gt, bt=bt, goff=goff, boff=boff: e.tensor_scalar(
                out=dst[:, sl], in0=sc2[:, 0:TB],
                scalar1=gt[:, goff + c:goff + c + 1],
                scalar2=bt[:, boff + c:boff + c + 1],
                op0=ALU.mult, op1=ALU.add), ["sc_ln2", gk, bk], [dstk])


def emit_l2_attn(k, W):
    xf, xb = k.tiles["xf"], k.tiles["xb"]
    xs, kxs2 = bb2(k, 0, 1)
    xsave = k.nc.dram_tensor("l2a_xsave", [128, 8 * NT], BF16).ap()
    zdram = k.nc.dram_tensor("l2a_z", [128, 16 * NT], BF16).ap()
    gsv = k.group("gpsimd")
    k.dma("gpsimd", lambda e: e.dma_start(out=xsave, in_=xb[:, :]),
          ["xb"], ["l2a_xsave"], gsv)
    for nm, shp in [("l2a_convw", (128, 256)), ("l2a_convb", (128, 64)),
                    ("l2a_dtb", (128, 64)), ("l2a_dsk", (128, 64)),
                    ("l2a_lnw", (128, 32)), ("l2a_lnb", (128, 32))]:
        load_const(k, nm, W[nm], shp, F32)
    stg = k.t("gw_stg0", (128, 512), F32)
    sel = k.t("sb_sel", (32, 4096), BF16)
    gs = k.group()
    k.dma("sync", lambda e: e.dma_start(out=sel[:, :], in_=W["l2a_sel"]),
          [], ["sb_sel"], gs)
    Af = k.t("l2a_Af", (128, 256), F32)
    bt = TView(xb, 0 * NT, NT)
    ht = TView(xb, 1 * NT, NT)
    dub = TView(xb, 2 * NT, NT)
    scf2 = TView(xb, 3 * NT, NT)
    dtb = TView(xb, 4 * NT, NT)
    bcBC = TView(xb, 5 * NT, 2 * NT)
    delt = k.t("l2a_delt", (128, NT), F32)
    af32 = k.t("l2a_a", (128, NT), F32)
    scf1 = af32
    prod = bt
    ych = k.t("l2a_y", (128, NT), F32)
    zt = scf2
    VKEYS = ["l2a_bt", "l2a_ht", "l2a_dub", "l2a_scf2", "l2a_dtbb",
             "l2a_bcBC", "l2a_dtw_v"]
    convw = k.tiles["c_l2a_convw"]
    convb = k.tiles["c_l2a_convb"]

    for li in range(4):
        k.dma("sync", lambda e, li=li: e.dma_start(
            out=stg[:, 0:256], in_=W["l2a_alog"][0:128, li * 256:(li + 1) * 256]),
            [], ["gw_stg0"], k.group())
        k.s(lambda e: e.activation(out=Af[:, :], in_=stg[:, 0:256],
                                   func=AF.Exp), ["gw_stg0"], ["l2a_Af"])
        k.v(lambda e: e.tensor_scalar_mul(out=Af[:, :], in0=Af[:, :], scalar1=-1.0),
            ["l2a_Af"], ["l2a_Af"])

        def evac_in(o, tb, ps, pskey, li=li):
            if o < 16:
                k.s(lambda e, ps=ps, o=o, tb=tb: e.activation(
                    out=xs[:, o * NT + tb * TB: o * NT + (tb + 1) * TB],
                    in_=ps[:, 0:TB], func=AF.Copy), [pskey], list(kxs2))
            else:
                zt2 = k.t("l2a_zt2", (128, TB), BF16)
                k.v(lambda e, ps=ps, zt2=zt2: e.tensor_copy(out=zt2[:, 0:TB],
                                                            in_=ps[:, 0:TB]),
                    [pskey], ["l2a_zt2"])
                gz = k.group("gpsimd")
                k.dma("gpsimd", lambda e, o=o, tb=tb, zt2=zt2: e.dma_start(
                    out=zdram[0:128, (o - 16) * NT + tb * TB:(o - 16) * NT + (tb + 1) * TB],
                    in_=zt2[:, 0:TB]), ["l2a_zt2"], ["l2a_zdram"], gz)

        gemm_fm(k, f"l2a_in{li}", apoff(W["l2a_inw"], li * 1024 * 4096),
                1024, 4096, xb_tile(k), evac_in)
        k.ctx.op("sync", lambda e: e.nop(), ["xb"], list(VKEYS) + ["xb"])

        for c in range(16):
            k.v(lambda e, c=c, li=li: e.tensor_scalar(
                out=scf2[:, 0:NT], in0=xs[:, c * NT:c * NT + NT],
                scalar1=0.0,
                scalar2=convb[:, li * 16 + c:li * 16 + c + 1],
                op0=ALU.mult, op1=ALU.add),
                list(kxs2) + ["c_l2a_convb"], ["l2a_scf2"])
            for j in range(4):
                lo = 3 - j
                k.v(lambda e, c=c, j=j, li=li, lo=lo: e.scalar_tensor_tensor(
                    out=scf2[:, lo:NT], in0=xs[:, c * NT: c * NT + NT - lo],
                    scalar=convw[:, li * 64 + c * 4 + j:li * 64 + c * 4 + j + 1],
                    in1=scf2[:, lo:NT],
                    op0=ALU.mult, op1=ALU.add),
                    list(kxs2) + ["c_l2a_convw", "l2a_scf2"], ["l2a_scf2"])
            k.s(lambda e, c=c: e.activation(out=xs[:, c * NT:(c + 1) * NT],
                                            in_=scf2[:, 0:NT], func=AF.Silu),
                ["l2a_scf2"], list(kxs2))

        def evac_xp(o, tb, ps, pskey):
            k.v(lambda e, ps=ps, tb=tb: e.tensor_copy(
                out=dtb[0:64, tb * TB:(tb + 1) * TB], in_=ps[0:64, 0:TB]),
                [pskey], ["l2a_dtbb"])
            k.v(lambda e, ps=ps, tb=tb: e.tensor_copy(
                out=bcBC[0:32, tb * TB:(tb + 1) * TB], in_=ps[64:96, 0:TB]),
                [pskey], ["l2a_bcBC"])

        gemm_fm(k, f"l2a_xp{li}", apoff(W["l2a_xpw"], li * 2048 * 96),
                2048, 96, lambda c: (xs, kxs2[c // 8], c * NT), evac_xp)

        dtw = TView(xb, 7 * NT, NT)
        for i4 in range(4):
            k.dma("sync", lambda e, i4=i4, li=li: e.dma_start(
                out=stg[0:64, 0:TB],
                in_=W["l2a_dtw"][li * 64:(li + 1) * 64, i4 * TB:(i4 + 1) * TB]),
                [], ["gw_stg0"], k.group())
            k.v(lambda e, i4=i4, dtw=dtw: e.tensor_copy(
                out=dtw[0:64, i4 * TB:(i4 + 1) * TB], in_=stg[0:64, 0:TB]),
                ["gw_stg0"], ["l2a_dtw_v"])
        scan_tog = [0]
        for c in range(16):
            psd = [k.bank() for _ in range(4)]
            for tb in range(NTB):
                ps, pskey = psd[tb]
                k.te(lambda e, ps=ps, c=c, tb=tb, dtw=dtw: e.matmul(
                    out=ps[:, 0:TB],
                    lhsT=dtw[0:64, c * 128:(c + 1) * 128],
                    rhs=dtb[0:64, tb * TB:(tb + 1) * TB],
                    start=True, stop=True), ["l2a_dtw_v", "l2a_dtbb"], [pskey])
                k.v(lambda e, ps=ps, c=c, tb=tb: e.tensor_scalar(
                    out=delt[:, tb * TB:(tb + 1) * TB], in0=ps[:, 0:TB],
                    scalar1=k.tiles["c_l2a_dtb"][:, li * 16 + c:li * 16 + c + 1],
                    scalar2=None, op0=ALU.add),
                    [pskey, "c_l2a_dtb"], ["l2a_delt"])
            k.s(lambda e: e.activation(out=scf1[:, :], in_=delt[:, :], func=AF.Abs),
                ["l2a_delt"], ["l2a_a"])
            k.s(lambda e: e.activation(out=scf1[:, :], in_=scf1[:, :], func=AF.Exp,
                                       scale=-1.0), ["l2a_a"], ["l2a_a"])
            onet, onek = k.constf(1.0)
            k.s(lambda e, onet=onet: e.activation(out=scf1[:, :], in_=scf1[:, :],
                                                  func=AF.Ln, bias=onet[:, 0:1]),
                ["l2a_a", onek], ["l2a_a"])
            k.v(lambda e: e.tensor_relu(out=scf2[:, 0:NT], in_=delt[:, :]),
                ["l2a_delt"], ["l2a_scf2"])
            k.v(lambda e: e.tensor_add(out=delt[:, :], in0=scf1[:, :], in1=scf2[:, 0:NT]),
                ["l2a_a", "l2a_scf2"], ["l2a_delt"])
            k.v(lambda e, c=c: e.tensor_mul(out=dub[:, :], in0=delt[:, :],
                                            in1=xs[:, c * NT:(c + 1) * NT]),
                ["l2a_delt"] + list(kxs2), ["l2a_dub"])
            k.v(lambda e: e.memset(ych[:, :], 0.0), [], ["l2a_y"])
            for n in range(16):
                k.s(lambda e, c=c, n=n: e.activation(
                    out=af32[:, :], in_=delt[:, :], func=AF.Exp,
                    scale=Af[:, c * 16 + n:c * 16 + n + 1]),
                    ["l2a_delt", "l2a_Af"], ["l2a_a"])
                for tb in range(NTB):
                    psb_, kpb = k.bank()
                    k.te(lambda e, psb_=psb_, n=n, tb=tb: e.matmul(
                        out=psb_[:, 0:TB], lhsT=sel[0:32, n * 128:(n + 1) * 128],
                        rhs=bcBC[0:32, tb * TB:(tb + 1) * TB],
                        start=True, stop=True), ["sb_sel", "l2a_bcBC"], [kpb])
                    k.v(lambda e, psb_=psb_, tb=tb: e.tensor_mul(
                        out=bt[:, tb * TB:(tb + 1) * TB],
                        in0=dub[:, tb * TB:(tb + 1) * TB], in1=psb_[:, 0:TB]),
                        ["l2a_dub", kpb], ["l2a_bt"])
                eng = k.v
                scan_tog[0] += 1
                eng(lambda e: e.tensor_tensor_scan(
                    out=ht[:, :], data0=af32[:, :], data1=bt[:, :],
                    initial=0.0, op0=ALU.mult, op1=ALU.add),
                    ["l2a_a", "l2a_bt"], ["l2a_ht"])
                for tb in range(NTB):
                    psc, kpc = k.bank()
                    k.te(lambda e, psc=psc, n=n, tb=tb: e.matmul(
                        out=psc[:, 0:TB], lhsT=sel[0:32, 2048 + n * 128:2048 + (n + 1) * 128],
                        rhs=bcBC[0:32, tb * TB:(tb + 1) * TB],
                        start=True, stop=True), ["sb_sel", "l2a_bcBC"], [kpc])
                    k.v(lambda e, psc=psc, tb=tb: e.tensor_mul(
                        out=prod[:, 0:TB], in0=ht[:, tb * TB:(tb + 1) * TB],
                        in1=psc[:, 0:TB]), ["l2a_ht", kpc], ["l2a_bt"])
                    k.v(lambda e, tb=tb: e.tensor_add(
                        out=ych[:, tb * TB:(tb + 1) * TB],
                        in0=ych[:, tb * TB:(tb + 1) * TB], in1=prod[:, 0:TB]),
                        ["l2a_y", "l2a_bt"], ["l2a_y"])
            k.v(lambda e, c=c, li=li: e.scalar_tensor_tensor(
                out=ych[:, :], in0=xs[:, c * NT:(c + 1) * NT],
                scalar=k.tiles["c_l2a_dsk"][:, li * 16 + c:li * 16 + c + 1],
                in1=ych[:, :], op0=ALU.mult, op1=ALU.add),
                list(kxs2) + ["c_l2a_dsk", "l2a_y"], ["l2a_y"])
            for tb in range(NTB):
                gz = k.group("gpsimd")
                k.dma("gpsimd", lambda e, c=c, tb=tb: e.dma_start(
                    out=zt[:, 0:TB],
                    in_=zdram[0:128, c * NT + tb * TB: c * NT + (tb + 1) * TB]),
                    ["l2a_zdram"], ["l2a_scf2"], gz)
                k.s(lambda e: e.activation(out=zt[:, 0:TB], in_=zt[:, 0:TB],
                                           func=AF.Silu), ["l2a_scf2"], ["l2a_scf2"])
                k.v(lambda e, c=c, tb=tb: e.tensor_mul(
                    out=xs[:, c * NT + tb * TB: c * NT + (tb + 1) * TB],
                    in0=ych[:, tb * TB:(tb + 1) * TB], in1=zt[:, 0:TB]),
                    ["l2a_y", "l2a_scf2"], list(kxs2))

        def evac_out(o, tb, ps, pskey):
            sl = slice(o * NT + tb * TB, o * NT + (tb + 1) * TB)
            k.v(lambda e, ps=ps, sl=sl: e.tensor_add(
                out=xf[:, sl], in0=xf[:, sl], in1=ps[:, 0:TB]), [pskey, "xf"], ["xf"])

        gemm_fm(k, f"l2a_out{li}", apoff(W["l2a_outw"], li * 2048 * 1024),
                2048, 1024, lambda c: (xs, kxs2[c // 8], c * NT), evac_out)
        k.ctx.op("sync", lambda e: e.nop(), list(VKEYS), ["xb"])
        ln_fm(k, f"l2a_ln{li}",
              gamma=(k.tiles["c_l2a_lnw"], "c_l2a_lnw", li * 8),
              beta=(k.tiles["c_l2a_lnb"], "c_l2a_lnb", li * 8))
    k.ctx.op("sync", lambda e: e.nop(), ["xb"], list(VKEYS) + ["xb"])
    gld2 = k.group("gpsimd")
    for c in range(8):
        k.dma("gpsimd", lambda e, c=c: e.dma_start(
            out=scf2[:, 0:NT],
            in_=xsave[0:128, c * NT:(c + 1) * NT]),
            ["l2a_xsave"], ["l2a_scf2"], gld2)
        k.v(lambda e, c=c: e.tensor_add(
            out=xf[:, c * NT:(c + 1) * NT], in0=xf[:, c * NT:(c + 1) * NT],
            in1=scf2[:, 0:NT]), ["l2a_scf2", "xf"], ["xf"])
        gld2 = k.group("gpsimd")
    k.ctx.op("sync", lambda e: e.nop(), list(VKEYS), ["xb"])
    ln_fm(k, "l2a_lnf")


def emit_l1_attn(k, W):
    xf, xb = k.tiles["xf"], k.tiles["xb"]
    identf, identfk = get_ident(k, W, "f32")
    qh_, kq = bb(k, 0)
    kh_, kk_ = bb(k, 1)
    vdram = k.nc.dram_tensor("l1a_v", [128, 8 * NT], BF16).ap()
    glin = TView(xb, 0, 8 * NT)
    odram = k.nc.dram_tensor("l1a_o", [128, 8 * NT], BF16).ap()
    for nm, shp in [("l1a_cw", (128, 96)), ("l1a_cb", (128, 24)),
                    ("l1a_alog", (128, 8)), ("l1a_dtb", (128, 8)),
                    ("l1a_onw", (128, 1))]:
        load_const(k, nm, W[nm], shp, F32)
    load_const(k, "l1a_eye", W["l1a_eye"], (128, 64), BF16)
    t1 = k.t("l1a_t1", (64, NT), BF16)
    brows = k.t("l1a_br", (16, NT), BF16)
    btall = k.t("l1a_btT", (64, 512), F32)
    gf = k.t("l1a_gf", (128, NT), F32)
    kbeta = k.t("l1a_kbeta", (128, NT), BF16)
    nrow = k.t("l2f_nrow", (2, TB), F32)
    sel2h = k.t("c_sel2h", (2, 128), F32)
    g0 = k.group()
    k.dma("sync", lambda e: e.dma_start(out=sel2h[:, :], in_=W["cst_sel2h"]),
          [], ["c_sel2h"], g0)
    Af = k.t("l1a_Af", (128, 8), F32)
    k.s(lambda e: e.activation(out=Af[:, :], in_=k.tiles["c_l1a_alog"][:, :],
                               func=AF.Exp), ["c_l1a_alog"], ["l1a_Af"])
    k.v(lambda e: e.tensor_scalar_mul(out=Af[:, :], in0=Af[:, :], scalar1=-1.0),
        ["c_l1a_Af" if False else "l1a_Af"], ["l1a_Af"])

    cwt = k.tiles["c_l1a_cw"]
    cbt = k.tiles["c_l1a_cb"]
    sc1 = k.t("l1a_sc1", (128, NT), F32)

    def mk_evac(dst, dk, qkv):
        def evac(o, tb, ps, pskey):
            k.v(lambda e, ps=ps, o=o, tb=tb: e.tensor_copy(
                out=dst[:, o * NT + tb * TB: o * NT + (tb + 1) * TB],
                in_=ps[:, 0:TB]), [pskey], [dk])
        return evac

    gemm_fm(k, "l1a_q", W["l1a_wq"], 1024, 1024, xb_tile(k), mk_evac(qh_, kq, 0))
    gemm_fm(k, "l1a_k", W["l1a_wk"], 1024, 1024, xb_tile(k), mk_evac(kh_, kk_, 1))
    vtmp = k.t("l1a_vtmp", (128, NT), BF16)

    def evac_v(o, tb, ps, pskey):
        k.v(lambda e, ps=ps, tb=tb: e.tensor_copy(
            out=vtmp[:, tb * TB:(tb + 1) * TB], in_=ps[:, 0:TB]),
            [pskey], ["l1a_vtmp"])
        if tb == NTB - 1:
            qkv = 2
            k.v(lambda e, o=o: e.tensor_scalar(
                out=sc1[:, 0:NT], in0=vtmp[:, 0:NT],
                scalar1=0.0, scalar2=cbt[:, qkv * 8 + o:qkv * 8 + o + 1],
                op0=ALU.mult, op1=ALU.add), ["l1a_vtmp", "c_l1a_cb"], ["l1a_sc1"])
            for j in range(4):
                lo = 3 - j
                k.v(lambda e, o=o, j=j, lo=lo: e.scalar_tensor_tensor(
                    out=sc1[:, lo:NT], in0=vtmp[:, 0:NT - lo],
                    scalar=cwt[:, qkv * 32 + o * 4 + j:qkv * 32 + o * 4 + j + 1],
                    in1=sc1[:, lo:NT], op0=ALU.mult, op1=ALU.add),
                    ["l1a_vtmp", "c_l1a_cw", "l1a_sc1"], ["l1a_sc1"])
            k.s(lambda e: e.activation(out=vtmp[:, 0:NT], in_=sc1[:, 0:NT],
                                       func=AF.Silu), ["l1a_sc1"], ["l1a_vtmp"])
            gv = k.group("gpsimd")
            k.dma("gpsimd", lambda e, o=o: e.dma_start(
                out=vdram[0:128, o * NT:(o + 1) * NT], in_=vtmp[:, 0:NT]),
                ["l1a_vtmp"], ["l1a_vdram"], gv)

    gemm_fm(k, "l1a_v", W["l1a_wv"], 1024, 1024, xb_tile(k), evac_v)
    for qkv, (dst, dk) in enumerate([(qh_, kq), (kh_, kk_)]):
        for c in range(8):
            k.v(lambda e, c=c, qkv=qkv, dst=dst: e.tensor_scalar(
                out=sc1[:, 0:NT], in0=dst[:, c * NT:c * NT + NT],
                scalar1=0.0, scalar2=cbt[:, qkv * 8 + c:qkv * 8 + c + 1],
                op0=ALU.mult, op1=ALU.add), [dk, "c_l1a_cb"], ["l1a_sc1"])
            for j in range(4):
                lo = 3 - j
                k.v(lambda e, c=c, j=j, qkv=qkv, lo=lo, dst=dst: e.scalar_tensor_tensor(
                    out=sc1[:, lo:NT], in0=dst[:, c * NT: c * NT + NT - lo],
                    scalar=cwt[:, qkv * 32 + c * 4 + j:qkv * 32 + c * 4 + j + 1],
                    in1=sc1[:, lo:NT], op0=ALU.mult, op1=ALU.add),
                    [dk, "c_l1a_cw", "l1a_sc1"], ["l1a_sc1"])
            k.s(lambda e, c=c, dst=dst: e.activation(
                out=dst[:, c * NT:(c + 1) * NT], in_=sc1[:, 0:NT], func=AF.Silu),
                ["l1a_sc1"], [dk])
    def evac_t1(o, tb, ps, pskey):
        k.v(lambda e, ps=ps, tb=tb: e.tensor_copy(
            out=t1[0:64, tb * TB:(tb + 1) * TB], in_=ps[0:64, 0:TB]), [pskey], ["l1a_t1"])

    gemm_fm(k, "l1a_f1", W["l1a_wf1"], 1024, 64, xb_tile(k), evac_t1)

    def evac_beta(o, tb, ps, pskey):
        k.s(lambda e, ps=ps, tb=tb: e.activation(
            out=brows[0:16, tb * TB:(tb + 1) * TB], in_=ps[0:16, 0:TB],
            func=AF.Sigmoid), [pskey], ["l1a_br"])

    gemm_fm(k, "l1a_wb", W["l1a_wbmat"], 1024, 16, xb_tile(k), evac_beta)
    k.ctx.op("sync", lambda e: e.nop(), ["xb"], ["l1a_glin", "xb"])

    def evac_glin(o, tb, ps, pskey):
        k.s(lambda e, ps=ps, o=o, tb=tb: e.activation(
            out=glin[:, o * NT + tb * TB: o * NT + (tb + 1) * TB],
            in_=ps[:, 0:TB], func=AF.Copy), [pskey], ["l1a_glin"])

    gemm_fm(k, "l1a_f2", W["l1a_wf2"], 64, 1024,
            lambda c: (t1, "l1a_t1", 0), evac_glin)
    identb, identbk = get_ident(k, W, "bf16")
    for c in range(32):
        ps, pskey = k.bankB()
        k.te(lambda e, c=c, ps=ps: e.transpose(
            out=ps[0:64, 0:16], in_=brows[0:16, c * 64:(c + 1) * 64],
            identity=identb[0:16, 0:16]), ["l1a_br", identbk], [pskey])
        k.v(lambda e, c=c, ps=ps: e.tensor_copy(
            out=btall[0:64, c * 16:(c + 1) * 16], in_=ps[0:64, 0:16]),
            [pskey], ["l1a_btT"])

    sqv = k.t("ln_sqb", (128, TB), BF16)
    o2h = k.t("c_ones2h", (128, 2), BF16)
    if True:
        k.v(lambda e: e.memset(o2h[:, :], 0.0), [], ["c_ones2h"])
        k.v(lambda e: e.memset(o2h[0:64, 0:1], 1.0), [], ["c_ones2h"])
        k.v(lambda e: e.memset(o2h[64:128, 1:2], 1.0), [], ["c_ones2h"])
    for qk, (src, skey) in enumerate([(qh_, kq), (kh_, kk_)]):
        for c in range(8):
            for tb in range(NTB):
                psn, kpn = k.bank()
                k.s(lambda e, c=c, tb=tb, src=src: e.activation(
                    out=sqv[:, 0:TB],
                    in_=src[:, c * NT + tb * TB: c * NT + tb * TB + TB],
                    func=AF.Square), [skey], ["ln_sqb"])
                k.te(lambda e, psn=psn: e.matmul(
                    out=psn[0:2, 0:TB], lhsT=o2h[:, 0:2], rhs=sqv[:, 0:TB],
                    start=True, stop=True), ["c_ones2h", "ln_sqb"], [kpn])
                k.v(lambda e, psn=psn: e.tensor_copy(out=nrow[0:2, 0:TB],
                                                     in_=psn[0:2, 0:TB]),
                    [kpn], ["l2f_nrow"])
                k.s(lambda e: e.activation(out=nrow[0:2, 0:TB], in_=nrow[0:2, 0:TB],
                                           func=AF.Sqrt), ["l2f_nrow"], ["l2f_nrow"])
                k.v(lambda e: e.tensor_scalar_add(out=nrow[0:2, 0:TB],
                                                  in0=nrow[0:2, 0:TB], scalar1=1e-6),
                    ["l2f_nrow"], ["l2f_nrow"])
                k.v(lambda e: e.reciprocal(out=nrow[0:2, 0:TB], in_=nrow[0:2, 0:TB]),
                    ["l2f_nrow"], ["l2f_nrow"])
                psb_, kpb = k.bank()
                k.te(lambda e, psb_=psb_: e.matmul(
                    out=psb_[:, 0:TB], lhsT=sel2h[0:2, 0:128],
                    rhs=nrow[0:2, 0:TB], start=True, stop=True),
                    ["c_sel2h", "l2f_nrow"], [kpb])
                mulc = 0.125 if qk == 0 else 1.0
                k.v(lambda e, psb_=psb_, c=c, tb=tb, src=src, mulc=mulc: e.scalar_tensor_tensor(
                    out=src[:, c * NT + tb * TB: c * NT + tb * TB + TB],
                    in0=src[:, c * NT + tb * TB: c * NT + tb * TB + TB],
                    scalar=mulc, in1=psb_[:, 0:TB], op0=ALU.mult, op1=ALU.mult),
                    [skey, kpb], [skey])

    BBIGh = k.tiles["BBIG"]
    BPITCH = getattr(k, "nbb", 3) * BBREG

    def bbap(region, off, dims):
        a = BBIGh[:, :]
        return bass.AP(a.tensor, a.offset + region * BBREG + off,
                       [list(d) for d in dims])

    ones64 = k.t("c_ones64", (128, 64), BF16)
    k.v(lambda e: e.memset(ones64[:, :], 1.0), [], ["c_ones64"])
    for nm, shp in [("l1a_m8neg", (128, 64)), ("l1a_m8incl", (128, 64)),
                    ("l1a_moff", (128, 64))]:
        load_const(k, nm, W[nm], shp, BF16)
    load_const(k, "l1a_bg2", W["l1a_bg2"], (128, 8), F32)
    selbt = k.t("c_l1a_selb", (16, 1024), BF16)
    gsb = k.group()
    k.dma("sync", lambda e: e.dma_start(out=selbt[:, :], in_=W["l1a_selb"]),
          [], ["c_l1a_selb"], gsb)
    eye = k.tiles["c_l1a_eye"]
    m8n = k.tiles["c_l1a_m8neg"]
    m8i = k.tiles["c_l1a_m8incl"]
    moff = k.tiles["c_l1a_moff"]
    selb = k.tiles["c_l1a_selb"]
    UCOLS = [392]
    UKEYS = []

    def ut(name, ncols):
        v = TView(sc1, UCOLS[0], ncols)
        UCOLS[0] += ncols
        UKEYS.append(name)
        return v

    def uap(v, dims, offset=0):
        d2 = [list(d) for d in dims]
        d2[0][0] = NT
        a = sc1[:, :]
        return bass.AP(a.tensor, a.offset + v.off + offset, d2)

    Gx = ut("u_Gx", 65)
    d8 = ut("u_d8", 64)
    exp8 = ut("u_exp8", 64)
    k.ctx.op("sync", lambda e: e.nop(), [],
             ["l1a_vtmp", "u_Bdv", "u_Bst", "u_A8C", "u_A8Q", "u_ot", "u_vst"])

    def vap(v, dims, offset=0):
        d2 = [list(d) for d in dims]
        d2[0][0] = NT
        a = vtmp[:, :]
        return bass.AP(a.tensor, a.offset + v.off + offset, d2)

    Bdv = TView(vtmp, 0, 576)
    Bst = TView(vtmp, 576, 392)
    A8C = TView(vtmp, 968, 64)
    A8Q = TView(vtmp, 1032, 64)
    BdE = ut("u_BdE", 64)
    k.v(lambda e: e.memset(Bdv[:, slice(None)], 0.0), [], ["u_Bdv"])
    eG = exp8
    A0n = ut("u_A0n", 64)
    A0q = ut("u_A0q", 64)
    e2 = d8
    K2 = ut("u_K2", 64)
    K2T = K2
    eGL = ut("u_eGL", 1)
    negNT = ut("u_negNT", 64)
    YdgT = ut("u_YdgT", 64)
    Yblk = YdgT
    WTa = ut("u_WTa", 64)
    WTb = ut("u_WTb", 64)
    DTo = ut("u_DTo", 64)
    DTd = ut("u_DTd", 64)
    Rr = ut("u_R", 64)
    Pp = ut("u_P", 64)
    Mm = ut("u_M", 64)
    Sa = ut("u_Sa", 64)
    Sb = ut("u_Sb", 64)
    assert UCOLS[0] <= NT, UCOLS[0]
    otile = TView(vtmp, 1096, 64)
    uvst = TView(vtmp, 1160, 64)

    import os
    _cut = os.environ.get("L1A_CUT", "")
    _nhp = 8
    if _cut.startswith("units:"):
        _nhp = int(_cut.split(":")[1])
    for hp in range(_nhp):
        k.ctx.op("sync", lambda e: e.nop(), list(UKEYS), ["l1a_sc1"])
        k.v(lambda e, hp=hp: e.tensor_scalar(
            out=sc1[:, 0:NT], in0=glin[:, hp * NT: (hp + 1) * NT],
            scalar1=k.tiles["c_l1a_dtb"][:, hp:hp + 1], scalar2=None, op0=ALU.add),
            ["l1a_glin", "c_l1a_dtb"], ["l1a_sc1"])
        k.v(lambda e: e.tensor_relu(out=gf[:, :], in_=sc1[:, 0:NT]),
            ["l1a_sc1"], ["l1a_gf"])
        k.s(lambda e: e.activation(out=sc1[:, 0:NT], in_=sc1[:, 0:NT], func=AF.Abs),
            ["l1a_sc1"], ["l1a_sc1"])
        k.s(lambda e: e.activation(out=sc1[:, 0:NT], in_=sc1[:, 0:NT], func=AF.Exp,
                                   scale=-1.0), ["l1a_sc1"], ["l1a_sc1"])
        onet, onek = k.constf(1.0)
        k.s(lambda e, onet=onet: e.activation(out=sc1[:, 0:NT], in_=sc1[:, 0:NT],
                                              func=AF.Ln, bias=onet[:, 0:1]),
            ["l1a_sc1", onek], ["l1a_sc1"])
        k.v(lambda e: e.tensor_add(out=gf[:, :], in0=gf[:, :], in1=sc1[:, 0:NT]),
            ["l1a_gf", "l1a_sc1"], ["l1a_gf"])
        k.v(lambda e, hp=hp: e.tensor_scalar(
            out=gf[:, :], in0=gf[:, :], scalar1=Af[:, hp:hp + 1], scalar2=-10.0,
            op0=ALU.mult, op1=ALU.max), ["l1a_gf", "l1a_Af"], ["l1a_gf"])
        k.ctx.op("sync", lambda e: e.nop(), [], ["l1a_sc1"] + list(UKEYS))
        for tb in range(NTB):
            psb_, kpb = k.bank()
            k.te(lambda e, hp=hp, psb_=psb_, tb=tb: e.matmul(
                out=psb_[:, 0:TB], lhsT=selb[0:16, hp * 128:(hp + 1) * 128],
                rhs=brows[0:16, tb * TB:(tb + 1) * TB], start=True, stop=True),
                ["c_l1a_selb", "l1a_br"], [kpb])
            k.v(lambda e, hp=hp, psb_=psb_, tb=tb: e.tensor_mul(
                out=kbeta[:, tb * TB:(tb + 1) * TB],
                in0=kh_[:, hp * NT + tb * TB: hp * NT + tb * TB + TB],
                in1=psb_[:, 0:TB]), [kk_, kpb], ["l1a_kbeta"])
        k.v(lambda e: e.memset(Sa[:, :], 0.0), [], ["u_Sa"])
        scur, snxt = (Sa, "u_Sa"), (Sb, "u_Sb")
        for c in range(32):
            coff = hp * NT + c * 64
            k.v(lambda e: e.memset(Gx[:, 0:1], 0.0), [], ["u_Gx"])
            k.v(lambda e, c=c: e.tensor_tensor_scan(
                out=Gx[:, 1:65], data0=ones64[:, :], data1=gf[:, c * 64:(c + 1) * 64],
                initial=0.0, op0=ALU.mult, op1=ALU.add),
                ["c_ones64", "l1a_gf"], ["u_Gx"])
            k.v(lambda e: e.tensor_sub(out=d8[:, :], in0=uap(Gx, [[65, 128], [1, 64]], 1),
                                       in1=uap(Gx, [[65, 128], [8, 8], [0, 8]])),
                ["u_Gx"], ["u_d8"])
            k.s(lambda e: e.activation(out=exp8[:, :], in_=d8[:, :], func=AF.Exp),
                ["u_d8"], ["u_exp8"])
            k.v(lambda e, coff=coff: e.tensor_mul(
                out=A8C[:, :], in0=exp8[:, :], in1=kbeta[:, c * 64:(c + 1) * 64]),
                ["u_exp8", "l1a_kbeta"], ["u_A8C"])
            k.v(lambda e, coff=coff: e.tensor_mul(
                out=A8Q[:, :], in0=exp8[:, :],
                in1=qh_[:, coff:coff + 64]), ["u_exp8", kq], ["u_A8Q"])
            k.s(lambda e: e.activation(out=BdE[:, :], in_=d8[:, :], func=AF.Exp,
                                       scale=-1.0), ["u_d8"], ["u_BdE"])
            k.v(lambda e, coff=coff: e.tensor_mul(
                out=BdE[:, :], in0=BdE[:, :], in1=kh_[:, coff:coff + 64]),
                [kk_, "u_BdE"], ["u_BdE"])
            k.v(lambda e: e.tensor_copy(
                out=vap(Bdv, [[576, 128], [72, 8], [1, 8]]),
                in_=uap(BdE, [[64, 128], [8, 8], [1, 8]])), ["u_BdE"], ["u_Bdv"])
            k.v(lambda e: e.tensor_copy(
                out=sc1[:, 0:392], in_=uap(Gx, [[65, 128], [8, 7], [0, 56]], 8)),
                ["u_Gx"], ["l1a_sc1"])
            k.v(lambda e: e.tensor_sub(
                out=sc1[:, 0:392], in0=sc1[:, 0:392],
                in1=uap(Gx, [[65, 128], [0, 7], [1, 56]], 1)), ["u_Gx", "l1a_sc1"], ["l1a_sc1"])
            k.s(lambda e: e.activation(out=sc1[:, 0:392], in_=sc1[:, 0:392], func=AF.Exp),
                ["l1a_sc1"], ["l1a_sc1"])
            k.v(lambda e, hp=hp, c=c: e.tensor_mul(
                out=Bst[:, :], in0=sc1[:, 0:392],
                in1=bbap(1, hp * NT + c * 64, [[BPITCH, 128], [0, 7], [1, 56]])),
                ["l1a_sc1", kk_], ["u_Bst"])
            k.s(lambda e: e.activation(out=eG[:, :],
                                       in_=uap(Gx, [[65, 128], [1, 64]], 1),
                                       func=AF.Exp), ["u_Gx"], ["u_exp8"])
            k.v(lambda e, coff=coff: e.scalar_tensor_tensor(
                out=A0n[:, :], in0=eG[:, :], scalar=-1.0,
                in1=kh_[:, coff:coff + 64], op0=ALU.mult, op1=ALU.mult),
                ["u_exp8", kk_], ["u_A0n"])
            k.v(lambda e, coff=coff: e.tensor_mul(
                out=A0q[:, :], in0=eG[:, :], in1=qh_[:, coff:coff + 64]),
                ["u_exp8", kq], ["u_A0q"])
            k.s(lambda e: e.activation(out=e2[:, :],
                                       in_=uap(Gx, [[65, 128], [1, 64]], 1),
                                       func=AF.Exp, scale=-1.0, bias=Gx[:, 64:65]),
                ["u_Gx"], ["u_d8"])
            k.v(lambda e, coff=coff: e.tensor_mul(
                out=K2[:, :], in0=e2[:, :], in1=kh_[:, coff:coff + 64]),
                ["u_d8", kk_], ["u_K2"])
            k.s(lambda e: e.activation(out=eGL[:, :], in_=Gx[:, 64:65], func=AF.Exp),
                ["u_Gx"], ["u_eGL"])
            for hh in range(2):
                hb = hh * 64
                h = 2 * hp + hh
                psC, kpC, iC = k.take_bank()
                psCd, kpCd, iCd = k.take_bank()
                for i in range(1, 8):
                    k.te(lambda e, psC=psC, i=i, hb=hb: e.matmul(
                        out=psC[0:8 * i, 8 * i:8 * i + 8],
                        lhsT=Bst[hb:hb + 64, (i - 1) * 56:(i - 1) * 56 + 8 * i],
                        rhs=A8C[hb:hb + 64, 8 * i:8 * i + 8],
                        start=True, stop=True), ["u_Bst", "u_A8C"], [kpC])
                for j in range(8):
                    k.te(lambda e, psCd=psCd, j=j, hb=hb: e.matmul(
                        out=psCd[0:64, 8 * j:8 * j + 8],
                        lhsT=Bdv[hb:hb + 64, j * 64:(j + 1) * 64],
                        rhs=A8C[hb:hb + 64, 8 * j:8 * j + 8],
                        start=True, stop=True), ["u_Bdv", "u_A8C"], [kpCd])
                k.v(lambda e, psC=psC, hb=hb: e.scalar_tensor_tensor(
                    out=negNT[hb:hb + 64, :], in0=psC[0:64, 0:64], scalar=-1.0,
                    in1=moff[hb:hb + 64, :], op0=ALU.mult, op1=ALU.mult),
                    [kpC, "c_l1a_moff"], ["u_negNT"])
                k.v(lambda e, psCd=psCd, hb=hb: e.tensor_mul(
                    out=YdgT[hb:hb + 64, :], in0=psCd[0:64, 0:64],
                    in1=m8n[hb:hb + 64, :]), [kpCd, "c_l1a_m8neg"], ["u_YdgT"])
                k.free_bank(iC)
                k.free_bank(iCd)
                psD, kpD, iD = k.take_bank()
                psDd, kpDd, iDd = k.take_bank()
                for i in range(1, 8):
                    k.te(lambda e, psD=psD, i=i, hb=hb: e.matmul(
                        out=psD[0:8 * i, 8 * i:8 * i + 8],
                        lhsT=Bst[hb:hb + 64, (i - 1) * 56:(i - 1) * 56 + 8 * i],
                        rhs=A8Q[hb:hb + 64, 8 * i:8 * i + 8],
                        start=True, stop=True), ["u_Bst", "u_A8Q"], [kpD])
                for j in range(8):
                    k.te(lambda e, psDd=psDd, j=j, hb=hb: e.matmul(
                        out=psDd[0:64, 8 * j:8 * j + 8],
                        lhsT=Bdv[hb:hb + 64, j * 64:(j + 1) * 64],
                        rhs=A8Q[hb:hb + 64, 8 * j:8 * j + 8],
                        start=True, stop=True), ["u_Bdv", "u_A8Q"], [kpDd])
                k.v(lambda e, psD=psD, hb=hb: e.tensor_mul(
                    out=DTo[hb:hb + 64, :], in0=psD[0:64, 0:64],
                    in1=moff[hb:hb + 64, :]), [kpD, "c_l1a_moff"], ["u_DTo"])
                k.v(lambda e, psDd=psDd, hb=hb: e.tensor_mul(
                    out=DTd[hb:hb + 64, :], in0=psDd[0:64, 0:64],
                    in1=m8i[hb:hb + 64, :]), [kpDd, "c_l1a_m8incl"], ["u_DTd"])
                k.free_bank(iD)
                k.free_bank(iDd)
                psT, kpT = k.bank()
                k.te(lambda e, psT=psT, hb=hb: e.transpose(
                    out=psT[0:64, 0:64], in_=YdgT[hb:hb + 64, :],
                    identity=identf[hb:hb + 64, hb:hb + 64]),
                    ["u_YdgT", identfk], [kpT])
                k.v(lambda e, psT=psT, hb=hb: e.tensor_copy(
                    out=Yblk[hb:hb + 64, :], in_=psT[0:64, 0:64]), [kpT], ["u_YdgT"])
                k.v(lambda e, hb=hb: e.tensor_copy(out=WTa[hb:hb + 64, :],
                                                   in_=eye[hb:hb + 64, :]),
                    ["c_l1a_eye"], ["u_WTa"])
                cur, nxt = (WTa, "u_WTa"), (WTb, "u_WTb")
                for it in range(7):
                    psW, kpW = k.bank()
                    k.te(lambda e, psW=psW, hb=hb, cur=cur: e.matmul(
                        out=psW[0:64, 0:64], lhsT=Yblk[hb:hb + 64, :],
                        rhs=cur[0][hb:hb + 64, :], start=True, stop=True),
                        ["u_YdgT", cur[1]], [kpW])
                    k.v(lambda e, psW=psW, hb=hb, nxt=nxt: e.tensor_add(
                        out=nxt[0][hb:hb + 64, :], in0=psW[0:64, 0:64],
                        in1=eye[hb:hb + 64, :]), [kpW, "c_l1a_eye"], [nxt[1]])
                    cur, nxt = nxt, cur
                WT, kWT = cur
                psR, kpR, iR = k.take_bank()
                if hh == 0:
                    gvl = k.group("gpsimd")
                    k.dma("gpsimd", lambda e, hp=hp, c=c: e.dma_start(
                        out=uvst[:, :],
                        in_=vdram[0:128, hp * NT + c * 64: hp * NT + (c + 1) * 64]),
                        ["l1a_vdram"], ["u_vst"], gvl)
                psB, kpB = k.bankB()
                k.te(lambda e, psB=psB, hb=hb: e.transpose(
                    out=psB[0:64, 0:64], in_=uvst[hb:hb + 64, :],
                    identity=identb[hb:hb + 64, hb:hb + 64]),
                    ["u_vst", identbk], [kpB])
                k.te(lambda e, psR=psR, hb=hb, scur=scur: e.matmul(
                    out=psR[0:64, 0:64], lhsT=A0n[hb:hb + 64, :],
                    rhs=scur[0][hb:hb + 64, :], start=True, stop=True),
                    ["u_A0n", scur[1]], [kpR])
                k.v(lambda e, psB=psB, hb=hb: e.tensor_copy(
                    out=Rr[hb:hb + 64, :], in_=psB[0:64, 0:64]), [kpB], ["u_R"])
                k.v(lambda e, psR=psR, hb=hb: e.tensor_add(
                    out=Rr[hb:hb + 64, :], in0=Rr[hb:hb + 64, :],
                    in1=psR[0:64, 0:64]), [kpR, "u_R"], ["u_R"])
                k.v(lambda e, hb=hb, c=c, h=h: e.tensor_scalar_mul(
                    out=Rr[hb:hb + 64, :], in0=Rr[hb:hb + 64, :],
                    scalar1=btall[0:64, c * 16 + h:c * 16 + h + 1]),
                    ["u_R", "l1a_btT"], ["u_R"])
                k.free_bank(iR)
                for i in range(8):
                    if i > 0:
                        psP, kpP = k.bank()
                        k.te(lambda e, psP=psP, hb=hb, i=i: e.matmul(
                            out=psP[0:64, 0:64], lhsT=negNT[hb:hb + 8 * i, 0:64],
                            rhs=Mm[hb:hb + 8 * i, :], start=True, stop=True),
                            ["u_negNT", "u_M"], [kpP])
                        k.v(lambda e, psP=psP, hb=hb: e.tensor_add(
                            out=Pp[hb:hb + 64, :], in0=psP[0:64, 0:64],
                            in1=Rr[hb:hb + 64, :]), [kpP, "u_R"], ["u_P"])
                        src_p = (Pp, "u_P")
                    else:
                        src_p = (Rr, "u_R")
                    psM, kpM = k.bank()
                    k.te(lambda e, psM=psM, hb=hb, WT=WT, src_p=src_p: e.matmul(
                        out=psM[0:64, 0:64], lhsT=WT[hb:hb + 64, :],
                        rhs=src_p[0][hb:hb + 64, :], start=True, stop=True),
                        [kWT, src_p[1]], [kpM])
                    k.v(lambda e, psM=psM, hb=hb: e.tensor_copy(
                        out=Mm[hb:hb + 64, :], in_=psM[0:64, 0:64]), [kpM], ["u_M"])
                psO, kpO, iO = k.take_bank()
                k.te(lambda e, psO=psO, hb=hb, scur=scur: e.matmul(
                    out=psO[0:64, 0:64], lhsT=scur[0][hb:hb + 64, :],
                    rhs=A0q[hb:hb + 64, :], start=True, stop=False),
                    [scur[1], "u_A0q"], [kpO])
                k.te(lambda e, psO=psO, hb=hb: e.matmul(
                    out=psO[0:64, 0:64], lhsT=Mm[hb:hb + 64, :],
                    rhs=DTo[hb:hb + 64, :], start=False, stop=False),
                    ["u_M", "u_DTo"], [kpO])
                k.te(lambda e, psO=psO, hb=hb: e.matmul(
                    out=psO[0:64, 0:64], lhsT=Mm[hb:hb + 64, :],
                    rhs=DTd[hb:hb + 64, :], start=False, stop=True),
                    ["u_M", "u_DTd"], [kpO])
                k.v(lambda e, psO=psO, hb=hb: e.tensor_copy(
                    out=otile[hb:hb + 64, :], in_=psO[0:64, 0:64]), [kpO], ["u_ot"])
                k.free_bank(iO)
                psK, kpK = k.bank()
                k.te(lambda e, psK=psK, hb=hb: e.transpose(
                    out=psK[0:64, 0:64], in_=K2[hb:hb + 64, :],
                    identity=identf[hb:hb + 64, hb:hb + 64]),
                    ["u_K2", identfk], [kpK])
                k.v(lambda e, psK=psK, hb=hb: e.tensor_copy(
                    out=K2T[hb:hb + 64, :], in_=psK[0:64, 0:64]), [kpK], ["u_K2"])
                psS, kpS = k.bank()
                k.te(lambda e, psS=psS, hb=hb: e.matmul(
                    out=psS[0:64, 0:64], lhsT=K2T[hb:hb + 64, :],
                    rhs=Mm[hb:hb + 64, :], start=True, stop=True),
                    ["u_K2", "u_M"], [kpS])
                k.v(lambda e, psS=psS, hb=hb, scur=scur, snxt=snxt: e.scalar_tensor_tensor(
                    out=snxt[0][hb:hb + 64, :], in0=scur[0][hb:hb + 64, :],
                    scalar=eGL[hb:hb + 64, 0:1], in1=psS[0:64, 0:64],
                    op0=ALU.mult, op1=ALU.add),
                    [scur[1], "u_eGL", kpS], [snxt[1]])
            scur, snxt = snxt, scur
            god = k.group("gpsimd")
            k.dma("gpsimd", lambda e, hp=hp, c=c: e.dma_start(
                out=odram[0:128, hp * NT + c * 64: hp * NT + (c + 1) * 64],
                in_=otile[:, :]), ["u_ot"], ["l1a_odram"], god)

    if _cut == "noend" or _cut.startswith("units:"):
        return
    for c in range(8):
        k.s(lambda e, c=c: e.activation(out=xb[:, c * NT:(c + 1) * NT],
                                        in_=xf[:, c * NT:(c + 1) * NT],
                                        func=AF.Copy), ["xf"], ["xb", "l1a_glin"])
    gemm_fm(k, "l1a_g1", W["l1a_wg1"], 1024, 64, xb_tile(k), evac_t1)
    sgate, ksg = kh_, kk_

    def evac_g2(o, tb, ps, pskey):
        k.s(lambda e, ps=ps, o=o, tb=tb: e.activation(
            out=sgate[:, o * NT + tb * TB: o * NT + (tb + 1) * TB],
            in_=ps[:, 0:TB], func=AF.Sigmoid,
            bias=k.tiles["c_l1a_bg2"][:, o:o + 1]), [pskey, "c_l1a_bg2"], [ksg])

    gemm_fm(k, "l1a_g2", W["l1a_wg2"], 64, 1024, lambda cc_: (t1, "l1a_t1", 0), evac_g2)
    o2h64 = k.t("c_o2h64", (128, 2), BF16)
    k.v(lambda e: e.memset(o2h64[:, :], 0.0), [], ["c_o2h64"])
    k.v(lambda e: e.memset(o2h64[0:64, 0:1], 1.0 / 64.0), [], ["c_o2h64"])
    k.v(lambda e: e.memset(o2h64[64:128, 1:2], 1.0 / 64.0), [], ["c_o2h64"])
    och = k.t("gw_bf0", (128, 512), BF16)
    for c in range(8):
        for tb in range(NTB):
            go = k.group("gpsimd")
            k.dma("gpsimd", lambda e, c=c, tb=tb: e.dma_start(
                out=och[:, 0:TB],
                in_=odram[0:128, c * NT + tb * TB: c * NT + (tb + 1) * TB]),
                ["l1a_odram"], ["gw_bf0"], go)
            psn, kpn = k.bank()
            k.s(lambda e: e.activation(out=sqv[:, 0:TB], in_=och[:, 0:TB],
                                       func=AF.Square), ["gw_bf0"], ["ln_sqb"])
            k.te(lambda e, psn=psn: e.matmul(
                out=psn[0:2, 0:TB], lhsT=o2h64[:, 0:2], rhs=sqv[:, 0:TB],
                start=True, stop=True), ["c_o2h64", "ln_sqb"], [kpn])
            k.v(lambda e, psn=psn: e.tensor_copy(out=nrow[0:2, 0:TB],
                                                 in_=psn[0:2, 0:TB]), [kpn], ["l2f_nrow"])
            epst, epsk = k.constf(1e-5)
            k.s(lambda e, epst=epst: e.activation(out=nrow[0:2, 0:TB],
                                                  in_=nrow[0:2, 0:TB], func=AF.Ln,
                                                  bias=epst[0:2, 0:1]),
                ["l2f_nrow", epsk], ["l2f_nrow"])
            k.s(lambda e: e.activation(out=nrow[0:2, 0:TB], in_=nrow[0:2, 0:TB],
                                       func=AF.Exp, scale=-0.5), ["l2f_nrow"], ["l2f_nrow"])
            psb_, kpb = k.bank()
            k.te(lambda e, psb_=psb_: e.matmul(
                out=psb_[:, 0:TB], lhsT=sel2h[0:2, 0:128], rhs=nrow[0:2, 0:TB],
                start=True, stop=True), ["c_sel2h", "l2f_nrow"], [kpb])
            k.v(lambda e, psb_=psb_: e.tensor_mul(
                out=och[:, 0:TB], in0=och[:, 0:TB], in1=psb_[:, 0:TB]),
                ["gw_bf0", kpb], ["gw_bf0"])
            k.v(lambda e, c=c, tb=tb: e.scalar_tensor_tensor(
                out=qh_[:, c * NT + tb * TB: c * NT + (tb + 1) * TB],
                in0=och[:, 0:TB], scalar=k.tiles["c_l1a_onw"][:, 0:1],
                in1=sgate[:, c * NT + tb * TB: c * NT + (tb + 1) * TB],
                op0=ALU.mult, op1=ALU.mult),
                ["gw_bf0", "c_l1a_onw", ksg], [kq])

    def evac_wo(o, tb, ps, pskey):
        sl = slice(o * NT + tb * TB, o * NT + (tb + 1) * TB)
        k.v(lambda e, ps=ps, sl=sl: e.tensor_add(
            out=xf[:, sl], in0=xf[:, sl], in1=ps[:, 0:TB]), [pskey, "xf"], ["xf"])

    gemm_fm(k, "l1a_wo", W["l1a_wo"], 1024, 1024, lambda cc_: (qh_, kq, cc_ * NT), evac_wo)
    k.ctx.op("sync", lambda e: e.nop(), ["l1a_br"], ["ln_rows"])
    ln_fm(k, "l1a_ln", rows_t=TView(brows, 0, 2 * TB))


"""Host-side weight preparation (pure layout: transposes, reshapes,
permutations, block-diagonal packing, constant tables)."""
import numpy as np
import ml_dtypes

NPF32 = np.float32


def cc(a):
    return np.ascontiguousarray(a, dtype=NPF32)


def prep_common():
    d = {}
    d["cst_ones_row"] = np.ones((1, 128), NPF32)
    d["cst_ones_mean"] = np.full((128, 1), 1.0 / 1024.0, NPF32)
    return d


def prep_l1f(ins):
    return {
        "l1f_wr": cc(ins["L1_ffn_W_r"].T),
        "l1f_wk": cc(ins["L1_ffn_W_k"].T),
        "l1f_wv": cc(ins["L1_ffn_W_v"].T),
        "l1f_mur": cc(ins["L1_ffn_mu_r"].reshape(8, 128).T),
        "l1f_muk": cc(ins["L1_ffn_mu_k"].reshape(8, 128).T),
    }


def prep_l3a(ins):
    return {
        "l3a_wq": cc(ins["L3_attn_w_q"].T),
        "l3a_wk": cc(ins["L3_attn_w_k"].T),
        "l3a_wv": cc(ins["L3_attn_w_v"].T),
        "l3a_wo": cc(ins["L3_attn_w_out"].T),
        "l3a_scale": cc(ins["L3_attn_softmax_scale"].reshape(1, 1)),
        "cst_ident_f": np.eye(128, dtype=NPF32),
    }


def prep_l3f(ins):
    sel = np.zeros((8, 1024), NPF32)
    for e in range(8):
        sel[e, e * 128:(e + 1) * 128] = 1.0
    return {
        "cst_sel8": sel,
        "l3f_wr": cc(ins["L3_ffn_Wr"]),
        "l3f_wg": cc(ins["L3_ffn_Wg"].reshape(8 * 1024, 2048)),
        "l3f_wu": cc(ins["L3_ffn_Wu"].reshape(8 * 1024, 2048)),
        "l3f_wd": cc(ins["L3_ffn_Wd"].reshape(8 * 2048, 1024)),
        "l3f_sg": cc(ins["L3_ffn_Sg"]),
        "l3f_su": cc(ins["L3_ffn_Su"]),
        "l3f_sd": cc(ins["L3_ffn_Sd"]),
        "cst_ident_f": np.eye(128, dtype=NPF32),
    }


def prep_l0f(ins):
    rm = ins["L0_ffn_rail_maker"]
    ru = ins["L0_ffn_rail_up_proj"]
    rm1 = np.zeros((128, 4096), NPF32)
    for h in range(4):
        for rg in range(8):
            t = h * 8 + rg
            blk = np.zeros((128, 128), NPF32)
            for ri in range(8):
                blk[ri * 16:(ri + 1) * 16, ri * 16:(ri + 1) * 16] = rm[h, 8 * rg + ri]
            rm1[:, t * 128:(t + 1) * 128] = blk
    rm2 = np.zeros((128 * 128, 128), NPF32)
    ti = 0
    for h in range(4):
        for rhalf in range(2):
            for rg4 in range(4):
                rg = rhalf * 4 + rg4
                for sub in range(4):
                    blk = np.zeros((128, 128), NPF32)
                    for ccv in range(2):
                        ri = sub * 2 + ccv
                        blk[ri * 16:(ri + 1) * 16, ccv * 64:(ccv + 1) * 64] = \
                            ru[h, 8 * rg + ri].T
                    rm2[ti * 128:(ti + 1) * 128, :] = blk
                    ti += 1
    return {
        "l0f_rm1": cc(rm1),
        "l0f_rm2": cc(rm2),
        "l0f_cproj": cc(ins["L0_ffn_w_cproj"].T),
    }


def prep_l0a(ins):
    Wqkv = ins["L0_attn_W_qkv"]
    b = ins["L0_attn_b_qkv"]
    qi = np.array([j * 192 + e for j in range(16) for e in range(64)])
    ki = qi + 64
    vi = qi + 128
    r = np.arange(128)
    Ult = (r[:, None] < r[None, :]).astype(NPF32)
    Ule = (r[:, None] <= r[None, :]).astype(NPF32)
    strip = np.concatenate([Ult, Ule], axis=1)
    return {
        "l0a_wq": cc(Wqkv[:, qi]),
        "l0a_wk": cc(Wqkv[:, ki]),
        "l0a_wv": cc(Wqkv[:, vi]),
        "l0a_bq": cc(b[qi].reshape(8, 128).T),
        "l0a_bk": cc(b[ki].reshape(8, 128).T),
        "l0a_bv": cc(b[vi].reshape(8, 128).T),
        "l0a_wo": cc(ins["L0_attn_W_out"]),
        "l0a_bo": cc(ins["L0_attn_b_out"].reshape(8, 128).T),
        "l0a_mask": cc(strip),
        "cst_ident_b": np.eye(128, dtype=NPF32).astype(ml_dtypes.bfloat16),
    }


def prep_l2f(ins):
    DIM_HEAD, WINDOW, n = 64, 512, 2048
    half_idx = np.arange(0, DIM_HEAD, 2, dtype=np.float64)
    inv_freq = 1.0 / 10000.0 ** (half_idx / DIM_HEAD)
    t = np.arange(n, dtype=np.float64)
    freqs = t[:, None] * inv_freq[None, :]
    freqs = np.concatenate([freqs, freqs], -1)
    base = (half_idx + 0.4 * DIM_HEAD) / (1.4 * DIM_HEAD)
    power = (t - n // 2) / float(WINDOW // 2)
    scale = base[None, :] ** power[:, None]
    scale = np.concatenate([scale, scale], -1)
    cos, sin = np.cos(freqs), np.sin(freqs)
    sgn = np.where(np.arange(DIM_HEAD) < 32, -1.0, 1.0)
    cq1 = (cos * scale).T
    cq2 = (sin * scale).T * sgn[:, None]
    ck1 = (cos / scale).T
    ck2 = (sin / scale).T * sgn[:, None]
    def dup(a):
        return cc(np.concatenate([a, a], axis=0))
    qs = ins["L2_ffn_q_scale"]
    ks = ins["L2_ffn_k_scale"]
    p = (np.arange(64) + 32) % 64
    def sc2(a):
        return cc(np.tile(a.T, (2, 1)))
    kj = np.arange(128)[:, None]
    v = np.arange(1408)[None, :]
    band = ((v >= 384 + kj) & (v <= 896 + kj)).astype(NPF32)
    sel2h = np.zeros((2, 128), NPF32)
    sel2h[0, 0:64] = 1.0
    sel2h[1, 64:128] = 1.0
    w1 = ins["L2_ffn_w1"]
    b1 = ins["L2_ffn_b1"]
    w1p = np.zeros((2, 1024, 5632), NPF32)
    b1p = np.zeros((2, 5632), NPF32)
    for i in range(22):
        na = min(128, 2730 - i * 128)
        w1p[:, :, (2 * i) * 128:(2 * i) * 128 + na] = w1[:, :, i * 128:i * 128 + na]
        w1p[:, :, (2 * i + 1) * 128:(2 * i + 1) * 128 + na] = \
            w1[:, :, 2730 + i * 128:2730 + i * 128 + na]
        b1p[:, (2 * i) * 128:(2 * i) * 128 + na] = b1[:, i * 128:i * 128 + na]
        b1p[:, (2 * i + 1) * 128:(2 * i + 1) * 128 + na] = \
            b1[:, 2730 + i * 128:2730 + i * 128 + na]
    w2 = ins["L2_ffn_w2"]
    w2p = np.zeros((2, 2816, 1024), NPF32)
    w2p[:, 0:2730, :] = w2
    return {
        "l2f_wqkv": cc(ins["L2_ffn_wqkv"].reshape(2 * 1024, 1536)),
        "l2f_wo": cc(ins["L2_ffn_wo"].reshape(2 * 512, 1024)),
        "l2f_qs": sc2(qs), "l2f_qsp": sc2(qs[:, p]),
        "l2f_ks": sc2(ks), "l2f_ksp": sc2(ks[:, p]),
        "l2f_cq1": dup(cq1), "l2f_cq2": dup(cq2),
        "l2f_ck1": dup(ck1), "l2f_ck2": dup(ck2),
        "l2f_lng": cc(ins["L2_ffn_ln_g"].reshape(2, 8, 128).transpose(2, 0, 1).reshape(128, 16)),
        "l2f_lnb": cc(ins["L2_ffn_ln_b"].reshape(2, 8, 128).transpose(2, 0, 1).reshape(128, 16)),
        "l2f_w1": cc(w1p.reshape(2 * 1024, 5632)),
        "l2f_b1": cc(b1p.reshape(2, 44, 128).transpose(2, 0, 1).reshape(128, 88)),
        "l2f_w2": cc(w2p.reshape(2 * 2816, 1024)),
        "l2f_b2": cc(ins["L2_ffn_b2"].reshape(2, 8, 128).transpose(2, 0, 1).reshape(128, 16)),
        "l2f_band": cc(band),
        "cst_sel2h": sel2h,
        "cst_ident_b": np.eye(128, dtype=NPF32).astype(ml_dtypes.bfloat16),
    }


def prep_l2a(ins):
    cw = ins["L2_attn_conv_w"]
    cb = ins["L2_attn_conv_b"]
    convw = np.zeros((128, 256), NPF32)
    convb = np.zeros((128, 64), NPF32)
    dtbt = np.zeros((128, 64), NPF32)
    dskt = np.zeros((128, 64), NPF32)
    alog = np.zeros((128, 1024), NPF32)
    for li in range(4):
        for c in range(16):
            for p in range(128):
                convw[p, li * 64 + c * 4: li * 64 + c * 4 + 4] = cw[li, c * 128 + p]
            convb[:, li * 16 + c] = cb[li, c * 128:(c + 1) * 128]
            dtbt[:, li * 16 + c] = ins["L2_attn_dt_proj_b"][li, c * 128:(c + 1) * 128]
            dskt[:, li * 16 + c] = ins["L2_attn_D_skip"][li, c * 128:(c + 1) * 128]
            alog[:, li * 256 + c * 16: li * 256 + (c + 1) * 16] = \
                ins["L2_attn_A_log"][li, c * 128:(c + 1) * 128, :]
    sel = np.zeros((32, 4096), NPF32)
    for n in range(16):
        sel[n, n * 128:(n + 1) * 128] = 1.0
        sel[16 + n, 2048 + n * 128:2048 + (n + 1) * 128] = 1.0
    lnw = np.zeros((128, 32), NPF32)
    lnb = np.zeros((128, 32), NPF32)
    for li in range(4):
        for c in range(8):
            lnw[:, li * 8 + c] = ins["L2_attn_ln_w"][li, c * 128:(c + 1) * 128]
            lnb[:, li * 8 + c] = ins["L2_attn_ln_b"][li, c * 128:(c + 1) * 128]
    return {
        "l2a_inw": cc(ins["L2_attn_in_proj_w"].transpose(0, 2, 1).reshape(4 * 1024, 4096)),
        "l2a_convw": cc(convw), "l2a_convb": cc(convb),
        "l2a_xpw": cc(ins["L2_attn_x_proj_w"].transpose(0, 2, 1).reshape(4 * 2048, 96)),
        "l2a_dtw": cc(ins["L2_attn_dt_proj_w"].transpose(0, 2, 1).reshape(4 * 64, 2048)),
        "l2a_dtb": cc(dtbt), "l2a_alog": cc(alog), "l2a_dsk": cc(dskt),
        "l2a_outw": cc(ins["L2_attn_out_proj_w"].transpose(0, 2, 1).reshape(4 * 2048, 1024)),
        "l2a_lnw": cc(lnw), "l2a_lnb": cc(lnb),
        "l2a_sel": sel.astype(ml_dtypes.bfloat16),
    }


def prep_l1a(ins):
    cw = np.zeros((128, 96), NPF32)
    cb = np.zeros((128, 24), NPF32)
    for qkv, (w_, b_) in enumerate([(ins["L1_attn_qcw"], ins["L1_attn_qcb"]),
                                    (ins["L1_attn_kcw"], ins["L1_attn_kcb"]),
                                    (ins["L1_attn_vcw"], ins["L1_attn_vcb"])]):
        for c in range(8):
            for p in range(128):
                cw[p, qkv * 32 + c * 4: qkv * 32 + c * 4 + 4] = w_[c * 128 + p]
            cb[:, qkv * 8 + c] = b_[c * 128:(c + 1) * 128]
    p = np.arange(128)
    alog = np.zeros((128, 8), NPF32)
    dtb = np.zeros((128, 8), NPF32)
    for hp in range(8):
        h = 2 * hp + p // 64
        alog[:, hp] = ins["L1_attn_A_log"][h]
        dtb[:, hp] = ins["L1_attn_dt_bias"][(h) * 64 + p % 64]
    eye = np.vstack([np.eye(64, dtype=NPF32)] * 2)
    s_ = np.arange(64)[:, None]
    t_ = np.arange(64)[None, :]
    w8 = (s_ // 8) == (t_ // 8)
    m8neg = np.vstack([(-1.0 * (w8 & (s_ < t_))).astype(NPF32)] * 2)
    m8incl = np.vstack([(1.0 * (w8 & (s_ <= t_))).astype(NPF32)] * 2)
    moff = np.vstack([(1.0 * ((s_ // 8) < (t_ // 8))).astype(NPF32)] * 2)
    selb = np.zeros((16, 1024), NPF32)
    for hp in range(8):
        for pp in range(128):
            selb[2 * hp + pp // 64, hp * 128 + pp] = 1.0
    return {
        "l1a_wq": cc(ins["L1_attn_Wq"]),
        "l1a_wk": cc(ins["L1_attn_Wk"]),
        "l1a_wv": cc(ins["L1_attn_Wv"]),
        "l1a_wf1": cc(ins["L1_attn_Wf1"]),
        "l1a_wf2": cc(ins["L1_attn_Wf2"]),
        "l1a_wbmat": cc(ins["L1_attn_Wb"]),
        "l1a_wg1": cc(ins["L1_attn_Wg1"]),
        "l1a_wg2": cc(ins["L1_attn_Wg2"]),
        "l1a_wo": cc(ins["L1_attn_Wo"]),
        "l1a_cw": cc(cw), "l1a_cb": cc(cb),
        "l1a_alog": cc(alog), "l1a_dtb": cc(dtb),
        "l1a_onw": cc(np.tile(ins["L1_attn_o_norm_w"], 2).reshape(128, 1)),
        "cst_ident_b": np.eye(128, dtype=NPF32).astype(ml_dtypes.bfloat16),
        "l1a_eye": cc(eye).astype(ml_dtypes.bfloat16),
        "l1a_m8neg": cc(m8neg).astype(ml_dtypes.bfloat16),
        "l1a_m8incl": cc(m8incl).astype(ml_dtypes.bfloat16),
        "l1a_moff": cc(moff).astype(ml_dtypes.bfloat16),
        "l1a_selb": cc(selb).astype(ml_dtypes.bfloat16),
        "l1a_bg2": cc(ins["L1_attn_bg2"].reshape(8, 128).T),
        "cst_sel2h": np.concatenate([np.concatenate([np.ones((1, 64), NPF32), np.zeros((1, 64), NPF32)], 1),
                                     np.concatenate([np.zeros((1, 64), NPF32), np.ones((1, 64), NPF32)], 1)], 0),
        "cst_ident_f": np.eye(128, dtype=NPF32),
    }


from concourse.bass_utils import run_bass_kernel_spmd

_BLOCKS = ["l0a", "l0f", "l1a", "l1f", "l2a", "l2f", "l3a", "l3f"]
_EMIT = {"l1f": emit_l1_ffn, "l3a": emit_l3_attn, "l3f": emit_l3_ffn,
         "l0f": emit_l0_ffn, "l0a": emit_l0_attn, "l2f": emit_l2_ffn,
         "l2a": emit_l2_attn, "l1a": emit_l1_attn}
_PERSIST = {"xf", "xb", "BBIG", "gw_stg0", "gw_bf0", "c_ident_f", "c_ident_b",
            "ln_sqb", "ln_rows", "l2f_nrow", "c_sel2h", "c_ones2h", "c_ones64",
            "c_ones_row", "c_ones_rowb"}


def _persistent(name):
    return name in _PERSIST or name.startswith("cf_") or name.startswith("cb_")


def _build_fused_program(allw):
    import contextlib
    import ml_dtypes
    nc = bass.Bass("TRN2", target_bir_lowering=False)
    es = contextlib.ExitStack()
    k = KCtx(nc, es)
    x_in = nc.dram_tensor("x_in", [1024, NT], F32, kind="ExternalInput").ap()
    x_out = nc.dram_tensor("x_out", [1024, NT], F32, kind="ExternalOutput").ap()
    waps = {}
    for name, arr in allw.items():
        dt = BF16 if arr.dtype == ml_dtypes.bfloat16 else F32
        waps[name] = nc.dram_tensor(name, list(arr.shape), dt,
                                    kind="ExternalInput").ap()
    k.arena = bass.AutoArena(nc)
    k.arena_persist = _persistent
    k.arena_epoch = 0
    xf = k.t("xf", (128, 8 * NT), F32)
    xb = k.t("xb", (128, 8 * NT), BF16)
    gld = k.group("gpsimd")
    for c in range(8):
        k.dma("gpsimd", lambda e, c=c: e.dma_start(
            out=xf[:, c * NT:(c + 1) * NT], in_=x_in[c * 128:(c + 1) * 128, :]),
            [], ["xf"], gld)
    for c in range(8):
        k.s(lambda e, c=c: e.activation(out=xb[:, c * NT:(c + 1) * NT],
                                        in_=xf[:, c * NT:(c + 1) * NT],
                                        func=AF.Copy), ["xf"], ["xb"])
    k.nbb = 2
    for bi, tag in enumerate(_BLOCKS):
        nt0 = set(k.tiles)
        op0 = len(k.ctx.ops)
        _EMIT[tag](k, waps)
        for name in list(k.tiles):
            if name not in nt0 and not _persistent(name):
                k.free_tile(name)
        k._retired = []
        k.arena.addr = 0
        k.arena_epoch += 1
        if bi < len(_BLOCKS) - 1:
            keys = set()
            for o in k.ctx.ops[op0:]:
                keys.update(o.reads)
                keys.update(o.writes)
            ep = f"EP_{_BLOCKS[bi + 1]}"
            k.ctx.extra_reads = []
            k.ctx.op("sync", lambda e: e.nop(), [], sorted(keys) + [ep])
            k.ctx.extra_reads = [ep]
    k.ctx.extra_reads = []
    gst = k.group("gpsimd")
    for c in range(8):
        k.dma("gpsimd", lambda e, c=c: e.dma_start(
            out=x_out[c * 128:(c + 1) * 128, :], in_=xf[:, c * NT:(c + 1) * NT]),
            ["xf"], [f"xo{c}"], gst)
    k.ctx.op("sync", lambda e: e.nop(), [f"xo{c}" for c in range(8)], [])
    k.arena.allocate(side="right")
    with nc.Block() as block:
        k.ctx.emit(block)
    es.close()
    return nc


_PROG = None


def kernel(**inputs):
    global _PROG
    x = np.asarray(inputs["L0_attn_inputs"], dtype=np.float32)
    allw = prep_common()
    for tag in _BLOCKS:
        allw.update(globals()["prep_" + tag](inputs))
    if _PROG is None:
        _PROG = _build_fused_program(allw)
    in_maps = []
    for b in range(8):
        m = {"x_in": np.ascontiguousarray(x[b].T)}
        m.update(allw)
        in_maps.append(m)
    res = run_bass_kernel_spmd(_PROG, in_maps, core_ids=list(range(8)))
    return np.stack([np.asarray(r["x_out"]).T for r in res.results]).astype(np.float32)


# version 05 (final)
# speedup vs baseline: 1.0148x; 1.0107x over previous
from collections import defaultdict

ENGINES = ("sync", "vector", "scalar", "tensor", "gpsimd")
N_DMA_SEMS = 40
SLOT_RANGES = {"sync": (0, 20), "gpsimd": (20, 32), "scalar": (32, 36),
               "vector": (36, 40), "tensor": (36, 40)}


class _Op:
    __slots__ = ("stream", "fn", "reads", "writes", "idx", "tag",
                 "is_dma", "group")

    def __init__(self, stream, fn, reads, writes, tag, is_dma=False, group=None):
        self.stream = stream
        self.fn = fn
        self.reads = reads
        self.writes = writes
        self.tag = tag
        self.idx = None
        self.is_dma = is_dma
        self.group = group


class DmaGroup:
    __slots__ = ("sem_slot", "base", "count", "gid", "prev_group")

    def __init__(self, gid, sem_slot, base, prev_group):
        self.gid = gid
        self.sem_slot = sem_slot
        self.base = base
        self.count = 0
        self.prev_group = prev_group

    @property
    def total(self):
        return self.base + self.count


class DepCtx:
    def __init__(self, nc, es):
        self.nc = nc
        self.es = es
        self.ops = []
        self.per_engine = defaultdict(list)
        self.stream_counts = defaultdict(int)
        self.groups = []
        self._next_slot = {e: SLOT_RANGES[e][0] for e in SLOT_RANGES}
        self.extra_reads = []
        self._slot_total = [0] * N_DMA_SEMS
        self._slot_last_group = [None] * N_DMA_SEMS

    def group(self, engine="sync"):
        lo, hi = SLOT_RANGES[engine]
        slot = self._next_slot[engine]
        nxt = slot + 1
        if nxt >= hi:
            nxt = lo
        self._next_slot[engine] = nxt
        g = DmaGroup(len(self.groups), slot, self._slot_total[slot],
                     self._slot_last_group[slot])
        self._slot_last_group[slot] = g
        self.groups.append(g)
        return g

    def op(self, engine, fn, reads=(), writes=(), tag=""):
        o = _Op(engine, fn, list(reads) + self.extra_reads, list(writes), tag)
        self._record(o)
        return o

    def dma(self, engine, fn, reads=(), writes=(), tag="", group=None):
        assert group is not None, "dma needs a group"
        lo, hi = SLOT_RANGES[engine]
        assert lo <= group.sem_slot < hi, \
            f"dma group slot {group.sem_slot} not in {engine} range (pass engine to group())"
        o = _Op(engine, fn, list(reads) + self.extra_reads, list(writes), tag,
                is_dma=True, group=group)
        group.count += 1
        self._slot_total[group.sem_slot] += 1
        self._record(o)
        return o

    def _record(self, o):
        self.ops.append(o)
        self.per_engine[o.stream].append(o)
        if not o.is_dma:
            self.stream_counts[o.stream] += 1
            o.idx = self.stream_counts[o.stream]
        else:
            o.idx = None

    def emit(self, block):
        def tok(o):
            if o.is_dma:
                return ("dmag%d" % o.group.sem_slot, o.group)
            return (o.stream, o.idx)

        mem = {}
        deps = {}
        seen_groups = set()
        pos = {}
        for i, o in enumerate(self.ops):
            pos[id(o)] = i
        stream_pos = defaultdict(dict)
        for o in self.ops:
            need = {}
            own_group = o.group if o.is_dma else None

            def add(t):
                if t is None:
                    return
                s, v = t
                if own_group is not None and s.startswith("dmag") and v is own_group:
                    return
                cur = need.get(s)
                if cur is None:
                    need[s] = v
                else:
                    if s.startswith("dmag"):
                        if v.total > cur.total:
                            need[s] = v
                    elif v > cur:
                        need[s] = v
            for k in o.reads:
                st = mem.get(k)
                if st is not None and st[0] is not None:
                    add(st[0])
            for k in o.writes:
                st = mem.get(k)
                if st is not None:
                    if st[0] is not None:
                        add(st[0])
                    for r in st[1]:
                        add(r)
            if o.is_dma and o.group.gid not in seen_groups:
                seen_groups.add(o.group.gid)
                prev = o.group.prev_group
                if prev is not None:
                    add(("dmag%d" % o.group.sem_slot, prev))
            if o.stream == "tensor" and not o.is_dma:
                need.pop("tensor", None)
            deps[id(o)] = need
            t = tok(o)
            if o.is_dma:
                stream_pos[t[0]][id(t[1])] = pos[id(o)]
            else:
                stream_pos[o.stream][o.idx] = pos[id(o)]
            for k in o.reads:
                st = mem.setdefault(k, [None, []])
                st[1].append(t)
            for k in o.writes:
                mem[k] = [t, []]

        for o in self.ops:
            need = deps[id(o)]
            for s in list(need):
                if s.startswith("dmag"):
                    g = need[s]
                    gp = stream_pos[s].get(id(g), -1)
                    assert gp <= pos[id(o)], \
                        f"FORWARD dma-group ref: op@{pos[id(o)]} tag={o.tag} waits group gid={g.gid} slot={g.sem_slot} last-dma@{gp}"
                    need[s] = g.total
                else:
                    vp = stream_pos[s].get(need[s], None)
                    assert vp is not None and vp <= pos[id(o)], \
                        f"FORWARD ref: op@{pos[id(o)]} tag={o.tag} stream {o.stream} waits {s}>={need[s]} recorded@{vp}"


        self.engine_sems = {}
        for e in ENGINES:
            if self.per_engine[e]:
                self.engine_sems[e] = self.es.enter_context(
                    self.nc.semaphore(f"esem_{e}"))
        used_slots = sorted({g.sem_slot for g in self.groups if g.count})
        self.dma_sems = {}
        for slot in used_slots:
            self.dma_sems[slot] = self.es.enter_context(
                self.nc.semaphore(f"dsem{slot}"))

        snap = defaultdict(dict)
        knows = {e: defaultdict(int) for e in ENGINES}
        plans = defaultdict(list)
        for o in self.ops:
            e = o.stream
            kn = knows[e]
            for s, v in sorted(deps[id(o)].items()):
                if kn[s] >= v:
                    continue
                plans[e].append(("wait", s, v))
                sn = snap[s].get(v)
                if sn:
                    for s2, v2 in sn.items():
                        if v2 > kn[s2]:
                            kn[s2] = v2
                kn[s] = max(kn[s], v)
            plans[e].append(("op", o))
            if o.is_dma:
                key = o.group.total
                s = "dmag%d" % o.group.sem_slot
                sn = snap[s].get(key)
                if sn is None:
                    sn = dict(kn)
                else:
                    for s2, v2 in kn.items():
                        pass
                    newsn = {}
                    for s2 in set(sn) & set(kn):
                        newsn[s2] = min(sn[s2], kn[s2])
                    sn = newsn
                sn[s] = max(sn.get(s, 0), key)
                snap[s][key] = sn
            else:
                sn = dict(kn)
                sn[e] = o.idx
                snap[e][o.idx] = sn

        self._plans = plans
        for e in ENGINES:
            plan = plans[e]
            if not plan:
                continue
            section = getattr(block, e)

            def make(plan=plan, e=e):
                def body(eng):
                    for item in plan:
                        if item[0] == "wait":
                            _, s, v = item
                            if s.startswith("dmag"):
                                eng.wait_ge(self.dma_sems[int(s[4:])], 16 * v)
                            else:
                                eng.wait_ge(self.engine_sems[s], v)
                        else:
                            o = item[1]
                            instr = o.fn(eng)
                            if o.is_dma:
                                instr.then_inc(self.dma_sems[o.group.sem_slot], 16)
                            else:
                                instr.then_inc(self.engine_sems[e], 1)
                return body
            section(make())


def sbuf(nc, es, name, shape, dtype):
    return es.enter_context(nc.sbuf_tensor(name, shape, dtype))


def psum(nc, es, name, shape, dtype):
    return es.enter_context(nc.psum_tensor(name, shape, dtype))


"""Block emitters for the megakernel. All layouts feature-major:
stream xf (fp32) / xb (bf16) as (128, 8*2048): chunk c holds channels
[128c, 128c+128), free = 2048 tokens.
"""
import numpy as np
import concourse.bass as bass
import concourse.mybir as mybir

F32 = mybir.dt.float32
BF16 = mybir.dt.bfloat16
AF = mybir.ActivationFunctionType
ALU = mybir.AluOpType

NT = 2048
TB = 512
NTB = NT // TB


def ap3(t, dims, offset=0):
    a = t[:, :] if not isinstance(t, bass.AP) else t
    return bass.AP(a.tensor, a.offset + offset, [list(d) for d in dims])


class TileDict(dict):
    def __init__(self):
        super().__init__()
        self.freed = {}

    def __missing__(self, key):
        return self.freed[key]


class KCtx:
    def __init__(self, nc, es):
        self.nc, self.es = nc, es
        self.ctx = DepCtx(nc, es)
        self.tiles = TileDict()
        self.psums = {}
        self.sbuf_bytes = 0
        self.pb = []
        for i in range(7):
            h = es.enter_context(nc.psum_tensor(f"pb{i}", [128, 512], F32))
            self.pb.append(h)
        self.pbB = es.enter_context(nc.psum_tensor("pbB", [128, 1024], BF16))
        self._ps_rr = 0

    def t(self, name, shape=None, dtype=None):
        if name in self.tiles and shape is not None:
            assert list(self.tiles[name].shape) == list(shape), \
                f"tile {name} shape mismatch {self.tiles[name].shape} vs {shape}"
        if name not in self.tiles:
            assert shape is not None, f"tile {name} not allocated yet"
            arena = getattr(self, "arena", None)
            if arena is not None and not self.arena_persist(name):
                h = arena.sbuf_tensor(name + f"_e{self.arena_epoch}",
                                      list(shape), dtype)
                cm = None
            else:
                cm = self.nc.sbuf_tensor(name, list(shape), dtype)
                h = cm.__enter__()
            self.tiles[name] = h
            self._cms = getattr(self, "_cms", {})
            self._cms[name] = cm
            per = int(np.prod(shape[1:])) * mybir.dt.size(dtype)
            self.sbuf_bytes += per * 128
        return self.tiles[name]

    def free_tile(self, name):
        cm = self._cms.pop(name)
        if cm is not None:
            cm.__exit__(None, None, None)
        self.tiles.freed[name] = self.tiles[name]
        del self.tiles[name]
        self._retired = getattr(self, "_retired", [])
        self._retired.append(name)

    def alias_fence(self, newkeys):
        old = getattr(self, "_retired", [])
        if old:
            self.ctx.op("sync", lambda e: e.nop(), list(old), list(newkeys))
            self._retired = []

    def bank(self):
        for _ in range(7):
            i = self._ps_rr
            self._ps_rr = (self._ps_rr + 1) % 7
            if i not in getattr(self, "_reserved", set()):
                return self.pb[i], f"pb{i}"
        raise RuntimeError("no free psum bank")

    def bankB(self):
        return self.pbB, "pbB"

    def take_bank(self):
        if not hasattr(self, "_reserved"):
            self._reserved = set()
        for i in range(7):
            if i not in self._reserved:
                self._reserved.add(i)
                return self.pb[i], f"pb{i}", i
        raise RuntimeError("all banks reserved")

    def free_bank(self, i):
        self._reserved.discard(i)

    def v(self, fn, r, w, tag=""):
        return self.ctx.op("vector", fn, r, w, tag)

    def s(self, fn, r, w, tag=""):
        return self.ctx.op("scalar", fn, r, w, tag)

    def g(self, fn, r, w, tag=""):
        return self.ctx.op("gpsimd", fn, r, w, tag)

    def te(self, fn, r, w, tag=""):
        return self.ctx.op("tensor", fn, r, w, tag)

    def dma(self, eng, fn, r, w, group, tag=""):
        return self.ctx.dma(eng, fn, r, w, tag, group=group)

    def group(self, engine="sync"):
        return self.ctx.group(engine)

    def constb(self, val):
        name = "cb_%x" % (np.float32(val).view(np.uint32))
        if name not in self.tiles:
            t = self.t(name, (128, 1), BF16)
            self.v(lambda e, t=t, val=val: e.memset(t[:, :], float(val)), [], [name])
        return self.tiles[name], name

    def ones_row(self, dtype="f32"):
        name = "c_ones_row" if dtype == "f32" else "c_ones_rowb"
        if name not in self.tiles:
            t = self.t(name, (1, 128), F32 if dtype == "f32" else BF16)
            self.v(lambda e, t=t: e.memset(t[:, :], 1.0), [], [name])
        return self.tiles[name], name

    def constf(self, val):
        name = "cf_%x" % (np.float32(val).view(np.uint32))
        if name not in self.tiles:
            t = self.t(name, (128, 1), F32)
            self.v(lambda e, t=t, val=val: e.memset(t[:, :], float(val)), [], [name])
        return self.tiles[name], name


def load_const(k, name, hbm_ap, shape, dtype, g=None):
    t = k.t("c_" + name, shape, dtype)
    g = g or k.group()
    k.dma("sync", lambda e: e.dma_start(out=t[:, :], in_=hbm_ap), [], ["c_" + name], g)
    return t


def gemm_fm(k, name, wap, cin, cout, xtile, evac, wq="sync", cast_eng="scalar",
            tokb=TB, ntok=NT, cin_off=0):
    nci = (cin + 127) // 128
    nco = (cout + 127) // 128
    ntb = ntok // tokb
    SLAB = 4
    nslab = (nci + SLAB - 1) // SLAB
    fresh = "gw_stg0" not in k.tiles
    wstg0 = k.t("gw_stg0", (128, SLAB * 128), F32)
    if fresh:
        k.v(lambda e: e.memset(wstg0[:, :], 0.0), [], ["gw_stg0"])
    wstg = [wstg0, wstg0]
    wbf0 = k.t("gw_bf0", (128, SLAB * 128), BF16)
    wbf = [wbf0, wbf0]
    par = 0
    for o in range(nco):
        ocols = min(128, cout - o * 128)
        for sl in range(nslab):
            c0 = sl * SLAB
            ncs = min(SLAB, nci - c0)
            stg, sbf = wstg[par], wbf[par]
            kst, kbf = "gw_stg0", "gw_bf0"
            prows = 128
            if c0 + ncs == nci and cin % 128:
                assert ncs == 1, "partial cin chunk must be alone in its slab"
                prows = cin % 128
            gld = k.group()
            src = bass.AP(wap.tensor,
                          wap.offset + (cin_off + c0 * 128) * cout + o * 128,
                          [[cout, prows], [128 * cout, ncs], [1, ocols]])
            dst = ap3(stg, [[SLAB * 128, prows], [128, ncs], [1, ocols]])
            k.dma(wq, lambda e, src=src, dst=dst: e.dma_start(out=dst, in_=src),
                  [], [kst], gld, tag=f"{name}_w{o}_{sl}")
            if ocols == 128:
                cin_ap = lambda stg=stg, n=ncs * 128, prows=prows: stg[0:prows, 0:n]
                cout_ap = lambda sbf=sbf, n=ncs * 128, prows=prows: sbf[0:prows, 0:n]
            else:
                cin_ap = lambda stg=stg, ncs=ncs, ocols=ocols, prows=prows: ap3(
                    stg, [[SLAB * 128, prows], [128, ncs], [1, ocols]])
                cout_ap = lambda sbf=sbf, ncs=ncs, ocols=ocols, prows=prows: ap3(
                    sbf, [[SLAB * 128, prows], [128, ncs], [1, ocols]])
            if cast_eng == "scalar":
                k.s(lambda e, cin_ap=cin_ap, cout_ap=cout_ap: e.activation(
                    out=cout_ap(), in_=cin_ap(), func=AF.Copy), [kst], [kbf])
            else:
                k.v(lambda e, cin_ap=cin_ap, cout_ap=cout_ap: e.tensor_copy(
                    out=cout_ap(), in_=cin_ap()), [kst], [kbf])
            if nslab == 1:
                for tb in range(ntb):
                    ps, pskey = k.bank()
                    for c in range(ncs):
                        xh, xk, xoff = xtile(c0 + c)
                        k.te(lambda e, ps=ps, sbf=sbf, c=c, xh=xh, xoff=xoff, tb=tb,
                             ocols=ocols, prows=prows,
                             first=(c == 0), last=(c == ncs - 1):
                             e.matmul(out=ps[0:ocols, 0:tokb],
                                      lhsT=sbf[0:prows, c * 128:c * 128 + ocols],
                                      rhs=xh[0:prows, xoff + tb * tokb: xoff + (tb + 1) * tokb],
                                      start=first, stop=last),
                             [kbf, xk], [pskey], tag=f"{name}_mm")
                    evac(o, tb, ps, pskey)
            else:
                if sl == 0:
                    held = [k.take_bank() for _ in range(ntb)]
                    k._gemm_held = held
                held = k._gemm_held
                for tb in range(ntb):
                    ps, pskey, _ = held[tb]
                    for c in range(ncs):
                        xh, xk, xoff = xtile(c0 + c)
                        k.te(lambda e, ps=ps, sbf=sbf, c=c, xh=xh, xoff=xoff, tb=tb,
                             ocols=ocols, first=(sl == 0 and c == 0),
                             last=(sl == nslab - 1 and c == ncs - 1):
                             e.matmul(out=ps[0:ocols, 0:tokb],
                                      lhsT=sbf[:, c * 128:c * 128 + ocols],
                                      rhs=xh[:, xoff + tb * tokb: xoff + (tb + 1) * tokb],
                                      start=first, stop=last),
                             [kbf, xk], [pskey], tag=f"{name}_mm")
                if sl == nslab - 1:
                    for tb in range(ntb):
                        ps, pskey, bi = held[tb]
                        evac(o, tb, ps, pskey)
                        k.free_bank(bi)
            par ^= 1


def bcast_row(k, name, row_t, row_key, row_off, dst, dst_key, dst_off, ones1,
              ntok=NT, scalemul=None):
    for tb in range(ntok // TB):
        ps, pskey = k.bank()
        k.te(lambda e, ps=ps, tb=tb: e.matmul(
            out=ps[:, 0:TB], lhsT=ones1[0:1, 0:128],
            rhs=row_t[0:1, row_off + tb * TB: row_off + (tb + 1) * TB],
            start=True, stop=True), ["c_ones_row", row_key], [pskey])
        k.v(lambda e, ps=ps, tb=tb: e.tensor_copy(
            out=dst[:, dst_off + tb * TB: dst_off + (tb + 1) * TB], in_=ps[:, 0:TB]),
            [pskey], [dst_key])


def ln_fm(k, tag, gamma=None, beta=None, nchunks=8, eps=1e-5, dinv=None, rows_t=None):
    xf = k.tiles["xf"]
    xb = k.tiles["xb"]
    sq = k.t("ln_sqb", (128, TB), BF16)
    rows = rows_t if rows_t is not None else k.t("ln_rows", (1, 2 * TB), BF16)
    onesN, onesNk = k.constf(1.0 / (nchunks * 128.0))
    onesNb, onesNbk = k.constb(1.0 / (nchunks * 128.0))
    ones1, ones1k = k.ones_row()
    for tb in range(NTB):
        psm, kpm = k.bank()
        psv, kpv = k.bank()
        for c in range(nchunks):
            k.te(lambda e, psm=psm, c=c, tb=tb: e.matmul(
                out=psm[0:1, 0:TB], lhsT=onesN[:, 0:1],
                rhs=xf[:, c * NT + tb * TB: c * NT + tb * TB + TB],
                start=(c == 0), stop=(c == nchunks - 1)),
                [onesNk, "xf"], [kpm])
        for c in range(nchunks):
            k.s(lambda e, c=c, tb=tb: e.activation(
                out=sq[:, 0:TB],
                in_=xf[:, c * NT + tb * TB: c * NT + tb * TB + TB],
                func=AF.Square), ["xf"], ["ln_sqb"], tag=tag + "_sq")
            k.te(lambda e, psv=psv, c=c: e.matmul(
                out=psv[0:1, 0:TB], lhsT=onesNb[:, 0:1],
                rhs=sq[:, 0:TB],
                start=(c == 0), stop=(c == nchunks - 1)),
                [onesNbk, "ln_sqb"], [kpv])
        k.v(lambda e, psm=psm: e.tensor_copy(out=rows[0:1, 0:TB], in_=psm[0:1, 0:TB]),
            [kpm], ["ln_rows"])
        k.s(lambda e: e.activation(out=rows[0:1, TB:2 * TB], in_=rows[0:1, 0:TB],
                                   func=AF.Square), ["ln_rows"], ["ln_rows"])
        k.v(lambda e, psv=psv: e.tensor_sub(out=rows[0:1, TB:2 * TB],
                                            in0=psv[0:1, 0:TB],
                                            in1=rows[0:1, TB:2 * TB]),
            [kpv, "ln_rows"], ["ln_rows"])
        epst, epsk = k.constf(eps)
        k.s(lambda e, epst=epst: e.activation(out=rows[0:1, TB:2 * TB],
                                   in_=rows[0:1, TB:2 * TB],
                                   func=AF.Ln, bias=epst[0:1, 0:1]),
            ["ln_rows", epsk], ["ln_rows"])
        k.s(lambda e: e.activation(out=rows[0:1, TB:2 * TB],
                                   in_=rows[0:1, TB:2 * TB],
                                   func=AF.Exp, scale=-0.5), ["ln_rows"], ["ln_rows"])
        pmb, kmb = k.bank()
        prb, krb = k.bank()
        ones1b, ones1bk = k.ones_row("bf16")
        k.te(lambda e, pmb=pmb, ones1b=ones1b: e.matmul(
            out=pmb[:, 0:TB], lhsT=ones1b[0:1, 0:128],
            rhs=rows[0:1, 0:TB], start=True, stop=True),
             [ones1bk, "ln_rows"], [kmb])
        k.te(lambda e, prb=prb, ones1b=ones1b: e.matmul(
            out=prb[:, 0:TB], lhsT=ones1b[0:1, 0:128],
            rhs=rows[0:1, TB:2 * TB], start=True, stop=True),
             [ones1bk, "ln_rows"], [krb])
        for c in range(nchunks):
            sl = slice(c * NT + tb * TB, c * NT + tb * TB + TB)
            k.v(lambda e, sl=sl, pmb=pmb: e.tensor_sub(
                out=xf[:, sl], in0=xf[:, sl], in1=pmb[:, 0:TB]),
                ["xf", kmb], ["xf"])
            k.v(lambda e, sl=sl, prb=prb: e.tensor_mul(
                out=xf[:, sl], in0=xf[:, sl], in1=prb[:, 0:TB]),
                ["xf", krb], ["xf"])
            if gamma is not None:
                gt, gk, goff = gamma
                bt, bk, boff = beta
                k.v(lambda e, sl=sl, c=c: e.tensor_scalar(
                    out=xf[:, sl], in0=xf[:, sl],
                    scalar1=gt[:, goff + c:goff + c + 1],
                    scalar2=bt[:, boff + c:boff + c + 1],
                    op0=ALU.mult, op1=ALU.add), ["xf", gk, bk], ["xf"])
            k.s(lambda e, sl=sl: e.activation(out=xb[:, sl], in_=xf[:, sl],
                                              func=AF.Copy),
                ["xf"], ["xb"], tag=tag + "_cast")


def xb_tile(k):
    xb = k.tiles["xb"]
    return lambda c: (xb, "xb", c * NT)


class TView:
    def __init__(self, h, off, size):
        self.h, self.off, self.size = h, off, size
        self.dtype = h.dtype

    def __getitem__(self, idx):
        p, f = idx
        st = self.off if f.start is None else self.off + f.start
        sp = self.off + self.size if f.stop is None else self.off + f.stop
        return self.h[p, st:sp]


BBREG = 8 * NT


def bb(k, idx, init=False):
    nbb = getattr(k, "nbb", 3)
    t = k.t("BBIG", (128, nbb * BBREG), BF16)
    name = f"R{idx}"
    v = TView(t, idx * BBREG, BBREG)
    if init:
        kk = ("bbinit", idx)
        if kk not in getattr(k, "_done", set()):
            k._done = getattr(k, "_done", set())
            k._done.add(kk)
            k.v(lambda e, v=v: e.memset(v[:, slice(None, None)], 0.0), [], [name])
    return v, name


def bb2(k, idx0=0, idx1=1):
    nbb = getattr(k, "nbb", 3)
    t = k.t("BBIG", (128, nbb * BBREG), BF16)
    assert idx1 == idx0 + 1
    return TView(t, idx0 * BBREG, 2 * BBREG), (f"R{idx0}", f"R{idx1}")


def emit_l1_ffn(k, W):
    xf, xb = k.tiles["xf"], k.tiles["xb"]
    load_const(k, "l1f_mur", W["l1f_mur"], (128, 8), F32)
    load_const(k, "l1f_muk", W["l1f_muk"], (128, 8), F32)
    b0, kb0 = bb(k, 0)
    b1, kb1 = bb(k, 1)
    xsh = k.t("sc_bf_a", (128, NT), BF16)
    dsc = k.t("sc_bf_b", (128, NT), BF16)
    relu_s = k.t("sc_f_a", (128, TB), F32)
    prod = k.t("sc_f_b", (128, TB), F32)
    mur = k.tiles["c_l1f_mur"]
    muk = k.tiles["c_l1f_muk"]
    for c in range(8):
        sl = slice(c * NT, (c + 1) * NT)
        k.v(lambda e: e.memset(xsh[:, 0:1], 0.0), [], ["sc_bf_a"])
        k.v(lambda e, c=c: e.tensor_copy(out=xsh[:, 1:NT],
                                         in_=xf[:, c * NT: c * NT + NT - 1]),
            ["xf"], ["sc_bf_a"])
        k.v(lambda e, sl=sl: e.tensor_sub(out=dsc[:, :], in0=xf[:, sl], in1=xsh[:, :]),
            ["xf", "sc_bf_a"], ["sc_bf_b"])
        k.v(lambda e, sl=sl, c=c: e.scalar_tensor_tensor(
            out=b0[:, sl], in0=dsc[:, :], scalar=muk[:, c:c + 1], in1=xsh[:, :],
            op0=ALU.mult, op1=ALU.add), ["sc_bf_b", "c_l1f_muk", "sc_bf_a"], [kb0])
        k.v(lambda e, sl=sl, c=c: e.scalar_tensor_tensor(
            out=xb[:, sl], in0=dsc[:, :], scalar=mur[:, c:c + 1], in1=xsh[:, :],
            op0=ALU.mult, op1=ALU.add), ["sc_bf_b", "c_l1f_mur", "sc_bf_a"], ["xb"])

    def evac_r(o, tb, ps, pskey):
        k.s(lambda e, o=o, tb=tb, ps=ps: e.activation(
            out=b1[:, o * NT + tb * TB: o * NT + (tb + 1) * TB],
            in_=ps[:, 0:TB], func=AF.Sigmoid), [pskey], [kb1])

    gemm_fm(k, "l1f_r", W["l1f_wr"], 1024, 1024,
            lambda c: (xb, "xb", c * NT), evac_r)

    def evac_k(o, tb, ps, pskey):
        k.v(lambda e, ps=ps: e.tensor_relu(out=relu_s[:, 0:TB], in_=ps[:, 0:TB]),
            [pskey], ["sc_f_a"])
        k.s(lambda e, o=o, tb=tb: e.activation(
            out=xb[:, o * NT + tb * TB: o * NT + (tb + 1) * TB],
            in_=relu_s[:, 0:TB], func=AF.Square), ["sc_f_a"], ["xb"])

    gemm_fm(k, "l1f_k", W["l1f_wk"], 1024, 1024,
            lambda c: (b0, kb0, c * NT), evac_k)

    def evac_v(o, tb, ps, pskey):
        sl = slice(o * NT + tb * TB, o * NT + (tb + 1) * TB)
        k.v(lambda e, ps=ps, sl=sl: e.tensor_mul(out=prod[:, 0:TB], in0=ps[:, 0:TB],
                                                 in1=b1[:, sl]),
            [pskey, kb1], ["sc_f_b"])
        k.v(lambda e, sl=sl: e.tensor_add(out=xf[:, sl], in0=xf[:, sl],
                                          in1=prod[:, 0:TB]),
            ["xf", "sc_f_b"], ["xf"])

    gemm_fm(k, "l1f_v", W["l1f_wv"], 1024, 1024,
            lambda c: (xb, "xb", c * NT), evac_v)

    ln_fm(k, "l1f_ln")


def get_ident(k, W, dtype="f32"):
    if dtype == "f32":
        name, hname = "c_ident_f", "cst_ident_f"
        dt = F32
    else:
        name, hname = "c_ident_b", "cst_ident_b"
        dt = BF16
    if name not in k.tiles:
        t = k.t(name, (128, 128), dt)
        g = k.group()
        k.dma("sync", lambda e, t=t: e.dma_start(out=t[:, :], in_=W[hname]),
              [], [name], g)
    return k.tiles[name], name


def emit_l3_attn(k, W):
    xf, xb = k.tiles["xf"], k.tiles["xb"]
    identf, identfk = get_ident(k, W, "f32")
    qb_, kqb = bb(k, 0)
    ob_, kob = bb(k, 1, init=True)
    ktil = k.t("l3a_k", (128, NT), BF16)
    vtil = k.t("l3a_v", (64, NT), F32)
    vtt = k.t("l3a_vtt", (128, 16 * 65), BF16)
    lrow = k.t("sc_row", (1, TB), BF16)
    rbc = k.t("sc_rbc", (128, TB), BF16)
    ptb = k.t("l3a_pt", (128, TB), BF16)
    sst = k.t("l3a_ss", (128, 1), F32)
    g = k.group()
    src = bass.AP(W["l3a_scale"].tensor, W["l3a_scale"].offset, [[0, 128], [1, 1]])
    k.dma("sync", lambda e: e.dma_start(out=sst[:, :], in_=src), [], ["l3a_ss"], g)

    def evac_q(o, tb, ps, pskey):
        k.s(lambda e, o=o, tb=tb, ps=ps: e.activation(
            out=qb_[:, o * NT + tb * TB: o * NT + (tb + 1) * TB],
            in_=ps[:, 0:TB], func=AF.Copy, scale=sst[:, 0:1]),
            [pskey, "l3a_ss"], [kqb])

    gemm_fm(k, "l3a_q", W["l3a_wq"], 1024, 1024, xb_tile(k), evac_q)

    def evac_k(o, tb, ps, pskey):
        k.s(lambda e, tb=tb, ps=ps: e.activation(
            out=ktil[0:64, tb * TB:(tb + 1) * TB], in_=ps[0:64, 0:TB],
            func=AF.Copy), [pskey], ["l3a_k"])

    gemm_fm(k, "l3a_kk", W["l3a_wk"], 1024, 64, xb_tile(k), evac_k)
    k.v(lambda e: e.tensor_copy(out=ktil[64:128, :], in_=ktil[0:64, :]),
        ["l3a_k"], ["l3a_k"])

    def evac_v(o, tb, ps, pskey):
        k.v(lambda e, tb=tb, ps=ps: e.tensor_copy(
            out=vtil[0:64, tb * TB:(tb + 1) * TB], in_=ps[0:64, 0:TB]),
            [pskey], ["l3a_v"])

    gemm_fm(k, "l3a_vv", W["l3a_wv"], 1024, 64, xb_tile(k), evac_v)

    k.v(lambda e: e.memset(vtt[:, :], 1.0), [], ["l3a_vtt"])
    for kt in range(16):
        ps, pskey = k.bank()
        k.te(lambda e, kt=kt, ps=ps: e.transpose(
            out=ps[:, 0:64], in_=vtil[0:64, kt * 128:(kt + 1) * 128],
            identity=identf[0:64, 0:64]),
            ["l3a_v", identfk], [pskey])
        k.v(lambda e, kt=kt, ps=ps: e.tensor_copy(
            out=vtt[:, kt * 65: kt * 65 + 64], in_=ps[:, 0:64]),
            [pskey], ["l3a_vtt"])

    ones1, ones1k = k.ones_row()
    for h in range(16):
        qsl = (h // 2) * NT
        qp = (h % 2) * 64
        for qb in range(NTB):
            pso, psok, io = k.take_bank()
            pst, pstk, ist = k.take_bank()
            for kt in range(16):
                k.te(lambda e, kt=kt, qb=qb, pst=pst, qp=qp, qsl=qsl: e.matmul(
                    out=pst[:, 0:TB],
                    lhsT=ktil[qp:qp + 64, kt * 128:(kt + 1) * 128],
                    rhs=qb_[qp:qp + 64, qsl + qb * TB: qsl + (qb + 1) * TB],
                    start=True, stop=True), ["l3a_k", kqb], [pstk])
                k.s(lambda e, pst=pst: e.activation(
                    out=ptb[:, 0:TB], in_=pst[:, 0:TB], func=AF.Exp),
                    [pstk], ["l3a_pt"])
                k.te(lambda e, kt=kt, pso=pso, first=(kt == 0), last=(kt == 15):
                     e.matmul(out=pso[0:65, 0:TB],
                              lhsT=vtt[:, kt * 65:(kt + 1) * 65],
                              rhs=ptb[:, 0:TB], start=first, stop=last),
                     ["l3a_vtt", "l3a_pt"], [psok])
            def _recip(e, pso=pso):
                with k.nc.allow_low_precision(reason="bf16 attn normalizer"):
                    return e.reciprocal(out=lrow[0:1, 0:TB], in_=pso[64:65, 0:TB])
            k.v(_recip, [psok], ["sc_row"])
            psr, psrk = k.bank()
            ones1b, ones1bk = k.ones_row("bf16")
            k.te(lambda e, psr=psr, ones1b=ones1b: e.matmul(
                out=psr[:, 0:TB], lhsT=ones1b[0:1, 0:128],
                rhs=lrow[0:1, 0:TB], start=True, stop=True),
                 [ones1bk, "sc_row"], [psrk])
            k.v(lambda e, psr=psr: e.tensor_copy(out=rbc[:, 0:TB], in_=psr[:, 0:TB]),
                [psrk], ["sc_rbc"])
            k.v(lambda e, pso=pso, qb=qb, qp=qp, qsl=qsl: e.tensor_mul(
                out=ob_[qp:qp + 64, qsl + qb * TB: qsl + (qb + 1) * TB],
                in0=pso[0:64, 0:TB], in1=rbc[0:64, 0:TB]),
                [psok, "sc_rbc"], [kob])
            k.free_bank(io)
            k.free_bank(ist)

    def evac_o(o, tb, ps, pskey):
        sl = slice(o * NT + tb * TB, o * NT + (tb + 1) * TB)
        k.v(lambda e, ps=ps, sl=sl: e.tensor_add(
            out=xf[:, sl], in0=xf[:, sl], in1=ps[:, 0:TB]), [pskey, "xf"], ["xf"])

    gemm_fm(k, "l3a_o", W["l3a_wo"], 1024, 1024, lambda c: (ob_, kob, c * NT), evac_o)
    ln_fm(k, "l3a_ln")


def apoff(ap, elems):
    return bass.AP(ap.tensor, ap.offset + elems, [list(d) for d in ap.ap])


def emit_l3_ffn(k, W):
    xf, xb = k.tiles["xf"], k.tiles["xb"]
    identf, identfk = get_ident(k, W, "f32")
    gbuf, kgbs = bb2(k, 0, 1)
    kgb = kgbs
    rt = k.t("l3f_rta", (8, NT), F32)
    rtm = k.t("l3f_rtm", (128, 16 * 8), F32)
    et = k.t("l3f_et", (128, 8), F32)
    pt = k.t("l3f_p", (128, 8), F32)
    sc1 = k.t("l3f_s1", (128, 1), F32)
    sc2 = k.t("l3f_s2", (128, 1), F32)
    sc3 = k.t("l3f_s3", (128, 1), F32)
    m12 = k.t("l3f_m12", (128, 2), F32)
    wtm = k.t("l3f_wtm", (128, 16 * 8), F32)
    wrows = rt
    wbc = k.t("l3f_wbc", (128, NT), BF16)
    tmp = k.t("sc_f_a", (128, TB), F32)

    def evac_r(o, tb, ps, pskey):
        k.v(lambda e, tb=tb, ps=ps: e.tensor_copy(
            out=rt[0:8, tb * TB:(tb + 1) * TB], in_=ps[0:8, 0:TB]), [pskey], ["l3f_rta"])

    gemm_fm(k, "l3f_r", W["l3f_wr"], 1024, 8, xb_tile(k), evac_r)

    for t16 in range(16):
        ps, pskey = k.bank()
        k.te(lambda e, t16=t16, ps=ps: e.transpose(
            out=ps[:, 0:8], in_=rt[0:8, t16 * 128:(t16 + 1) * 128],
            identity=identf[0:8, 0:8]), ["l3f_rta", identfk], [pskey])
        k.v(lambda e, t16=t16, ps=ps: e.tensor_copy(
            out=rtm[:, t16 * 8:(t16 + 1) * 8], in_=ps[:, 0:8]), [pskey], ["l3f_rtm"])
    for t16 in range(16):
        sl = slice(t16 * 8, (t16 + 1) * 8)
        k.v(lambda e, sl=sl: e.reduce_max(out=sc1[:, 0:1], in_=rtm[:, sl],
                                          axis=mybir.AxisListType.X, negate=True),
            ["l3f_rtm"], ["l3f_s1"])
        k.s(lambda e, sl=sl: e.activation(out=et[:, :], in_=rtm[:, sl], func=AF.Exp,
                                          bias=sc1[:, 0:1]), ["l3f_rtm", "l3f_s1"], ["l3f_et"])
        k.v(lambda e: e.reduce_sum(out=sc2[:, 0:1], in_=et[:, :],
                                   axis=mybir.AxisListType.X), ["l3f_et"], ["l3f_s2"])
        k.v(lambda e: e.reciprocal(out=sc2[:, 0:1], in_=sc2[:, 0:1]), ["l3f_s2"], ["l3f_s2"])
        k.v(lambda e: e.tensor_scalar_mul(out=pt[:, :], in0=et[:, :], scalar1=sc2[:, 0:1]),
            ["l3f_et", "l3f_s2"], ["l3f_p"])
        k.v(lambda e: e.reduce_max(out=m12[:, 0:1], in_=pt[:, :],
                                   axis=mybir.AxisListType.X), ["l3f_p"], ["l3f_m12"])
        k.v(lambda e: e.tensor_single_scalar(out=et[:, :], in_=pt[:, :],
                                             scalar=m12[:, 0:1], op=ALU.is_equal),
            ["l3f_p", "l3f_m12"], ["l3f_et"])
        k.v(lambda e: e.scalar_tensor_tensor(out=sc3[:, 0:1] if False else tmp[:, 0:8],
                                             in0=et[:, :], scalar=-1e30, in1=pt[:, :],
                                             op0=ALU.mult, op1=ALU.add),
            ["l3f_et", "l3f_p"], ["sc_f_a"])
        k.v(lambda e: e.reduce_max(out=m12[:, 1:2], in_=tmp[:, 0:8],
                                   axis=mybir.AxisListType.X), ["sc_f_a"], ["l3f_m12"])
        k.v(lambda e: e.tensor_single_scalar(out=tmp[:, 0:8], in_=pt[:, :],
                                             scalar=m12[:, 1:2], op=ALU.is_equal),
            ["l3f_p", "l3f_m12"], ["sc_f_a"])
        k.v(lambda e: e.tensor_add(out=et[:, :], in0=et[:, :], in1=tmp[:, 0:8]),
            ["l3f_et", "sc_f_a"], ["l3f_et"])
        k.v(lambda e: e.reduce_sum(out=sc3[:, 0:1], in_=m12[:, 0:2],
                                   axis=mybir.AxisListType.X), ["l3f_m12"], ["l3f_s3"])
        k.v(lambda e: e.tensor_scalar_add(out=sc3[:, 0:1], in0=sc3[:, 0:1],
                                          scalar1=1e-20), ["l3f_s3"], ["l3f_s3"])
        k.v(lambda e: e.reciprocal(out=sc3[:, 0:1], in_=sc3[:, 0:1]), ["l3f_s3"], ["l3f_s3"])
        k.v(lambda e: e.tensor_mul(out=et[:, :], in0=et[:, :], in1=pt[:, :]),
            ["l3f_et", "l3f_p"], ["l3f_et"])
        k.v(lambda e, sl=sl: e.tensor_scalar_mul(out=wtm[:, sl], in0=et[:, :],
                                                 scalar1=sc3[:, 0:1]),
            ["l3f_et", "l3f_s3"], ["l3f_wtm"])
    for t16 in range(16):
        ps, pskey = k.bank()
        k.te(lambda e, t16=t16, ps=ps: e.transpose(
            out=ps[0:8, 0:128], in_=wtm[:, t16 * 8:(t16 + 1) * 8],
            identity=identf[:, :]), ["l3f_wtm", identfk], [pskey])
        k.v(lambda e, t16=t16, ps=ps: e.tensor_copy(
            out=wrows[0:8, t16 * 128:(t16 + 1) * 128], in_=ps[0:8, 0:128]),
            [pskey], ["l3f_rta"])

    def evac_sg(o, tb, ps, pskey):
        k.s(lambda e, o=o, tb=tb, ps=ps: e.activation(
            out=gbuf[:, o * NT + tb * TB: o * NT + (tb + 1) * TB],
            in_=ps[:, 0:TB], func=AF.Silu), [pskey], list(kgb))

    gemm_fm(k, "l3f_sg", W["l3f_sg"], 1024, 2048, xb_tile(k), evac_sg)

    def evac_su(o, tb, ps, pskey):
        sl = slice(o * NT + tb * TB, o * NT + (tb + 1) * TB)
        k.v(lambda e, ps=ps, sl=sl: e.tensor_mul(
            out=gbuf[:, sl], in0=ps[:, 0:TB], in1=gbuf[:, sl]), [pskey] + list(kgb), list(kgb))

    gemm_fm(k, "l3f_su", W["l3f_su"], 1024, 2048, xb_tile(k), evac_su)

    def evac_sd(o, tb, ps, pskey):
        sl = slice(o * NT + tb * TB, o * NT + (tb + 1) * TB)
        k.v(lambda e, ps=ps, sl=sl: e.tensor_add(
            out=xf[:, sl], in0=xf[:, sl], in1=ps[:, 0:TB]), [pskey, "xf"], ["xf"])

    gemm_fm(k, "l3f_sd", W["l3f_sd"], 2048, 1024,
            lambda c: (gbuf, kgbs[0] if c < 8 else kgbs[1], c * NT), evac_sd)

    sel = k.t("c_sel8", (8, 1024), F32)
    gsel = k.group()
    k.dma("sync", lambda e: e.dma_start(out=sel[:, :], in_=W["cst_sel8"]),
          [], ["c_sel8"], gsel)
    for ex in range(8):
        for tb in range(NTB):
            ps, pskey = k.bank()
            k.te(lambda e, ex=ex, tb=tb, ps=ps: e.matmul(
                out=ps[:, 0:TB], lhsT=sel[0:8, ex * 128:(ex + 1) * 128],
                rhs=wrows[0:8, tb * TB:(tb + 1) * TB], start=True, stop=True),
                ["c_sel8", "l3f_rta"], [pskey])
            k.v(lambda e, tb=tb, ps=ps: e.tensor_copy(
                out=wbc[:, tb * TB:(tb + 1) * TB], in_=ps[:, 0:TB]), [pskey], ["l3f_wbc"])
        gemm_fm(k, f"l3f_g{ex}", apoff(W["l3f_wg"], ex * 1024 * 2048), 1024, 2048,
                xb_tile(k), evac_sg)
        gemm_fm(k, f"l3f_u{ex}", apoff(W["l3f_wu"], ex * 1024 * 2048), 1024, 2048,
                xb_tile(k), evac_su)

        def evac_d(o, tb, ps, pskey):
            sl = slice(o * NT + tb * TB, o * NT + (tb + 1) * TB)
            k.v(lambda e, ps=ps, tb=tb: e.tensor_mul(
                out=tmp[:, 0:TB], in0=ps[:, 0:TB],
                in1=wbc[:, tb * TB:(tb + 1) * TB]), [pskey, "l3f_wbc"], ["sc_f_a"])
            k.v(lambda e, sl=sl: e.tensor_add(
                out=xf[:, sl], in0=xf[:, sl], in1=tmp[:, 0:TB]), ["xf", "sc_f_a"], ["xf"])

        gemm_fm(k, f"l3f_d{ex}", apoff(W["l3f_wd"], ex * 2048 * 1024), 2048, 1024,
                lambda c: (gbuf, kgbs[0] if c < 8 else kgbs[1], c * NT), evac_d)
    ln_fm(k, "l3f_ln")


def emit_l0_ffn(k, W):
    xf, xb = k.tiles["xf"], k.tiles["xb"]
    hb, khbs = bb2(k, 0, 1)
    ybuf = k.t("l0f_y", (128, 4 * NT), BF16)
    rm1 = k.t("sb_l0f_rm1", (128, 4096), BF16)
    rm2 = [k.t(f"l0f_rm2_{i}", (128, 2048), BF16) for i in range(2)]
    stg = k.t("gw_stg0", (128, 512), F32)
    for i in range(8):
        g = k.group()
        k.dma("sync", lambda e, i=i: e.dma_start(
            out=stg[:, :], in_=W["l0f_rm1"][0:128, i * 512:(i + 1) * 512]),
            [], ["gw_stg0"], g)
        k.s(lambda e, i=i: e.activation(out=rm1[:, i * 512:(i + 1) * 512],
                                        in_=stg[:, :], func=AF.Copy),
            ["gw_stg0"], ["sb_l0f_rm1"])
    for s in range(8):
        h, rhalf = s // 2, s % 2
        for rg4 in range(4):
            rg = rhalf * 4 + rg4
            t1i = h * 8 + rg
            for tb in range(NTB):
                ps, pskey = k.bank()
                k.te(lambda e, ps=ps, t1i=t1i, rg=rg, tb=tb: e.matmul(
                    out=ps[:, 0:TB], lhsT=rm1[:, t1i * 128:(t1i + 1) * 128],
                    rhs=xb[:, rg * NT + tb * TB: rg * NT + (tb + 1) * TB],
                    start=True, stop=True), ["sb_l0f_rm1", "xb"], [pskey])
                k.s(lambda e, ps=ps, rg4=rg4, tb=tb: e.activation(
                    out=ybuf[:, rg4 * NT + tb * TB: rg4 * NT + (tb + 1) * TB],
                    in_=ps[:, 0:TB], func=AF.Copy), [pskey], ["l0f_y"])
        par = s % 2
        r2, kr2 = rm2[par], f"l0f_rm2_{par}"
        for i in range(4):
            g = k.group()
            rm2ap = W["l0f_rm2"]
            src_ap = bass.AP(rm2ap.tensor,
                             rm2ap.offset + (s * 16 + i * 4) * 128 * 128,
                             [[128, 128], [128 * 128, 4], [1, 128]])
            dst_ap = ap3(stg, [[512, 128], [128, 4], [1, 128]])
            k.dma("sync", lambda e, src_ap=src_ap, dst_ap=dst_ap: e.dma_start(
                out=dst_ap, in_=src_ap), [], ["gw_stg0"], g)
            k.s(lambda e, i=i, r2=r2: e.activation(
                out=r2[:, i * 512:(i + 1) * 512], in_=stg[:, :], func=AF.Copy),
                ["gw_stg0"], [kr2])
        for rg4 in range(4):
            for sub in range(4):
                sc = rg4 * 4 + sub
                for tb in range(NTB):
                    ps, pskey = k.bank()
                    k.te(lambda e, ps=ps, sc=sc, rg4=rg4, tb=tb, r2=r2: e.matmul(
                        out=ps[:, 0:TB], lhsT=r2[:, sc * 128:(sc + 1) * 128],
                        rhs=ybuf[:, rg4 * NT + tb * TB: rg4 * NT + (tb + 1) * TB],
                        start=True, stop=True), [kr2, "l0f_y"], [pskey])
                    k.s(lambda e, ps=ps, sc=sc, tb=tb: e.activation(
                        out=hb[:, sc * NT + tb * TB: sc * NT + (tb + 1) * TB],
                        in_=ps[:, 0:TB], func=AF.Gelu), [pskey], [khbs[sc // 8]])

        def evac_c(o, tb, ps, pskey):
            sl = slice(o * NT + tb * TB, o * NT + (tb + 1) * TB)
            k.v(lambda e, ps=ps, sl=sl: e.tensor_add(
                out=xf[:, sl], in0=xf[:, sl], in1=ps[:, 0:TB]), [pskey, "xf"], ["xf"])

        gemm_fm(k, f"l0f_c{s}", W["l0f_cproj"], 2048, 1024,
                lambda c: (hb, khbs[c // 8], c * NT), evac_c, cin_off=s * 2048)
    ln_fm(k, "l0f_ln")


def emit_l0_attn(k, W):
    xf, xb = k.tiles["xf"], k.tiles["xb"]
    identf, identfk = get_ident(k, W, "bf16")
    qasm, kqa = bb(k, 0)
    kbuf, kkb = bb(k, 1)
    vdram0 = k.nc.dram_tensor("l0a_v", [128, 8 * NT], BF16).ap()
    vstg = k.t("l0a_vstg", (128, TB), BF16)
    vld = [k.t("l0a_vld0", (64, 128), BF16), k.t("l0a_vld1", (64, 128), BF16)]
    SCALE = float((2 * 1024) ** (-0.5))
    vtt = k.t("l0a_vtt", (128, 16 * 65), BF16)
    kstage = k.t("l0a_ks", (128, 8 * 128), BF16)
    ptb = k.t("l0a_pt", (128, TB), BF16)
    lrow = ptb
    rbc = k.t("sc_rbc", (128, TB), BF16)
    mask = k.t("sb_l0a_mask", (128, 256), BF16)
    gm = k.group()
    stg = k.t("gw_stg0", (128, 512), F32)
    k.dma("sync", lambda e: e.dma_start(
        out=stg[:, 0:256], in_=W["l0a_mask"][0:128, 0:256]),
        [], ["gw_stg0"], k.group())
    k.s(lambda e: e.activation(out=mask[:, 0:256], in_=stg[:, 0:256], func=AF.Copy),
        ["gw_stg0"], ["sb_l0a_mask"])
    for nm in ("bq", "bk", "bv", "bo"):
        load_const(k, "l0a_" + nm, W["l0a_" + nm], (128, 8), F32)
    bq = k.tiles["c_l0a_bq"]
    k.v(lambda e: e.tensor_scalar_mul(out=bq[:, :], in0=bq[:, :], scalar1=SCALE),
        ["c_l0a_bq"], ["c_l0a_bq"])

    def evac_v(o, tb, ps, pskey):
        k.v(lambda e, o=o, tb=tb, ps=ps: e.tensor_scalar(
            out=vstg[:, 0:TB],
            in0=ps[:, 0:TB], scalar1=k.tiles["c_l0a_bv"][:, o:o + 1],
            scalar2=None, op0=ALU.add),
            [pskey, "c_l0a_bv"], ["l0a_vstg"])
        gvs = k.group("gpsimd")
        k.dma("gpsimd", lambda e, o=o, tb=tb: e.dma_start(
            out=vdram0[0:128, o * NT + tb * TB: o * NT + (tb + 1) * TB],
            in_=vstg[:, 0:TB]), ["l0a_vstg"], ["l0a_vdram"], gvs)

    gemm_fm(k, "l0a_v", W["l0a_wv"], 1024, 1024, xb_tile(k), evac_v)

    def evac_k(o, tb, ps, pskey):
        k.v(lambda e, o=o, tb=tb, ps=ps: e.tensor_scalar(
            out=kbuf[:, o * NT + tb * TB: o * NT + (tb + 1) * TB],
            in0=ps[:, 0:TB], scalar1=k.tiles["c_l0a_bk"][:, o:o + 1],
            scalar2=None, op0=ALU.add),
            [pskey, "c_l0a_bk"], [kkb])

    gemm_fm(k, "l0a_k", W["l0a_wk"], 1024, 1024, xb_tile(k), evac_k)

    def evac_q(o, tb, ps, pskey):
        for h4 in range(4):
            h = tb * 4 + h4
            for jh in range(2):
                j = 2 * o + jh
                k.v(lambda e, ps=ps, h=h, jh=jh, j=j, h4=h4, o=o: e.tensor_scalar(
                    out=qasm[(h % 2) * 64:(h % 2) * 64 + 64,
                             (h // 2) * NT + j * 128:(h // 2) * NT + (j + 1) * 128],
                    in0=ps[jh * 64:jh * 64 + 64, h4 * 128:(h4 + 1) * 128],
                    scalar1=SCALE,
                    scalar2=k.tiles["c_l0a_bq"][jh * 64:jh * 64 + 64, o:o + 1],
                    op0=ALU.mult, op1=ALU.add),
                    [pskey, "c_l0a_bq"], [kqa])

    gemm_fm(k, "l0a_q", W["l0a_wq"], 1024, 1024, xb_tile(k), evac_q)

    ones1, ones1k = k.ones_row()
    for h in range(16):
        qp = (h % 2) * 64
        hsl = (h // 2) * NT
        k.v(lambda e: e.memset(vtt[:, :], 1.0), [], ["l0a_vtt"])
        for j in range(16):
            jp = (j % 2) * 64
            vl, vlk = vld[j % 2], f"l0a_vld{j % 2}"
            gvl = k.group("gpsimd")
            k.dma("gpsimd", lambda e, j=j, jp=jp, h=h, vl=vl: e.dma_start(
                out=vl[0:64, 0:128],
                in_=vdram0[jp:jp + 64,
                           (j // 2) * NT + h * 128:(j // 2) * NT + (h + 1) * 128]),
                ["l0a_vdram"], [vlk], gvl)
            ps, pskey = k.bankB()
            k.te(lambda e, ps=ps, vl=vl: e.transpose(
                out=ps[:, 0:64], in_=vl[0:64, 0:128],
                identity=identf[0:64, 0:64]),
                [vlk, identfk], [pskey])
            k.v(lambda e, ps=ps, j=j: e.tensor_copy(
                out=vtt[:, j * 65: j * 65 + 64], in_=ps[:, 0:64]), [pskey], ["l0a_vtt"])
        nst = 0
        koff = {}
        for j in range(16):
            if (j % 2) != (h % 2):
                k.v(lambda e, j=j, qp=qp, h=h, nst=nst: e.tensor_copy(
                    out=kstage[qp:qp + 64, nst * 128:(nst + 1) * 128],
                    in_=kbuf[(j % 2) * 64:(j % 2) * 64 + 64,
                             (j // 2) * NT + h * 128:(j // 2) * NT + (h + 1) * 128]),
                    [kkb], ["l0a_ks"])
                koff[j] = nst
                nst += 1
        for qb in range(NTB):
            pso, psok, io = k.take_bank()
            pst, pstk, ist = k.take_bank()
            for j in range(16):
                if (j % 2) == (h % 2):
                    lhsT_fn = (lambda j=j, h=h: kbuf[(j % 2) * 64:(j % 2) * 64 + 64,
                               (j // 2) * NT + h * 128:(j // 2) * NT + (h + 1) * 128])
                    lkey = kkb
                else:
                    lhsT_fn = (lambda j=j, qp=qp, ko=koff[j]: kstage[qp:qp + 64,
                               ko * 128:(ko + 1) * 128])
                    lkey = "l0a_ks"
                k.te(lambda e, pst=pst, lhsT_fn=lhsT_fn, qp=qp, hsl=hsl, qb=qb: e.matmul(
                    out=pst[:, 0:TB], lhsT=lhsT_fn(),
                    rhs=qasm[qp:qp + 64, hsl + qb * TB: hsl + (qb + 1) * TB],
                    start=True, stop=True), [lkey, kqa], [pstk])
                k.s(lambda e, pst=pst: e.activation(
                    out=ptb[:, 0:TB], in_=pst[:, 0:TB], func=AF.Exp),
                    [pstk], ["l0a_pt"])
                for j4 in range(4):
                    jq = qb * 4 + j4
                    moff = 0 if jq < j else 128
                    k.v(lambda e, j4=j4, moff=moff: e.tensor_mul(
                        out=ptb[:, j4 * 128:(j4 + 1) * 128],
                        in0=ptb[:, j4 * 128:(j4 + 1) * 128],
                        in1=mask[:, moff:moff + 128]),
                        ["l0a_pt", "sb_l0a_mask"], ["l0a_pt"])
                k.te(lambda e, j=j, pso=pso, first=(j == 0), last=(j == 15): e.matmul(
                    out=pso[0:65, 0:TB], lhsT=vtt[:, j * 65:(j + 1) * 65],
                    rhs=ptb[:, 0:TB], start=first, stop=last),
                    ["l0a_vtt", "l0a_pt"], [psok])
            def _recip(e, pso=pso):
                with k.nc.allow_low_precision(reason="bf16 attn normalizer"):
                    return e.reciprocal(out=lrow[0:1, 0:TB], in_=pso[64:65, 0:TB])
            k.v(_recip, [psok], ["l0a_pt"])
            psr, psrk = k.bank()
            ones1b, ones1bk = k.ones_row("bf16")
            k.te(lambda e, psr=psr, ones1b=ones1b: e.matmul(
                out=psr[:, 0:TB], lhsT=ones1b[0:1, 0:128],
                rhs=lrow[0:1, 0:TB], start=True, stop=True),
                 [ones1bk, "l0a_pt"], [psrk])
            k.v(lambda e, psr=psr: e.tensor_copy(out=rbc[:, 0:TB], in_=psr[:, 0:TB]),
                [psrk], ["sc_rbc"])
            for j4 in range(4):
                j = qb * 4 + j4
                k.v(lambda e, pso=pso, j=j, j4=j4, h=h: e.tensor_mul(
                    out=xb[(j % 2) * 64:(j % 2) * 64 + 64,
                           (j // 2) * NT + h * 128:(j // 2) * NT + (h + 1) * 128],
                    in0=pso[0:64, j4 * 128:(j4 + 1) * 128],
                    in1=rbc[0:64, j4 * 128:(j4 + 1) * 128]),
                    [psok, "sc_rbc"], ["xb"])
            k.free_bank(io)
            k.free_bank(ist)

    def evac_o(o, tb, ps, pskey):
        sl = slice(o * NT + tb * TB, o * NT + (tb + 1) * TB)
        k.v(lambda e, ps=ps, sl=sl, o=o: e.scalar_tensor_tensor(
            out=xf[:, sl], in0=ps[:, 0:TB], scalar=k.tiles["c_l0a_bo"][:, o:o + 1],
            in1=xf[:, sl], op0=ALU.add, op1=ALU.add),
            [pskey, "c_l0a_bo", "xf"], ["xf"])

    gemm_fm(k, "l0a_o", W["l0a_wo"], 1024, 1024, lambda c: (xb, "xb", c * NT), evac_o)
    ln_fm(k, "l0a_ln")


def shift_xb(k, tag):
    xb = k.tiles["xb"]
    sc = k.t("sc_shift", (128, NT), BF16)
    for c in range(4, 8):
        k.v(lambda e, c=c: e.memset(sc[:, 0:1], 0.0), [], ["sc_shift"])
        k.v(lambda e, c=c: e.tensor_copy(out=sc[:, 1:NT],
                                         in_=xb[:, c * NT: c * NT + NT - 1]),
            ["xb"], ["sc_shift"])
        k.v(lambda e, c=c: e.tensor_copy(out=xb[:, c * NT:(c + 1) * NT], in_=sc[:, :]),
            ["sc_shift", "xb"], ["xb"])


def emit_l2_ffn(k, W):
    import os
    STOP = os.environ.get("L2F_STOP", "")
    xf, xb = k.tiles["xf"], k.tiles["xb"]
    xsave = k.nc.dram_tensor("l2f_xsave", [128, 8 * NT], BF16).ap()
    gsv = k.group("gpsimd")
    k.dma("gpsimd", lambda e: e.dma_start(out=xsave, in_=xb[:, :]),
          ["xb"], ["l2f_xsave"], gsv)
    identb, identbk = get_ident(k, W, "bf16")
    r0, kr0 = bb(k, 0)
    r1, kr1 = bb(k, 1)
    vtt = k.t("l2f_vtt", (128, 16 * 65), BF16)
    ptb = k.t("l0a_pt", (128, TB), BF16)
    lrow = ptb
    rbc = k.t("sc_rbc", (128, TB), BF16)
    band = k.t("sb_band", (128, 1408), BF16)
    tbl = [k.t(f"l2f_tbl{i}", (128, NT), BF16) for i in range(4)]
    nrow = k.t("l2f_nrow", (2, TB), F32)
    qperm = k.t("sc_shift", (128, NT), BF16)
    stg = k.t("gw_stg0", (128, 512), F32)
    for nm in ("l2f_qs", "l2f_qsp", "l2f_ks", "l2f_ksp"):
        load_const(k, nm, W[nm], (128, 2), F32)
    load_const(k, "l2f_lng", W["l2f_lng"], (128, 16), F32)
    load_const(k, "l2f_lnb", W["l2f_lnb"], (128, 16), F32)
    load_const(k, "l2f_b1", W["l2f_b1"], (128, 88), F32)
    load_const(k, "l2f_b2", W["l2f_b2"], (128, 16), F32)
    sel2h = k.t("c_sel2h", (2, 128), F32)
    g = k.group()
    k.dma("sync", lambda e: e.dma_start(out=sel2h[:, :], in_=W["cst_sel2h"]),
          [], ["c_sel2h"], g)
    for i in range(3):
        n = min(512, 1408 - i * 512)
        k.dma("sync", lambda e, i=i, n=n: e.dma_start(
            out=stg[:, 0:n], in_=W["l2f_band"][0:128, i * 512:i * 512 + n]),
            [], ["gw_stg0"], k.group())
        k.s(lambda e, i=i, n=n: e.activation(out=band[:, i * 512:i * 512 + n],
                                             in_=stg[:, 0:n], func=AF.Copy),
            ["gw_stg0"], ["sb_band"])

    ones1b, ones1bk = k.ones_row("bf16")
    LAYERS = [int(c) for c in os.environ.get("L2F_LAYERS", "01")]
    for l in LAYERS:
        shift_xb(k, f"l2f{l}")

        def evac_qkv(o, tb, ps, pskey):
            dst, dk = (r0, kr0) if o < 8 else (r1, kr1)
            oo = o if o < 8 else o - 8
            k.s(lambda e, ps=ps, dst=dst, oo=oo, tb=tb: e.activation(
                out=dst[:, oo * NT + tb * TB: oo * NT + (tb + 1) * TB],
                in_=ps[:, 0:TB], func=AF.Copy), [pskey], [dk])

        gemm_fm(k, f"l2f_qkv{l}", apoff(W["l2f_wqkv"], l * 1024 * 1536),
                1024, 1536, xb_tile(k), evac_qkv)

        specs = [("l2f_cq1", "c_l2f_qs", 8.0, 0), ("l2f_cq2", "c_l2f_qsp", 8.0, 1),
                 ("l2f_ck1", "c_l2f_ks", 1.0, 2), ("l2f_ck2", "c_l2f_ksp", 1.0, 3)]
        for src_name, sc_name, mulc, ti in specs:
            for i in range(4):
                k.dma("sync", lambda e, src_name=src_name, i=i: e.dma_start(
                    out=stg[:, :], in_=W[src_name][0:128, i * 512:(i + 1) * 512]),
                    [], ["gw_stg0"], k.group())
                k.v(lambda e, i=i, ti=ti, sc_name=sc_name, mulc=mulc, l=l: e.tensor_scalar(
                    out=tbl[ti][:, i * 512:(i + 1) * 512], in0=stg[:, :],
                    scalar1=k.tiles[sc_name][:, l:l + 1], scalar2=float(mulc),
                    op0=ALU.mult, op1=ALU.mult),
                    ["gw_stg0", sc_name], [f"l2f_tbl{ti}"])
        if "c_ones2h" not in k.tiles:
            o2 = k.t("c_ones2h", (128, 2), BF16)
            k.v(lambda e, o2=o2: e.memset(o2[:, :], 0.0), [], ["c_ones2h"])
            k.v(lambda e, o2=o2: e.memset(o2[0:64, 0:1], 1.0), [], ["c_ones2h"])
            k.v(lambda e, o2=o2: e.memset(o2[64:128, 1:2], 1.0), [], ["c_ones2h"])
        for qk in range(2):
            for c4 in range(4):
                src_c = qk * 4 + c4
                sqv = k.t("ln_sqb", (128, TB), BF16)
                for tb in range(NTB):
                    psn, kpn = k.bank()
                    k.s(lambda e, src_c=src_c, tb=tb: e.activation(
                        out=sqv[:, 0:TB],
                        in_=r0[:, src_c * NT + tb * TB: src_c * NT + tb * TB + TB],
                        func=AF.Square), [kr0], ["ln_sqb"])
                    k.te(lambda e, psn=psn: e.matmul(
                        out=psn[0:2, 0:TB],
                        lhsT=k.tiles["c_ones2h"][:, 0:2],
                        rhs=sqv[:, 0:TB], start=True, stop=True),
                        ["c_ones2h", "ln_sqb"], [kpn])
                    k.v(lambda e, psn=psn: e.tensor_copy(out=nrow[0:2, 0:TB],
                                                         in_=psn[0:2, 0:TB]),
                        [kpn], ["l2f_nrow"])
                    epst, epsk = k.constf(1e-24)
                    k.s(lambda e, epst=epst: e.activation(
                        out=nrow[0:2, 0:TB], in_=nrow[0:2, 0:TB], func=AF.Ln,
                        bias=epst[0:2, 0:1]), ["l2f_nrow", epsk], ["l2f_nrow"])
                    k.s(lambda e: e.activation(out=nrow[0:2, 0:TB], in_=nrow[0:2, 0:TB],
                                               func=AF.Exp, scale=-0.5),
                        ["l2f_nrow"], ["l2f_nrow"])
                    psb_, kpb = k.bank()
                    k.te(lambda e, psb_=psb_: e.matmul(
                        out=psb_[:, 0:TB], lhsT=sel2h[0:2, 0:128],
                        rhs=nrow[0:2, 0:TB], start=True, stop=True),
                        ["c_sel2h", "l2f_nrow"], [kpb])
                    k.v(lambda e, psb_=psb_, src_c=src_c, tb=tb: e.tensor_mul(
                        out=xb[:, src_c * NT + tb * TB: src_c * NT + tb * TB + TB],
                        in0=r0[:, src_c * NT + tb * TB: src_c * NT + tb * TB + TB],
                        in1=psb_[:, 0:TB]), [kpb, kr0], ["xb"])
                gp = k.group("gpsimd")
                for half in range(2):
                    b0 = half * 64
                    k.dma("gpsimd", lambda e, src_c=src_c, b0=b0: e.dma_start(
                        out=qperm[b0:b0 + 32, :],
                        in_=xb[b0 + 32:b0 + 64, src_c * NT:(src_c + 1) * NT]),
                        ["xb"], ["sc_shift"], gp)
                    k.dma("gpsimd", lambda e, src_c=src_c, b0=b0: e.dma_start(
                        out=qperm[b0 + 32:b0 + 64, :],
                        in_=xb[b0:b0 + 32, src_c * NT:(src_c + 1) * NT]),
                        ["xb"], ["sc_shift"], gp)
                t1i = qk * 2
                k.v(lambda e, src_c=src_c, t1i=t1i: e.tensor_mul(
                    out=xb[:, src_c * NT:(src_c + 1) * NT],
                    in0=xb[:, src_c * NT:(src_c + 1) * NT], in1=tbl[t1i][:, :]),
                    ["xb", f"l2f_tbl{t1i}"], ["xb"])
                k.v(lambda e, t1i=t1i: e.tensor_mul(
                    out=qperm[:, :], in0=qperm[:, :], in1=tbl[t1i + 1][:, :]),
                    ["sc_shift", f"l2f_tbl{t1i + 1}"], ["sc_shift"])
                k.v(lambda e, src_c=src_c: e.tensor_add(
                    out=xb[:, src_c * NT:(src_c + 1) * NT],
                    in0=xb[:, src_c * NT:(src_c + 1) * NT], in1=qperm[:, :]),
                    ["xb", "sc_shift"], ["xb"])
        if STOP == "qk":
            return
        for h in range(8):
            qp = (h % 2) * 64
            qc = h // 2
            kc = 4 + h // 2
            k.v(lambda e: e.memset(vtt[:, :], 1.0), [], ["l2f_vtt"])
            for m in range(16):
                ps, pskey = k.bankB()
                k.te(lambda e, ps=ps, m=m, qp=qp, qc=qc: e.transpose(
                    out=ps[:, 0:64],
                    in_=r1[qp:qp + 64, qc * NT + m * 128: qc * NT + (m + 1) * 128],
                    identity=identb[qp:qp + 64, qp:qp + 64]),
                    [kr1, identbk], [pskey])
                k.v(lambda e, ps=ps, m=m: e.tensor_copy(
                    out=vtt[:, m * 65: m * 65 + 64], in_=ps[:, 0:64]),
                    [pskey], ["l2f_vtt"])
            for w in range(4):
                m0 = max(0, 4 * w - 4)
                m1 = 4 * w + 4
                pso, psok, io = k.take_bank()
                pst, pstk, ist = k.take_bank()
                nmm = m1 - m0
                for mi, m in enumerate(range(m0, m1)):
                    k.te(lambda e, pst=pst, m=m, qp=qp, kc=kc, qc=qc, w=w: e.matmul(
                        out=pst[:, 0:TB],
                        lhsT=xb[qp:qp + 64, kc * NT + m * 128: kc * NT + (m + 1) * 128],
                        rhs=xb[qp:qp + 64, qc * NT + w * TB: qc * NT + (w + 1) * TB],
                        start=True, stop=True), ["xb"], [pstk])
                    k.s(lambda e, pst=pst: e.activation(
                        out=ptb[:, 0:TB], in_=pst[:, 0:TB], func=AF.Exp),
                        [pstk], ["l0a_pt"])
                    D0 = 512 * w - 128 * m
                    k.v(lambda e, D0=D0: e.tensor_mul(
                        out=ptb[:, 0:TB], in0=ptb[:, 0:TB],
                        in1=band[:, D0 + 384: D0 + 384 + TB]),
                        ["l0a_pt", "sb_band"], ["l0a_pt"])
                    k.te(lambda e, m=m, pso=pso, first=(mi == 0), last=(mi == nmm - 1):
                         e.matmul(out=pso[0:65, 0:TB],
                                  lhsT=vtt[:, m * 65:(m + 1) * 65],
                                  rhs=ptb[:, 0:TB], start=first, stop=last),
                         ["l2f_vtt", "l0a_pt"], [psok])
                def _recip(e, pso=pso):
                    with k.nc.allow_low_precision(reason="bf16 attn normalizer"):
                        return e.reciprocal(out=lrow[0:1, 0:TB], in_=pso[64:65, 0:TB])
                k.v(_recip, [psok], ["l0a_pt"])
                psr, psrk = k.bank()
                k.te(lambda e, psr=psr: e.matmul(
                    out=psr[:, 0:TB], lhsT=ones1b[0:1, 0:128],
                    rhs=lrow[0:1, 0:TB], start=True, stop=True),
                    [ones1bk, "l0a_pt"], [psrk])
                k.v(lambda e, psr=psr: e.tensor_copy(out=rbc[:, 0:TB], in_=psr[:, 0:TB]),
                    [psrk], ["sc_rbc"])
                k.v(lambda e, pso=pso, h=h, w=w, qp=qp: e.tensor_mul(
                    out=r1[qp:qp + 64, (4 + h // 2) * NT + w * TB:(4 + h // 2) * NT + (w + 1) * TB],
                    in0=pso[0:64, 0:TB], in1=rbc[0:64, 0:TB]),
                    [psok, "sc_rbc"], [kr1])
                k.free_bank(io)
                k.free_bank(ist)

        if STOP == "attn":
            return

        def evac_wo(o, tb, ps, pskey):
            sl = slice(o * NT + tb * TB, o * NT + (tb + 1) * TB)
            k.v(lambda e, ps=ps, sl=sl: e.tensor_add(
                out=xf[:, sl], in0=xf[:, sl], in1=ps[:, 0:TB]), [pskey, "xf"], ["xf"])

        gemm_fm(k, f"l2f_wo{l}", apoff(W["l2f_wo"], l * 512 * 1024), 512, 1024,
                lambda c: (r1, kr1, (4 + c) * NT), evac_wo)

        for c in range(8):
            k.s(lambda e, c=c: e.activation(out=xb[:, c * NT:(c + 1) * NT],
                                            in_=xf[:, c * NT:(c + 1) * NT],
                                            func=AF.Copy), ["xf"], ["xb"])
        shift_xb(k, f"l2f{l}b")
        ln_from_xb_to(k, f"l2f_iln{l}", r0, kr0,
                      gamma=(k.tiles["c_l2f_lng"], "c_l2f_lng", l * 8),
                      beta=(k.tiles["c_l2f_lnb"], "c_l2f_lnb", l * 8))

        astash = k.t("l2f_astash", (128, 2 * TB), BF16)
        for th in range(2):
            def hb_place(c):
                if c < 16:
                    return r1, kr1, c * 1024
                return xb, "xb", (c - 16) * 1024

            def evac_w1(o, tb, ps, pskey, l=l, th=th):
                pair = o // 2
                if o % 2 == 0:
                    k.v(lambda e, ps=ps, tb=tb, o=o, l=l: e.tensor_scalar(
                        out=astash[:, tb * TB:(tb + 1) * TB], in0=ps[:, 0:TB],
                        scalar1=k.tiles["c_l2f_b1"][:, l * 44 + o:l * 44 + o + 1],
                        scalar2=None, op0=ALU.add),
                        [pskey, "c_l2f_b1"], ["l2f_astash"])
                else:
                    gl = k.t("sc_gel", (128, TB), BF16)
                    k.s(lambda e, ps=ps, o=o, l=l: e.activation(
                        out=gl[:, 0:TB], in_=ps[:, 0:TB], func=AF.Gelu,
                        bias=k.tiles["c_l2f_b1"][:, l * 44 + o:l * 44 + o + 1]),
                        [pskey, "c_l2f_b1"], ["sc_gel"])
                    hh, hk, hoff = hb_place(pair)
                    k.v(lambda e, hh=hh, hoff=hoff, tb=tb, gl=gl: e.tensor_mul(
                        out=hh[:, hoff + tb * TB: hoff + (tb + 1) * TB],
                        in0=astash[:, tb * TB:(tb + 1) * TB], in1=gl[:, 0:TB]),
                        ["l2f_astash", "sc_gel"], [hk])

            gemm_fm(k, f"l2f_w1_{l}_{th}", apoff(W["l2f_w1"], l * 1024 * 5632),
                    1024, 5632, lambda c: (r0, kr0, c * NT + th * 1024),
                    evac_w1, ntok=1024)

            def evac_w2(o, tb, ps, pskey, l=l, th=th):
                sl = slice(o * NT + th * 1024 + tb * TB, o * NT + th * 1024 + (tb + 1) * TB)
                k.v(lambda e, ps=ps, sl=sl, o=o, l=l: e.scalar_tensor_tensor(
                    out=xf[:, sl], in0=ps[:, 0:TB],
                    scalar=k.tiles["c_l2f_b2"][:, l * 8 + o:l * 8 + o + 1],
                    in1=xf[:, sl], op0=ALU.add, op1=ALU.add),
                    [pskey, "c_l2f_b2", "xf"], ["xf"])

            gemm_fm(k, f"l2f_w2_{l}_{th}", apoff(W["l2f_w2"], l * 2816 * 1024),
                    2816, 1024, hb_place, evac_w2, ntok=1024)
        for c in range(8):
            k.s(lambda e, c=c: e.activation(out=xb[:, c * NT:(c + 1) * NT],
                                            in_=xf[:, c * NT:(c + 1) * NT],
                                            func=AF.Copy), ["xf"], ["xb"])
        if STOP == "layer0":
            return
    if STOP == "noln":
        return
    sc = k.t("sc_shift", (128, NT), BF16)
    gld2 = k.group("gpsimd")
    for c in range(8):
        k.dma("gpsimd", lambda e, c=c: e.dma_start(
            out=sc[:, :], in_=xsave[0:128, c * NT:(c + 1) * NT]),
            ["l2f_xsave"], ["sc_shift"], gld2)
        k.v(lambda e, c=c: e.tensor_add(
            out=xf[:, c * NT:(c + 1) * NT], in0=xf[:, c * NT:(c + 1) * NT],
            in1=sc[:, :]), ["sc_shift", "xf"], ["xf"])
        gld2 = k.group("gpsimd")
    ln_fm(k, "l2f_ln")


def ln_from_xb_to(k, tag, dst, dstk, gamma=None, beta=None, eps=1e-5, rows_t=None):
    xb = k.tiles["xb"]
    sq = k.t("ln_sqb", (128, TB), BF16)
    rows = rows_t if rows_t is not None else k.t("ln_rows", (1, 2 * TB), BF16)
    onesNb, onesNbk = k.constb(1.0 / 1024.0)
    ones1b, ones1bk = k.ones_row("bf16")
    for tb in range(NTB):
        psm, kpm = k.bank()
        psv, kpv = k.bank()
        for c in range(8):
            k.te(lambda e, psm=psm, c=c, tb=tb: e.matmul(
                out=psm[0:1, 0:TB], lhsT=onesNb[:, 0:1],
                rhs=xb[:, c * NT + tb * TB: c * NT + tb * TB + TB],
                start=(c == 0), stop=(c == 7)), [onesNbk, "xb"], [kpm])
        for c in range(8):
            k.s(lambda e, c=c, tb=tb: e.activation(
                out=sq[:, 0:TB], in_=xb[:, c * NT + tb * TB: c * NT + tb * TB + TB],
                func=AF.Square), ["xb"], ["ln_sqb"])
            k.te(lambda e, psv=psv, c=c: e.matmul(
                out=psv[0:1, 0:TB], lhsT=onesNb[:, 0:1], rhs=sq[:, 0:TB],
                start=(c == 0), stop=(c == 7)), [onesNbk, "ln_sqb"], [kpv])
        k.v(lambda e, psm=psm: e.tensor_copy(out=rows[0:1, 0:TB], in_=psm[0:1, 0:TB]),
            [kpm], ["ln_rows"])
        k.s(lambda e: e.activation(out=rows[0:1, TB:2 * TB], in_=rows[0:1, 0:TB],
                                   func=AF.Square), ["ln_rows"], ["ln_rows"])
        k.v(lambda e, psv=psv: e.tensor_sub(out=rows[0:1, TB:2 * TB],
                                            in0=psv[0:1, 0:TB],
                                            in1=rows[0:1, TB:2 * TB]),
            [kpv, "ln_rows"], ["ln_rows"])
        epst, epsk = k.constf(eps)
        k.s(lambda e, epst=epst: e.activation(out=rows[0:1, TB:2 * TB],
                                              in_=rows[0:1, TB:2 * TB],
                                              func=AF.Ln, bias=epst[0:1, 0:1]),
            ["ln_rows", epsk], ["ln_rows"])
        k.s(lambda e: e.activation(out=rows[0:1, TB:2 * TB], in_=rows[0:1, TB:2 * TB],
                                   func=AF.Exp, scale=-0.5), ["ln_rows"], ["ln_rows"])
        pmb, kmb = k.bank()
        prb, krb = k.bank()
        k.te(lambda e, pmb=pmb: e.matmul(out=pmb[:, 0:TB], lhsT=ones1b[0:1, 0:128],
                                         rhs=rows[0:1, 0:TB], start=True, stop=True),
             [ones1bk, "ln_rows"], [kmb])
        k.te(lambda e, prb=prb: e.matmul(out=prb[:, 0:TB], lhsT=ones1b[0:1, 0:128],
                                         rhs=rows[0:1, TB:2 * TB], start=True, stop=True),
             [ones1bk, "ln_rows"], [krb])
        sc2 = k.t("sc_ln2", (128, TB), F32)
        for c in range(8):
            sl = slice(c * NT + tb * TB, c * NT + tb * TB + TB)
            k.v(lambda e, sl=sl, pmb=pmb: e.tensor_sub(
                out=sc2[:, 0:TB], in0=xb[:, sl], in1=pmb[:, 0:TB]),
                ["xb", kmb], ["sc_ln2"])
            k.v(lambda e, prb=prb: e.tensor_mul(
                out=sc2[:, 0:TB], in0=sc2[:, 0:TB], in1=prb[:, 0:TB]),
                ["sc_ln2", krb], ["sc_ln2"])
            gt, gk, goff = gamma
            bt, bk, boff = beta
            k.v(lambda e, sl=sl, c=c, gt=gt, bt=bt, goff=goff, boff=boff: e.tensor_scalar(
                out=dst[:, sl], in0=sc2[:, 0:TB],
                scalar1=gt[:, goff + c:goff + c + 1],
                scalar2=bt[:, boff + c:boff + c + 1],
                op0=ALU.mult, op1=ALU.add), ["sc_ln2", gk, bk], [dstk])


def emit_l2_attn(k, W):
    xf, xb = k.tiles["xf"], k.tiles["xb"]
    xs, kxs2 = bb2(k, 0, 1)
    xsave = k.nc.dram_tensor("l2a_xsave", [128, 8 * NT], BF16).ap()
    zdram = k.nc.dram_tensor("l2a_z", [128, 16 * NT], BF16).ap()
    gsv = k.group("gpsimd")
    k.dma("gpsimd", lambda e: e.dma_start(out=xsave, in_=xb[:, :]),
          ["xb"], ["l2a_xsave"], gsv)
    for nm, shp in [("l2a_convw", (128, 256)), ("l2a_convb", (128, 64)),
                    ("l2a_dtb", (128, 64)), ("l2a_dsk", (128, 64)),
                    ("l2a_lnw", (128, 32)), ("l2a_lnb", (128, 32))]:
        load_const(k, nm, W[nm], shp, F32)
    stg = k.t("gw_stg0", (128, 512), F32)
    sel = k.t("sb_sel", (32, 4096), BF16)
    gs = k.group()
    k.dma("sync", lambda e: e.dma_start(out=sel[:, :], in_=W["l2a_sel"]),
          [], ["sb_sel"], gs)
    Af = k.t("l2a_Af", (128, 256), F32)
    bt = TView(xb, 0 * NT, NT)
    ht = TView(xb, 1 * NT, NT)
    dub = TView(xb, 2 * NT, NT)
    scf2 = TView(xb, 3 * NT, NT)
    dtb = TView(xb, 4 * NT, NT)
    bcBC = TView(xb, 5 * NT, 2 * NT)
    delt = k.t("l2a_delt", (128, NT), F32)
    af32 = k.t("l2a_a", (128, NT), F32)
    scf1 = af32
    prod = bt
    ych = k.t("l2a_y", (128, NT), F32)
    zt = scf2
    VKEYS = ["l2a_bt", "l2a_ht", "l2a_dub", "l2a_scf2", "l2a_dtbb",
             "l2a_bcBC", "l2a_dtw_v"]
    convw = k.tiles["c_l2a_convw"]
    convb = k.tiles["c_l2a_convb"]

    for li in range(4):
        k.dma("sync", lambda e, li=li: e.dma_start(
            out=stg[:, 0:256], in_=W["l2a_alog"][0:128, li * 256:(li + 1) * 256]),
            [], ["gw_stg0"], k.group())
        k.s(lambda e: e.activation(out=Af[:, :], in_=stg[:, 0:256],
                                   func=AF.Exp), ["gw_stg0"], ["l2a_Af"])
        k.v(lambda e: e.tensor_scalar_mul(out=Af[:, :], in0=Af[:, :], scalar1=-1.0),
            ["l2a_Af"], ["l2a_Af"])

        def evac_in(o, tb, ps, pskey, li=li):
            if o < 16:
                k.s(lambda e, ps=ps, o=o, tb=tb: e.activation(
                    out=xs[:, o * NT + tb * TB: o * NT + (tb + 1) * TB],
                    in_=ps[:, 0:TB], func=AF.Copy), [pskey], list(kxs2))
            else:
                zt2 = k.t("l2a_zt2", (128, TB), BF16)
                k.v(lambda e, ps=ps, zt2=zt2: e.tensor_copy(out=zt2[:, 0:TB],
                                                            in_=ps[:, 0:TB]),
                    [pskey], ["l2a_zt2"])
                gz = k.group("gpsimd")
                k.dma("gpsimd", lambda e, o=o, tb=tb, zt2=zt2: e.dma_start(
                    out=zdram[0:128, (o - 16) * NT + tb * TB:(o - 16) * NT + (tb + 1) * TB],
                    in_=zt2[:, 0:TB]), ["l2a_zt2"], ["l2a_zdram"], gz)

        gemm_fm(k, f"l2a_in{li}", apoff(W["l2a_inw"], li * 1024 * 4096),
                1024, 4096, xb_tile(k), evac_in)
        k.ctx.op("sync", lambda e: e.nop(), ["xb"], list(VKEYS) + ["xb"])

        for c in range(16):
            k.v(lambda e, c=c, li=li: e.tensor_scalar(
                out=scf2[:, 0:NT], in0=xs[:, c * NT:c * NT + NT],
                scalar1=0.0,
                scalar2=convb[:, li * 16 + c:li * 16 + c + 1],
                op0=ALU.mult, op1=ALU.add),
                list(kxs2) + ["c_l2a_convb"], ["l2a_scf2"])
            for j in range(4):
                lo = 3 - j
                k.v(lambda e, c=c, j=j, li=li, lo=lo: e.scalar_tensor_tensor(
                    out=scf2[:, lo:NT], in0=xs[:, c * NT: c * NT + NT - lo],
                    scalar=convw[:, li * 64 + c * 4 + j:li * 64 + c * 4 + j + 1],
                    in1=scf2[:, lo:NT],
                    op0=ALU.mult, op1=ALU.add),
                    list(kxs2) + ["c_l2a_convw", "l2a_scf2"], ["l2a_scf2"])
            k.s(lambda e, c=c: e.activation(out=xs[:, c * NT:(c + 1) * NT],
                                            in_=scf2[:, 0:NT], func=AF.Silu),
                ["l2a_scf2"], list(kxs2))

        def evac_xp(o, tb, ps, pskey):
            k.v(lambda e, ps=ps, tb=tb: e.tensor_copy(
                out=dtb[0:64, tb * TB:(tb + 1) * TB], in_=ps[0:64, 0:TB]),
                [pskey], ["l2a_dtbb"])
            k.v(lambda e, ps=ps, tb=tb: e.tensor_copy(
                out=bcBC[0:32, tb * TB:(tb + 1) * TB], in_=ps[64:96, 0:TB]),
                [pskey], ["l2a_bcBC"])

        gemm_fm(k, f"l2a_xp{li}", apoff(W["l2a_xpw"], li * 2048 * 96),
                2048, 96, lambda c: (xs, kxs2[c // 8], c * NT), evac_xp)

        dtw = TView(xb, 7 * NT, NT)
        for i4 in range(4):
            k.dma("sync", lambda e, i4=i4, li=li: e.dma_start(
                out=stg[0:64, 0:TB],
                in_=W["l2a_dtw"][li * 64:(li + 1) * 64, i4 * TB:(i4 + 1) * TB]),
                [], ["gw_stg0"], k.group())
            k.v(lambda e, i4=i4, dtw=dtw: e.tensor_copy(
                out=dtw[0:64, i4 * TB:(i4 + 1) * TB], in_=stg[0:64, 0:TB]),
                ["gw_stg0"], ["l2a_dtw_v"])
        scan_tog = [0]
        for c in range(16):
            psd = [k.bank() for _ in range(4)]
            for tb in range(NTB):
                ps, pskey = psd[tb]
                k.te(lambda e, ps=ps, c=c, tb=tb, dtw=dtw: e.matmul(
                    out=ps[:, 0:TB],
                    lhsT=dtw[0:64, c * 128:(c + 1) * 128],
                    rhs=dtb[0:64, tb * TB:(tb + 1) * TB],
                    start=True, stop=True), ["l2a_dtw_v", "l2a_dtbb"], [pskey])
                k.v(lambda e, ps=ps, c=c, tb=tb: e.tensor_scalar(
                    out=delt[:, tb * TB:(tb + 1) * TB], in0=ps[:, 0:TB],
                    scalar1=k.tiles["c_l2a_dtb"][:, li * 16 + c:li * 16 + c + 1],
                    scalar2=None, op0=ALU.add),
                    [pskey, "c_l2a_dtb"], ["l2a_delt"])
            k.s(lambda e: e.activation(out=scf1[:, :], in_=delt[:, :], func=AF.Abs),
                ["l2a_delt"], ["l2a_a"])
            k.s(lambda e: e.activation(out=scf1[:, :], in_=scf1[:, :], func=AF.Exp,
                                       scale=-1.0), ["l2a_a"], ["l2a_a"])
            onet, onek = k.constf(1.0)
            k.s(lambda e, onet=onet: e.activation(out=scf1[:, :], in_=scf1[:, :],
                                                  func=AF.Ln, bias=onet[:, 0:1]),
                ["l2a_a", onek], ["l2a_a"])
            k.v(lambda e: e.tensor_relu(out=scf2[:, 0:NT], in_=delt[:, :]),
                ["l2a_delt"], ["l2a_scf2"])
            k.v(lambda e: e.tensor_add(out=delt[:, :], in0=scf1[:, :], in1=scf2[:, 0:NT]),
                ["l2a_a", "l2a_scf2"], ["l2a_delt"])
            k.v(lambda e, c=c: e.tensor_mul(out=dub[:, :], in0=delt[:, :],
                                            in1=xs[:, c * NT:(c + 1) * NT]),
                ["l2a_delt"] + list(kxs2), ["l2a_dub"])
            k.v(lambda e: e.memset(ych[:, :], 0.0), [], ["l2a_y"])
            for n in range(16):
                k.s(lambda e, c=c, n=n: e.activation(
                    out=af32[:, :], in_=delt[:, :], func=AF.Exp,
                    scale=Af[:, c * 16 + n:c * 16 + n + 1]),
                    ["l2a_delt", "l2a_Af"], ["l2a_a"])
                for tb in range(NTB):
                    psb_, kpb = k.bank()
                    k.te(lambda e, psb_=psb_, n=n, tb=tb: e.matmul(
                        out=psb_[:, 0:TB], lhsT=sel[0:32, n * 128:(n + 1) * 128],
                        rhs=bcBC[0:32, tb * TB:(tb + 1) * TB],
                        start=True, stop=True), ["sb_sel", "l2a_bcBC"], [kpb])
                    k.v(lambda e, psb_=psb_, tb=tb: e.tensor_mul(
                        out=bt[:, tb * TB:(tb + 1) * TB],
                        in0=dub[:, tb * TB:(tb + 1) * TB], in1=psb_[:, 0:TB]),
                        ["l2a_dub", kpb], ["l2a_bt"])
                eng = k.v
                scan_tog[0] += 1
                eng(lambda e: e.tensor_tensor_scan(
                    out=ht[:, :], data0=af32[:, :], data1=bt[:, :],
                    initial=0.0, op0=ALU.mult, op1=ALU.add),
                    ["l2a_a", "l2a_bt"], ["l2a_ht"])
                for tb in range(NTB):
                    psc, kpc = k.bank()
                    k.te(lambda e, psc=psc, n=n, tb=tb: e.matmul(
                        out=psc[:, 0:TB], lhsT=sel[0:32, 2048 + n * 128:2048 + (n + 1) * 128],
                        rhs=bcBC[0:32, tb * TB:(tb + 1) * TB],
                        start=True, stop=True), ["sb_sel", "l2a_bcBC"], [kpc])
                    k.v(lambda e, psc=psc, tb=tb: e.tensor_mul(
                        out=prod[:, 0:TB], in0=ht[:, tb * TB:(tb + 1) * TB],
                        in1=psc[:, 0:TB]), ["l2a_ht", kpc], ["l2a_bt"])
                    k.v(lambda e, tb=tb: e.tensor_add(
                        out=ych[:, tb * TB:(tb + 1) * TB],
                        in0=ych[:, tb * TB:(tb + 1) * TB], in1=prod[:, 0:TB]),
                        ["l2a_y", "l2a_bt"], ["l2a_y"])
            k.v(lambda e, c=c, li=li: e.scalar_tensor_tensor(
                out=ych[:, :], in0=xs[:, c * NT:(c + 1) * NT],
                scalar=k.tiles["c_l2a_dsk"][:, li * 16 + c:li * 16 + c + 1],
                in1=ych[:, :], op0=ALU.mult, op1=ALU.add),
                list(kxs2) + ["c_l2a_dsk", "l2a_y"], ["l2a_y"])
            for tb in range(NTB):
                gz = k.group("gpsimd")
                k.dma("gpsimd", lambda e, c=c, tb=tb: e.dma_start(
                    out=zt[:, 0:TB],
                    in_=zdram[0:128, c * NT + tb * TB: c * NT + (tb + 1) * TB]),
                    ["l2a_zdram"], ["l2a_scf2"], gz)
                k.s(lambda e: e.activation(out=zt[:, 0:TB], in_=zt[:, 0:TB],
                                           func=AF.Silu), ["l2a_scf2"], ["l2a_scf2"])
                k.v(lambda e, c=c, tb=tb: e.tensor_mul(
                    out=xs[:, c * NT + tb * TB: c * NT + (tb + 1) * TB],
                    in0=ych[:, tb * TB:(tb + 1) * TB], in1=zt[:, 0:TB]),
                    ["l2a_y", "l2a_scf2"], list(kxs2))

        def evac_out(o, tb, ps, pskey):
            sl = slice(o * NT + tb * TB, o * NT + (tb + 1) * TB)
            k.v(lambda e, ps=ps, sl=sl: e.tensor_add(
                out=xf[:, sl], in0=xf[:, sl], in1=ps[:, 0:TB]), [pskey, "xf"], ["xf"])

        gemm_fm(k, f"l2a_out{li}", apoff(W["l2a_outw"], li * 2048 * 1024),
                2048, 1024, lambda c: (xs, kxs2[c // 8], c * NT), evac_out)
        k.ctx.op("sync", lambda e: e.nop(), list(VKEYS), ["xb"])
        ln_fm(k, f"l2a_ln{li}",
              gamma=(k.tiles["c_l2a_lnw"], "c_l2a_lnw", li * 8),
              beta=(k.tiles["c_l2a_lnb"], "c_l2a_lnb", li * 8))
    k.ctx.op("sync", lambda e: e.nop(), ["xb"], list(VKEYS) + ["xb"])
    gld2 = k.group("gpsimd")
    for c in range(8):
        k.dma("gpsimd", lambda e, c=c: e.dma_start(
            out=scf2[:, 0:NT],
            in_=xsave[0:128, c * NT:(c + 1) * NT]),
            ["l2a_xsave"], ["l2a_scf2"], gld2)
        k.v(lambda e, c=c: e.tensor_add(
            out=xf[:, c * NT:(c + 1) * NT], in0=xf[:, c * NT:(c + 1) * NT],
            in1=scf2[:, 0:NT]), ["l2a_scf2", "xf"], ["xf"])
        gld2 = k.group("gpsimd")
    k.ctx.op("sync", lambda e: e.nop(), list(VKEYS), ["xb"])
    ln_fm(k, "l2a_lnf")


def emit_l1_attn(k, W):
    xf, xb = k.tiles["xf"], k.tiles["xb"]
    identf, identfk = get_ident(k, W, "f32")
    qh_, kq = bb(k, 0)
    kh_, kk_ = bb(k, 1)
    vdram = k.nc.dram_tensor("l1a_v", [128, 8 * NT], BF16).ap()
    glin = TView(xb, 0, 8 * NT)
    odram = k.nc.dram_tensor("l1a_o", [128, 8 * NT], BF16).ap()
    for nm, shp in [("l1a_cw", (128, 96)), ("l1a_cb", (128, 24)),
                    ("l1a_alog", (128, 8)), ("l1a_dtb", (128, 8)),
                    ("l1a_onw", (128, 1))]:
        load_const(k, nm, W[nm], shp, F32)
    load_const(k, "l1a_eye", W["l1a_eye"], (128, 64), BF16)
    t1 = k.t("l1a_t1", (64, NT), BF16)
    brows = k.t("l1a_br", (16, NT), BF16)
    btall = k.t("l1a_btT", (64, 512), F32)
    gf = k.t("l1a_gf", (128, NT), F32)
    kbeta = k.t("l1a_kbeta", (128, NT), BF16)
    nrow = k.t("l2f_nrow", (2, TB), F32)
    sel2h = k.t("c_sel2h", (2, 128), F32)
    g0 = k.group()
    k.dma("sync", lambda e: e.dma_start(out=sel2h[:, :], in_=W["cst_sel2h"]),
          [], ["c_sel2h"], g0)
    Af = k.t("l1a_Af", (128, 8), F32)
    k.s(lambda e: e.activation(out=Af[:, :], in_=k.tiles["c_l1a_alog"][:, :],
                               func=AF.Exp), ["c_l1a_alog"], ["l1a_Af"])
    k.v(lambda e: e.tensor_scalar_mul(out=Af[:, :], in0=Af[:, :], scalar1=-1.0),
        ["c_l1a_Af" if False else "l1a_Af"], ["l1a_Af"])

    cwt = k.tiles["c_l1a_cw"]
    cbt = k.tiles["c_l1a_cb"]
    sc1 = k.t("l1a_sc1", (128, NT), F32)

    def mk_evac(dst, dk, qkv):
        def evac(o, tb, ps, pskey):
            k.v(lambda e, ps=ps, o=o, tb=tb: e.tensor_copy(
                out=dst[:, o * NT + tb * TB: o * NT + (tb + 1) * TB],
                in_=ps[:, 0:TB]), [pskey], [dk])
        return evac

    gemm_fm(k, "l1a_q", W["l1a_wq"], 1024, 1024, xb_tile(k), mk_evac(qh_, kq, 0))
    gemm_fm(k, "l1a_k", W["l1a_wk"], 1024, 1024, xb_tile(k), mk_evac(kh_, kk_, 1))
    vtmp = k.t("l1a_vtmp", (128, NT), BF16)

    def evac_v(o, tb, ps, pskey):
        k.v(lambda e, ps=ps, tb=tb: e.tensor_copy(
            out=vtmp[:, tb * TB:(tb + 1) * TB], in_=ps[:, 0:TB]),
            [pskey], ["l1a_vtmp"])
        if tb == NTB - 1:
            qkv = 2
            k.v(lambda e, o=o: e.tensor_scalar(
                out=sc1[:, 0:NT], in0=vtmp[:, 0:NT],
                scalar1=0.0, scalar2=cbt[:, qkv * 8 + o:qkv * 8 + o + 1],
                op0=ALU.mult, op1=ALU.add), ["l1a_vtmp", "c_l1a_cb"], ["l1a_sc1"])
            for j in range(4):
                lo = 3 - j
                k.v(lambda e, o=o, j=j, lo=lo: e.scalar_tensor_tensor(
                    out=sc1[:, lo:NT], in0=vtmp[:, 0:NT - lo],
                    scalar=cwt[:, qkv * 32 + o * 4 + j:qkv * 32 + o * 4 + j + 1],
                    in1=sc1[:, lo:NT], op0=ALU.mult, op1=ALU.add),
                    ["l1a_vtmp", "c_l1a_cw", "l1a_sc1"], ["l1a_sc1"])
            k.s(lambda e: e.activation(out=vtmp[:, 0:NT], in_=sc1[:, 0:NT],
                                       func=AF.Silu), ["l1a_sc1"], ["l1a_vtmp"])
            gv = k.group("gpsimd")
            k.dma("gpsimd", lambda e, o=o: e.dma_start(
                out=vdram[0:128, o * NT:(o + 1) * NT], in_=vtmp[:, 0:NT]),
                ["l1a_vtmp"], ["l1a_vdram"], gv)

    gemm_fm(k, "l1a_v", W["l1a_wv"], 1024, 1024, xb_tile(k), evac_v)
    for qkv, (dst, dk) in enumerate([(qh_, kq), (kh_, kk_)]):
        for c in range(8):
            k.v(lambda e, c=c, qkv=qkv, dst=dst: e.tensor_scalar(
                out=sc1[:, 0:NT], in0=dst[:, c * NT:c * NT + NT],
                scalar1=0.0, scalar2=cbt[:, qkv * 8 + c:qkv * 8 + c + 1],
                op0=ALU.mult, op1=ALU.add), [dk, "c_l1a_cb"], ["l1a_sc1"])
            for j in range(4):
                lo = 3 - j
                k.v(lambda e, c=c, j=j, qkv=qkv, lo=lo, dst=dst: e.scalar_tensor_tensor(
                    out=sc1[:, lo:NT], in0=dst[:, c * NT: c * NT + NT - lo],
                    scalar=cwt[:, qkv * 32 + c * 4 + j:qkv * 32 + c * 4 + j + 1],
                    in1=sc1[:, lo:NT], op0=ALU.mult, op1=ALU.add),
                    [dk, "c_l1a_cw", "l1a_sc1"], ["l1a_sc1"])
            k.s(lambda e, c=c, dst=dst: e.activation(
                out=dst[:, c * NT:(c + 1) * NT], in_=sc1[:, 0:NT], func=AF.Silu),
                ["l1a_sc1"], [dk])
    def evac_t1(o, tb, ps, pskey):
        k.v(lambda e, ps=ps, tb=tb: e.tensor_copy(
            out=t1[0:64, tb * TB:(tb + 1) * TB], in_=ps[0:64, 0:TB]), [pskey], ["l1a_t1"])

    gemm_fm(k, "l1a_f1", W["l1a_wf1"], 1024, 64, xb_tile(k), evac_t1)

    def evac_beta(o, tb, ps, pskey):
        k.s(lambda e, ps=ps, tb=tb: e.activation(
            out=brows[0:16, tb * TB:(tb + 1) * TB], in_=ps[0:16, 0:TB],
            func=AF.Sigmoid), [pskey], ["l1a_br"])

    gemm_fm(k, "l1a_wb", W["l1a_wbmat"], 1024, 16, xb_tile(k), evac_beta)
    k.ctx.op("sync", lambda e: e.nop(), ["xb"], ["l1a_glin", "xb"])

    def evac_glin(o, tb, ps, pskey):
        k.s(lambda e, ps=ps, o=o, tb=tb: e.activation(
            out=glin[:, o * NT + tb * TB: o * NT + (tb + 1) * TB],
            in_=ps[:, 0:TB], func=AF.Copy), [pskey], ["l1a_glin"])

    gemm_fm(k, "l1a_f2", W["l1a_wf2"], 64, 1024,
            lambda c: (t1, "l1a_t1", 0), evac_glin)
    identb, identbk = get_ident(k, W, "bf16")
    for c in range(32):
        ps, pskey = k.bankB()
        k.te(lambda e, c=c, ps=ps: e.transpose(
            out=ps[0:64, 0:16], in_=brows[0:16, c * 64:(c + 1) * 64],
            identity=identb[0:16, 0:16]), ["l1a_br", identbk], [pskey])
        k.v(lambda e, c=c, ps=ps: e.tensor_copy(
            out=btall[0:64, c * 16:(c + 1) * 16], in_=ps[0:64, 0:16]),
            [pskey], ["l1a_btT"])

    sqv = k.t("ln_sqb", (128, TB), BF16)
    o2h = k.t("c_ones2h", (128, 2), BF16)
    if True:
        k.v(lambda e: e.memset(o2h[:, :], 0.0), [], ["c_ones2h"])
        k.v(lambda e: e.memset(o2h[0:64, 0:1], 1.0), [], ["c_ones2h"])
        k.v(lambda e: e.memset(o2h[64:128, 1:2], 1.0), [], ["c_ones2h"])
    for qk, (src, skey) in enumerate([(qh_, kq), (kh_, kk_)]):
        for c in range(8):
            for tb in range(NTB):
                psn, kpn = k.bank()
                k.s(lambda e, c=c, tb=tb, src=src: e.activation(
                    out=sqv[:, 0:TB],
                    in_=src[:, c * NT + tb * TB: c * NT + tb * TB + TB],
                    func=AF.Square), [skey], ["ln_sqb"])
                k.te(lambda e, psn=psn: e.matmul(
                    out=psn[0:2, 0:TB], lhsT=o2h[:, 0:2], rhs=sqv[:, 0:TB],
                    start=True, stop=True), ["c_ones2h", "ln_sqb"], [kpn])
                k.v(lambda e, psn=psn: e.tensor_copy(out=nrow[0:2, 0:TB],
                                                     in_=psn[0:2, 0:TB]),
                    [kpn], ["l2f_nrow"])
                k.s(lambda e: e.activation(out=nrow[0:2, 0:TB], in_=nrow[0:2, 0:TB],
                                           func=AF.Sqrt), ["l2f_nrow"], ["l2f_nrow"])
                k.v(lambda e: e.tensor_scalar_add(out=nrow[0:2, 0:TB],
                                                  in0=nrow[0:2, 0:TB], scalar1=1e-6),
                    ["l2f_nrow"], ["l2f_nrow"])
                k.v(lambda e: e.reciprocal(out=nrow[0:2, 0:TB], in_=nrow[0:2, 0:TB]),
                    ["l2f_nrow"], ["l2f_nrow"])
                psb_, kpb = k.bank()
                k.te(lambda e, psb_=psb_: e.matmul(
                    out=psb_[:, 0:TB], lhsT=sel2h[0:2, 0:128],
                    rhs=nrow[0:2, 0:TB], start=True, stop=True),
                    ["c_sel2h", "l2f_nrow"], [kpb])
                mulc = 0.125 if qk == 0 else 1.0
                k.v(lambda e, psb_=psb_, c=c, tb=tb, src=src, mulc=mulc: e.scalar_tensor_tensor(
                    out=src[:, c * NT + tb * TB: c * NT + tb * TB + TB],
                    in0=src[:, c * NT + tb * TB: c * NT + tb * TB + TB],
                    scalar=mulc, in1=psb_[:, 0:TB], op0=ALU.mult, op1=ALU.mult),
                    [skey, kpb], [skey])

    BBIGh = k.tiles["BBIG"]
    BPITCH = getattr(k, "nbb", 3) * BBREG

    def bbap(region, off, dims):
        a = BBIGh[:, :]
        return bass.AP(a.tensor, a.offset + region * BBREG + off,
                       [list(d) for d in dims])

    ones64 = k.t("c_ones64", (128, 64), BF16)
    k.v(lambda e: e.memset(ones64[:, :], 1.0), [], ["c_ones64"])
    for nm, shp in [("l1a_m8neg", (128, 64)), ("l1a_m8incl", (128, 64)),
                    ("l1a_moff", (128, 64))]:
        load_const(k, nm, W[nm], shp, BF16)
    load_const(k, "l1a_bg2", W["l1a_bg2"], (128, 8), F32)
    selbt = k.t("c_l1a_selb", (16, 1024), BF16)
    gsb = k.group()
    k.dma("sync", lambda e: e.dma_start(out=selbt[:, :], in_=W["l1a_selb"]),
          [], ["c_l1a_selb"], gsb)
    eye = k.tiles["c_l1a_eye"]
    m8n = k.tiles["c_l1a_m8neg"]
    m8i = k.tiles["c_l1a_m8incl"]
    moff = k.tiles["c_l1a_moff"]
    selb = k.tiles["c_l1a_selb"]
    UCOLS = [392]
    UKEYS = []

    def ut(name, ncols):
        v = TView(sc1, UCOLS[0], ncols)
        UCOLS[0] += ncols
        UKEYS.append(name)
        return v

    def uap(v, dims, offset=0):
        d2 = [list(d) for d in dims]
        d2[0][0] = NT
        a = sc1[:, :]
        return bass.AP(a.tensor, a.offset + v.off + offset, d2)

    Gx = ut("u_Gx", 65)
    d8 = ut("u_d8", 64)
    exp8 = ut("u_exp8", 64)
    k.ctx.op("sync", lambda e: e.nop(), [],
             ["l1a_vtmp", "u_Bdv", "u_Bst", "u_A8CQ", "u_ot", "u_vst"])

    def vap(v, dims, offset=0):
        d2 = [list(d) for d in dims]
        d2[0][0] = NT
        a = vtmp[:, :]
        return bass.AP(a.tensor, a.offset + v.off + offset, d2)

    Bdv = TView(vtmp, 0, 576)
    Bst = TView(vtmp, 576, 392)
    A8CQ = TView(vtmp, 968, 128)
    BdE = ut("u_BdE", 64)
    k.v(lambda e: e.memset(Bdv[:, slice(None)], 0.0), [], ["u_Bdv"])
    eG = exp8
    A0n = ut("u_A0n", 64)
    A0q = ut("u_A0q", 64)
    e2 = d8
    K2 = ut("u_K2", 64)
    K2T = K2
    eGL = ut("u_eGL", 1)
    negNT = ut("u_negNT", 64)
    YdgT = ut("u_YdgT", 64)
    Yblk = YdgT
    WTa = ut("u_WTa", 64)
    WTb = ut("u_WTb", 64)
    DTo = ut("u_DTo", 64)
    DTd = ut("u_DTd", 64)
    Rr = ut("u_R", 64)
    Pp = ut("u_P", 64)
    Mm = ut("u_M", 64)
    Sa = ut("u_Sa", 64)
    Sb = ut("u_Sb", 64)
    assert UCOLS[0] <= NT, UCOLS[0]
    otile = TView(vtmp, 1096, 64)
    uvst = TView(vtmp, 1160, 64)

    import os
    _cut = os.environ.get("L1A_CUT", "")
    _nhp = 8
    if _cut.startswith("units:"):
        _nhp = int(_cut.split(":")[1])
    for hp in range(_nhp):
        k.ctx.op("sync", lambda e: e.nop(), list(UKEYS), ["l1a_sc1"])
        k.v(lambda e, hp=hp: e.tensor_scalar(
            out=sc1[:, 0:NT], in0=glin[:, hp * NT: (hp + 1) * NT],
            scalar1=k.tiles["c_l1a_dtb"][:, hp:hp + 1], scalar2=None, op0=ALU.add),
            ["l1a_glin", "c_l1a_dtb"], ["l1a_sc1"])
        k.v(lambda e: e.tensor_relu(out=gf[:, :], in_=sc1[:, 0:NT]),
            ["l1a_sc1"], ["l1a_gf"])
        k.s(lambda e: e.activation(out=sc1[:, 0:NT], in_=sc1[:, 0:NT], func=AF.Abs),
            ["l1a_sc1"], ["l1a_sc1"])
        k.s(lambda e: e.activation(out=sc1[:, 0:NT], in_=sc1[:, 0:NT], func=AF.Exp,
                                   scale=-1.0), ["l1a_sc1"], ["l1a_sc1"])
        onet, onek = k.constf(1.0)
        k.s(lambda e, onet=onet: e.activation(out=sc1[:, 0:NT], in_=sc1[:, 0:NT],
                                              func=AF.Ln, bias=onet[:, 0:1]),
            ["l1a_sc1", onek], ["l1a_sc1"])
        k.v(lambda e: e.tensor_add(out=gf[:, :], in0=gf[:, :], in1=sc1[:, 0:NT]),
            ["l1a_gf", "l1a_sc1"], ["l1a_gf"])
        k.v(lambda e, hp=hp: e.tensor_scalar(
            out=gf[:, :], in0=gf[:, :], scalar1=Af[:, hp:hp + 1], scalar2=-10.0,
            op0=ALU.mult, op1=ALU.max), ["l1a_gf", "l1a_Af"], ["l1a_gf"])
        k.ctx.op("sync", lambda e: e.nop(), [], ["l1a_sc1"] + list(UKEYS))
        for tb in range(NTB):
            psb_, kpb = k.bank()
            k.te(lambda e, hp=hp, psb_=psb_, tb=tb: e.matmul(
                out=psb_[:, 0:TB], lhsT=selb[0:16, hp * 128:(hp + 1) * 128],
                rhs=brows[0:16, tb * TB:(tb + 1) * TB], start=True, stop=True),
                ["c_l1a_selb", "l1a_br"], [kpb])
            k.v(lambda e, hp=hp, psb_=psb_, tb=tb: e.tensor_mul(
                out=kbeta[:, tb * TB:(tb + 1) * TB],
                in0=kh_[:, hp * NT + tb * TB: hp * NT + tb * TB + TB],
                in1=psb_[:, 0:TB]), [kk_, kpb], ["l1a_kbeta"])
        k.v(lambda e: e.memset(Sa[:, :], 0.0), [], ["u_Sa"])
        scur, snxt = (Sa, "u_Sa"), (Sb, "u_Sb")
        for c in range(32):
            coff = hp * NT + c * 64
            k.v(lambda e: e.memset(Gx[:, 0:1], 0.0), [], ["u_Gx"])
            k.v(lambda e, c=c: e.tensor_tensor_scan(
                out=Gx[:, 1:65], data0=ones64[:, :], data1=gf[:, c * 64:(c + 1) * 64],
                initial=0.0, op0=ALU.mult, op1=ALU.add),
                ["c_ones64", "l1a_gf"], ["u_Gx"])
            k.v(lambda e: e.tensor_sub(out=d8[:, :], in0=uap(Gx, [[65, 128], [1, 64]], 1),
                                       in1=uap(Gx, [[65, 128], [8, 8], [0, 8]])),
                ["u_Gx"], ["u_d8"])
            k.s(lambda e: e.activation(out=exp8[:, :], in_=d8[:, :], func=AF.Exp),
                ["u_d8"], ["u_exp8"])
            k.v(lambda e, c=c: e.tensor_mul(
                out=vap(A8CQ, [[0, 128], [16, 8], [1, 8]]),
                in0=uap(exp8, [[0, 128], [8, 8], [1, 8]]),
                in1=ap3(kbeta, [[NT, 128], [8, 8], [1, 8]], c * 64)),
                ["u_exp8", "l1a_kbeta"], ["u_A8CQ"])
            k.v(lambda e, coff=coff: e.tensor_mul(
                out=vap(A8CQ, [[0, 128], [16, 8], [1, 8]], 8),
                in0=uap(exp8, [[0, 128], [8, 8], [1, 8]]),
                in1=bbap(0, coff, [[BPITCH, 128], [8, 8], [1, 8]])),
                ["u_exp8", kq], ["u_A8CQ"])
            k.s(lambda e: e.activation(out=BdE[:, :], in_=d8[:, :], func=AF.Exp,
                                       scale=-1.0), ["u_d8"], ["u_BdE"])
            k.v(lambda e, coff=coff: e.tensor_mul(
                out=BdE[:, :], in0=BdE[:, :], in1=kh_[:, coff:coff + 64]),
                [kk_, "u_BdE"], ["u_BdE"])
            k.v(lambda e: e.tensor_copy(
                out=vap(Bdv, [[576, 128], [72, 8], [1, 8]]),
                in_=uap(BdE, [[64, 128], [8, 8], [1, 8]])), ["u_BdE"], ["u_Bdv"])
            k.v(lambda e: e.tensor_copy(
                out=sc1[:, 0:392], in_=uap(Gx, [[65, 128], [8, 7], [0, 56]], 8)),
                ["u_Gx"], ["l1a_sc1"])
            k.v(lambda e: e.tensor_sub(
                out=sc1[:, 0:392], in0=sc1[:, 0:392],
                in1=uap(Gx, [[65, 128], [0, 7], [1, 56]], 1)), ["u_Gx", "l1a_sc1"], ["l1a_sc1"])
            k.s(lambda e: e.activation(out=sc1[:, 0:392], in_=sc1[:, 0:392], func=AF.Exp),
                ["l1a_sc1"], ["l1a_sc1"])
            k.v(lambda e, hp=hp, c=c: e.tensor_mul(
                out=Bst[:, :], in0=sc1[:, 0:392],
                in1=bbap(1, hp * NT + c * 64, [[BPITCH, 128], [0, 7], [1, 56]])),
                ["l1a_sc1", kk_], ["u_Bst"])
            k.s(lambda e: e.activation(out=eG[:, :],
                                       in_=uap(Gx, [[65, 128], [1, 64]], 1),
                                       func=AF.Exp), ["u_Gx"], ["u_exp8"])
            k.v(lambda e, coff=coff: e.scalar_tensor_tensor(
                out=A0n[:, :], in0=eG[:, :], scalar=-1.0,
                in1=kh_[:, coff:coff + 64], op0=ALU.mult, op1=ALU.mult),
                ["u_exp8", kk_], ["u_A0n"])
            k.v(lambda e, coff=coff: e.tensor_mul(
                out=A0q[:, :], in0=eG[:, :], in1=qh_[:, coff:coff + 64]),
                ["u_exp8", kq], ["u_A0q"])
            k.s(lambda e: e.activation(out=e2[:, :],
                                       in_=uap(Gx, [[65, 128], [1, 64]], 1),
                                       func=AF.Exp, scale=-1.0, bias=Gx[:, 64:65]),
                ["u_Gx"], ["u_d8"])
            k.v(lambda e, coff=coff: e.tensor_mul(
                out=K2[:, :], in0=e2[:, :], in1=kh_[:, coff:coff + 64]),
                ["u_d8", kk_], ["u_K2"])
            k.s(lambda e: e.activation(out=eGL[:, :], in_=Gx[:, 64:65], func=AF.Exp),
                ["u_Gx"], ["u_eGL"])
            for hh in range(2):
                hb = hh * 64
                h = 2 * hp + hh
                psC, kpC, iC = k.take_bank()
                psCd, kpCd, iCd = k.take_bank()
                for i in range(1, 8):
                    k.te(lambda e, psC=psC, i=i, hb=hb: e.matmul(
                        out=psC[0:8 * i, 16 * i:16 * i + 16],
                        lhsT=Bst[hb:hb + 64, (i - 1) * 56:(i - 1) * 56 + 8 * i],
                        rhs=A8CQ[hb:hb + 64, 16 * i:16 * i + 16],
                        start=True, stop=True), ["u_Bst", "u_A8CQ"], [kpC])
                for j in range(8):
                    k.te(lambda e, psCd=psCd, j=j, hb=hb: e.matmul(
                        out=psCd[0:64, 16 * j:16 * j + 16],
                        lhsT=Bdv[hb:hb + 64, j * 64:(j + 1) * 64],
                        rhs=A8CQ[hb:hb + 64, 16 * j:16 * j + 16],
                        start=True, stop=True), ["u_Bdv", "u_A8CQ"], [kpCd])
                k.v(lambda e, psC=psC, hb=hb: e.scalar_tensor_tensor(
                    out=negNT[hb:hb + 64, :],
                    in0=ap3(psC, [[512, 64], [16, 8], [1, 8]]), scalar=-1.0,
                    in1=moff[hb:hb + 64, :], op0=ALU.mult, op1=ALU.mult),
                    [kpC, "c_l1a_moff"], ["u_negNT"])
                k.v(lambda e, psCd=psCd, hb=hb: e.tensor_mul(
                    out=YdgT[hb:hb + 64, :],
                    in0=ap3(psCd, [[512, 64], [16, 8], [1, 8]]),
                    in1=m8n[hb:hb + 64, :]), [kpCd, "c_l1a_m8neg"], ["u_YdgT"])
                k.v(lambda e, psC=psC, hb=hb: e.tensor_mul(
                    out=DTo[hb:hb + 64, :],
                    in0=ap3(psC, [[512, 64], [16, 8], [1, 8]], 8),
                    in1=moff[hb:hb + 64, :]), [kpC, "c_l1a_moff"], ["u_DTo"])
                k.v(lambda e, psCd=psCd, hb=hb: e.tensor_mul(
                    out=DTd[hb:hb + 64, :],
                    in0=ap3(psCd, [[512, 64], [16, 8], [1, 8]], 8),
                    in1=m8i[hb:hb + 64, :]), [kpCd, "c_l1a_m8incl"], ["u_DTd"])
                k.v(lambda e, hb=hb: e.tensor_add(
                    out=DTo[hb:hb + 64, :], in0=DTo[hb:hb + 64, :],
                    in1=DTd[hb:hb + 64, :]), ["u_DTo", "u_DTd"], ["u_DTo"])
                k.free_bank(iC)
                k.free_bank(iCd)
                psT, kpT = k.bank()
                k.te(lambda e, psT=psT, hb=hb: e.transpose(
                    out=psT[0:64, 0:64], in_=YdgT[hb:hb + 64, :],
                    identity=identf[hb:hb + 64, hb:hb + 64]),
                    ["u_YdgT", identfk], [kpT])
                k.v(lambda e, psT=psT, hb=hb: e.tensor_copy(
                    out=Yblk[hb:hb + 64, :], in_=psT[0:64, 0:64]), [kpT], ["u_YdgT"])
                k.v(lambda e, hb=hb: e.tensor_copy(out=WTa[hb:hb + 64, :],
                                                   in_=eye[hb:hb + 64, :]),
                    ["c_l1a_eye"], ["u_WTa"])
                cur, nxt = (WTa, "u_WTa"), (WTb, "u_WTb")
                for it in range(7):
                    psW, kpW = k.bank()
                    k.te(lambda e, psW=psW, hb=hb, cur=cur: e.matmul(
                        out=psW[0:64, 0:64], lhsT=Yblk[hb:hb + 64, :],
                        rhs=cur[0][hb:hb + 64, :], start=True, stop=True),
                        ["u_YdgT", cur[1]], [kpW])
                    k.v(lambda e, psW=psW, hb=hb, nxt=nxt: e.tensor_add(
                        out=nxt[0][hb:hb + 64, :], in0=psW[0:64, 0:64],
                        in1=eye[hb:hb + 64, :]), [kpW, "c_l1a_eye"], [nxt[1]])
                    cur, nxt = nxt, cur
                WT, kWT = cur
                psR, kpR, iR = k.take_bank()
                if hh == 0:
                    gvl = k.group("gpsimd")
                    k.dma("gpsimd", lambda e, hp=hp, c=c: e.dma_start(
                        out=uvst[:, :],
                        in_=vdram[0:128, hp * NT + c * 64: hp * NT + (c + 1) * 64]),
                        ["l1a_vdram"], ["u_vst"], gvl)
                psB, kpB = k.bankB()
                k.te(lambda e, psB=psB, hb=hb: e.transpose(
                    out=psB[0:64, 0:64], in_=uvst[hb:hb + 64, :],
                    identity=identb[hb:hb + 64, hb:hb + 64]),
                    ["u_vst", identbk], [kpB])
                k.te(lambda e, psR=psR, hb=hb, scur=scur: e.matmul(
                    out=psR[0:64, 0:64], lhsT=A0n[hb:hb + 64, :],
                    rhs=scur[0][hb:hb + 64, :], start=True, stop=True),
                    ["u_A0n", scur[1]], [kpR])
                k.v(lambda e, psB=psB, hb=hb: e.tensor_copy(
                    out=Rr[hb:hb + 64, :], in_=psB[0:64, 0:64]), [kpB], ["u_R"])
                k.v(lambda e, psR=psR, hb=hb: e.tensor_add(
                    out=Rr[hb:hb + 64, :], in0=Rr[hb:hb + 64, :],
                    in1=psR[0:64, 0:64]), [kpR, "u_R"], ["u_R"])
                k.v(lambda e, hb=hb, c=c, h=h: e.tensor_scalar_mul(
                    out=Rr[hb:hb + 64, :], in0=Rr[hb:hb + 64, :],
                    scalar1=btall[0:64, c * 16 + h:c * 16 + h + 1]),
                    ["u_R", "l1a_btT"], ["u_R"])
                k.free_bank(iR)
                for i in range(8):
                    if i > 0:
                        psP, kpP = k.bank()
                        k.te(lambda e, psP=psP, hb=hb, i=i: e.matmul(
                            out=psP[0:64, 0:64], lhsT=negNT[hb:hb + 8 * i, 0:64],
                            rhs=Mm[hb:hb + 8 * i, :], start=True, stop=True),
                            ["u_negNT", "u_M"], [kpP])
                        k.v(lambda e, psP=psP, hb=hb: e.tensor_add(
                            out=Pp[hb:hb + 64, :], in0=psP[0:64, 0:64],
                            in1=Rr[hb:hb + 64, :]), [kpP, "u_R"], ["u_P"])
                        src_p = (Pp, "u_P")
                    else:
                        src_p = (Rr, "u_R")
                    psM, kpM = k.bank()
                    k.te(lambda e, psM=psM, hb=hb, WT=WT, src_p=src_p: e.matmul(
                        out=psM[0:64, 0:64], lhsT=WT[hb:hb + 64, :],
                        rhs=src_p[0][hb:hb + 64, :], start=True, stop=True),
                        [kWT, src_p[1]], [kpM])
                    k.v(lambda e, psM=psM, hb=hb: e.tensor_copy(
                        out=Mm[hb:hb + 64, :], in_=psM[0:64, 0:64]), [kpM], ["u_M"])
                psO, kpO, iO = k.take_bank()
                k.te(lambda e, psO=psO, hb=hb, scur=scur: e.matmul(
                    out=psO[0:64, 0:64], lhsT=scur[0][hb:hb + 64, :],
                    rhs=A0q[hb:hb + 64, :], start=True, stop=False),
                    [scur[1], "u_A0q"], [kpO])
                k.te(lambda e, psO=psO, hb=hb: e.matmul(
                    out=psO[0:64, 0:64], lhsT=Mm[hb:hb + 64, :],
                    rhs=DTo[hb:hb + 64, :], start=False, stop=True),
                    ["u_M", "u_DTo"], [kpO])
                k.v(lambda e, psO=psO, hb=hb: e.tensor_copy(
                    out=otile[hb:hb + 64, :], in_=psO[0:64, 0:64]), [kpO], ["u_ot"])
                k.free_bank(iO)
                psK, kpK = k.bank()
                k.te(lambda e, psK=psK, hb=hb: e.transpose(
                    out=psK[0:64, 0:64], in_=K2[hb:hb + 64, :],
                    identity=identf[hb:hb + 64, hb:hb + 64]),
                    ["u_K2", identfk], [kpK])
                k.v(lambda e, psK=psK, hb=hb: e.tensor_copy(
                    out=K2T[hb:hb + 64, :], in_=psK[0:64, 0:64]), [kpK], ["u_K2"])
                psS, kpS = k.bank()
                k.te(lambda e, psS=psS, hb=hb: e.matmul(
                    out=psS[0:64, 0:64], lhsT=K2T[hb:hb + 64, :],
                    rhs=Mm[hb:hb + 64, :], start=True, stop=True),
                    ["u_K2", "u_M"], [kpS])
                k.v(lambda e, psS=psS, hb=hb, scur=scur, snxt=snxt: e.scalar_tensor_tensor(
                    out=snxt[0][hb:hb + 64, :], in0=scur[0][hb:hb + 64, :],
                    scalar=eGL[hb:hb + 64, 0:1], in1=psS[0:64, 0:64],
                    op0=ALU.mult, op1=ALU.add),
                    [scur[1], "u_eGL", kpS], [snxt[1]])
            scur, snxt = snxt, scur
            god = k.group("gpsimd")
            k.dma("gpsimd", lambda e, hp=hp, c=c: e.dma_start(
                out=odram[0:128, hp * NT + c * 64: hp * NT + (c + 1) * 64],
                in_=otile[:, :]), ["u_ot"], ["l1a_odram"], god)

    if _cut == "noend" or _cut.startswith("units:"):
        return
    for c in range(8):
        k.s(lambda e, c=c: e.activation(out=xb[:, c * NT:(c + 1) * NT],
                                        in_=xf[:, c * NT:(c + 1) * NT],
                                        func=AF.Copy), ["xf"], ["xb", "l1a_glin"])
    gemm_fm(k, "l1a_g1", W["l1a_wg1"], 1024, 64, xb_tile(k), evac_t1)
    sgate, ksg = kh_, kk_

    def evac_g2(o, tb, ps, pskey):
        k.s(lambda e, ps=ps, o=o, tb=tb: e.activation(
            out=sgate[:, o * NT + tb * TB: o * NT + (tb + 1) * TB],
            in_=ps[:, 0:TB], func=AF.Sigmoid,
            bias=k.tiles["c_l1a_bg2"][:, o:o + 1]), [pskey, "c_l1a_bg2"], [ksg])

    gemm_fm(k, "l1a_g2", W["l1a_wg2"], 64, 1024, lambda cc_: (t1, "l1a_t1", 0), evac_g2)
    o2h64 = k.t("c_o2h64", (128, 2), BF16)
    k.v(lambda e: e.memset(o2h64[:, :], 0.0), [], ["c_o2h64"])
    k.v(lambda e: e.memset(o2h64[0:64, 0:1], 1.0 / 64.0), [], ["c_o2h64"])
    k.v(lambda e: e.memset(o2h64[64:128, 1:2], 1.0 / 64.0), [], ["c_o2h64"])
    och = k.t("gw_bf0", (128, 512), BF16)
    for c in range(8):
        for tb in range(NTB):
            go = k.group("gpsimd")
            k.dma("gpsimd", lambda e, c=c, tb=tb: e.dma_start(
                out=och[:, 0:TB],
                in_=odram[0:128, c * NT + tb * TB: c * NT + (tb + 1) * TB]),
                ["l1a_odram"], ["gw_bf0"], go)
            psn, kpn = k.bank()
            k.s(lambda e: e.activation(out=sqv[:, 0:TB], in_=och[:, 0:TB],
                                       func=AF.Square), ["gw_bf0"], ["ln_sqb"])
            k.te(lambda e, psn=psn: e.matmul(
                out=psn[0:2, 0:TB], lhsT=o2h64[:, 0:2], rhs=sqv[:, 0:TB],
                start=True, stop=True), ["c_o2h64", "ln_sqb"], [kpn])
            k.v(lambda e, psn=psn: e.tensor_copy(out=nrow[0:2, 0:TB],
                                                 in_=psn[0:2, 0:TB]), [kpn], ["l2f_nrow"])
            epst, epsk = k.constf(1e-5)
            k.s(lambda e, epst=epst: e.activation(out=nrow[0:2, 0:TB],
                                                  in_=nrow[0:2, 0:TB], func=AF.Ln,
                                                  bias=epst[0:2, 0:1]),
                ["l2f_nrow", epsk], ["l2f_nrow"])
            k.s(lambda e: e.activation(out=nrow[0:2, 0:TB], in_=nrow[0:2, 0:TB],
                                       func=AF.Exp, scale=-0.5), ["l2f_nrow"], ["l2f_nrow"])
            psb_, kpb = k.bank()
            k.te(lambda e, psb_=psb_: e.matmul(
                out=psb_[:, 0:TB], lhsT=sel2h[0:2, 0:128], rhs=nrow[0:2, 0:TB],
                start=True, stop=True), ["c_sel2h", "l2f_nrow"], [kpb])
            k.v(lambda e, psb_=psb_: e.tensor_mul(
                out=och[:, 0:TB], in0=och[:, 0:TB], in1=psb_[:, 0:TB]),
                ["gw_bf0", kpb], ["gw_bf0"])
            k.v(lambda e, c=c, tb=tb: e.scalar_tensor_tensor(
                out=qh_[:, c * NT + tb * TB: c * NT + (tb + 1) * TB],
                in0=och[:, 0:TB], scalar=k.tiles["c_l1a_onw"][:, 0:1],
                in1=sgate[:, c * NT + tb * TB: c * NT + (tb + 1) * TB],
                op0=ALU.mult, op1=ALU.mult),
                ["gw_bf0", "c_l1a_onw", ksg], [kq])

    def evac_wo(o, tb, ps, pskey):
        sl = slice(o * NT + tb * TB, o * NT + (tb + 1) * TB)
        k.v(lambda e, ps=ps, sl=sl: e.tensor_add(
            out=xf[:, sl], in0=xf[:, sl], in1=ps[:, 0:TB]), [pskey, "xf"], ["xf"])

    gemm_fm(k, "l1a_wo", W["l1a_wo"], 1024, 1024, lambda cc_: (qh_, kq, cc_ * NT), evac_wo)
    k.ctx.op("sync", lambda e: e.nop(), ["l1a_br"], ["ln_rows"])
    ln_fm(k, "l1a_ln", rows_t=TView(brows, 0, 2 * TB))


"""Host-side weight preparation (pure layout: transposes, reshapes,
permutations, block-diagonal packing, constant tables)."""
import numpy as np
import ml_dtypes

NPF32 = np.float32


def cc(a):
    return np.ascontiguousarray(a, dtype=NPF32)


def prep_common():
    d = {}
    d["cst_ones_row"] = np.ones((1, 128), NPF32)
    d["cst_ones_mean"] = np.full((128, 1), 1.0 / 1024.0, NPF32)
    return d


def prep_l1f(ins):
    return {
        "l1f_wr": cc(ins["L1_ffn_W_r"].T),
        "l1f_wk": cc(ins["L1_ffn_W_k"].T),
        "l1f_wv": cc(ins["L1_ffn_W_v"].T),
        "l1f_mur": cc(ins["L1_ffn_mu_r"].reshape(8, 128).T),
        "l1f_muk": cc(ins["L1_ffn_mu_k"].reshape(8, 128).T),
    }


def prep_l3a(ins):
    return {
        "l3a_wq": cc(ins["L3_attn_w_q"].T),
        "l3a_wk": cc(ins["L3_attn_w_k"].T),
        "l3a_wv": cc(ins["L3_attn_w_v"].T),
        "l3a_wo": cc(ins["L3_attn_w_out"].T),
        "l3a_scale": cc(ins["L3_attn_softmax_scale"].reshape(1, 1)),
        "cst_ident_f": np.eye(128, dtype=NPF32),
    }


def prep_l3f(ins):
    sel = np.zeros((8, 1024), NPF32)
    for e in range(8):
        sel[e, e * 128:(e + 1) * 128] = 1.0
    return {
        "cst_sel8": sel,
        "l3f_wr": cc(ins["L3_ffn_Wr"]),
        "l3f_wg": cc(ins["L3_ffn_Wg"].reshape(8 * 1024, 2048)),
        "l3f_wu": cc(ins["L3_ffn_Wu"].reshape(8 * 1024, 2048)),
        "l3f_wd": cc(ins["L3_ffn_Wd"].reshape(8 * 2048, 1024)),
        "l3f_sg": cc(ins["L3_ffn_Sg"]),
        "l3f_su": cc(ins["L3_ffn_Su"]),
        "l3f_sd": cc(ins["L3_ffn_Sd"]),
        "cst_ident_f": np.eye(128, dtype=NPF32),
    }


def prep_l0f(ins):
    rm = ins["L0_ffn_rail_maker"]
    ru = ins["L0_ffn_rail_up_proj"]
    rm1 = np.zeros((128, 4096), NPF32)
    for h in range(4):
        for rg in range(8):
            t = h * 8 + rg
            blk = np.zeros((128, 128), NPF32)
            for ri in range(8):
                blk[ri * 16:(ri + 1) * 16, ri * 16:(ri + 1) * 16] = rm[h, 8 * rg + ri]
            rm1[:, t * 128:(t + 1) * 128] = blk
    rm2 = np.zeros((128 * 128, 128), NPF32)
    ti = 0
    for h in range(4):
        for rhalf in range(2):
            for rg4 in range(4):
                rg = rhalf * 4 + rg4
                for sub in range(4):
                    blk = np.zeros((128, 128), NPF32)
                    for ccv in range(2):
                        ri = sub * 2 + ccv
                        blk[ri * 16:(ri + 1) * 16, ccv * 64:(ccv + 1) * 64] = \
                            ru[h, 8 * rg + ri].T
                    rm2[ti * 128:(ti + 1) * 128, :] = blk
                    ti += 1
    return {
        "l0f_rm1": cc(rm1),
        "l0f_rm2": cc(rm2),
        "l0f_cproj": cc(ins["L0_ffn_w_cproj"].T),
    }


def prep_l0a(ins):
    Wqkv = ins["L0_attn_W_qkv"]
    b = ins["L0_attn_b_qkv"]
    qi = np.array([j * 192 + e for j in range(16) for e in range(64)])
    ki = qi + 64
    vi = qi + 128
    r = np.arange(128)
    Ult = (r[:, None] < r[None, :]).astype(NPF32)
    Ule = (r[:, None] <= r[None, :]).astype(NPF32)
    strip = np.concatenate([Ult, Ule], axis=1)
    return {
        "l0a_wq": cc(Wqkv[:, qi]),
        "l0a_wk": cc(Wqkv[:, ki]),
        "l0a_wv": cc(Wqkv[:, vi]),
        "l0a_bq": cc(b[qi].reshape(8, 128).T),
        "l0a_bk": cc(b[ki].reshape(8, 128).T),
        "l0a_bv": cc(b[vi].reshape(8, 128).T),
        "l0a_wo": cc(ins["L0_attn_W_out"]),
        "l0a_bo": cc(ins["L0_attn_b_out"].reshape(8, 128).T),
        "l0a_mask": cc(strip),
        "cst_ident_b": np.eye(128, dtype=NPF32).astype(ml_dtypes.bfloat16),
    }


def prep_l2f(ins):
    DIM_HEAD, WINDOW, n = 64, 512, 2048
    half_idx = np.arange(0, DIM_HEAD, 2, dtype=np.float64)
    inv_freq = 1.0 / 10000.0 ** (half_idx / DIM_HEAD)
    t = np.arange(n, dtype=np.float64)
    freqs = t[:, None] * inv_freq[None, :]
    freqs = np.concatenate([freqs, freqs], -1)
    base = (half_idx + 0.4 * DIM_HEAD) / (1.4 * DIM_HEAD)
    power = (t - n // 2) / float(WINDOW // 2)
    scale = base[None, :] ** power[:, None]
    scale = np.concatenate([scale, scale], -1)
    cos, sin = np.cos(freqs), np.sin(freqs)
    sgn = np.where(np.arange(DIM_HEAD) < 32, -1.0, 1.0)
    cq1 = (cos * scale).T
    cq2 = (sin * scale).T * sgn[:, None]
    ck1 = (cos / scale).T
    ck2 = (sin / scale).T * sgn[:, None]
    def dup(a):
        return cc(np.concatenate([a, a], axis=0))
    qs = ins["L2_ffn_q_scale"]
    ks = ins["L2_ffn_k_scale"]
    p = (np.arange(64) + 32) % 64
    def sc2(a):
        return cc(np.tile(a.T, (2, 1)))
    kj = np.arange(128)[:, None]
    v = np.arange(1408)[None, :]
    band = ((v >= 384 + kj) & (v <= 896 + kj)).astype(NPF32)
    sel2h = np.zeros((2, 128), NPF32)
    sel2h[0, 0:64] = 1.0
    sel2h[1, 64:128] = 1.0
    w1 = ins["L2_ffn_w1"]
    b1 = ins["L2_ffn_b1"]
    w1p = np.zeros((2, 1024, 5632), NPF32)
    b1p = np.zeros((2, 5632), NPF32)
    for i in range(22):
        na = min(128, 2730 - i * 128)
        w1p[:, :, (2 * i) * 128:(2 * i) * 128 + na] = w1[:, :, i * 128:i * 128 + na]
        w1p[:, :, (2 * i + 1) * 128:(2 * i + 1) * 128 + na] = \
            w1[:, :, 2730 + i * 128:2730 + i * 128 + na]
        b1p[:, (2 * i) * 128:(2 * i) * 128 + na] = b1[:, i * 128:i * 128 + na]
        b1p[:, (2 * i + 1) * 128:(2 * i + 1) * 128 + na] = \
            b1[:, 2730 + i * 128:2730 + i * 128 + na]
    w2 = ins["L2_ffn_w2"]
    w2p = np.zeros((2, 2816, 1024), NPF32)
    w2p[:, 0:2730, :] = w2
    return {
        "l2f_wqkv": cc(ins["L2_ffn_wqkv"].reshape(2 * 1024, 1536)),
        "l2f_wo": cc(ins["L2_ffn_wo"].reshape(2 * 512, 1024)),
        "l2f_qs": sc2(qs), "l2f_qsp": sc2(qs[:, p]),
        "l2f_ks": sc2(ks), "l2f_ksp": sc2(ks[:, p]),
        "l2f_cq1": dup(cq1), "l2f_cq2": dup(cq2),
        "l2f_ck1": dup(ck1), "l2f_ck2": dup(ck2),
        "l2f_lng": cc(ins["L2_ffn_ln_g"].reshape(2, 8, 128).transpose(2, 0, 1).reshape(128, 16)),
        "l2f_lnb": cc(ins["L2_ffn_ln_b"].reshape(2, 8, 128).transpose(2, 0, 1).reshape(128, 16)),
        "l2f_w1": cc(w1p.reshape(2 * 1024, 5632)),
        "l2f_b1": cc(b1p.reshape(2, 44, 128).transpose(2, 0, 1).reshape(128, 88)),
        "l2f_w2": cc(w2p.reshape(2 * 2816, 1024)),
        "l2f_b2": cc(ins["L2_ffn_b2"].reshape(2, 8, 128).transpose(2, 0, 1).reshape(128, 16)),
        "l2f_band": cc(band),
        "cst_sel2h": sel2h,
        "cst_ident_b": np.eye(128, dtype=NPF32).astype(ml_dtypes.bfloat16),
    }


def prep_l2a(ins):
    cw = ins["L2_attn_conv_w"]
    cb = ins["L2_attn_conv_b"]
    convw = np.zeros((128, 256), NPF32)
    convb = np.zeros((128, 64), NPF32)
    dtbt = np.zeros((128, 64), NPF32)
    dskt = np.zeros((128, 64), NPF32)
    alog = np.zeros((128, 1024), NPF32)
    for li in range(4):
        for c in range(16):
            for p in range(128):
                convw[p, li * 64 + c * 4: li * 64 + c * 4 + 4] = cw[li, c * 128 + p]
            convb[:, li * 16 + c] = cb[li, c * 128:(c + 1) * 128]
            dtbt[:, li * 16 + c] = ins["L2_attn_dt_proj_b"][li, c * 128:(c + 1) * 128]
            dskt[:, li * 16 + c] = ins["L2_attn_D_skip"][li, c * 128:(c + 1) * 128]
            alog[:, li * 256 + c * 16: li * 256 + (c + 1) * 16] = \
                ins["L2_attn_A_log"][li, c * 128:(c + 1) * 128, :]
    sel = np.zeros((32, 4096), NPF32)
    for n in range(16):
        sel[n, n * 128:(n + 1) * 128] = 1.0
        sel[16 + n, 2048 + n * 128:2048 + (n + 1) * 128] = 1.0
    lnw = np.zeros((128, 32), NPF32)
    lnb = np.zeros((128, 32), NPF32)
    for li in range(4):
        for c in range(8):
            lnw[:, li * 8 + c] = ins["L2_attn_ln_w"][li, c * 128:(c + 1) * 128]
            lnb[:, li * 8 + c] = ins["L2_attn_ln_b"][li, c * 128:(c + 1) * 128]
    return {
        "l2a_inw": cc(ins["L2_attn_in_proj_w"].transpose(0, 2, 1).reshape(4 * 1024, 4096)),
        "l2a_convw": cc(convw), "l2a_convb": cc(convb),
        "l2a_xpw": cc(ins["L2_attn_x_proj_w"].transpose(0, 2, 1).reshape(4 * 2048, 96)),
        "l2a_dtw": cc(ins["L2_attn_dt_proj_w"].transpose(0, 2, 1).reshape(4 * 64, 2048)),
        "l2a_dtb": cc(dtbt), "l2a_alog": cc(alog), "l2a_dsk": cc(dskt),
        "l2a_outw": cc(ins["L2_attn_out_proj_w"].transpose(0, 2, 1).reshape(4 * 2048, 1024)),
        "l2a_lnw": cc(lnw), "l2a_lnb": cc(lnb),
        "l2a_sel": sel.astype(ml_dtypes.bfloat16),
    }


def prep_l1a(ins):
    cw = np.zeros((128, 96), NPF32)
    cb = np.zeros((128, 24), NPF32)
    for qkv, (w_, b_) in enumerate([(ins["L1_attn_qcw"], ins["L1_attn_qcb"]),
                                    (ins["L1_attn_kcw"], ins["L1_attn_kcb"]),
                                    (ins["L1_attn_vcw"], ins["L1_attn_vcb"])]):
        for c in range(8):
            for p in range(128):
                cw[p, qkv * 32 + c * 4: qkv * 32 + c * 4 + 4] = w_[c * 128 + p]
            cb[:, qkv * 8 + c] = b_[c * 128:(c + 1) * 128]
    p = np.arange(128)
    alog = np.zeros((128, 8), NPF32)
    dtb = np.zeros((128, 8), NPF32)
    for hp in range(8):
        h = 2 * hp + p // 64
        alog[:, hp] = ins["L1_attn_A_log"][h]
        dtb[:, hp] = ins["L1_attn_dt_bias"][(h) * 64 + p % 64]
    eye = np.vstack([np.eye(64, dtype=NPF32)] * 2)
    s_ = np.arange(64)[:, None]
    t_ = np.arange(64)[None, :]
    w8 = (s_ // 8) == (t_ // 8)
    m8neg = np.vstack([(-1.0 * (w8 & (s_ < t_))).astype(NPF32)] * 2)
    m8incl = np.vstack([(1.0 * (w8 & (s_ <= t_))).astype(NPF32)] * 2)
    moff = np.vstack([(1.0 * ((s_ // 8) < (t_ // 8))).astype(NPF32)] * 2)
    selb = np.zeros((16, 1024), NPF32)
    for hp in range(8):
        for pp in range(128):
            selb[2 * hp + pp // 64, hp * 128 + pp] = 1.0
    return {
        "l1a_wq": cc(ins["L1_attn_Wq"]),
        "l1a_wk": cc(ins["L1_attn_Wk"]),
        "l1a_wv": cc(ins["L1_attn_Wv"]),
        "l1a_wf1": cc(ins["L1_attn_Wf1"]),
        "l1a_wf2": cc(ins["L1_attn_Wf2"]),
        "l1a_wbmat": cc(ins["L1_attn_Wb"]),
        "l1a_wg1": cc(ins["L1_attn_Wg1"]),
        "l1a_wg2": cc(ins["L1_attn_Wg2"]),
        "l1a_wo": cc(ins["L1_attn_Wo"]),
        "l1a_cw": cc(cw), "l1a_cb": cc(cb),
        "l1a_alog": cc(alog), "l1a_dtb": cc(dtb),
        "l1a_onw": cc(np.tile(ins["L1_attn_o_norm_w"], 2).reshape(128, 1)),
        "cst_ident_b": np.eye(128, dtype=NPF32).astype(ml_dtypes.bfloat16),
        "l1a_eye": cc(eye).astype(ml_dtypes.bfloat16),
        "l1a_m8neg": cc(m8neg).astype(ml_dtypes.bfloat16),
        "l1a_m8incl": cc(m8incl).astype(ml_dtypes.bfloat16),
        "l1a_moff": cc(moff).astype(ml_dtypes.bfloat16),
        "l1a_selb": cc(selb).astype(ml_dtypes.bfloat16),
        "l1a_bg2": cc(ins["L1_attn_bg2"].reshape(8, 128).T),
        "cst_sel2h": np.concatenate([np.concatenate([np.ones((1, 64), NPF32), np.zeros((1, 64), NPF32)], 1),
                                     np.concatenate([np.zeros((1, 64), NPF32), np.ones((1, 64), NPF32)], 1)], 0),
        "cst_ident_f": np.eye(128, dtype=NPF32),
    }


from concourse.bass_utils import run_bass_kernel_spmd

_BLOCKS = ["l0a", "l0f", "l1a", "l1f", "l2a", "l2f", "l3a", "l3f"]
_EMIT = {"l1f": emit_l1_ffn, "l3a": emit_l3_attn, "l3f": emit_l3_ffn,
         "l0f": emit_l0_ffn, "l0a": emit_l0_attn, "l2f": emit_l2_ffn,
         "l2a": emit_l2_attn, "l1a": emit_l1_attn}
_PERSIST = {"xf", "xb", "BBIG", "gw_stg0", "gw_bf0", "c_ident_f", "c_ident_b",
            "ln_sqb", "ln_rows", "l2f_nrow", "c_sel2h", "c_ones2h", "c_ones64",
            "c_ones_row", "c_ones_rowb"}


def _persistent(name):
    return name in _PERSIST or name.startswith("cf_") or name.startswith("cb_")


def _build_fused_program(allw):
    import contextlib
    import ml_dtypes
    nc = bass.Bass("TRN2", target_bir_lowering=False)
    es = contextlib.ExitStack()
    k = KCtx(nc, es)
    x_in = nc.dram_tensor("x_in", [1024, NT], F32, kind="ExternalInput").ap()
    x_out = nc.dram_tensor("x_out", [1024, NT], F32, kind="ExternalOutput").ap()
    waps = {}
    for name, arr in allw.items():
        dt = BF16 if arr.dtype == ml_dtypes.bfloat16 else F32
        waps[name] = nc.dram_tensor(name, list(arr.shape), dt,
                                    kind="ExternalInput").ap()
    k.arena = bass.AutoArena(nc)
    k.arena_persist = _persistent
    k.arena_epoch = 0
    xf = k.t("xf", (128, 8 * NT), F32)
    xb = k.t("xb", (128, 8 * NT), BF16)
    gld = k.group("gpsimd")
    for c in range(8):
        k.dma("gpsimd", lambda e, c=c: e.dma_start(
            out=xf[:, c * NT:(c + 1) * NT], in_=x_in[c * 128:(c + 1) * 128, :]),
            [], ["xf"], gld)
    for c in range(8):
        k.s(lambda e, c=c: e.activation(out=xb[:, c * NT:(c + 1) * NT],
                                        in_=xf[:, c * NT:(c + 1) * NT],
                                        func=AF.Copy), ["xf"], ["xb"])
    k.nbb = 2
    for bi, tag in enumerate(_BLOCKS):
        nt0 = set(k.tiles)
        op0 = len(k.ctx.ops)
        _EMIT[tag](k, waps)
        for name in list(k.tiles):
            if name not in nt0 and not _persistent(name):
                k.free_tile(name)
        k._retired = []
        k.arena.addr = 0
        k.arena_epoch += 1
        if bi < len(_BLOCKS) - 1:
            keys = set()
            for o in k.ctx.ops[op0:]:
                keys.update(o.reads)
                keys.update(o.writes)
            ep = f"EP_{_BLOCKS[bi + 1]}"
            k.ctx.extra_reads = []
            k.ctx.op("sync", lambda e: e.nop(), [], sorted(keys) + [ep])
            k.ctx.extra_reads = [ep]
    k.ctx.extra_reads = []
    gst = k.group("gpsimd")
    for c in range(8):
        k.dma("gpsimd", lambda e, c=c: e.dma_start(
            out=x_out[c * 128:(c + 1) * 128, :], in_=xf[:, c * NT:(c + 1) * NT]),
            ["xf"], [f"xo{c}"], gst)
    k.ctx.op("sync", lambda e: e.nop(), [f"xo{c}" for c in range(8)], [])
    k.arena.allocate(side="right")
    with nc.Block() as block:
        k.ctx.emit(block)
    es.close()
    return nc


_PROG = None


def kernel(**inputs):
    global _PROG
    x = np.asarray(inputs["L0_attn_inputs"], dtype=np.float32)
    allw = prep_common()
    for tag in _BLOCKS:
        allw.update(globals()["prep_" + tag](inputs))
    if _PROG is None:
        _PROG = _build_fused_program(allw)
    in_maps = []
    for b in range(8):
        m = {"x_in": np.ascontiguousarray(x[b].T)}
        m.update(allw)
        in_maps.append(m)
    res = run_bass_kernel_spmd(_PROG, in_maps, core_ids=list(range(8)))
    return np.stack([np.asarray(r["x_out"]).T for r in res.results]).astype(np.float32)
